# version 02
# baseline (speedup 1.0000x reference)
import numpy as np
import ml_dtypes
from contextlib import ExitStack

import concourse.bass as bass
import concourse.mybir as mybir
from concourse.bass_utils import run_bass_kernel_spmd

_DEBUG_DUMP = False

F32 = mybir.dt.float32
BF16 = mybir.dt.bfloat16
U8 = mybir.dt.uint8
AF = mybir.ActivationFunctionType
ALU = mybir.AluOpType
AX = mybir.AxisListType

B, S, D = 2, 4096, 2048
NJ, QT = 8, 128
T = NJ * QT
NH, HD = 16, 128
SCALE = float(HD) ** -0.5
NT, DS, GRID = 64, 32, 16
LNEPS = 1e-5
BIG = 1.0e9
FBIG = 1.0e30
EXPSHIFT = 20.0
NKT = 32
CCQ = 18


def _dbg():
    return _DEBUG_DUMP


def build_nc(stop=3):
    nc = bass.Bass("TRN2", target_bir_lowering=False, debug=False,
                   detect_race_conditions=True)

    d_x = nc.dram_tensor("x", [T, D], F32, kind="ExternalInput").ap()
    d_wqkv = nc.dram_tensor("wqkv", [D, 2304], F32, kind="ExternalInput").ap()
    d_bqkv = nc.dram_tensor("bqkv", [2304], F32, kind="ExternalInput").ap()
    d_wproj = nc.dram_tensor("wproj", [D, D], F32, kind="ExternalInput").ap()
    d_bproj = nc.dram_tensor("bproj", [1, D], F32, kind="ExternalInput").ap()
    d_gamma = nc.dram_tensor("gamma", [1, D], F32, kind="ExternalInput").ap()
    d_beta = nc.dram_tensor("beta", [1, D], F32, kind="ExternalInput").ap()
    d_bases = nc.dram_tensor("bases", [NT, DS * GRID], F32, kind="ExternalInput").ap()
    d_slopes = nc.dram_tensor("slopes", [NT, DS * GRID], F32, kind="ExternalInput").ap()
    d_oscale = nc.dram_tensor("oscale", [NT, 1], F32, kind="ExternalInput").ap()
    d_msk = nc.dram_tensor("msk", [NKT * QT, QT], BF16, kind="ExternalInput").ap()
    d_cf = nc.dram_tensor("constf", [128, 256], F32, kind="ExternalInput").ap()
    d_idnb = nc.dram_tensor("idnb", [128, 128], BF16, kind="ExternalInput").ap()
    d_y = nc.dram_tensor("y", [T, D], F32, kind="ExternalOutput").ap()
    dbg = {}
    if _dbg():
        for nm, shp in [("dSTAT", [128, 16]), ("dAFC", [128, 48]),
                        ("dOHALL", [128, 512]), ("dXS", [128, 256]),
                        ("dMNXS", [128, 32]), ("dARB", [128, 2]),
                        ("dTKN", [128, 16]), ("dSS", [128, 2048]),
                        ("dGAM", [128, 2048]), ("dBET", [128, 2048]),
                        ("dDBG1", [128, 16]), ("dDBG2", [128, 1024]), ("dSQS", [128, 8]),
                        ("dDBG3", [128, 1024])]:
            dbg[nm] = nc.dram_tensor(nm, shp, F32, kind="ExternalOutput").ap()

    d_ssrow = nc.dram_tensor("ssrow", [NT, DS], F32)
    d_ar_in = [nc.dram_tensor(f"ar_in{i}", [NT, 2], F32) for i in range(2)]
    d_ar_out = [nc.dram_tensor(f"ar_out{i}", [NT, 2], F32) for i in range(2)]
    d_kv_in = nc.dram_tensor("kv_in", [2 * 128, T], BF16)
    d_kv_out = nc.dram_tensor("kv_out", [4 * 2 * 128, T], BF16)

    est = ExitStack()
    with est:
        def sbt(name, shape, dt):
            return est.enter_context(nc.sbuf_tensor(name, shape, dt))

        X = sbt("X", [128, NJ * D], F32)
        MSK = sbt("MSK", [128, NKT * QT], BF16)
        SS128 = sbt("SS128", [128, D], BF16)
        GAM = sbt("GAM", [128, D], F32)
        BET = sbt("BET", [128, D], F32)
        CF = sbt("CF", [128, 256], F32)
        IDNB = sbt("IDNB", [128, 128], BF16)
        SL64 = sbt("SL64", [128, DS * GRID], F32)
        TBP = sbt("TBP", [128, DS * GRID], F32)
        MISC = sbt("MISC", [128, 4], F32)
        BQ = sbt("BQ", [128, CCQ], F32)
        BPR = sbt("BPR", [1, D], BF16)
        ONE1 = sbt("ONE1", [33, 128], F32)
        ONEB2 = sbt("ONEB2", [128, 2], BF16)
        ONEBR = sbt("ONEBR", [1, 128], BF16)
        BIGT = sbt("BIGT", [128, 64], F32)
        NBIGT = sbt("NBIGT", [128, 64], F32)
        R1 = sbt("R1", [128, 16384], BF16)
        R2 = sbt("R2", [128, 18432], BF16)
        KT = sbt("KT", [128, NKT * QT], BF16)
        VT = sbt("VT", [128, NKT * QT], BF16)
        PT = [sbt(f"PT{i}", [128, 1024], BF16) for i in range(2)]
        QSQ = sbt("QSQ", [128, 1024], BF16)
        VTRB = sbt("VTRB", [128, 1024], F32)
        TB2 = sbt("TB2", [128, DS * GRID], F32)
        TS2 = sbt("TS2", [128, DS * GRID], F32)
        OHALL = sbt("OHALL", [128, NJ * NT], F32)
        XS = sbt("XS", [128, NJ * DS], F32)
        MNXS = sbt("MNXS", [128, NJ * 4], F32)
        STAT = sbt("STAT", [128, 16], F32)
        SQS = sbt("SQS", [128, 8], F32)
        AFC = sbt("AFC", [128, 48], F32)
        TKN = sbt("TKN", [128, 16], F32)
        TSC = sbt("TSC", [128, 64], F32)
        CS8 = sbt("CS8", [128, 8], F32)
        MM1 = sbt("MM1", [128, 64], F32)
        MMIN = sbt("MMIN", [128, 64], F32)
        MMAX = sbt("MMAX", [128, 64], F32)
        EQM = sbt("EQM", [128, 64], U8)
        MG256 = sbt("MG256", [128, 256], U8)
        SP1 = sbt("SP1", [128, DS], F32)
        SP2 = sbt("SP2", [128, DS], F32)
        BAC = sbt("BAC", [128, DS], F32)
        SAC = sbt("SAC", [128, DS], F32)
        T64A = sbt("T64A", [128, 128], F32)
        SC1 = sbt("SC1", [128, 1], F32)
        SC2 = sbt("SC2", [128, 1], F32)
        ARB = sbt("ARB", [128, 2], F32)
        OSC = sbt("OSC", [128, 1], F32)
        BB = sbt("BB", [128, 1], F32)
        KQM = sbt("KQM", [1, 8], F32)
        DBG1 = sbt("DBG1", [128, 16], F32)

        WBUF = [KT[:, 0:2048], KT[:, 2048:4096]]
        XNv = [R1[:, 0:4096].bitcast(F32), R1[:, 4096:8192].bitcast(F32)]
        TBALL = R1[:, 8192:16384].bitcast(F32)
        SCTB = R2[:, 8192:12288].bitcast(F32)
        TSALL = R2[:, 4096:8192]
        SCT = R2[:, 0:4096].bitcast(F32)
        SGN = [R2[:, 8192 + i * 4096: 8192 + i * 4096 + 2048] for i in range(2)]
        QQB = [R2[:, 10240 + i * 4096: 10240 + i * 4096 + 2048] for i in range(2)]
        GROW = R2[:, 0:4096].bitcast(F32)[0:1, :]
        BROW = R2[:, 4096:8192].bitcast(F32)[0:1, :]
        XN3B = [R1[:, 0:8192], R1[:, 8192:16384]]
        X3Fh = R1[:, 8192:16384]
        AOB = R1
        QKVB = R2
        WPB = [R2[:, 0:8192], R2[:, 8192:16384]]
        OHF = VTRB
        RB = VTRB
        VTO = VTRB[:, 0:512].bitcast(BF16)
        QSQF = QSQ.bitcast(F32)
        G64 = QSQF
        SREC = [QSQF[0:1, 0:512], QSQF[32:33, 0:512]]
        PSBKF = None

        PSA = est.enter_context(nc.psum_tensor("PSA", [128, 1024], F32))
        PSB = est.enter_context(nc.psum_tensor("PSB", [128, 1024], F32))
        PSC = est.enter_context(nc.psum_tensor("PSC", [128, 1024], F32))
        PSD = est.enter_context(nc.psum_tensor("PSD", [128, 512], F32))
        PSBK = est.enter_context(nc.psum_tensor("PSBK", [128, 1024], BF16))
        PSBKF = PSBK.ap().bitcast(F32)

        sems = {}
        for s_ in ("di", "dw", "dwa", "dwb", "dk", "cc", "t", "v", "s", "g",
                   "so"):
            sems[s_] = est.enter_context(nc.semaphore(s_))
        cnt = {k: 0 for k in sems}
        ops = {e: [] for e in ("sync", "vector", "scalar", "tensor", "gpsimd")}

        def on(eng, fn):
            ops[eng].append(fn)

        def ewait(eng, s_, val=None):
            v = cnt[s_] if val is None else val
            if v <= 0:
                return
            on(eng, lambda e, s_=s_, v=v: e.wait_ge(sems[s_], v))

        def edma(eng, s_, out, in_, slow=False):
            cnt[s_] += 16
            if slow:
                on(eng, lambda e, out=out, in_=in_, s_=s_:
                   e.dma_start(out=out, in_=in_, allow_slow_non_contiguous=True)
                   .then_inc(sems[s_], 16))
            else:
                on(eng, lambda e, out=out, in_=in_, s_=s_:
                   e.dma_start(out=out, in_=in_).then_inc(sems[s_], 16))

        def vsync():
            ewait("vector", "v", cnt["v"])

        def vrun(fn):
            run("vector", fn, inc="v")

        def run(eng, fn, inc=None):
            if eng == "vector":
                ewait("vector", "v", cnt["v"])
                if inc is None:
                    inc = "v"
            if inc is not None:
                cnt[inc] += 1
            def _f(e, fn=fn, inc=inc):
                r = fn(e)
                if inc is not None:
                    r.then_inc(sems[inc], 1)
            on(eng, _f)

        IOTA64 = CF[:, 0:64]
        IOTAD8 = CF[:, 64:128]
        IDNF = CF[:, 128:256]

        edma("sync", "di", X[:].rearrange("p (j d) -> p j d", d=D),
             d_x.rearrange("(j p) d -> p j d", p=128))
        edma("sync", "di", MSK[:].rearrange("p (n q) -> p n q", q=QT),
             d_msk.rearrange("(n p) q -> p n q", p=128))
        edma("sync", "di", CF[:], d_cf)
        edma("sync", "di", IDNB[:], d_idnb)
        edma("sync", "di", SL64[0:NT, :], d_slopes)
        edma("sync", "di", TBP[0:NT, :], d_bases)
        edma("sync", "di", OSC[0:NT, :], d_oscale)
        edma("sync", "di", BQ[:], d_bqkv.rearrange("(cc p) -> p cc", p=128),
             slow=True)
        ld_all = cnt["di"]
        edma("gpsimd", "dw", GROW, d_gamma)
        edma("gpsimd", "dw", BROW, d_beta)
        edma("gpsimd", "dw", BPR[:], d_bproj)
        g_rows = cnt["dw"]

        def _init(e):
            e.memset(ONE1[:], 1.0)
            e.memset(ONEB2[:], 1.0)
            e.memset(ONEBR[:], 1.0)
            e.memset(BIGT[:], BIG)
            e.memset(NBIGT[:], -BIG)
            return e.memset(MISC[:], 0.0)
        run("gpsimd", _init, inc="g")
        g_init = cnt["g"]

        ewait("tensor", "dw", g_rows)
        ewait("tensor", "g", g_init)
        for part in range(2):
            row = GROW if part == 0 else BROW
            dst = GAM if part == 0 else BET
            for h in range(2):
                ewait("tensor", "v", cnt["v"])
                for q in range(2):
                    run("tensor", lambda e, row=row, h=h, q=q: e.matmul(
                        PSA[:, q * 512:(q + 1) * 512], ONE1[0:1, :],
                        row[0:1, h * 1024 + q * 512: h * 1024 + (q + 1) * 512],
                        start=True, stop=True), inc="t" if q == 1 else None)
                ewait("vector", "t", cnt["t"])
                run("vector", lambda e, dst=dst, h=h: e.tensor_copy(
                    out=dst[:, h * 1024:(h + 1) * 1024], in_=PSA[:, 0:1024]),
                    inc="v")
        v_gb = cnt["v"]

        ewait("vector", "di", ld_all)
        if _dbg():
            run("vector", lambda e: e.tensor_copy(
                out=VTRB[:], in_=X[:, 7 * 2048: 7 * 2048 + 1024]), inc="v")
            ewait("sync", "v", cnt["v"])
            edma("sync", "so", dbg["dDBG2"], VTRB[:])
        run("vector", lambda e: e.tensor_reduce(
            out=TSC[0:NT, 0:DS],
            in_=SL64[0:NT, :].rearrange("p (j g) -> p j g", g=GRID),
            axis=AX.X, op=ALU.add), inc="v")
        ewait("scalar", "v", cnt["v"])
        run("scalar", lambda e: e.activation(
            out=TSC[0:NT, 0:DS], in_=TSC[0:NT, 0:DS], func=AF.Sign), inc="s")
        s_ss = cnt["s"]
        ewait("gpsimd", "g", g_init)
        run("gpsimd", lambda e: e.iota(
            G64[0:NT, :], [[0, DS], [1, GRID]], channel_multiplier=0,
            allow_small_or_imprecise_dtypes=True), inc="g")
        g_g64 = cnt["g"]
        ewait("vector", "g", g_g64)
        vrun(lambda e: e.tensor_tensor(
            out=G64[0:NT, :], in0=SL64[0:NT, :], in1=G64[0:NT, :], op=ALU.mult))
        vrun(lambda e: e.tensor_tensor(
            out=TBP[0:NT, :], in0=TBP[0:NT, :], in1=G64[0:NT, :],
            op=ALU.subtract))
        v_tbp = cnt["v"]
        ewait("gpsimd", "s", s_ss)
        edma("gpsimd", "dk", d_ssrow.ap(), TSC[0:NT, 0:DS])
        ewait("gpsimd", "dk")
        edma("gpsimd", "dk", VTRB[0:1, 0:1024],
             d_ssrow.ap().rearrange("a b -> (a b)")[None, 0:1024])
        edma("gpsimd", "dk", VTRB[32:33, 0:1024],
             d_ssrow.ap().rearrange("a b -> (a b)")[None, 1024:2048])
        g_ssrow = cnt["dk"]
        ewait("tensor", "dk", g_ssrow)
        for h in range(2):
            hp = 0 if h == 0 else 32
            ewait("tensor", "v", cnt["v"])
            for q in range(2):
                run("tensor", lambda e, hp=hp, q=q: e.matmul(
                    PSB[:, q * 512:(q + 1) * 512], ONE1[hp:hp + 1, :],
                    VTRB[hp:hp + 1, q * 512:(q + 1) * 512],
                    start=True, stop=True), inc="t" if q == 1 else None)
            ewait("vector", "t", cnt["t"])
            run("vector", lambda e, h=h: e.tensor_copy(
                out=SS128[:, h * 1024:(h + 1) * 1024], in_=PSB[:, 0:1024]),
                inc="v")
        v_ss = cnt["v"]

        def ffn(ar_idx):
            ewait("vector", "t", cnt["t"])
            ewait("scalar", "t", cnt["t"])
            ewait("vector", "v", cnt["v"])
            ewait("scalar", "v", cnt["v"])
            ewait("scalar", "g", cnt["g"])
            ewait("vector", "g", cnt["g"])
            _vred = [0] * NJ
            for j in range(NJ):
                Xj = X[:, j * D:(j + 1) * D]
                if j >= 2:
                    ewait("scalar", "v", _vred[j - 2])
                run("scalar", lambda e, Xj=Xj, j=j: e.activation(
                    out=SGN[j % 2], in_=Xj, func=AF.Square), inc="s")
                s_sq = cnt["s"]
                vrun(lambda e, Xj=Xj, j=j: e.reduce_sum(
                    out=STAT[:, j:j + 1], in_=Xj, axis=AX.X))
                ewait("vector", "s", s_sq)
                vrun(lambda e, j=j: e.reduce_sum(
                    out=SQS[:, j:j + 1], in_=SGN[j % 2], axis=AX.X))
                _vred[j] = cnt["v"]
            MU = AFC[:, 0:8]; VAR = AFC[:, 8:16]; RS1 = AFC[:, 16:24]
            RS2 = AFC[:, 24:32]; AFj = AFC[:, 32:40]; CFj = AFC[:, 40:48]
            AY = STAT[:, 0:8]; CY = STAT[:, 8:16]
            vrun(lambda e: e.tensor_scalar(out=MU, in0=STAT[:, 0:8],
                                           scalar1=1.0 / D, scalar2=None,
                                           op0=ALU.mult))
            vrun(lambda e: e.tensor_scalar(out=VAR, in0=SQS[:], scalar1=1.0 / D,
                                           scalar2=None, op0=ALU.mult))
            vrun(lambda e: e.tensor_tensor(out=TSC[:, 0:8], in0=MU, in1=MU,
                                           op=ALU.mult))
            vrun(lambda e: e.tensor_tensor(out=VAR, in0=VAR, in1=TSC[:, 0:8],
                                           op=ALU.subtract))
            vrun(lambda e: e.tensor_scalar(out=RS1, in0=VAR, scalar1=LNEPS,
                                           scalar2=None, op0=ALU.add))
            vrun(lambda e: e.reciprocal(out=RS1, in_=RS1))
            ewait("scalar", "v", cnt["v"])
            run("scalar", lambda e: e.activation(out=RS1, in_=RS1, func=AF.Sqrt),
                inc="s")
            ewait("vector", "s", cnt["s"])
            vrun(lambda e: e.tensor_tensor(out=RS2, in0=RS1, in1=RS1, op=ALU.mult))
            vrun(lambda e: e.tensor_tensor(out=RS2, in0=RS2, in1=VAR, op=ALU.mult))
            vrun(lambda e: e.tensor_scalar(out=RS2, in0=RS2, scalar1=0.25,
                                           scalar2=LNEPS, op0=ALU.mult,
                                           op1=ALU.add))
            vrun(lambda e: e.reciprocal(out=RS2, in_=RS2))
            ewait("scalar", "v", cnt["v"])
            run("scalar", lambda e: e.activation(out=RS2, in_=RS2, func=AF.Sqrt),
                inc="s")
            ewait("vector", "s", cnt["s"])
            vrun(lambda e: e.tensor_tensor(out=AFj, in0=RS1, in1=RS2, op=ALU.mult))
            vrun(lambda e: e.tensor_scalar(out=AFj, in0=AFj, scalar1=0.5,
                                           scalar2=None, op0=ALU.mult))
            vrun(lambda e: e.tensor_tensor(out=CFj, in0=MU, in1=AFj, op=ALU.mult))
            vrun(lambda e: e.tensor_scalar(out=CFj, in0=CFj, scalar1=-1.0,
                                           scalar2=None, op0=ALU.mult))
            vrun(lambda e: e.tensor_tensor(out=CY, in0=MU, in1=RS1, op=ALU.mult))
            vrun(lambda e: e.tensor_scalar(out=CY, in0=CY, scalar1=-0.5,
                                           scalar2=None, op0=ALU.mult))
            vrun(lambda e: e.tensor_scalar(out=AY, in0=RS1, scalar1=0.5,
                                           scalar2=1.0, op0=ALU.mult, op1=ALU.add))

            s_sgn = [0] * NJ
            v_qq = [0] * NJ
            for j in range(NJ):
                Xj = X[:, j * D:(j + 1) * D]
                XN = XNv[j % 2]
                if j >= 2:
                    ewait("vector", "s", s_sgn[j - 2])
                vrun(lambda e, Xj=Xj, XN=XN, j=j: e.tensor_scalar(
                    out=XN, in0=Xj, scalar1=AFj[:, j:j + 1],
                    scalar2=CFj[:, j:j + 1], op0=ALU.mult, op1=ALU.add))
                vrun(lambda e, XN=XN: e.tensor_tensor(
                    out=XN, in0=XN, in1=GAM[:], op=ALU.mult))
                vrun(lambda e, XN=XN: e.tensor_tensor(
                    out=XN, in0=XN, in1=BET[:], op=ALU.add))
                v_xn = cnt["v"]
                ewait("scalar", "v", v_xn)
                if j >= 2:
                    ewait("scalar", "v", v_qq[j - 2])
                run("scalar", lambda e, XN=XN, j=j: e.activation(
                    out=SGN[j % 2], in_=XN, func=AF.Sign), inc="s")
                s_sgn[j] = cnt["s"]
                ewait("vector", "s", s_sgn[j])
                vrun(lambda e, j=j: e.tensor_tensor(
                    out=QQB[j % 2], in0=SGN[j % 2], in1=SS128[:], op=ALU.mult))
                v_qq[j] = cnt["v"]
                vrun(lambda e, j=j: e.tensor_reduce(
                    out=TSC[:, 0:64],
                    in_=QQB[j % 2].rearrange("p (t k) -> p t k", k=DS),
                    axis=AX.X, op=ALU.add))
                OHj = OHALL[:, j * 64:(j + 1) * 64]
                vrun(lambda e: e.tensor_reduce(
                    out=CS8[:], in_=TSC[:, 0:64].rearrange(
                        "p (c t) -> p c t", t=8), axis=AX.X, op=ALU.add))
                vrun(lambda e: e.reduce_max(out=SC1[:], in_=CS8[:], axis=AX.X))
                vrun(lambda e: e.tensor_scalar(out=EQM[:, 0:8], in0=CS8[:],
                                               scalar1=SC1[:], scalar2=None,
                                               op0=ALU.is_equal))
                vrun(lambda e: e.select(out=MM1[:, 0:8], mask=EQM[:, 0:8],
                                        on_true=IOTA64[:, 0:8],
                                        on_false=BIGT[:, 0:8], add_drain=True))
                vrun(lambda e: e.tensor_reduce(out=SC1[:], in_=MM1[:, 0:8],
                                               axis=AX.X, op=ALU.min))
                vrun(lambda e: e.tensor_scalar(out=MM1[:], in0=IOTAD8[:],
                                               scalar1=SC1[:], scalar2=65536.0,
                                               op0=ALU.is_equal, op1=ALU.mult))
                vrun(lambda e: e.tensor_tensor(out=MM1[:], in0=MM1[:],
                                               in1=TSC[:, 0:64], op=ALU.add))
                vrun(lambda e: e.reduce_max(out=SC2[:], in_=MM1[:], axis=AX.X))
                vrun(lambda e: e.tensor_scalar(out=EQM[:], in0=MM1[:],
                                               scalar1=SC2[:], scalar2=None,
                                               op0=ALU.is_equal))
                vrun(lambda e: e.select(out=MM1[:], mask=EQM[:],
                                        on_true=IOTA64[:], on_false=BIGT[:], add_drain=True))
                vrun(lambda e: e.tensor_reduce(out=SC1[:], in_=MM1[:],
                                               axis=AX.X, op=ALU.min))
                vrun(lambda e, OHj=OHj: e.tensor_scalar(
                    out=OHj, in0=IOTA64[:], scalar1=SC1[:], scalar2=None,
                    op0=ALU.is_equal))
                v_oh = cnt["v"]
                ewait("vector", "g", cnt["g"])
                vrun(lambda e, XN=XN, OHj=OHj: e.tensor_tensor(
                    out=SCT[:].rearrange("p (a b) -> p a b", b=DS),
                    in0=XN.rearrange("p (a b) -> p a b", b=DS),
                    in1=OHj.unsqueeze(2).broadcast_to([128, 64, DS]),
                    op=ALU.mult))
                vrun(lambda e, j=j: e.tensor_reduce(
                    out=XS[:, j * DS:(j + 1) * DS],
                    in_=SCT[:].rearrange("p (t k) -> p k t", k=DS),
                    axis=AX.X, op=ALU.add))
                vrun(lambda e, j=j: e.tensor_reduce(
                    out=TKN[:, j:j + 1], in_=XS[:, j * DS:(j + 1) * DS],
                    axis=AX.X, op=ALU.min))
                vrun(lambda e, j=j: e.tensor_reduce(
                    out=TKN[:, 8 + j:9 + j], in_=XS[:, j * DS:(j + 1) * DS],
                    axis=AX.X, op=ALU.max))
                vrun(lambda e, j=j, OHj=OHj: e.tensor_scalar(
                    out=EQM[:], in0=OHj, scalar1=0.5, scalar2=None, op0=ALU.is_gt))
                vrun(lambda e, j=j, OHj=OHj: e.select(
                    out=MM1[:], mask=EQM[:],
                    on_true=TKN[:, j:j + 1].broadcast_to([128, 64]),
                    on_false=BIGT[:], add_drain=True))
                if j == 0:
                    vrun(lambda e: e.tensor_copy(out=MMIN[:], in_=MM1[:]))
                else:
                    vrun(lambda e: e.tensor_tensor(out=MMIN[:], in0=MMIN[:],
                                                   in1=MM1[:], op=ALU.min))
                vrun(lambda e, j=j, OHj=OHj: e.select(
                    out=MM1[:], mask=EQM[:],
                    on_true=TKN[:, 8 + j:9 + j].broadcast_to([128, 64]),
                    on_false=NBIGT[:], add_drain=True))
                if j == 0:
                    vrun(lambda e: e.tensor_copy(out=MMAX[:], in_=MM1[:]))
                else:
                    vrun(lambda e: e.tensor_tensor(out=MMAX[:], in0=MMAX[:],
                                                   in1=MM1[:], op=ALU.max))
                ewait("tensor", "v", v_oh)
                run("tensor", lambda e, OHj=OHj: e.transpose(
                    PSD[0:64, 128:256], OHj, IDNF), inc="t")
                ewait("vector", "t", cnt["t"])
                vrun(lambda e, j=j: e.tensor_copy(
                    out=OHF[0:64, j * 128:(j + 1) * 128], in_=PSD[0:64, 128:256]))
            v_loopA = cnt["v"]

            ewait("tensor", "v", v_loopA)
            def _mmT(e):
                e.transpose(PSD[0:64, 128:256], MMIN[:], IDNF)
                return e.transpose(PSD[0:64, 256:384], MMAX[:], IDNF)
            run("tensor", _mmT, inc="t")
            ewait("vector", "t", cnt["t"])
            vrun(lambda e: e.tensor_copy(out=T64A[0:64, :], in_=PSD[0:64, 128:256]))
            vrun(lambda e: e.tensor_reduce(out=ARB[0:64, 0:1], in_=T64A[0:64, :],
                                           axis=AX.X, op=ALU.min))
            vrun(lambda e: e.tensor_copy(out=T64A[0:64, :], in_=PSD[0:64, 256:384]))
            vrun(lambda e: e.tensor_reduce(out=ARB[0:64, 1:2], in_=T64A[0:64, :],
                                           axis=AX.X, op=ALU.max))
            vrun(lambda e: e.tensor_scalar(out=ARB[0:64, 0:1], in0=ARB[0:64, 0:1],
                                           scalar1=-1.0, scalar2=None,
                                           op0=ALU.mult))
            ewait("gpsimd", "v", cnt["v"])
            edma("gpsimd", "dk", d_ar_in[ar_idx].ap(), ARB[0:64, 0:2])
            ewait("gpsimd", "dk")
            cnt["cc"] += 1
            on("gpsimd", lambda e, ar_idx=ar_idx: e.collective_compute(
                "AllReduce", ALU.max, replica_groups=[list(range(8))],
                ins=[d_ar_in[ar_idx].ap().opt()],
                outs=[d_ar_out[ar_idx].ap().opt()]).then_inc(sems["cc"], 1))
            ewait("gpsimd", "cc")
            edma("gpsimd", "dk", ARB[0:64, 0:2], d_ar_out[ar_idx].ap())
            ewait("vector", "dk", cnt["dk"])
            vrun(lambda e: e.tensor_scalar(out=MISC[0:64, 0:1], in0=ARB[0:64, 0:1],
                                           scalar1=-1.0, scalar2=None,
                                           op0=ALU.mult))
            vrun(lambda e: e.tensor_copy(out=MISC[0:64, 1:2], in_=ARB[0:64, 1:2]))
            vrun(lambda e: e.tensor_copy(out=MISC[0:64, 2:3], in_=OSC[0:64, :]))
            v_misc = cnt["v"]

            ewait("tensor", "v", max(v_misc, v_tbp))
            v_tabev = 0
            for j in range(NJ):
                OHFj = OHF[0:NT, j * 128:(j + 1) * 128]
                if v_tabev:
                    ewait("tensor", "v", v_tabev)
                run("tensor", lambda e, OHFj=OHFj: e.matmul(
                    PSA[:, 0:512], OHFj, TBP[0:NT, :], start=True, stop=True))
                run("tensor", lambda e, OHFj=OHFj: e.matmul(
                    PSB[:, 0:512], OHFj, SL64[0:NT, :], start=True, stop=True))
                run("tensor", lambda e, OHFj=OHFj: e.matmul(
                    PSD[:, 0:4], OHFj, MISC[0:NT, :], start=True, stop=True),
                    inc="t")
                ewait("vector", "t", cnt["t"])
                def _tabev(e, j=j):
                    e.tensor_copy(out=TBALL[:, j * 512:(j + 1) * 512],
                                  in_=PSA[:, 0:512])
                    e.tensor_copy(out=TSALL[:, j * 512:(j + 1) * 512],
                                  in_=PSB[:, 0:512])
                    return e.tensor_copy(out=MNXS[:, j * 4:(j + 1) * 4],
                                         in_=PSD[:, 0:4])
                run("vector", _tabev, inc="v")
                v_tabev = cnt["v"]

            MNv = MNXS[:].rearrange("p (j c) -> p j c", c=4)[:, :, 0:1]
            MXv = MNXS[:].rearrange("p (j c) -> p j c", c=4)[:, :, 1:2]
            SCv = MNXS[:].rearrange("p (j c) -> p j c", c=4)[:, :, 2:3]
            XSv = XS[:].rearrange("p (j k) -> p j k", k=DS)
            R8 = TSC[:, 8:16]; INV8 = TSC[:, 16:24]; SCH8 = TSC[:, 24:32]
            vsync()
            vrun(lambda e: e.tensor_tensor(
                out=R8.unsqueeze(2), in0=MXv, in1=MNv, op=ALU.subtract))
            vrun(lambda e: e.tensor_scalar(out=R8, in0=R8, scalar1=1e-8,
                                           scalar2=None, op0=ALU.add))
            vrun(lambda e: e.reciprocal(out=INV8, in_=R8))
            vrun(lambda e: e.tensor_scalar(out=SCH8.unsqueeze(2), in0=SCv,
                                           scalar1=0.5, scalar2=None,
                                           op0=ALU.mult))
            SP1v = SCT[:, 0:256]
            SP2v = SCT[:, 256:512]
            BACv = SCT[:, 512:768]
            SACv = SCT[:, 768:1024]
            MGv = EQM[:, 0:64]
            vrun(lambda e: e.tensor_tensor(
                out=SP1v.rearrange("p (j k) -> p j k", k=DS), in0=XSv,
                in1=MNv.broadcast_to([128, NJ, DS]), op=ALU.subtract))
            vrun(lambda e: e.tensor_tensor(
                out=SP1v.rearrange("p (j k) -> p j k", k=DS),
                in0=SP1v.rearrange("p (j k) -> p j k", k=DS),
                in1=INV8.unsqueeze(2).broadcast_to([128, NJ, DS]), op=ALU.mult))
            vrun(lambda e: e.tensor_scalar(out=SP1v, in0=SP1v,
                                           scalar1=1.0 - 1e-6, scalar2=0.0,
                                           op0=ALU.min, op1=ALU.max))
            vrun(lambda e: e.tensor_scalar(out=SP1v, in0=SP1v,
                                           scalar1=float(GRID), scalar2=None,
                                           op0=ALU.mult))
            TBv = TBALL[:].rearrange("p (j k g) -> p j k g", k=DS, g=GRID)
            TSv = TSALL[:].rearrange("p (j k g) -> p j k g", k=DS, g=GRID)
            vrun(lambda e: e.tensor_copy(
                out=BACv.rearrange("p (j k) -> p j k", k=DS), in_=TBv[:, :, :, 0]))
            vrun(lambda e: e.tensor_copy(
                out=SACv.rearrange("p (j k) -> p j k", k=DS), in_=TSv[:, :, :, 0]))
            for g in range(1, GRID):
                vrun(lambda e, g=g: e.tensor_scalar(
                    out=MG256[:], in0=SP1v, scalar1=float(g), scalar2=None,
                    op0=ALU.is_ge))
                vrun(lambda e, g=g: e.select(
                    out=BACv.rearrange("p (j k) -> p j k", k=DS),
                    mask=MG256[:].rearrange("p (j k) -> p j k", k=DS),
                    on_true=TBv[:, :, :, g],
                    on_false=BACv.rearrange("p (j k) -> p j k", k=DS),
                    add_drain=True))
                vrun(lambda e, g=g: e.select(
                    out=SACv.rearrange("p (j k) -> p j k", k=DS),
                    mask=MG256[:].rearrange("p (j k) -> p j k", k=DS),
                    on_true=TSv[:, :, :, g],
                    on_false=SACv.rearrange("p (j k) -> p j k", k=DS),
                    add_drain=True))
            vrun(lambda e: e.tensor_tensor(out=SP2v, in0=SACv, in1=SP1v,
                                           op=ALU.mult))
            vrun(lambda e: e.tensor_tensor(out=SP2v, in0=SP2v, in1=BACv,
                                           op=ALU.add))
            vrun(lambda e: e.tensor_tensor(
                out=SP2v.rearrange("p (j k) -> p j k", k=DS),
                in0=SP2v.rearrange("p (j k) -> p j k", k=DS),
                in1=SCH8.unsqueeze(2).broadcast_to([128, NJ, DS]), op=ALU.mult))
            g_scat = 0
            for j in range(NJ):
                Xj = X[:, j * D:(j + 1) * D]
                OHj = OHALL[:, j * 64:(j + 1) * 64]
                if g_scat:
                    ewait("vector", "g", g_scat)
                vrun(lambda e, Xj=Xj, j=j: e.tensor_scalar(
                    out=Xj, in0=Xj, scalar1=AY[:, j:j + 1],
                    scalar2=CY[:, j:j + 1], op0=ALU.mult, op1=ALU.add))
                vrun(lambda e, OHj=OHj, j=j: e.tensor_tensor(
                    out=SCTB[:].rearrange("p (a b) -> p a b", b=DS),
                    in0=OHj.unsqueeze(2).broadcast_to([128, 64, DS]),
                    in1=SP2v.rearrange("p (j2 k) -> p j2 k", k=DS)[:, j, :]
                        .unsqueeze(1).broadcast_to([128, 64, DS]),
                    op=ALU.mult))
                ewait("gpsimd", "v", cnt["v"])
                run("gpsimd", lambda e, Xj=Xj: e.tensor_tensor(
                    out=Xj, in0=Xj, in1=SCTB[:], op=ALU.add), inc="g")
                g_scat = cnt["g"]
            ewait("vector", "g", cnt["g"])
            ewait("scalar", "g", cnt["g"])
            ewait("tensor", "g", cnt["g"])

        def attention():
            _vred = [0] * NJ
            for j in range(NJ):
                Xj = X[:, j * D:(j + 1) * D]
                if j >= 2:
                    ewait("scalar", "v", _vred[j - 2])
                run("scalar", lambda e, Xj=Xj, j=j: e.activation(
                    out=SGN[j % 2], in_=Xj, func=AF.Square), inc="s")
                s_sq = cnt["s"]
                run("vector", lambda e, Xj=Xj, j=j: e.reduce_sum(
                    out=STAT[:, j:j + 1], in_=Xj, axis=AX.X))
                ewait("vector", "s", s_sq)
                run("vector", lambda e, j=j: e.reduce_sum(
                    out=SQS[:, j:j + 1], in_=SGN[j % 2], axis=AX.X), inc="v")
                _vred[j] = cnt["v"]
            A3 = AFC[:, 0:8]; C3 = AFC[:, 8:16]
            ewait("vector", "s", cnt["s"])
            vsync()
            vrun(lambda e: e.tensor_scalar(out=A3, in0=SQS[:], scalar1=1.0 / D,
                                           scalar2=None, op0=ALU.mult))
            vrun(lambda e: e.tensor_scalar(out=C3, in0=STAT[:, 0:8],
                                           scalar1=1.0 / D, scalar2=None,
                                           op0=ALU.mult))
            vrun(lambda e: e.tensor_tensor(out=AFC[:, 16:24], in0=C3, in1=C3,
                                           op=ALU.mult))
            vrun(lambda e: e.tensor_tensor(out=A3, in0=A3, in1=AFC[:, 16:24],
                                           op=ALU.subtract))
            vrun(lambda e: e.tensor_scalar(out=A3, in0=A3, scalar1=LNEPS,
                                           scalar2=None, op0=ALU.add))
            vrun(lambda e: e.reciprocal(out=A3, in_=A3))
            ewait("scalar", "v", cnt["v"])
            run("scalar", lambda e: e.activation(out=A3, in_=A3, func=AF.Sqrt),
                inc="s")
            ewait("vector", "s", cnt["s"])
            vrun(lambda e: e.tensor_tensor(out=C3, in0=C3, in1=A3, op=ALU.mult))
            vrun(lambda e: e.tensor_scalar(out=C3, in0=C3, scalar1=-1.0,
                                           scalar2=None, op0=ALU.mult))
            v_l3 = cnt["v"]

            ewait("gpsimd", "v", v_l3)
            ewait("gpsimd", "s", cnt["s"])
            dw_cc = {}
            def wqkv_load(hf, cc):
                s_ = "dwa" if cc % 2 == 0 else "dwb"
                edma("gpsimd", s_, WBUF[cc % 2][:].rearrange(
                    "p (dc c) -> p dc c", c=128),
                    d_wqkv.rearrange("(dc p) c -> p dc c", p=128)[:, :, cc * 128:(cc + 1) * 128])
                dw_cc[(hf, cc)] = (s_, cnt[s_])
            wqkv_load(0, 0)
            wqkv_load(0, 1)
            s_x3f = 0
            s_evac_prev = [0, 0]
            s_qkv_ev = [0, 0]
            for hf in range(2):
                v_xn3 = 0
                for j4 in range(4):
                    j = hf * 4 + j4
                    Xj = X[:, j * D:(j + 1) * D]
                    if hf == 1 and j4 == 0:
                        ewait("vector", "t", cnt["t"])
                    vrun(lambda e, Xj=Xj, j=j, j4=j4: e.tensor_scalar(
                        out=XN3B[0][:, j4 * 2048:(j4 + 1) * 2048], in0=Xj,
                        scalar1=A3[:, j:j + 1], scalar2=C3[:, j:j + 1],
                        op0=ALU.mult, op1=ALU.add))
                v_xn3 = cnt["v"]
                X3F = XN3B[1]
                ewait("tensor", "v", v_xn3)
                for dc in range(16):
                    if s_evac_prev[0]:
                        ewait("tensor", "s", s_evac_prev[0])
                    for j4 in range(4):
                        run("tensor", lambda e, dc=dc, j4=j4: e.transpose(
                            PSBK[:, j4 * 128:(j4 + 1) * 128],
                            XN3B[0][:, j4 * 2048 + dc * 128: j4 * 2048 + (dc + 1) * 128],
                            IDNB[:]), inc="t" if j4 == 3 else None)
                    t_tp = cnt["t"]
                    ewait("scalar", "t", t_tp)
                    run("scalar", lambda e, dc=dc, X3F=X3F: e.copy(
                        out=X3F[:, dc * 512:(dc + 1) * 512],
                        in_=PSBK[:, 0:512]), inc="s")
                    s_evac_prev[0] = cnt["s"]
                s_x3f = cnt["s"]
                for cc in range(CCQ):
                    ewait("tensor", dw_cc[(hf, cc)][0], dw_cc[(hf, cc)][1])
                    ewait("tensor", "s", s_x3f)
                    half = cc % 2
                    if s_qkv_ev[half]:
                        ewait("tensor", "s", s_qkv_ev[half])
                    for dc in range(16):
                        run("tensor", lambda e, cc=cc, dc=dc, half=half, X3F=X3F:
                            e.matmul(
                                PSA[:, half * 512:(half + 1) * 512],
                                WBUF[cc % 2][:, dc * 128:(dc + 1) * 128],
                                X3F[:, dc * 512:(dc + 1) * 512],
                                start=(dc == 0), stop=(dc == 15)),
                            inc="t" if dc == 15 else None)
                    t_qkv = cnt["t"]
                    ewait("scalar", "t", t_qkv)
                    run("scalar", lambda e, cc=cc, hf=hf, half=half: e.activation(
                        out=QKVB[:, cc * 1024 + hf * 512: cc * 1024 + hf * 512 + 512],
                        in_=PSA[:, half * 512:(half + 1) * 512], func=AF.Identity,
                        bias=BQ[:, cc:cc + 1], scale=1.0), inc="s")
                    s_qkv_ev[half] = cnt["s"]
                    nhf, ncc = hf, cc + 2
                    if ncc >= CCQ:
                        nhf, ncc = hf + 1, ncc - CCQ
                    if nhf < 2 and (nhf, ncc) not in dw_cc:
                        ewait("gpsimd", "t", t_qkv)
                        wqkv_load(nhf, ncc)
            s_qkv_done = cnt["s"]

            ewait("tensor", "s", s_qkv_done)
            v_vto = 0
            for j in range(NJ):
                if v_vto:
                    ewait("tensor", "v", v_vto)
                run("tensor", lambda e, j=j: e.transpose(
                    PSBK[:, 0:128],
                    QKVB[:, 17 * 1024 + j * 128: 17 * 1024 + (j + 1) * 128],
                    IDNB[:]), inc="t")
                ewait("vector", "t", cnt["t"])
                run("vector", lambda e, j=j: e.tensor_copy(
                    out=VTO[:, j * 128:(j + 1) * 128],
                    in_=PSBK[:, 0:128]), inc="v")
                v_vto = cnt["v"]
            ewait("gpsimd", "v", v_vto)
            ewait("gpsimd", "s", s_qkv_done)
            edma("gpsimd", "dk", d_kv_in.ap()[0:128, :], QKVB[:, 16 * 1024:17 * 1024])
            edma("gpsimd", "dk", d_kv_in.ap()[128:256, :], VTO[:])
            ewait("gpsimd", "dk")
            cnt["cc"] += 1
            on("gpsimd", lambda e: e.collective_compute(
                "AllGather", ALU.bypass,
                replica_groups=[[0, 1, 2, 3], [4, 5, 6, 7]],
                ins=[d_kv_in.ap().opt()],
                outs=[d_kv_out.ap().opt()]).then_inc(sems["cc"], 1))
            ewait("gpsimd", "cc")
            kvo = d_kv_out.ap().rearrange("(r b p) t -> r b p t", b=2, p=128)
            edma("gpsimd", "dk",
                 KT[:].rearrange("p (s r q) -> p s r q", r=4, q=128),
                 kvo[:, 0, :, :].rearrange("r p (s q) -> p s r q", q=128))
            edma("gpsimd", "dk",
                 VT[:].rearrange("p (s r q) -> p s r q", r=4, q=128),
                 kvo[:, 1, :, :].rearrange("r p (s q) -> p s r q", q=128))
            dk_kv = cnt["dk"]

            v_qsq_prev = 0
            for dc in range(16):
                ewait("vector", "t", cnt["t"])
                vrun(lambda e, dc=dc: e.tensor_tensor(
                    out=QSQ[:], in0=QKVB[:, dc * 1024:(dc + 1) * 1024],
                    in1=QKVB[:, dc * 1024:(dc + 1) * 1024], op=ALU.mult))
                ewait("tensor", "v", cnt["v"])
                run("tensor", lambda e, dc=dc: e.matmul(
                    PSD[0:2, 0:512], ONEB2[:], QSQ[:, 0:512],
                    start=(dc == 0), stop=(dc == 15)))
                run("tensor", lambda e, dc=dc: e.matmul(
                    PSBKF[0:2, 0:512], ONEB2[:], QSQ[:, 512:1024],
                    start=(dc == 0), stop=(dc == 15)), inc="t")
            t_qn = cnt["t"]
            ewait("vector", "t", t_qn)
            def _qmax(e):
                e.tensor_reduce(out=KQM[0:1, 0:1], in_=PSD[0:1, 0:512], axis=AX.X,
                                op=ALU.max)
                return e.tensor_reduce(out=KQM[0:1, 1:2], in_=PSBKF[0:1, 0:512],
                                       axis=AX.X, op=ALU.max)
            run("vector", _qmax, inc="v")
            vrun(lambda e: e.tensor_tensor(
                out=KQM[0:1, 0:1], in0=KQM[0:1, 0:1], in1=KQM[0:1, 1:2],
                op=ALU.max))
            ewait("vector", "dk", dk_kv)
            for kc in range(4):
                vrun(lambda e, kc=kc: e.tensor_tensor(
                    out=QSQ[:], in0=KT[:, kc * 1024:(kc + 1) * 1024],
                    in1=KT[:, kc * 1024:(kc + 1) * 1024], op=ALU.mult))
                ewait("tensor", "v", cnt["v"])
                if kc > 0:
                    ewait("tensor", "v", v_kn)
                run("tensor", lambda e: e.matmul(
                    PSD[0:2, 0:512], ONEB2[:], QSQ[:, 0:512], start=True,
                    stop=True))
                run("tensor", lambda e: e.matmul(
                    PSBKF[0:2, 0:512], ONEB2[:], QSQ[:, 512:1024], start=True,
                    stop=True), inc="t")
                ewait("vector", "t", cnt["t"])
                def _kmaxA(e):
                    e.tensor_reduce(out=KQM[0:1, 2:3], in_=PSD[0:1, 0:512],
                                    axis=AX.X, op=ALU.max)
                    return e.tensor_reduce(out=KQM[0:1, 3:4], in_=PSBKF[0:1, 0:512],
                                           axis=AX.X, op=ALU.max)
                run("vector", _kmaxA, inc="v")
                vsync()
                vrun(lambda e: e.tensor_tensor(out=KQM[0:1, 2:3],
                                               in0=KQM[0:1, 2:3],
                                               in1=KQM[0:1, 3:4], op=ALU.max))
                if kc == 0:
                    vrun(lambda e: e.tensor_copy(out=KQM[0:1, 4:5],
                                                 in_=KQM[0:1, 2:3]))
                else:
                    vrun(lambda e: e.tensor_tensor(out=KQM[0:1, 4:5],
                                                   in0=KQM[0:1, 4:5],
                                                   in1=KQM[0:1, 2:3],
                                                   op=ALU.max))
                v_kn = cnt["v"]
            run("vector", lambda e: e.tensor_tensor(
                out=KQM[0:1, 5:6], in0=KQM[0:1, 0:1], in1=KQM[0:1, 4:5],
                op=ALU.mult), inc="v")
            ewait("scalar", "v", cnt["v"])
            run("scalar", lambda e: e.activation(out=KQM[0:1, 5:6],
                                                 in_=KQM[0:1, 5:6], func=AF.Sqrt),
                inc="s")
            ewait("vector", "s", cnt["s"])
            vrun(lambda e: e.tensor_scalar(
                out=KQM[0:1, 6:7], in0=KQM[0:1, 5:6], scalar1=-SCALE,
                scalar2=EXPSHIFT, op0=ALU.mult, op1=ALU.add))
            ewait("tensor", "v", cnt["v"])
            run("tensor", lambda e: e.matmul(
                PSD[:, 384:385], ONE1[0:1, :], KQM[0:1, 6:7], start=True,
                stop=True), inc="t")
            ewait("vector", "t", cnt["t"])
            run("vector", lambda e: e.tensor_copy(out=BB[:], in_=PSD[:, 384:385]),
                inc="v")
            v_bb = cnt["v"]

            ewait("tensor", "dk", dk_kv)
            ewait("tensor", "v", v_bb)
            ewait("scalar", "v", v_bb)
            SSTRIP = [PSA, PSC]
            exp_done = {}
            pv_done = {}
            v_aob = 0
            v_rb = 0
            for j in range(NJ):
                EJ = 4 * (j + 1)
                for hf in range(2):
                    for kt in range(EJ):
                        buf = kt % 2
                        strip = SSTRIP[buf]
                        prev = exp_done.get(buf)
                        if prev:
                            ewait("tensor", "s", prev)
                        if v_rb:
                            ewait("tensor", "v", v_rb)
                            v_rb = 0
                        for h2 in range(8):
                            h = hf * 8 + h2
                            run("tensor", lambda e, h=h, h2=h2, j=j, kt=kt,
                                strip=strip: e.matmul(
                                    strip[:, h2 * 128:(h2 + 1) * 128],
                                    KT[:, kt * 128:(kt + 1) * 128],
                                    QKVB[:, h * 1024 + j * 128: h * 1024 + (j + 1) * 128],
                                    start=True, stop=True),
                                inc="t" if h2 == 7 else None)
                        t_sc = cnt["t"]
                        ewait("scalar", "t", t_sc)
                        prevpv = pv_done.get(buf)
                        if prevpv:
                            ewait("scalar", "t", prevpv)
                        run("scalar", lambda e, buf=buf, strip=strip: e.activation(
                            out=PT[buf][:], in_=strip[:], func=AF.Exp,
                            bias=BB[:], scale=SCALE), inc="s")
                        exp_done[buf] = cnt["s"]
                        masked = kt >= 4 * j
                        if masked:
                            ewait("vector", "s", exp_done[buf])
                            run("vector", lambda e, buf=buf, kt=kt: e.tensor_tensor(
                                out=PT[buf][:].rearrange("p (h q) -> p h q", q=128),
                                in0=PT[buf][:].rearrange("p (h q) -> p h q", q=128),
                                in1=MSK[:, kt * 128:(kt + 1) * 128].unsqueeze(1)
                                    .broadcast_to([128, 8, 128]),
                                op=ALU.mult), inc="v")
                            v_msk = cnt["v"]
                        if masked:
                            ewait("tensor", "v", v_msk)
                        else:
                            ewait("tensor", "s", exp_done[buf])
                        if kt == 0 and v_aob:
                            ewait("tensor", "v", v_aob)
                        for h2 in range(8):
                            run("tensor", lambda e, h2=h2, kt=kt, buf=buf, EJ=EJ:
                                e.matmul(
                                    PSB[:, h2 * 128:(h2 + 1) * 128],
                                    VT[:, kt * 128:(kt + 1) * 128],
                                    PT[buf][:, h2 * 128:(h2 + 1) * 128],
                                    start=(kt == 0 and h2 % 4 == 0),
                                    stop=(kt == EJ - 1 and h2 % 4 == 3)))
                        run("tensor", lambda e, kt=kt, buf=buf, EJ=EJ: e.matmul(
                            PSD[0:2, 0:512], ONEB2[:], PT[buf][:, 0:512],
                            start=(kt == 0), stop=(kt == EJ - 1)))
                        run("tensor", lambda e, kt=kt, buf=buf, EJ=EJ: e.matmul(
                            PSBKF[0:2, 0:512], ONEB2[:], PT[buf][:, 512:1024],
                            start=(kt == 0), stop=(kt == EJ - 1)), inc="t")
                        pv_done[buf] = cnt["t"]
                    t_last = cnt["t"]
                    ewait("vector", "t", t_last)
                    vrun(lambda e: e.tensor_copy(out=SREC[0], in_=PSD[0:1, 0:512]))
                    vrun(lambda e: e.tensor_copy(out=SREC[1],
                                                 in_=PSBKF[0:1, 0:512]))
                    vrun(lambda e: e.reciprocal(out=SREC[0], in_=SREC[0]))
                    vrun(lambda e: e.reciprocal(out=SREC[1], in_=SREC[1]))
                    v_recd = cnt["v"]
                    ewait("tensor", "v", v_recd)
                    for q in range(2):
                        qp = 0 if q == 0 else 32
                        run("tensor", lambda e, q=q, qp=qp: e.matmul(
                            PSA[:, q * 512:(q + 1) * 512], ONE1[qp:qp + 1, :],
                            SREC[q], start=True, stop=True),
                            inc="t" if q == 1 else None)
                    t_rb = cnt["t"]
                    ewait("vector", "t", t_rb)
                    run("vector", lambda e: e.tensor_copy(out=RB[:],
                                                          in_=PSA[:, 0:1024]),
                        inc="v")
                    v_rb = cnt["v"]
                    aobv = AOB[:].rearrange("p (h t) -> p h t", t=1024)
                    run("vector", lambda e, j=j, hf=hf, aobv=aobv: e.tensor_tensor(
                        out=aobv[:, hf * 8:(hf + 1) * 8, j * 128:(j + 1) * 128],
                        in0=PSB[:].rearrange("p (h q) -> p h q", q=128),
                        in1=RB[:].rearrange("p (h q) -> p h q", q=128),
                        op=ALU.mult), inc="v")
                    v_aob = cnt["v"]
                    exp_done = {}
                    pv_done = {}

            dwp = {}
            def wp_load(wq):
                s_ = "dwa" if wq % 2 == 0 else "dwb"
                edma("gpsimd", s_, WPB[wq % 2][:].rearrange(
                    "p (dc c) -> p dc c", c=512),
                    d_wproj.rearrange("(dc p) c -> p dc c", p=128)[:, :, wq * 512:(wq + 1) * 512])
                dwp[wq] = (s_, cnt[s_])
            ewait("gpsimd", "v", v_aob)
            wp_load(0)
            wp_load(1)
            v_pe = [0, 0]
            t_wq_last = 0
            for wq in range(4):
                if wq >= 1 and wq + 1 < 4 and (wq + 1) not in dwp:
                    pass
                ewait("tensor", dwp[wq][0], dwp[wq][1])
                ewait("tensor", "v", v_aob)
                for j in range(NJ):
                    half = j % 2
                    if v_pe[half]:
                        ewait("tensor", "v", v_pe[half])
                    for dc in range(16):
                        run("tensor", lambda e, wq=wq, j=j, dc=dc, half=half:
                            e.matmul(
                                PSA[:, half * 512:(half + 1) * 512],
                                AOB[:, dc * 1024 + j * 128: dc * 1024 + (j + 1) * 128],
                                WPB[wq % 2][:, dc * 512:(dc + 1) * 512],
                                start=(dc == 0), stop=False))
                    run("tensor", lambda e, wq=wq, half=half: e.matmul(
                        PSA[:, half * 512:(half + 1) * 512], ONEBR[:],
                        BPR[0:1, wq * 512:(wq + 1) * 512],
                        start=False, stop=True), inc="t")
                    t_pp = cnt["t"]
                    ewait("vector", "t", t_pp)
                    run("vector", lambda e, wq=wq, j=j, half=half: e.tensor_tensor(
                        out=X[:, j * D + wq * 512: j * D + (wq + 1) * 512],
                        in0=X[:, j * D + wq * 512: j * D + (wq + 1) * 512],
                        in1=PSA[:, half * 512:(half + 1) * 512], op=ALU.add),
                        inc="v")
                    v_pe[half] = cnt["v"]
                t_wq_last = cnt["t"]
                if wq + 2 < 4:
                    ewait("gpsimd", "t", t_wq_last)
                    wp_load(wq + 2)
            ewait("vector", "v", cnt["v"])
            ewait("scalar", "v", cnt["v"])
            ewait("tensor", "v", cnt["v"])
            ewait("gpsimd", "v", cnt["v"])

        ewait("vector", "v", max(v_gb, v_ss))
        ffn(0)
        if stop >= 2:
            attention()
        if stop >= 3:
            ffn(1)

        ewait("sync", "v", cnt["v"])
        ewait("sync", "g", cnt["g"])
        ewait("sync", "s", cnt["s"])
        ewait("sync", "t", cnt["t"])
        edma("sync", "so", d_y.rearrange("(j p) d -> p j d", p=128),
             X[:].rearrange("p (j d) -> p j d", d=D))
        if _dbg():
            edma("sync", "so", dbg["dSTAT"], STAT[:])
            edma("sync", "so", dbg["dAFC"], AFC[:])
            edma("sync", "so", dbg["dOHALL"], OHALL[:])
            edma("sync", "so", dbg["dXS"], XS[:])
            edma("sync", "so", dbg["dMNXS"], MNXS[:])
            edma("sync", "so", dbg["dARB"], ARB[:, 0:2])
            edma("sync", "so", dbg["dTKN"], TKN[:])
            edma("gpsimd", "so", dbg["dSS"], SS128[:])
            edma("sync", "so", dbg["dGAM"], GAM[:])
            edma("sync", "so", dbg["dBET"], BET[:])
            edma("sync", "so", dbg["dDBG1"], DBG1[:])
            edma("sync", "so", dbg["dSQS"], SQS[:])
        ewait("sync", "so")

        block = est.enter_context(nc.Block())

        @block.sync
        def _(e):
            for f in ops["sync"]:
                f(e)

        @block.vector
        def _(e):
            for f in ops["vector"]:
                f(e)

        @block.scalar
        def _(e):
            for f in ops["scalar"]:
                f(e)

        @block.tensor
        def _(e):
            for f in ops["tensor"]:
                f(e)

        @block.gpsimd
        def _(e):
            for f in ops["gpsimd"]:
                f(e)

    return nc



def _host_constf():
    cf = np.zeros((128, 256), np.float32)
    cf[:, 0:64] = np.arange(64, dtype=np.float32)[None, :]
    cf[:, 64:128] = (np.arange(64, dtype=np.float32) // 8)[None, :]
    cf[:, 128:256] = np.eye(128, dtype=np.float32)
    return cf


def _host_masks(cid):
    m = np.zeros((NKT, QT, QT), np.float32)
    for kt in range(NKT):
        g = 4 * (kt // 4) + cid
        kpos = kt * QT + np.arange(QT)[:, None]
        qpos = g * QT + np.arange(QT)[None, :]
        m[kt] = (kpos <= qpos).astype(np.float32)
    return m.reshape(NKT * QT, QT).astype(ml_dtypes.bfloat16)


def kernel(L0_attn_x, L0_attn_W_qkv, L0_attn_b_qkv, L0_attn_W_proj,
           L0_attn_b_proj, L0_ffn_ln_gamma, L0_ffn_ln_beta,
           L0_ffn_spline_bases, L0_ffn_spline_slopes, L0_ffn_output_scale):
    x = np.ascontiguousarray(np.asarray(L0_attn_x, dtype=np.float32))
    wqkv = np.ascontiguousarray(np.asarray(L0_attn_W_qkv, dtype=np.float32))
    bqkv = np.ascontiguousarray(np.asarray(L0_attn_b_qkv, dtype=np.float32))
    wproj = np.ascontiguousarray(np.asarray(L0_attn_W_proj, dtype=np.float32))
    bproj = np.ascontiguousarray(np.asarray(L0_attn_b_proj, dtype=np.float32)).reshape(1, D)
    gamma = np.ascontiguousarray(np.asarray(L0_ffn_ln_gamma, dtype=np.float32)).reshape(1, D)
    beta = np.ascontiguousarray(np.asarray(L0_ffn_ln_beta, dtype=np.float32)).reshape(1, D)
    bases = np.ascontiguousarray(np.asarray(L0_ffn_spline_bases, dtype=np.float32)).reshape(NT, DS * GRID)
    slopes = np.ascontiguousarray(np.asarray(L0_ffn_spline_slopes, dtype=np.float32)).reshape(NT, DS * GRID)
    oscale = np.ascontiguousarray(np.asarray(L0_ffn_output_scale, dtype=np.float32)).reshape(NT, 1)

    cf = _host_constf()
    idn = np.eye(128, dtype=ml_dtypes.bfloat16)

    nc = build_nc(stop=_BUILD_STOP)
    in_maps = []
    for core in range(8):
        batch, cid = core // 4, core % 4
        xs = np.concatenate(
            [x[batch, (4 * j + cid) * QT:(4 * j + cid + 1) * QT, :]
             for j in range(NJ)], axis=0)
        in_maps.append({
            "x": np.ascontiguousarray(xs),
            "wqkv": wqkv, "bqkv": bqkv, "wproj": wproj, "bproj": bproj,
            "gamma": gamma, "beta": beta, "bases": bases, "slopes": slopes,
            "oscale": oscale, "msk": _host_masks(cid), "constf": cf,
            "idnb": idn,
        })
    res = run_bass_kernel_spmd(nc, in_maps, core_ids=list(range(8)))
    out = np.empty((B, S, D), np.float32)
    for core in range(8):
        batch, cid = core // 4, core % 4
        y = res.results[core]["y"]
        for j in range(NJ):
            g = 4 * j + cid
            out[batch, g * QT:(g + 1) * QT, :] = y[j * QT:(j + 1) * QT, :]
    return out


_BUILD_STOP = 3


# version 03
# speedup vs baseline: 1.2515x; 1.2515x over previous
import numpy as np
import ml_dtypes
from contextlib import ExitStack

import concourse.bass as bass
import concourse.mybir as mybir
from concourse.bass_utils import run_bass_kernel_spmd

_DEBUG_DUMP = False

F32 = mybir.dt.float32
BF16 = mybir.dt.bfloat16
U8 = mybir.dt.uint8
AF = mybir.ActivationFunctionType
ALU = mybir.AluOpType
AX = mybir.AxisListType

B, S, D = 2, 4096, 2048
NJ, QT = 8, 128
T = NJ * QT
NH, HD = 16, 128
SCALE = float(HD) ** -0.5
NT, DS, GRID = 64, 32, 16
LNEPS = 1e-5
BIG = 1.0e9
FBIG = 1.0e30
EXPSHIFT = 20.0
NKT = 32
CCQ = 18


def _dbg():
    return _DEBUG_DUMP


def build_nc(stop=3):
    nc = bass.Bass("TRN2", target_bir_lowering=False, debug=False,
                   detect_race_conditions=True)

    d_x = nc.dram_tensor("x", [T, D], F32, kind="ExternalInput").ap()
    d_wqkv = nc.dram_tensor("wqkv", [D, 2304], F32, kind="ExternalInput").ap()
    d_bqkv = nc.dram_tensor("bqkv", [2304], F32, kind="ExternalInput").ap()
    d_wproj = nc.dram_tensor("wproj", [D, D], F32, kind="ExternalInput").ap()
    d_bproj = nc.dram_tensor("bproj", [1, D], F32, kind="ExternalInput").ap()
    d_gamma = nc.dram_tensor("gamma", [1, D], F32, kind="ExternalInput").ap()
    d_beta = nc.dram_tensor("beta", [1, D], F32, kind="ExternalInput").ap()
    d_bases = nc.dram_tensor("bases", [NT, DS * GRID], F32, kind="ExternalInput").ap()
    d_slopes = nc.dram_tensor("slopes", [NT, DS * GRID], F32, kind="ExternalInput").ap()
    d_oscale = nc.dram_tensor("oscale", [NT, 1], F32, kind="ExternalInput").ap()
    d_msk = nc.dram_tensor("msk", [NKT * QT, QT], BF16, kind="ExternalInput").ap()
    d_cf = nc.dram_tensor("constf", [128, 256], F32, kind="ExternalInput").ap()
    d_idnb = nc.dram_tensor("idnb", [128, 128], BF16, kind="ExternalInput").ap()
    d_y = nc.dram_tensor("y", [T, D], F32, kind="ExternalOutput").ap()
    dbg = {}
    if _dbg():
        for nm, shp in [("dSTAT", [128, 16]), ("dAFC", [128, 48]),
                        ("dOHALL", [128, 512]), ("dXS", [128, 256]),
                        ("dMNXS", [128, 32]), ("dARB", [128, 2]),
                        ("dTKN", [128, 16]), ("dSS", [128, 2048]),
                        ("dGAM", [128, 2048]), ("dBET", [128, 2048]),
                        ("dDBG1", [128, 16]), ("dDBG2", [128, 1024]), ("dSQS", [128, 8]),
                        ("dDBG3", [128, 1024])]:
            dbg[nm] = nc.dram_tensor(nm, shp, F32, kind="ExternalOutput").ap()

    d_ssrow = nc.dram_tensor("ssrow", [NT, DS], F32)
    d_ar_in = [nc.dram_tensor(f"ar_in{i}", [NT, 2], F32) for i in range(2)]
    d_ar_out = [nc.dram_tensor(f"ar_out{i}", [NT, 2], F32) for i in range(2)]
    d_kv_in = nc.dram_tensor("kv_in", [2 * 128, T], BF16)
    d_kv_out = nc.dram_tensor("kv_out", [4 * 2 * 128, T], BF16)

    est = ExitStack()
    with est:
        def sbt(name, shape, dt):
            return est.enter_context(nc.sbuf_tensor(name, shape, dt))

        X = sbt("X", [128, NJ * D], F32)
        MSK = sbt("MSK", [128, NKT * QT], BF16)
        SS128 = sbt("SS128", [128, D], BF16)
        GAM = sbt("GAM", [128, D], F32)
        BET = sbt("BET", [128, D], F32)
        CF = sbt("CF", [128, 256], F32)
        IDNB = sbt("IDNB", [128, 128], BF16)
        SL64 = sbt("SL64", [128, DS * GRID], F32)
        TBP = sbt("TBP", [128, DS * GRID], F32)
        MISC = sbt("MISC", [128, 4], F32)
        BQ = sbt("BQ", [128, CCQ], F32)
        BPR = sbt("BPR", [1, D], BF16)
        ONE1 = sbt("ONE1", [33, 128], F32)
        ONEB2 = sbt("ONEB2", [128, 2], BF16)
        ONEBR = sbt("ONEBR", [1, 128], BF16)
        BIGT = sbt("BIGT", [128, 64], F32)
        NBIGT = sbt("NBIGT", [128, 64], F32)
        R1 = sbt("R1", [128, 16384], BF16)
        R2 = sbt("R2", [128, 18432], BF16)
        KT = sbt("KT", [128, NKT * QT], BF16)
        VT = sbt("VT", [128, NKT * QT], BF16)
        PT = [sbt(f"PT{i}", [128, 1024], BF16) for i in range(2)]
        QSQ = sbt("QSQ", [128, 1024], BF16)
        VTRB = sbt("VTRB", [128, 1024], F32)
        TB2 = sbt("TB2", [128, DS * GRID], F32)
        TS2 = sbt("TS2", [128, DS * GRID], F32)
        OHALL = sbt("OHALL", [128, NJ * NT], F32)
        XS = sbt("XS", [128, NJ * DS], F32)
        MNXS = sbt("MNXS", [128, NJ * 4], F32)
        STAT = sbt("STAT", [128, 16], F32)
        SQS = sbt("SQS", [128, 8], F32)
        AFC = sbt("AFC", [128, 48], F32)
        TKN = sbt("TKN", [128, 16], F32)
        TSC = sbt("TSC", [128, 64], F32)
        CS8 = sbt("CS8", [128, 8], F32)
        MM1 = sbt("MM1", [128, 64], F32)
        MMIN = sbt("MMIN", [128, 64], F32)
        MMAX = sbt("MMAX", [128, 64], F32)
        EQM = sbt("EQM", [128, 64], U8)
        MG256 = sbt("MG256", [128, 256], U8)
        SP1 = sbt("SP1", [128, DS], F32)
        SP2 = sbt("SP2", [128, DS], F32)
        BAC = sbt("BAC", [128, DS], F32)
        SAC = sbt("SAC", [128, DS], F32)
        T64A = sbt("T64A", [128, 128], F32)
        SC1 = sbt("SC1", [128, 1], F32)
        SC2 = sbt("SC2", [128, 1], F32)
        ARB = sbt("ARB", [128, 2], F32)
        OSC = sbt("OSC", [128, 1], F32)
        BB = sbt("BB", [128, 1], F32)
        KQM = sbt("KQM", [1, 8], F32)
        DBG1 = sbt("DBG1", [128, 16], F32)

        WBUF = [KT[:, 0:2048], KT[:, 2048:4096]]
        XNv = [R1[:, 0:4096].bitcast(F32), R1[:, 4096:8192].bitcast(F32)]
        TBALL = R1[:, 8192:16384].bitcast(F32)
        SCTB = R2[:, 8192:12288].bitcast(F32)
        TSALL = R2[:, 4096:8192]
        SCT = R2[:, 0:4096].bitcast(F32)
        SGN = [R2[:, 8192 + i * 4096: 8192 + i * 4096 + 2048] for i in range(2)]
        QQB = [R2[:, 10240 + i * 4096: 10240 + i * 4096 + 2048] for i in range(2)]
        GROW = R2[:, 0:4096].bitcast(F32)[0:1, :]
        BROW = R2[:, 4096:8192].bitcast(F32)[0:1, :]
        XN3B = [R1[:, 0:8192], R1[:, 8192:16384]]
        X3Fh = R1[:, 8192:16384]
        AOB = R1
        QKVB = R2
        WPB = [R2[:, 0:8192], R2[:, 8192:16384]]
        OHF = VTRB
        RB = VTRB
        VTO = VTRB[:, 0:512].bitcast(BF16)
        QSQF = QSQ.bitcast(F32)
        G64 = QSQF
        SREC = [QSQF[0:1, 0:512], QSQF[32:33, 0:512]]
        PSBKF = None

        PSA = est.enter_context(nc.psum_tensor("PSA", [128, 1024], F32))
        PSB = est.enter_context(nc.psum_tensor("PSB", [128, 1024], F32))
        PSC = est.enter_context(nc.psum_tensor("PSC", [128, 1024], F32))
        PSD = est.enter_context(nc.psum_tensor("PSD", [128, 512], F32))
        PSBK = est.enter_context(nc.psum_tensor("PSBK", [128, 1024], BF16))
        PSBKF = PSBK.ap().bitcast(F32)

        sems = {}
        for s_ in ("di", "dw", "dwa", "dwb", "dk", "cc", "t", "v", "s", "g",
                   "so"):
            sems[s_] = est.enter_context(nc.semaphore(s_))
        cnt = {k: 0 for k in sems}
        ops = {e: [] for e in ("sync", "vector", "scalar", "tensor", "gpsimd")}

        def on(eng, fn):
            ops[eng].append(fn)

        def ewait(eng, s_, val=None):
            v = cnt[s_] if val is None else val
            if v <= 0:
                return
            on(eng, lambda e, s_=s_, v=v: e.wait_ge(sems[s_], v))

        def edma(eng, s_, out, in_, slow=False):
            cnt[s_] += 16
            if slow:
                on(eng, lambda e, out=out, in_=in_, s_=s_:
                   e.dma_start(out=out, in_=in_, allow_slow_non_contiguous=True)
                   .then_inc(sems[s_], 16))
            else:
                on(eng, lambda e, out=out, in_=in_, s_=s_:
                   e.dma_start(out=out, in_=in_).then_inc(sems[s_], 16))

        def vsync():
            ewait("vector", "v", cnt["v"])

        def vrun(fn):
            run("vector", fn, inc="v")

        def run(eng, fn, inc=None):
            if eng == "vector":
                ewait("vector", "v", cnt["v"])
                if inc is None:
                    inc = "v"
            if inc is not None:
                cnt[inc] += 1
            def _f(e, fn=fn, inc=inc):
                r = fn(e)
                if inc is not None:
                    r.then_inc(sems[inc], 1)
            on(eng, _f)

        IOTA64 = CF[:, 0:64]
        IOTAD8 = CF[:, 64:128]
        IDNF = CF[:, 128:256]

        edma("sync", "di", X[:].rearrange("p (j d) -> p j d", d=D),
             d_x.rearrange("(j p) d -> p j d", p=128))
        edma("sync", "di", MSK[:].rearrange("p (n q) -> p n q", q=QT),
             d_msk.rearrange("(n p) q -> p n q", p=128))
        edma("sync", "di", CF[:], d_cf)
        edma("sync", "di", IDNB[:], d_idnb)
        edma("sync", "di", SL64[0:NT, :], d_slopes)
        edma("sync", "di", TBP[0:NT, :], d_bases)
        edma("sync", "di", OSC[0:NT, :], d_oscale)
        edma("sync", "di", BQ[:], d_bqkv.rearrange("(cc p) -> p cc", p=128),
             slow=True)
        ld_all = cnt["di"]
        edma("gpsimd", "dw", GROW, d_gamma)
        edma("gpsimd", "dw", BROW, d_beta)
        edma("gpsimd", "dw", BPR[:], d_bproj)
        g_rows = cnt["dw"]

        def _init(e):
            e.memset(ONE1[:], 1.0)
            e.memset(ONEB2[:], 1.0)
            e.memset(ONEBR[:], 1.0)
            e.memset(BIGT[:], BIG)
            e.memset(NBIGT[:], -BIG)
            return e.memset(MISC[:], 0.0)
        run("gpsimd", _init, inc="g")
        g_init = cnt["g"]

        ewait("tensor", "dw", g_rows)
        ewait("tensor", "g", g_init)
        for part in range(2):
            row = GROW if part == 0 else BROW
            dst = GAM if part == 0 else BET
            for h in range(2):
                ewait("tensor", "v", cnt["v"])
                for q in range(2):
                    run("tensor", lambda e, row=row, h=h, q=q: e.matmul(
                        PSA[:, q * 512:(q + 1) * 512], ONE1[0:1, :],
                        row[0:1, h * 1024 + q * 512: h * 1024 + (q + 1) * 512],
                        start=True, stop=True), inc="t" if q == 1 else None)
                ewait("vector", "t", cnt["t"])
                run("vector", lambda e, dst=dst, h=h: e.tensor_copy(
                    out=dst[:, h * 1024:(h + 1) * 1024], in_=PSA[:, 0:1024]),
                    inc="v")
        v_gb = cnt["v"]

        ewait("vector", "di", ld_all)
        if _dbg():
            run("vector", lambda e: e.tensor_copy(
                out=VTRB[:], in_=X[:, 7 * 2048: 7 * 2048 + 1024]), inc="v")
            ewait("sync", "v", cnt["v"])
            edma("sync", "so", dbg["dDBG2"], VTRB[:])
        run("vector", lambda e: e.tensor_reduce(
            out=TSC[0:NT, 0:DS],
            in_=SL64[0:NT, :].rearrange("p (j g) -> p j g", g=GRID),
            axis=AX.X, op=ALU.add), inc="v")
        ewait("scalar", "v", cnt["v"])
        run("scalar", lambda e: e.activation(
            out=TSC[0:NT, 0:DS], in_=TSC[0:NT, 0:DS], func=AF.Sign), inc="s")
        s_ss = cnt["s"]
        ewait("gpsimd", "g", g_init)
        run("gpsimd", lambda e: e.iota(
            G64[0:NT, :], [[0, DS], [1, GRID]], channel_multiplier=0,
            allow_small_or_imprecise_dtypes=True), inc="g")
        g_g64 = cnt["g"]
        ewait("vector", "g", g_g64)
        vrun(lambda e: e.tensor_tensor(
            out=G64[0:NT, :], in0=SL64[0:NT, :], in1=G64[0:NT, :], op=ALU.mult))
        vrun(lambda e: e.tensor_tensor(
            out=TBP[0:NT, :], in0=TBP[0:NT, :], in1=G64[0:NT, :],
            op=ALU.subtract))
        v_tbp = cnt["v"]
        ewait("gpsimd", "s", s_ss)
        edma("gpsimd", "dk", d_ssrow.ap(), TSC[0:NT, 0:DS])
        ewait("gpsimd", "dk")
        edma("gpsimd", "dk", VTRB[0:1, 0:1024],
             d_ssrow.ap().rearrange("a b -> (a b)")[None, 0:1024])
        edma("gpsimd", "dk", VTRB[32:33, 0:1024],
             d_ssrow.ap().rearrange("a b -> (a b)")[None, 1024:2048])
        g_ssrow = cnt["dk"]
        ewait("tensor", "dk", g_ssrow)
        for h in range(2):
            hp = 0 if h == 0 else 32
            ewait("tensor", "v", cnt["v"])
            for q in range(2):
                run("tensor", lambda e, hp=hp, q=q: e.matmul(
                    PSB[:, q * 512:(q + 1) * 512], ONE1[hp:hp + 1, :],
                    VTRB[hp:hp + 1, q * 512:(q + 1) * 512],
                    start=True, stop=True), inc="t" if q == 1 else None)
            ewait("vector", "t", cnt["t"])
            run("vector", lambda e, h=h: e.tensor_copy(
                out=SS128[:, h * 1024:(h + 1) * 1024], in_=PSB[:, 0:1024]),
                inc="v")
        v_ss = cnt["v"]

        def ffn(ar_idx):
            ewait("vector", "t", cnt["t"])
            ewait("scalar", "t", cnt["t"])
            ewait("vector", "v", cnt["v"])
            ewait("scalar", "v", cnt["v"])
            ewait("scalar", "g", cnt["g"])
            ewait("vector", "g", cnt["g"])
            _vred = [0] * NJ
            for j in range(NJ):
                Xj = X[:, j * D:(j + 1) * D]
                if j >= 2:
                    ewait("scalar", "v", _vred[j - 2])
                run("scalar", lambda e, Xj=Xj, j=j: e.activation(
                    out=SGN[j % 2], in_=Xj, func=AF.Square), inc="s")
                s_sq = cnt["s"]
                vrun(lambda e, Xj=Xj, j=j: e.reduce_sum(
                    out=STAT[:, j:j + 1], in_=Xj, axis=AX.X))
                ewait("vector", "s", s_sq)
                vrun(lambda e, j=j: e.reduce_sum(
                    out=SQS[:, j:j + 1], in_=SGN[j % 2], axis=AX.X))
                _vred[j] = cnt["v"]
            MU = AFC[:, 0:8]; VAR = AFC[:, 8:16]; RS1 = AFC[:, 16:24]
            RS2 = AFC[:, 24:32]; AFj = AFC[:, 32:40]; CFj = AFC[:, 40:48]
            AY = STAT[:, 0:8]; CY = STAT[:, 8:16]
            vrun(lambda e: e.tensor_scalar(out=MU, in0=STAT[:, 0:8],
                                           scalar1=1.0 / D, scalar2=None,
                                           op0=ALU.mult))
            vrun(lambda e: e.tensor_scalar(out=VAR, in0=SQS[:], scalar1=1.0 / D,
                                           scalar2=None, op0=ALU.mult))
            vrun(lambda e: e.tensor_tensor(out=TSC[:, 0:8], in0=MU, in1=MU,
                                           op=ALU.mult))
            vrun(lambda e: e.tensor_tensor(out=VAR, in0=VAR, in1=TSC[:, 0:8],
                                           op=ALU.subtract))
            vrun(lambda e: e.tensor_scalar(out=RS1, in0=VAR, scalar1=LNEPS,
                                           scalar2=None, op0=ALU.add))
            vrun(lambda e: e.reciprocal(out=RS1, in_=RS1))
            ewait("scalar", "v", cnt["v"])
            run("scalar", lambda e: e.activation(out=RS1, in_=RS1, func=AF.Sqrt),
                inc="s")
            ewait("vector", "s", cnt["s"])
            vrun(lambda e: e.tensor_tensor(out=RS2, in0=RS1, in1=RS1, op=ALU.mult))
            vrun(lambda e: e.tensor_tensor(out=RS2, in0=RS2, in1=VAR, op=ALU.mult))
            vrun(lambda e: e.tensor_scalar(out=RS2, in0=RS2, scalar1=0.25,
                                           scalar2=LNEPS, op0=ALU.mult,
                                           op1=ALU.add))
            vrun(lambda e: e.reciprocal(out=RS2, in_=RS2))
            ewait("scalar", "v", cnt["v"])
            run("scalar", lambda e: e.activation(out=RS2, in_=RS2, func=AF.Sqrt),
                inc="s")
            ewait("vector", "s", cnt["s"])
            vrun(lambda e: e.tensor_tensor(out=AFj, in0=RS1, in1=RS2, op=ALU.mult))
            vrun(lambda e: e.tensor_scalar(out=AFj, in0=AFj, scalar1=0.5,
                                           scalar2=None, op0=ALU.mult))
            vrun(lambda e: e.tensor_tensor(out=CFj, in0=MU, in1=AFj, op=ALU.mult))
            vrun(lambda e: e.tensor_scalar(out=CFj, in0=CFj, scalar1=-1.0,
                                           scalar2=None, op0=ALU.mult))
            vrun(lambda e: e.tensor_tensor(out=CY, in0=MU, in1=RS1, op=ALU.mult))
            vrun(lambda e: e.tensor_scalar(out=CY, in0=CY, scalar1=-0.5,
                                           scalar2=None, op0=ALU.mult))
            vrun(lambda e: e.tensor_scalar(out=AY, in0=RS1, scalar1=0.5,
                                           scalar2=1.0, op0=ALU.mult, op1=ALU.add))

            s_sgn = [0] * NJ
            v_qq = [0] * NJ
            for j in range(NJ):
                Xj = X[:, j * D:(j + 1) * D]
                XN = XNv[j % 2]
                if j >= 2:
                    ewait("vector", "s", s_sgn[j - 2])
                vrun(lambda e, Xj=Xj, XN=XN, j=j: e.tensor_scalar(
                    out=XN, in0=Xj, scalar1=AFj[:, j:j + 1],
                    scalar2=CFj[:, j:j + 1], op0=ALU.mult, op1=ALU.add))
                vrun(lambda e, XN=XN: e.tensor_tensor(
                    out=XN, in0=XN, in1=GAM[:], op=ALU.mult))
                vrun(lambda e, XN=XN: e.tensor_tensor(
                    out=XN, in0=XN, in1=BET[:], op=ALU.add))
                v_xn = cnt["v"]
                ewait("scalar", "v", v_xn)
                if j >= 2:
                    ewait("scalar", "v", v_qq[j - 2])
                run("scalar", lambda e, XN=XN, j=j: e.activation(
                    out=SGN[j % 2], in_=XN, func=AF.Sign), inc="s")
                s_sgn[j] = cnt["s"]
                ewait("vector", "s", s_sgn[j])
                vrun(lambda e, j=j: e.tensor_tensor(
                    out=QQB[j % 2], in0=SGN[j % 2], in1=SS128[:], op=ALU.mult))
                v_qq[j] = cnt["v"]
                vrun(lambda e, j=j: e.tensor_reduce(
                    out=TSC[:, 0:64],
                    in_=QQB[j % 2].rearrange("p (t k) -> p t k", k=DS),
                    axis=AX.X, op=ALU.add))
                OHj = OHALL[:, j * 64:(j + 1) * 64]
                vrun(lambda e: e.tensor_reduce(
                    out=CS8[:], in_=TSC[:, 0:64].rearrange(
                        "p (c t) -> p c t", t=8), axis=AX.X, op=ALU.add))
                vrun(lambda e: e.reduce_max(out=SC1[:], in_=CS8[:], axis=AX.X))
                vrun(lambda e: e.tensor_scalar(out=EQM[:, 0:8], in0=CS8[:],
                                               scalar1=SC1[:], scalar2=None,
                                               op0=ALU.is_equal))
                vrun(lambda e: e.select(out=MM1[:, 0:8], mask=EQM[:, 0:8],
                                        on_true=IOTA64[:, 0:8],
                                        on_false=BIGT[:, 0:8], add_drain=True))
                vrun(lambda e: e.tensor_reduce(out=SC1[:], in_=MM1[:, 0:8],
                                               axis=AX.X, op=ALU.min))
                vrun(lambda e: e.tensor_scalar(out=MM1[:], in0=IOTAD8[:],
                                               scalar1=SC1[:], scalar2=65536.0,
                                               op0=ALU.is_equal, op1=ALU.mult))
                vrun(lambda e: e.tensor_tensor(out=MM1[:], in0=MM1[:],
                                               in1=TSC[:, 0:64], op=ALU.add))
                vrun(lambda e: e.reduce_max(out=SC2[:], in_=MM1[:], axis=AX.X))
                vrun(lambda e: e.tensor_scalar(out=EQM[:], in0=MM1[:],
                                               scalar1=SC2[:], scalar2=None,
                                               op0=ALU.is_equal))
                vrun(lambda e: e.select(out=MM1[:], mask=EQM[:],
                                        on_true=IOTA64[:], on_false=BIGT[:], add_drain=True))
                vrun(lambda e: e.tensor_reduce(out=SC1[:], in_=MM1[:],
                                               axis=AX.X, op=ALU.min))
                vrun(lambda e, OHj=OHj: e.tensor_scalar(
                    out=OHj, in0=IOTA64[:], scalar1=SC1[:], scalar2=None,
                    op0=ALU.is_equal))
                v_oh = cnt["v"]
                ewait("vector", "g", cnt["g"])
                vrun(lambda e, XN=XN, OHj=OHj: e.tensor_tensor(
                    out=SCT[:].rearrange("p (a b) -> p a b", b=DS),
                    in0=XN.rearrange("p (a b) -> p a b", b=DS),
                    in1=OHj.unsqueeze(2).broadcast_to([128, 64, DS]),
                    op=ALU.mult))
                vrun(lambda e, j=j: e.tensor_reduce(
                    out=XS[:, j * DS:(j + 1) * DS],
                    in_=SCT[:].rearrange("p (t k) -> p k t", k=DS),
                    axis=AX.X, op=ALU.add))
                vrun(lambda e, j=j: e.tensor_reduce(
                    out=TKN[:, j:j + 1], in_=XS[:, j * DS:(j + 1) * DS],
                    axis=AX.X, op=ALU.min))
                vrun(lambda e, j=j: e.tensor_reduce(
                    out=TKN[:, 8 + j:9 + j], in_=XS[:, j * DS:(j + 1) * DS],
                    axis=AX.X, op=ALU.max))
                vrun(lambda e, j=j, OHj=OHj: e.tensor_scalar(
                    out=EQM[:], in0=OHj, scalar1=0.5, scalar2=None, op0=ALU.is_gt))
                vrun(lambda e, j=j, OHj=OHj: e.select(
                    out=MM1[:], mask=EQM[:],
                    on_true=TKN[:, j:j + 1].broadcast_to([128, 64]),
                    on_false=BIGT[:], add_drain=True))
                if j == 0:
                    vrun(lambda e: e.tensor_copy(out=MMIN[:], in_=MM1[:]))
                else:
                    vrun(lambda e: e.tensor_tensor(out=MMIN[:], in0=MMIN[:],
                                                   in1=MM1[:], op=ALU.min))
                vrun(lambda e, j=j, OHj=OHj: e.select(
                    out=MM1[:], mask=EQM[:],
                    on_true=TKN[:, 8 + j:9 + j].broadcast_to([128, 64]),
                    on_false=NBIGT[:], add_drain=True))
                if j == 0:
                    vrun(lambda e: e.tensor_copy(out=MMAX[:], in_=MM1[:]))
                else:
                    vrun(lambda e: e.tensor_tensor(out=MMAX[:], in0=MMAX[:],
                                                   in1=MM1[:], op=ALU.max))
                ewait("tensor", "v", v_oh)
                run("tensor", lambda e, OHj=OHj: e.transpose(
                    PSD[0:64, 128:256], OHj, IDNF), inc="t")
                ewait("vector", "t", cnt["t"])
                vrun(lambda e, j=j: e.tensor_copy(
                    out=OHF[0:64, j * 128:(j + 1) * 128], in_=PSD[0:64, 128:256]))
            v_loopA = cnt["v"]

            ewait("tensor", "v", v_loopA)
            def _mmT(e):
                e.transpose(PSD[0:64, 128:256], MMIN[:], IDNF)
                return e.transpose(PSD[0:64, 256:384], MMAX[:], IDNF)
            run("tensor", _mmT, inc="t")
            ewait("vector", "t", cnt["t"])
            vrun(lambda e: e.tensor_copy(out=T64A[0:64, :], in_=PSD[0:64, 128:256]))
            vrun(lambda e: e.tensor_reduce(out=ARB[0:64, 0:1], in_=T64A[0:64, :],
                                           axis=AX.X, op=ALU.min))
            vrun(lambda e: e.tensor_copy(out=T64A[0:64, :], in_=PSD[0:64, 256:384]))
            vrun(lambda e: e.tensor_reduce(out=ARB[0:64, 1:2], in_=T64A[0:64, :],
                                           axis=AX.X, op=ALU.max))
            vrun(lambda e: e.tensor_scalar(out=ARB[0:64, 0:1], in0=ARB[0:64, 0:1],
                                           scalar1=-1.0, scalar2=None,
                                           op0=ALU.mult))
            ewait("gpsimd", "v", cnt["v"])
            edma("gpsimd", "dk", d_ar_in[ar_idx].ap(), ARB[0:64, 0:2])
            ewait("gpsimd", "dk")
            cnt["cc"] += 1
            on("gpsimd", lambda e, ar_idx=ar_idx: e.collective_compute(
                "AllReduce", ALU.max, replica_groups=[list(range(8))],
                ins=[d_ar_in[ar_idx].ap().opt()],
                outs=[d_ar_out[ar_idx].ap().opt()]).then_inc(sems["cc"], 1))
            ewait("gpsimd", "cc")
            edma("gpsimd", "dk", ARB[0:64, 0:2], d_ar_out[ar_idx].ap())
            ewait("vector", "dk", cnt["dk"])
            vrun(lambda e: e.tensor_scalar(out=MISC[0:64, 0:1], in0=ARB[0:64, 0:1],
                                           scalar1=-1.0, scalar2=None,
                                           op0=ALU.mult))
            vrun(lambda e: e.tensor_copy(out=MISC[0:64, 1:2], in_=ARB[0:64, 1:2]))
            vrun(lambda e: e.tensor_copy(out=MISC[0:64, 2:3], in_=OSC[0:64, :]))
            v_misc = cnt["v"]

            ewait("tensor", "v", max(v_misc, v_tbp))
            v_tabev = 0
            for j in range(NJ):
                OHFj = OHF[0:NT, j * 128:(j + 1) * 128]
                if v_tabev:
                    ewait("tensor", "v", v_tabev)
                run("tensor", lambda e, OHFj=OHFj: e.matmul(
                    PSA[:, 0:512], OHFj, TBP[0:NT, :], start=True, stop=True))
                run("tensor", lambda e, OHFj=OHFj: e.matmul(
                    PSB[:, 0:512], OHFj, SL64[0:NT, :], start=True, stop=True))
                run("tensor", lambda e, OHFj=OHFj: e.matmul(
                    PSD[:, 0:4], OHFj, MISC[0:NT, :], start=True, stop=True),
                    inc="t")
                ewait("vector", "t", cnt["t"])
                def _tabev(e, j=j):
                    e.tensor_copy(out=TBALL[:, j * 512:(j + 1) * 512],
                                  in_=PSA[:, 0:512])
                    e.tensor_copy(out=TSALL[:, j * 512:(j + 1) * 512],
                                  in_=PSB[:, 0:512])
                    return e.tensor_copy(out=MNXS[:, j * 4:(j + 1) * 4],
                                         in_=PSD[:, 0:4])
                run("vector", _tabev, inc="v")
                v_tabev = cnt["v"]

            MNv = MNXS[:].rearrange("p (j c) -> p j c", c=4)[:, :, 0:1]
            MXv = MNXS[:].rearrange("p (j c) -> p j c", c=4)[:, :, 1:2]
            SCv = MNXS[:].rearrange("p (j c) -> p j c", c=4)[:, :, 2:3]
            XSv = XS[:].rearrange("p (j k) -> p j k", k=DS)
            R8 = TSC[:, 8:16]; INV8 = TSC[:, 16:24]; SCH8 = TSC[:, 24:32]
            vsync()
            vrun(lambda e: e.tensor_tensor(
                out=R8.unsqueeze(2), in0=MXv, in1=MNv, op=ALU.subtract))
            vrun(lambda e: e.tensor_scalar(out=R8, in0=R8, scalar1=1e-8,
                                           scalar2=None, op0=ALU.add))
            vrun(lambda e: e.reciprocal(out=INV8, in_=R8))
            vrun(lambda e: e.tensor_scalar(out=SCH8.unsqueeze(2), in0=SCv,
                                           scalar1=0.5, scalar2=None,
                                           op0=ALU.mult))
            SP1v = SCT[:, 0:256]
            SP2v = SCT[:, 256:512]
            BACv = SCT[:, 512:768]
            SACv = SCT[:, 768:1024]
            MGv = EQM[:, 0:64]
            vrun(lambda e: e.tensor_tensor(
                out=SP1v.rearrange("p (j k) -> p j k", k=DS), in0=XSv,
                in1=MNv.broadcast_to([128, NJ, DS]), op=ALU.subtract))
            vrun(lambda e: e.tensor_tensor(
                out=SP1v.rearrange("p (j k) -> p j k", k=DS),
                in0=SP1v.rearrange("p (j k) -> p j k", k=DS),
                in1=INV8.unsqueeze(2).broadcast_to([128, NJ, DS]), op=ALU.mult))
            vrun(lambda e: e.tensor_scalar(out=SP1v, in0=SP1v,
                                           scalar1=1.0 - 1e-6, scalar2=0.0,
                                           op0=ALU.min, op1=ALU.max))
            vrun(lambda e: e.tensor_scalar(out=SP1v, in0=SP1v,
                                           scalar1=float(GRID), scalar2=None,
                                           op0=ALU.mult))
            TBv = TBALL[:].rearrange("p (j k g) -> p j k g", k=DS, g=GRID)
            TSv = TSALL[:].rearrange("p (j k g) -> p j k g", k=DS, g=GRID)
            vrun(lambda e: e.tensor_copy(
                out=BACv.rearrange("p (j k) -> p j k", k=DS), in_=TBv[:, :, :, 0]))
            vrun(lambda e: e.tensor_copy(
                out=SACv.rearrange("p (j k) -> p j k", k=DS), in_=TSv[:, :, :, 0]))
            for g in range(1, GRID):
                vrun(lambda e, g=g: e.tensor_scalar(
                    out=MG256[:], in0=SP1v, scalar1=float(g), scalar2=None,
                    op0=ALU.is_ge))
                vrun(lambda e, g=g: e.select(
                    out=BACv.rearrange("p (j k) -> p j k", k=DS),
                    mask=MG256[:].rearrange("p (j k) -> p j k", k=DS),
                    on_true=TBv[:, :, :, g],
                    on_false=BACv.rearrange("p (j k) -> p j k", k=DS),
                    add_drain=True))
                vrun(lambda e, g=g: e.select(
                    out=SACv.rearrange("p (j k) -> p j k", k=DS),
                    mask=MG256[:].rearrange("p (j k) -> p j k", k=DS),
                    on_true=TSv[:, :, :, g],
                    on_false=SACv.rearrange("p (j k) -> p j k", k=DS),
                    add_drain=True))
            vrun(lambda e: e.tensor_tensor(out=SP2v, in0=SACv, in1=SP1v,
                                           op=ALU.mult))
            vrun(lambda e: e.tensor_tensor(out=SP2v, in0=SP2v, in1=BACv,
                                           op=ALU.add))
            vrun(lambda e: e.tensor_tensor(
                out=SP2v.rearrange("p (j k) -> p j k", k=DS),
                in0=SP2v.rearrange("p (j k) -> p j k", k=DS),
                in1=SCH8.unsqueeze(2).broadcast_to([128, NJ, DS]), op=ALU.mult))
            g_scat = 0
            for j in range(NJ):
                Xj = X[:, j * D:(j + 1) * D]
                OHj = OHALL[:, j * 64:(j + 1) * 64]
                if g_scat:
                    ewait("vector", "g", g_scat)
                vrun(lambda e, Xj=Xj, j=j: e.tensor_scalar(
                    out=Xj, in0=Xj, scalar1=AY[:, j:j + 1],
                    scalar2=CY[:, j:j + 1], op0=ALU.mult, op1=ALU.add))
                vrun(lambda e, OHj=OHj, j=j: e.tensor_tensor(
                    out=SCTB[:].rearrange("p (a b) -> p a b", b=DS),
                    in0=OHj.unsqueeze(2).broadcast_to([128, 64, DS]),
                    in1=SP2v.rearrange("p (j2 k) -> p j2 k", k=DS)[:, j, :]
                        .unsqueeze(1).broadcast_to([128, 64, DS]),
                    op=ALU.mult))
                ewait("gpsimd", "v", cnt["v"])
                run("gpsimd", lambda e, Xj=Xj: e.tensor_tensor(
                    out=Xj, in0=Xj, in1=SCTB[:], op=ALU.add), inc="g")
                g_scat = cnt["g"]
            ewait("vector", "g", cnt["g"])
            ewait("scalar", "g", cnt["g"])
            ewait("tensor", "g", cnt["g"])

        def attention():
            _vred = [0] * NJ
            for j in range(NJ):
                Xj = X[:, j * D:(j + 1) * D]
                if j >= 2:
                    ewait("scalar", "v", _vred[j - 2])
                run("scalar", lambda e, Xj=Xj, j=j: e.activation(
                    out=SGN[j % 2], in_=Xj, func=AF.Square), inc="s")
                s_sq = cnt["s"]
                run("vector", lambda e, Xj=Xj, j=j: e.reduce_sum(
                    out=STAT[:, j:j + 1], in_=Xj, axis=AX.X))
                ewait("vector", "s", s_sq)
                run("vector", lambda e, j=j: e.reduce_sum(
                    out=SQS[:, j:j + 1], in_=SGN[j % 2], axis=AX.X), inc="v")
                _vred[j] = cnt["v"]
            A3 = AFC[:, 0:8]; C3 = AFC[:, 8:16]
            ewait("vector", "s", cnt["s"])
            vsync()
            vrun(lambda e: e.tensor_scalar(out=A3, in0=SQS[:], scalar1=1.0 / D,
                                           scalar2=None, op0=ALU.mult))
            vrun(lambda e: e.tensor_scalar(out=C3, in0=STAT[:, 0:8],
                                           scalar1=1.0 / D, scalar2=None,
                                           op0=ALU.mult))
            vrun(lambda e: e.tensor_tensor(out=AFC[:, 16:24], in0=C3, in1=C3,
                                           op=ALU.mult))
            vrun(lambda e: e.tensor_tensor(out=A3, in0=A3, in1=AFC[:, 16:24],
                                           op=ALU.subtract))
            vrun(lambda e: e.tensor_scalar(out=A3, in0=A3, scalar1=LNEPS,
                                           scalar2=None, op0=ALU.add))
            vrun(lambda e: e.reciprocal(out=A3, in_=A3))
            ewait("scalar", "v", cnt["v"])
            run("scalar", lambda e: e.activation(out=A3, in_=A3, func=AF.Sqrt),
                inc="s")
            ewait("vector", "s", cnt["s"])
            vrun(lambda e: e.tensor_tensor(out=C3, in0=C3, in1=A3, op=ALU.mult))
            vrun(lambda e: e.tensor_scalar(out=C3, in0=C3, scalar1=-1.0,
                                           scalar2=None, op0=ALU.mult))
            v_l3 = cnt["v"]

            ewait("gpsimd", "v", v_l3)
            ewait("gpsimd", "s", cnt["s"])
            dw_cc = {}
            def wqkv_load(hf, cc):
                s_ = "dwa" if cc % 2 == 0 else "dwb"
                edma("gpsimd", s_, WBUF[cc % 2][:].rearrange(
                    "p (dc c) -> p dc c", c=128),
                    d_wqkv.rearrange("(dc p) c -> p dc c", p=128)[:, :, cc * 128:(cc + 1) * 128])
                dw_cc[(hf, cc)] = (s_, cnt[s_])
            wqkv_load(0, 0)
            wqkv_load(0, 1)
            s_x3f = 0
            s_evac_prev = [0, 0]
            s_qkv_ev = [0, 0]
            for hf in range(2):
                v_xn3 = 0
                for j4 in range(4):
                    j = hf * 4 + j4
                    Xj = X[:, j * D:(j + 1) * D]
                    if hf == 1 and j4 == 0:
                        ewait("vector", "t", cnt["t"])
                    vrun(lambda e, Xj=Xj, j=j, j4=j4: e.tensor_scalar(
                        out=XN3B[0][:, j4 * 2048:(j4 + 1) * 2048], in0=Xj,
                        scalar1=A3[:, j:j + 1], scalar2=C3[:, j:j + 1],
                        op0=ALU.mult, op1=ALU.add))
                v_xn3 = cnt["v"]
                X3F = XN3B[1]
                ewait("tensor", "v", v_xn3)
                for dc in range(16):
                    if s_evac_prev[0]:
                        ewait("tensor", "s", s_evac_prev[0])
                    for j4 in range(4):
                        run("tensor", lambda e, dc=dc, j4=j4: e.transpose(
                            PSBK[:, j4 * 128:(j4 + 1) * 128],
                            XN3B[0][:, j4 * 2048 + dc * 128: j4 * 2048 + (dc + 1) * 128],
                            IDNB[:]), inc="t" if j4 == 3 else None)
                    t_tp = cnt["t"]
                    ewait("scalar", "t", t_tp)
                    run("scalar", lambda e, dc=dc, X3F=X3F: e.copy(
                        out=X3F[:, dc * 512:(dc + 1) * 512],
                        in_=PSBK[:, 0:512]), inc="s")
                    s_evac_prev[0] = cnt["s"]
                s_x3f = cnt["s"]
                for cc in range(CCQ):
                    ewait("tensor", dw_cc[(hf, cc)][0], dw_cc[(hf, cc)][1])
                    ewait("tensor", "s", s_x3f)
                    half = cc % 2
                    if s_qkv_ev[half]:
                        ewait("tensor", "s", s_qkv_ev[half])
                    for dc in range(16):
                        run("tensor", lambda e, cc=cc, dc=dc, half=half, X3F=X3F:
                            e.matmul(
                                PSA[:, half * 512:(half + 1) * 512],
                                WBUF[cc % 2][:, dc * 128:(dc + 1) * 128],
                                X3F[:, dc * 512:(dc + 1) * 512],
                                start=(dc == 0), stop=(dc == 15)),
                            inc="t" if dc == 15 else None)
                    t_qkv = cnt["t"]
                    ewait("scalar", "t", t_qkv)
                    run("scalar", lambda e, cc=cc, hf=hf, half=half: e.activation(
                        out=QKVB[:, cc * 1024 + hf * 512: cc * 1024 + hf * 512 + 512],
                        in_=PSA[:, half * 512:(half + 1) * 512], func=AF.Identity,
                        bias=BQ[:, cc:cc + 1], scale=1.0), inc="s")
                    s_qkv_ev[half] = cnt["s"]
                    nhf, ncc = hf, cc + 2
                    if ncc >= CCQ:
                        nhf, ncc = hf + 1, ncc - CCQ
                    if nhf < 2 and (nhf, ncc) not in dw_cc:
                        ewait("gpsimd", "t", t_qkv)
                        wqkv_load(nhf, ncc)
            s_qkv_done = cnt["s"]

            ewait("tensor", "s", s_qkv_done)
            v_vto = 0
            for j in range(NJ):
                if v_vto:
                    ewait("tensor", "v", v_vto)
                run("tensor", lambda e, j=j: e.transpose(
                    PSBK[:, 0:128],
                    QKVB[:, 17 * 1024 + j * 128: 17 * 1024 + (j + 1) * 128],
                    IDNB[:]), inc="t")
                ewait("vector", "t", cnt["t"])
                run("vector", lambda e, j=j: e.tensor_copy(
                    out=VTO[:, j * 128:(j + 1) * 128],
                    in_=PSBK[:, 0:128]), inc="v")
                v_vto = cnt["v"]
            ewait("gpsimd", "v", v_vto)
            ewait("gpsimd", "s", s_qkv_done)
            edma("gpsimd", "dk", d_kv_in.ap()[0:128, :], QKVB[:, 16 * 1024:17 * 1024])
            edma("gpsimd", "dk", d_kv_in.ap()[128:256, :], VTO[:])
            ewait("gpsimd", "dk")
            cnt["cc"] += 1
            on("gpsimd", lambda e: e.collective_compute(
                "AllGather", ALU.bypass,
                replica_groups=[[0, 1, 2, 3], [4, 5, 6, 7]],
                ins=[d_kv_in.ap().opt()],
                outs=[d_kv_out.ap().opt()]).then_inc(sems["cc"], 1))
            ewait("gpsimd", "cc")
            kvo = d_kv_out.ap().rearrange("(r b p) t -> r b p t", b=2, p=128)
            edma("gpsimd", "dk",
                 KT[:].rearrange("p (s r q) -> p s r q", r=4, q=128),
                 kvo[:, 0, :, :].rearrange("r p (s q) -> p s r q", q=128))
            edma("gpsimd", "dk",
                 VT[:].rearrange("p (s r q) -> p s r q", r=4, q=128),
                 kvo[:, 1, :, :].rearrange("r p (s q) -> p s r q", q=128))
            dk_kv = cnt["dk"]

            v_qsq_prev = 0
            for dc in range(16):
                ewait("vector", "t", cnt["t"])
                vrun(lambda e, dc=dc: e.tensor_tensor(
                    out=QSQ[:], in0=QKVB[:, dc * 1024:(dc + 1) * 1024],
                    in1=QKVB[:, dc * 1024:(dc + 1) * 1024], op=ALU.mult))
                ewait("tensor", "v", cnt["v"])
                run("tensor", lambda e, dc=dc: e.matmul(
                    PSD[0:2, 0:512], ONEB2[:], QSQ[:, 0:512],
                    start=(dc == 0), stop=(dc == 15)))
                run("tensor", lambda e, dc=dc: e.matmul(
                    PSBKF[0:2, 0:512], ONEB2[:], QSQ[:, 512:1024],
                    start=(dc == 0), stop=(dc == 15)), inc="t")
            t_qn = cnt["t"]
            ewait("vector", "t", t_qn)
            def _qmax(e):
                e.tensor_reduce(out=KQM[0:1, 0:1], in_=PSD[0:1, 0:512], axis=AX.X,
                                op=ALU.max)
                return e.tensor_reduce(out=KQM[0:1, 1:2], in_=PSBKF[0:1, 0:512],
                                       axis=AX.X, op=ALU.max)
            run("vector", _qmax, inc="v")
            vrun(lambda e: e.tensor_tensor(
                out=KQM[0:1, 0:1], in0=KQM[0:1, 0:1], in1=KQM[0:1, 1:2],
                op=ALU.max))
            ewait("vector", "dk", dk_kv)
            for kc in range(4):
                vrun(lambda e, kc=kc: e.tensor_tensor(
                    out=QSQ[:], in0=KT[:, kc * 1024:(kc + 1) * 1024],
                    in1=KT[:, kc * 1024:(kc + 1) * 1024], op=ALU.mult))
                ewait("tensor", "v", cnt["v"])
                if kc > 0:
                    ewait("tensor", "v", v_kn)
                run("tensor", lambda e: e.matmul(
                    PSD[0:2, 0:512], ONEB2[:], QSQ[:, 0:512], start=True,
                    stop=True))
                run("tensor", lambda e: e.matmul(
                    PSBKF[0:2, 0:512], ONEB2[:], QSQ[:, 512:1024], start=True,
                    stop=True), inc="t")
                ewait("vector", "t", cnt["t"])
                def _kmaxA(e):
                    e.tensor_reduce(out=KQM[0:1, 2:3], in_=PSD[0:1, 0:512],
                                    axis=AX.X, op=ALU.max)
                    return e.tensor_reduce(out=KQM[0:1, 3:4], in_=PSBKF[0:1, 0:512],
                                           axis=AX.X, op=ALU.max)
                run("vector", _kmaxA, inc="v")
                vsync()
                vrun(lambda e: e.tensor_tensor(out=KQM[0:1, 2:3],
                                               in0=KQM[0:1, 2:3],
                                               in1=KQM[0:1, 3:4], op=ALU.max))
                if kc == 0:
                    vrun(lambda e: e.tensor_copy(out=KQM[0:1, 4:5],
                                                 in_=KQM[0:1, 2:3]))
                else:
                    vrun(lambda e: e.tensor_tensor(out=KQM[0:1, 4:5],
                                                   in0=KQM[0:1, 4:5],
                                                   in1=KQM[0:1, 2:3],
                                                   op=ALU.max))
                v_kn = cnt["v"]
            run("vector", lambda e: e.tensor_tensor(
                out=KQM[0:1, 5:6], in0=KQM[0:1, 0:1], in1=KQM[0:1, 4:5],
                op=ALU.mult), inc="v")
            ewait("scalar", "v", cnt["v"])
            run("scalar", lambda e: e.activation(out=KQM[0:1, 5:6],
                                                 in_=KQM[0:1, 5:6], func=AF.Sqrt),
                inc="s")
            ewait("vector", "s", cnt["s"])
            vrun(lambda e: e.tensor_scalar(
                out=KQM[0:1, 6:7], in0=KQM[0:1, 5:6], scalar1=-SCALE,
                scalar2=EXPSHIFT, op0=ALU.mult, op1=ALU.add))
            ewait("tensor", "v", cnt["v"])
            run("tensor", lambda e: e.matmul(
                PSD[:, 384:385], ONE1[0:1, :], KQM[0:1, 6:7], start=True,
                stop=True), inc="t")
            ewait("vector", "t", cnt["t"])
            run("vector", lambda e: e.tensor_copy(out=BB[:], in_=PSD[:, 384:385]),
                inc="v")
            v_bb = cnt["v"]

            def emit_pv(kt, EJ):
                buf = kt % 2
                kind, val = cons_done[buf]
                ewait("tensor", kind, val)
                if kt == 0 and v_aob:
                    ewait("tensor", "v", v_aob)
                for h2 in range(8):
                    run("tensor", lambda e, h2=h2, kt=kt, buf=buf, EJ=EJ:
                        e.matmul(
                            PSB[:, h2 * 128:(h2 + 1) * 128],
                            VT[:, kt * 128:(kt + 1) * 128],
                            PT[buf][:, h2 * 128:(h2 + 1) * 128],
                            start=(kt == 0 and h2 % 4 == 0),
                            stop=(kt == EJ - 1 and h2 % 4 == 3)))
                run("tensor", lambda e, kt=kt, buf=buf, EJ=EJ: e.matmul(
                    PSD[0:2, 0:512], ONEB2[:], PT[buf][:, 0:512],
                    start=(kt == 0), stop=(kt == EJ - 1)))
                run("tensor", lambda e, kt=kt, buf=buf, EJ=EJ: e.matmul(
                    PSBKF[0:2, 0:512], ONEB2[:], PT[buf][:, 512:1024],
                    start=(kt == 0), stop=(kt == EJ - 1)), inc="t")
                pv_done[kt % 2] = cnt["t"]

            ewait("tensor", "dk", dk_kv)
            ewait("tensor", "v", v_bb)
            ewait("scalar", "v", v_bb)
            SSTRIP = [PSA, PSC]
            exp_done = {}
            cons_done = {}
            pv_done = {}
            v_aob = 0
            v_rb = 0
            for j in range(NJ):
                EJ = 4 * (j + 1)
                for hf in range(2):
                    pending = None
                    for kt in range(EJ):
                        buf = kt % 2
                        strip = SSTRIP[buf]
                        prev = exp_done.get(buf)
                        if prev:
                            ewait("tensor", "s", prev)
                        if v_rb:
                            ewait("tensor", "v", v_rb)
                            v_rb = 0
                        for h2 in range(8):
                            h = hf * 8 + h2
                            run("tensor", lambda e, h=h, h2=h2, j=j, kt=kt,
                                strip=strip: e.matmul(
                                    strip[:, h2 * 128:(h2 + 1) * 128],
                                    KT[:, kt * 128:(kt + 1) * 128],
                                    QKVB[:, h * 1024 + j * 128: h * 1024 + (j + 1) * 128],
                                    start=True, stop=True),
                                inc="t" if h2 == 7 else None)
                        t_sc = cnt["t"]
                        ewait("scalar", "t", t_sc)
                        prevpv = pv_done.get(buf)
                        if prevpv:
                            ewait("scalar", "t", prevpv)
                        run("scalar", lambda e, buf=buf, strip=strip: e.activation(
                            out=PT[buf][:], in_=strip[:], func=AF.Exp,
                            bias=BB[:], scale=SCALE), inc="s")
                        exp_done[buf] = cnt["s"]
                        if kt >= 4 * j:
                            ewait("vector", "s", exp_done[buf])
                            run("vector", lambda e, buf=buf, kt=kt: e.tensor_tensor(
                                out=PT[buf][:].rearrange("p (h q) -> p h q", q=128),
                                in0=PT[buf][:].rearrange("p (h q) -> p h q", q=128),
                                in1=MSK[:, kt * 128:(kt + 1) * 128].unsqueeze(1)
                                    .broadcast_to([128, 8, 128]),
                                op=ALU.mult))
                            cons_done[buf] = ("v", cnt["v"])
                        else:
                            cons_done[buf] = ("s", exp_done[buf])
                        if pending is not None:
                            emit_pv(pending, EJ)
                        pending = kt
                    emit_pv(pending, EJ)
                    pending = None
                    t_last = cnt["t"]
                    ewait("vector", "t", t_last)
                    vrun(lambda e: e.tensor_copy(out=SREC[0], in_=PSD[0:1, 0:512]))
                    vrun(lambda e: e.tensor_copy(out=SREC[1],
                                                 in_=PSBKF[0:1, 0:512]))
                    vrun(lambda e: e.reciprocal(out=SREC[0], in_=SREC[0]))
                    vrun(lambda e: e.reciprocal(out=SREC[1], in_=SREC[1]))
                    v_recd = cnt["v"]
                    ewait("tensor", "v", v_recd)
                    for q in range(2):
                        qp = 0 if q == 0 else 32
                        run("tensor", lambda e, q=q, qp=qp: e.matmul(
                            PSA[:, q * 512:(q + 1) * 512], ONE1[qp:qp + 1, :],
                            SREC[q], start=True, stop=True),
                            inc="t" if q == 1 else None)
                    t_rb = cnt["t"]
                    ewait("vector", "t", t_rb)
                    run("vector", lambda e: e.tensor_copy(out=RB[:],
                                                          in_=PSA[:, 0:1024]))
                    v_rb = cnt["v"]
                    run("vector", lambda e, j=j, hf=hf: e.tensor_tensor(
                        out=AOB[:, j * 2048 + hf * 1024: j * 2048 + (hf + 1) * 1024],
                        in0=PSB[:], in1=RB[:], op=ALU.mult))
                    v_aob = cnt["v"]
                    exp_done = {}
                    pv_done = {}
                    cons_done = {}

            dwp = {}
            def wp_load(wq):
                s_ = "dwa" if wq % 2 == 0 else "dwb"
                edma("gpsimd", s_, WPB[wq % 2][:].rearrange(
                    "p (dc c) -> p dc c", c=512),
                    d_wproj.rearrange("(dc p) c -> p dc c", p=128)[:, :, wq * 512:(wq + 1) * 512])
                dwp[wq] = (s_, cnt[s_])
            ewait("gpsimd", "v", v_aob)
            wp_load(0)
            wp_load(1)
            v_pe = [0, 0]
            t_wq_last = 0
            for wq in range(4):
                if wq >= 1 and wq + 1 < 4 and (wq + 1) not in dwp:
                    pass
                ewait("tensor", dwp[wq][0], dwp[wq][1])
                ewait("tensor", "v", v_aob)
                for j in range(NJ):
                    half = j % 2
                    if v_pe[half]:
                        ewait("tensor", "v", v_pe[half])
                    for dc in range(16):
                        run("tensor", lambda e, wq=wq, j=j, dc=dc, half=half:
                            e.matmul(
                                PSA[:, half * 512:(half + 1) * 512],
                                AOB[:, j * 2048 + dc * 128: j * 2048 + (dc + 1) * 128],
                                WPB[wq % 2][:, dc * 512:(dc + 1) * 512],
                                start=(dc == 0), stop=False))
                    run("tensor", lambda e, wq=wq, half=half: e.matmul(
                        PSA[:, half * 512:(half + 1) * 512], ONEBR[:],
                        BPR[0:1, wq * 512:(wq + 1) * 512],
                        start=False, stop=True), inc="t")
                    t_pp = cnt["t"]
                    ewait("vector", "t", t_pp)
                    run("vector", lambda e, wq=wq, j=j, half=half: e.tensor_tensor(
                        out=X[:, j * D + wq * 512: j * D + (wq + 1) * 512],
                        in0=X[:, j * D + wq * 512: j * D + (wq + 1) * 512],
                        in1=PSA[:, half * 512:(half + 1) * 512], op=ALU.add),
                        inc="v")
                    v_pe[half] = cnt["v"]
                t_wq_last = cnt["t"]
                if wq + 2 < 4:
                    ewait("gpsimd", "t", t_wq_last)
                    wp_load(wq + 2)
            ewait("vector", "v", cnt["v"])
            ewait("scalar", "v", cnt["v"])
            ewait("tensor", "v", cnt["v"])
            ewait("gpsimd", "v", cnt["v"])

        ewait("vector", "v", max(v_gb, v_ss))
        ffn(0)
        if stop >= 2:
            attention()
        if stop >= 3:
            ffn(1)

        ewait("sync", "v", cnt["v"])
        ewait("sync", "g", cnt["g"])
        ewait("sync", "s", cnt["s"])
        ewait("sync", "t", cnt["t"])
        edma("sync", "so", d_y.rearrange("(j p) d -> p j d", p=128),
             X[:].rearrange("p (j d) -> p j d", d=D))
        if _dbg():
            edma("sync", "so", dbg["dSTAT"], STAT[:])
            edma("sync", "so", dbg["dAFC"], AFC[:])
            edma("sync", "so", dbg["dOHALL"], OHALL[:])
            edma("sync", "so", dbg["dXS"], XS[:])
            edma("sync", "so", dbg["dMNXS"], MNXS[:])
            edma("sync", "so", dbg["dARB"], ARB[:, 0:2])
            edma("sync", "so", dbg["dTKN"], TKN[:])
            edma("gpsimd", "so", dbg["dSS"], SS128[:])
            edma("sync", "so", dbg["dGAM"], GAM[:])
            edma("sync", "so", dbg["dBET"], BET[:])
            edma("sync", "so", dbg["dDBG1"], DBG1[:])
            edma("sync", "so", dbg["dSQS"], SQS[:])
        ewait("sync", "so")

        block = est.enter_context(nc.Block())

        @block.sync
        def _(e):
            for f in ops["sync"]:
                f(e)

        @block.vector
        def _(e):
            for f in ops["vector"]:
                f(e)

        @block.scalar
        def _(e):
            for f in ops["scalar"]:
                f(e)

        @block.tensor
        def _(e):
            for f in ops["tensor"]:
                f(e)

        @block.gpsimd
        def _(e):
            for f in ops["gpsimd"]:
                f(e)

    return nc



def _host_constf():
    cf = np.zeros((128, 256), np.float32)
    cf[:, 0:64] = np.arange(64, dtype=np.float32)[None, :]
    cf[:, 64:128] = (np.arange(64, dtype=np.float32) // 8)[None, :]
    cf[:, 128:256] = np.eye(128, dtype=np.float32)
    return cf


def _host_masks(cid):
    m = np.zeros((NKT, QT, QT), np.float32)
    for kt in range(NKT):
        g = 4 * (kt // 4) + cid
        kpos = kt * QT + np.arange(QT)[:, None]
        qpos = g * QT + np.arange(QT)[None, :]
        m[kt] = (kpos <= qpos).astype(np.float32)
    return m.reshape(NKT * QT, QT).astype(ml_dtypes.bfloat16)


def kernel(L0_attn_x, L0_attn_W_qkv, L0_attn_b_qkv, L0_attn_W_proj,
           L0_attn_b_proj, L0_ffn_ln_gamma, L0_ffn_ln_beta,
           L0_ffn_spline_bases, L0_ffn_spline_slopes, L0_ffn_output_scale):
    x = np.ascontiguousarray(np.asarray(L0_attn_x, dtype=np.float32))
    wqkv = np.ascontiguousarray(np.asarray(L0_attn_W_qkv, dtype=np.float32))
    bqkv = np.ascontiguousarray(np.asarray(L0_attn_b_qkv, dtype=np.float32))
    wproj = np.ascontiguousarray(np.asarray(L0_attn_W_proj, dtype=np.float32))
    bproj = np.ascontiguousarray(np.asarray(L0_attn_b_proj, dtype=np.float32)).reshape(1, D)
    gamma = np.ascontiguousarray(np.asarray(L0_ffn_ln_gamma, dtype=np.float32)).reshape(1, D)
    beta = np.ascontiguousarray(np.asarray(L0_ffn_ln_beta, dtype=np.float32)).reshape(1, D)
    bases = np.ascontiguousarray(np.asarray(L0_ffn_spline_bases, dtype=np.float32)).reshape(NT, DS * GRID)
    slopes = np.ascontiguousarray(np.asarray(L0_ffn_spline_slopes, dtype=np.float32)).reshape(NT, DS * GRID)
    oscale = np.ascontiguousarray(np.asarray(L0_ffn_output_scale, dtype=np.float32)).reshape(NT, 1)

    cf = _host_constf()
    idn = np.eye(128, dtype=ml_dtypes.bfloat16)

    nc = build_nc(stop=_BUILD_STOP)
    in_maps = []
    for core in range(8):
        batch, cid = core // 4, core % 4
        xs = np.concatenate(
            [x[batch, (4 * j + cid) * QT:(4 * j + cid + 1) * QT, :]
             for j in range(NJ)], axis=0)
        in_maps.append({
            "x": np.ascontiguousarray(xs),
            "wqkv": wqkv, "bqkv": bqkv, "wproj": wproj, "bproj": bproj,
            "gamma": gamma, "beta": beta, "bases": bases, "slopes": slopes,
            "oscale": oscale, "msk": _host_masks(cid), "constf": cf,
            "idnb": idn,
        })
    res = run_bass_kernel_spmd(nc, in_maps, core_ids=list(range(8)))
    out = np.empty((B, S, D), np.float32)
    for core in range(8):
        batch, cid = core // 4, core % 4
        y = res.results[core]["y"]
        for j in range(NJ):
            g = 4 * j + cid
            out[batch, g * QT:(g + 1) * QT, :] = y[j * QT:(j + 1) * QT, :]
    return out


_BUILD_STOP = 3


# version 04
# speedup vs baseline: 1.2735x; 1.0176x over previous
import numpy as np
import ml_dtypes
from contextlib import ExitStack

import concourse.bass as bass
import concourse.mybir as mybir
from concourse.bass_utils import run_bass_kernel_spmd

_DEBUG_DUMP = False

F32 = mybir.dt.float32
BF16 = mybir.dt.bfloat16
U8 = mybir.dt.uint8
AF = mybir.ActivationFunctionType
ALU = mybir.AluOpType
AX = mybir.AxisListType

B, S, D = 2, 4096, 2048
NJ, QT = 8, 128
T = NJ * QT
NH, HD = 16, 128
SCALE = float(HD) ** -0.5
NT, DS, GRID = 64, 32, 16
LNEPS = 1e-5
BIG = 1.0e9
FBIG = 1.0e30
EXPSHIFT = 20.0
NKT = 32
CCQ = 18


def _dbg():
    return _DEBUG_DUMP


def build_nc(stop=3):
    nc = bass.Bass("TRN2", target_bir_lowering=False, debug=False,
                   detect_race_conditions=True)

    d_x = nc.dram_tensor("x", [T, D], F32, kind="ExternalInput").ap()
    d_wqkv = nc.dram_tensor("wqkv", [D, 2304], F32, kind="ExternalInput").ap()
    d_bqkv = nc.dram_tensor("bqkv", [2304], F32, kind="ExternalInput").ap()
    d_wproj = nc.dram_tensor("wproj", [D, D], F32, kind="ExternalInput").ap()
    d_bproj = nc.dram_tensor("bproj", [1, D], F32, kind="ExternalInput").ap()
    d_gamma = nc.dram_tensor("gamma", [1, D], F32, kind="ExternalInput").ap()
    d_beta = nc.dram_tensor("beta", [1, D], F32, kind="ExternalInput").ap()
    d_bases = nc.dram_tensor("bases", [NT, DS * GRID], F32, kind="ExternalInput").ap()
    d_slopes = nc.dram_tensor("slopes", [NT, DS * GRID], F32, kind="ExternalInput").ap()
    d_oscale = nc.dram_tensor("oscale", [NT, 1], F32, kind="ExternalInput").ap()
    d_msk = nc.dram_tensor("msk", [NKT * QT, QT], BF16, kind="ExternalInput").ap()
    d_cf = nc.dram_tensor("constf", [128, 256], F32, kind="ExternalInput").ap()
    d_idnb = nc.dram_tensor("idnb", [128, 128], BF16, kind="ExternalInput").ap()
    d_y = nc.dram_tensor("y", [T, D], F32, kind="ExternalOutput").ap()
    dbg = {}
    if _dbg():
        for nm, shp in [("dSTAT", [128, 16]), ("dAFC", [128, 48]),
                        ("dOHALL", [128, 512]), ("dXS", [128, 256]),
                        ("dMNXS", [128, 32]), ("dARB", [128, 2]),
                        ("dTKN", [128, 16]), ("dSS", [128, 2048]),
                        ("dGAM", [128, 2048]), ("dBET", [128, 2048]),
                        ("dDBG1", [128, 16]), ("dDBG2", [128, 1024]), ("dSQS", [128, 8]),
                        ("dDBG3", [128, 1024])]:
            dbg[nm] = nc.dram_tensor(nm, shp, F32, kind="ExternalOutput").ap()

    d_ssrow = nc.dram_tensor("ssrow", [NT, DS], F32)
    d_ar_in = [nc.dram_tensor(f"ar_in{i}", [NT, 2], F32) for i in range(2)]
    d_ar_out = [nc.dram_tensor(f"ar_out{i}", [NT, 2], F32) for i in range(2)]
    d_kv_in = nc.dram_tensor("kv_in", [2 * 128, T], BF16)
    d_kv_out = nc.dram_tensor("kv_out", [4 * 2 * 128, T], BF16)

    est = ExitStack()
    with est:
        def sbt(name, shape, dt):
            return est.enter_context(nc.sbuf_tensor(name, shape, dt))

        X = sbt("X", [128, NJ * D], F32)
        MSK = sbt("MSK", [128, NKT * QT], BF16)
        SS128 = sbt("SS128", [128, D], BF16)
        GAM = sbt("GAM", [128, D], F32)
        BET = sbt("BET", [128, D], F32)
        CF = sbt("CF", [128, 256], F32)
        IDNB = sbt("IDNB", [128, 128], BF16)
        SL64 = sbt("SL64", [128, DS * GRID], F32)
        TBP = sbt("TBP", [128, DS * GRID], F32)
        MISC = sbt("MISC", [128, 4], F32)
        BQ = sbt("BQ", [128, CCQ], F32)
        BPR = sbt("BPR", [1, D], BF16)
        ONE1 = sbt("ONE1", [33, 128], F32)
        ONEB2 = sbt("ONEB2", [128, 2], BF16)
        ONEBR = sbt("ONEBR", [1, 128], BF16)
        BIGT = sbt("BIGT", [128, 64], F32)
        NBIGT = sbt("NBIGT", [128, 64], F32)
        R1 = sbt("R1", [128, 16384], BF16)
        R2 = sbt("R2", [128, 18432], BF16)
        KT = sbt("KT", [128, NKT * QT], BF16)
        VT = sbt("VT", [128, NKT * QT], BF16)
        PT = [sbt(f"PT{i}", [128, 1024], BF16) for i in range(2)]
        QSQ = sbt("QSQ", [128, 1024], BF16)
        VTRB = sbt("VTRB", [128, 1024], F32)
        TB2 = sbt("TB2", [128, DS * GRID], F32)
        TS2 = sbt("TS2", [128, DS * GRID], F32)
        OHALL = sbt("OHALL", [128, NJ * NT], F32)
        XS = sbt("XS", [128, NJ * DS], F32)
        MNXS = sbt("MNXS", [128, NJ * 4], F32)
        STAT = sbt("STAT", [128, 16], F32)
        SQS = sbt("SQS", [128, 8], F32)
        AFC = sbt("AFC", [128, 48], F32)
        TKN = sbt("TKN", [128, 16], F32)
        TSC = sbt("TSC", [128, 64], F32)
        TSCALL = sbt("TSCALL", [128, 512], F32)
        MMB2 = None
        MM1 = sbt("MM1", [128, 64], F32)
        MMIN = sbt("MMIN", [128, 64], F32)
        MMAX = sbt("MMAX", [128, 64], F32)
        EQM = sbt("EQM", [128, 512], U8)

        SC1 = sbt("SC1", [128, 1], F32)
        SC2 = sbt("SC2", [128, 1], F32)
        ARB = sbt("ARB", [128, 2], F32)
        OSC = sbt("OSC", [128, 1], F32)
        BB = sbt("BB", [128, 1], F32)
        KQM = sbt("KQM", [1, 8], F32)

        WBUF = [KT[:, 0:2048], KT[:, 2048:4096]]
        XNv = [R1[:, 0:4096].bitcast(F32), R1[:, 4096:8192].bitcast(F32)]
        TBALL = R1[:, 8192:16384].bitcast(F32)
        SCTB = R2[:, 8192:12288].bitcast(F32)
        TSALL = R2[:, 4096:8192]
        SCT = R2[:, 0:4096].bitcast(F32)
        SGN = [R2[:, 8192 + i * 4096: 8192 + i * 4096 + 2048] for i in range(2)]
        QQB = [R2[:, 10240 + i * 4096: 10240 + i * 4096 + 2048] for i in range(2)]
        GROW = R2[:, 0:4096].bitcast(F32)[0:1, :]
        BROW = R2[:, 4096:8192].bitcast(F32)[0:1, :]
        XN3B = [R1[:, 0:8192], R1[:, 8192:16384]]
        X3Fh = R1[:, 8192:16384]
        AOB = R1
        QKVB = R2
        WPB = [R2[:, 0:8192], R2[:, 8192:16384]]
        OHF = VTRB
        RB = VTRB
        VTO = VTRB[:, 0:512].bitcast(BF16)
        QSQF = QSQ.bitcast(F32)
        G64 = QSQF
        SREC = [QSQF[0:1, 0:512], QSQF[32:33, 0:512]]
        MMB2 = SCT[:, 512:640]
        MG256 = EQM[:, 0:256]

        T64A = SCT[:, 1024:1152]
        DBG1 = SCT[:, 1152:1168]
        PSBKF = None

        PSA = est.enter_context(nc.psum_tensor("PSA", [128, 1024], F32))
        PSB = est.enter_context(nc.psum_tensor("PSB", [128, 1024], F32))
        PSC = est.enter_context(nc.psum_tensor("PSC", [128, 1024], F32))
        PSD = est.enter_context(nc.psum_tensor("PSD", [128, 512], F32))
        PSBK = est.enter_context(nc.psum_tensor("PSBK", [128, 1024], BF16))
        PSBKF = PSBK.ap().bitcast(F32)

        sems = {}
        for s_ in ("di", "dw", "dwa", "dwb", "dk", "cc", "t", "v", "s", "g",
                   "so"):
            sems[s_] = est.enter_context(nc.semaphore(s_))
        cnt = {k: 0 for k in sems}
        ops = {e: [] for e in ("sync", "vector", "scalar", "tensor", "gpsimd")}

        def on(eng, fn):
            ops[eng].append(fn)

        def ewait(eng, s_, val=None):
            v = cnt[s_] if val is None else val
            if v <= 0:
                return
            on(eng, lambda e, s_=s_, v=v: e.wait_ge(sems[s_], v))

        def edma(eng, s_, out, in_, slow=False):
            cnt[s_] += 16
            if slow:
                on(eng, lambda e, out=out, in_=in_, s_=s_:
                   e.dma_start(out=out, in_=in_, allow_slow_non_contiguous=True)
                   .then_inc(sems[s_], 16))
            else:
                on(eng, lambda e, out=out, in_=in_, s_=s_:
                   e.dma_start(out=out, in_=in_).then_inc(sems[s_], 16))

        def vsync():
            ewait("vector", "v", cnt["v"])

        def vrun(fn):
            run("vector", fn, inc="v")

        def run(eng, fn, inc=None):
            if eng == "vector":
                ewait("vector", "v", cnt["v"])
                if inc is None:
                    inc = "v"
            elif eng == "scalar":
                ewait("scalar", "s", cnt["s"])
                if inc is None:
                    inc = "s"
            if inc is not None:
                cnt[inc] += 1
            def _f(e, fn=fn, inc=inc):
                r = fn(e)
                if inc is not None:
                    r.then_inc(sems[inc], 1)
            on(eng, _f)

        IOTA64 = CF[:, 0:64]
        IOTAD8 = CF[:, 64:128]
        IDNF = CF[:, 128:256]

        edma("sync", "di", X[:].rearrange("p (j d) -> p j d", d=D),
             d_x.rearrange("(j p) d -> p j d", p=128))
        edma("sync", "di", MSK[:].rearrange("p (n q) -> p n q", q=QT),
             d_msk.rearrange("(n p) q -> p n q", p=128))
        edma("sync", "di", CF[:], d_cf)
        edma("sync", "di", IDNB[:], d_idnb)
        edma("sync", "di", SL64[0:NT, :], d_slopes)
        edma("sync", "di", TBP[0:NT, :], d_bases)
        edma("sync", "di", OSC[0:NT, :], d_oscale)
        edma("sync", "di", BQ[:], d_bqkv.rearrange("(cc p) -> p cc", p=128),
             slow=True)
        ld_all = cnt["di"]
        edma("gpsimd", "dw", GROW, d_gamma)
        edma("gpsimd", "dw", BROW, d_beta)
        edma("gpsimd", "dw", BPR[:], d_bproj)
        g_rows = cnt["dw"]

        def _init(e):
            e.memset(ONE1[:], 1.0)
            e.memset(ONEB2[:], 1.0)
            e.memset(ONEBR[:], 1.0)
            e.memset(BIGT[:], BIG)
            e.memset(NBIGT[:], -BIG)
            return e.memset(MISC[:], 0.0)
        run("gpsimd", _init, inc="g")
        g_init = cnt["g"]

        ewait("tensor", "dw", g_rows)
        ewait("tensor", "g", g_init)
        for part in range(2):
            row = GROW if part == 0 else BROW
            dst = GAM if part == 0 else BET
            for h in range(2):
                ewait("tensor", "v", cnt["v"])
                for q in range(2):
                    run("tensor", lambda e, row=row, h=h, q=q: e.matmul(
                        PSA[:, q * 512:(q + 1) * 512], ONE1[0:1, :],
                        row[0:1, h * 1024 + q * 512: h * 1024 + (q + 1) * 512],
                        start=True, stop=True), inc="t" if q == 1 else None)
                ewait("vector", "t", cnt["t"])
                run("vector", lambda e, dst=dst, h=h: e.tensor_copy(
                    out=dst[:, h * 1024:(h + 1) * 1024], in_=PSA[:, 0:1024]),
                    inc="v")
        v_gb = cnt["v"]

        ewait("vector", "di", ld_all)
        if _dbg():
            run("vector", lambda e: e.tensor_copy(
                out=VTRB[:], in_=X[:, 7 * 2048: 7 * 2048 + 1024]), inc="v")
            ewait("sync", "v", cnt["v"])
            edma("sync", "so", dbg["dDBG2"], VTRB[:])
        run("vector", lambda e: e.tensor_reduce(
            out=TSC[0:NT, 0:DS],
            in_=SL64[0:NT, :].rearrange("p (j g) -> p j g", g=GRID),
            axis=AX.X, op=ALU.add), inc="v")
        ewait("scalar", "v", cnt["v"])
        run("scalar", lambda e: e.activation(
            out=TSC[0:NT, 0:DS], in_=TSC[0:NT, 0:DS], func=AF.Sign), inc="s")
        s_ss = cnt["s"]
        ewait("gpsimd", "g", g_init)
        run("gpsimd", lambda e: e.iota(
            G64[0:NT, :], [[0, DS], [1, GRID]], channel_multiplier=0,
            allow_small_or_imprecise_dtypes=True), inc="g")
        g_g64 = cnt["g"]
        ewait("vector", "g", g_g64)
        vrun(lambda e: e.tensor_tensor(
            out=G64[0:NT, :], in0=SL64[0:NT, :], in1=G64[0:NT, :], op=ALU.mult))
        vrun(lambda e: e.tensor_tensor(
            out=TBP[0:NT, :], in0=TBP[0:NT, :], in1=G64[0:NT, :],
            op=ALU.subtract))
        v_tbp = cnt["v"]
        ewait("gpsimd", "s", s_ss)
        edma("gpsimd", "dk", d_ssrow.ap(), TSC[0:NT, 0:DS])
        ewait("gpsimd", "dk")
        edma("gpsimd", "dk", VTRB[0:1, 0:1024],
             d_ssrow.ap().rearrange("a b -> (a b)")[None, 0:1024])
        edma("gpsimd", "dk", VTRB[32:33, 0:1024],
             d_ssrow.ap().rearrange("a b -> (a b)")[None, 1024:2048])
        g_ssrow = cnt["dk"]
        ewait("tensor", "dk", g_ssrow)
        for h in range(2):
            hp = 0 if h == 0 else 32
            ewait("tensor", "v", cnt["v"])
            for q in range(2):
                run("tensor", lambda e, hp=hp, q=q: e.matmul(
                    PSB[:, q * 512:(q + 1) * 512], ONE1[hp:hp + 1, :],
                    VTRB[hp:hp + 1, q * 512:(q + 1) * 512],
                    start=True, stop=True), inc="t" if q == 1 else None)
            ewait("vector", "t", cnt["t"])
            run("vector", lambda e, h=h: e.tensor_copy(
                out=SS128[:, h * 1024:(h + 1) * 1024], in_=PSB[:, 0:1024]),
                inc="v")
        v_ss = cnt["v"]

        def ffn(ar_idx):
            ewait("vector", "t", cnt["t"])
            ewait("scalar", "t", cnt["t"])
            ewait("vector", "v", cnt["v"])
            ewait("scalar", "v", cnt["v"])
            ewait("scalar", "g", cnt["g"])
            ewait("vector", "g", cnt["g"])
            for j in range(NJ):
                Xj = X[:, j * D:(j + 1) * D]
                run("scalar", lambda e, Xj=Xj, j=j: e.activation(
                    out=SGN[j % 2], in_=Xj, func=AF.Square,
                    accum_out=SQS[:, j:j + 1]), inc="s")
                vrun(lambda e, Xj=Xj, j=j: e.reduce_sum(
                    out=STAT[:, j:j + 1], in_=Xj, axis=AX.X))
            MU = AFC[:, 0:8]; VAR = AFC[:, 8:16]; RS1 = AFC[:, 16:24]
            RS2 = AFC[:, 24:32]; AFj = AFC[:, 32:40]; CFj = AFC[:, 40:48]
            AY = STAT[:, 0:8]; CY = STAT[:, 8:16]
            ewait("vector", "s", cnt["s"])
            vrun(lambda e: e.tensor_scalar(out=MU, in0=STAT[:, 0:8],
                                           scalar1=1.0 / D, scalar2=None,
                                           op0=ALU.mult))
            vrun(lambda e: e.tensor_scalar(out=VAR, in0=SQS[:], scalar1=1.0 / D,
                                           scalar2=None, op0=ALU.mult))
            vrun(lambda e: e.tensor_tensor(out=TSC[:, 0:8], in0=MU, in1=MU,
                                           op=ALU.mult))
            vrun(lambda e: e.tensor_tensor(out=VAR, in0=VAR, in1=TSC[:, 0:8],
                                           op=ALU.subtract))
            vrun(lambda e: e.tensor_scalar(out=RS1, in0=VAR, scalar1=LNEPS,
                                           scalar2=None, op0=ALU.add))
            vrun(lambda e: e.reciprocal(out=RS1, in_=RS1))
            ewait("scalar", "v", cnt["v"])
            run("scalar", lambda e: e.activation(out=RS1, in_=RS1, func=AF.Sqrt),
                inc="s")
            ewait("vector", "s", cnt["s"])
            vrun(lambda e: e.tensor_tensor(out=RS2, in0=RS1, in1=RS1, op=ALU.mult))
            vrun(lambda e: e.tensor_tensor(out=RS2, in0=RS2, in1=VAR, op=ALU.mult))
            vrun(lambda e: e.tensor_scalar(out=RS2, in0=RS2, scalar1=0.25,
                                           scalar2=LNEPS, op0=ALU.mult,
                                           op1=ALU.add))
            vrun(lambda e: e.reciprocal(out=RS2, in_=RS2))
            ewait("scalar", "v", cnt["v"])
            run("scalar", lambda e: e.activation(out=RS2, in_=RS2, func=AF.Sqrt),
                inc="s")
            ewait("vector", "s", cnt["s"])
            vrun(lambda e: e.tensor_tensor(out=AFj, in0=RS1, in1=RS2, op=ALU.mult))
            vrun(lambda e: e.tensor_scalar(out=AFj, in0=AFj, scalar1=0.5,
                                           scalar2=None, op0=ALU.mult))
            vrun(lambda e: e.tensor_tensor(out=CFj, in0=MU, in1=AFj, op=ALU.mult))
            vrun(lambda e: e.tensor_scalar(out=CFj, in0=CFj, scalar1=-1.0,
                                           scalar2=None, op0=ALU.mult))
            vrun(lambda e: e.tensor_tensor(out=CY, in0=MU, in1=RS1, op=ALU.mult))
            vrun(lambda e: e.tensor_scalar(out=CY, in0=CY, scalar1=-0.5,
                                           scalar2=None, op0=ALU.mult))
            vrun(lambda e: e.tensor_scalar(out=AY, in0=RS1, scalar1=0.5,
                                           scalar2=1.0, op0=ALU.mult, op1=ALU.add))

            s_sgn = [0] * NJ
            g_gam = [0] * NJ
            for jp in range(0, NJ, 2):
                for j in (jp, jp + 1):
                    Xj = X[:, j * D:(j + 1) * D]
                    XN = XNv[j % 2]
                    if j >= 2:
                        ewait("vector", "s", s_sgn[j - 2])
                        ewait("vector", "g", g_gam[j - 2])
                    vrun(lambda e, Xj=Xj, XN=XN, j=j: e.tensor_scalar(
                        out=XN, in0=Xj, scalar1=AFj[:, j:j + 1],
                        scalar2=CFj[:, j:j + 1], op0=ALU.mult, op1=ALU.add))
                    ewait("gpsimd", "v", cnt["v"])
                    run("gpsimd", lambda e, XN=XN: e.tensor_tensor(
                        out=XN, in0=XN, in1=GAM[:], op=ALU.mult), inc="g")
                    g_gam[j] = cnt["g"]
                    ewait("vector", "g", g_gam[j])
                    vrun(lambda e, XN=XN: e.tensor_tensor(
                        out=XN, in0=XN, in1=BET[:], op=ALU.add))
                    v_xn = cnt["v"]
                    ewait("scalar", "v", v_xn)
                    run("scalar", lambda e, XN=XN, j=j: e.activation(
                        out=SGN[j % 2], in_=XN, func=AF.Sign), inc="s")
                    s_sgn[j] = cnt["s"]
                    ewait("vector", "s", s_sgn[j])
                    vrun(lambda e, j=j: e.tensor_tensor(
                        out=QQB[j % 2], in0=SGN[j % 2], in1=SS128[:],
                        op=ALU.mult))
                    vrun(lambda e, j=j: e.tensor_reduce(
                        out=TSCALL[:, j * 64:(j + 1) * 64],
                        in_=QQB[j % 2].rearrange("p (t k) -> p t k", k=DS),
                        axis=AX.X, op=ALU.add))
                TS2v = TSCALL[:, jp * 64:(jp + 2) * 64]
                CS64 = MMIN[:, 0:16]
                CM8 = MM1[:, 0:2]; BC8 = MM1[:, 2:4]
                TM8 = MM1[:, 4:6]; TI8 = MM1[:, 6:8]
                vrun(lambda e, TS2v=TS2v, CS64=CS64: e.tensor_reduce(
                    out=CS64, in_=TS2v.rearrange("p (a c t) -> p (a c) t", t=8,
                                                 c=8), axis=AX.X, op=ALU.add))
                vrun(lambda e, CS64=CS64, CM8=CM8: e.tensor_reduce(
                    out=CM8, in_=CS64.rearrange("p (a c) -> p a c", c=8),
                    axis=AX.X, op=ALU.max))
                vrun(lambda e, CS64=CS64, CM8=CM8: e.tensor_tensor(
                    out=EQM[:, 0:16].rearrange("p (a c) -> p a c", c=8),
                    in0=CS64.rearrange("p (a c) -> p a c", c=8),
                    in1=CM8.unsqueeze(2).broadcast_to([128, 2, 8]),
                    op=ALU.is_equal))
                for a in range(2):
                    vrun(lambda e, a=a: e.select(
                        out=MMAX[:, a * 8:(a + 1) * 8],
                        mask=EQM[:, a * 8:(a + 1) * 8],
                        on_true=IOTA64[:, 0:8], on_false=BIGT[:, 0:8],
                        add_drain=True))
                vrun(lambda e, BC8=BC8: e.tensor_reduce(
                    out=BC8, in_=MMAX[:, 0:16].rearrange("p (a c) -> p a c", c=8),
                    axis=AX.X, op=ALU.min))
                vrun(lambda e, BC8=BC8: e.tensor_tensor(
                    out=MMB2.rearrange("p (a t) -> p a t", t=64),
                    in0=IOTAD8[:].unsqueeze(1).broadcast_to([128, 2, 64]),
                    in1=BC8.unsqueeze(2).broadcast_to([128, 2, 64]),
                    op=ALU.is_equal))
                vrun(lambda e, TS2v=TS2v: e.tensor_scalar(
                    out=MMB2, in0=MMB2, scalar1=65536.0, scalar2=None,
                    op0=ALU.mult))
                vrun(lambda e, TS2v=TS2v: e.tensor_tensor(
                    out=MMB2, in0=MMB2, in1=TS2v, op=ALU.add))
                vrun(lambda e, TM8=TM8: e.tensor_reduce(
                    out=TM8, in_=MMB2.rearrange("p (a t) -> p a t", t=64),
                    axis=AX.X, op=ALU.max))
                vrun(lambda e, TM8=TM8: e.tensor_tensor(
                    out=EQM[:, 0:128].rearrange("p (a t) -> p a t", t=64),
                    in0=MMB2.rearrange("p (a t) -> p a t", t=64),
                    in1=TM8.unsqueeze(2).broadcast_to([128, 2, 64]),
                    op=ALU.is_equal))
                for a in range(2):
                    vrun(lambda e, a=a: e.select(
                        out=MMB2[:, a * 64:(a + 1) * 64],
                        mask=EQM[:, a * 64:(a + 1) * 64],
                        on_true=IOTA64[:], on_false=BIGT[:], add_drain=True))
                vrun(lambda e, TI8=TI8: e.tensor_reduce(
                    out=TI8, in_=MMB2.rearrange("p (a t) -> p a t", t=64),
                    axis=AX.X, op=ALU.min))
                vrun(lambda e, TI8=TI8, jp=jp: e.tensor_tensor(
                    out=OHALL[:, jp * 64:(jp + 2) * 64].rearrange(
                        "p (a t) -> p a t", t=64),
                    in0=IOTA64[:].unsqueeze(1).broadcast_to([128, 2, 64]),
                    in1=TI8.unsqueeze(2).broadcast_to([128, 2, 64]),
                    op=ALU.is_equal))
                v_oh = cnt["v"]
                for j in (jp, jp + 1):
                    XN = XNv[j % 2]
                    OHj = OHALL[:, j * 64:(j + 1) * 64]
                    ewait("vector", "g", cnt["g"])
                    vrun(lambda e, XN=XN, OHj=OHj: e.tensor_tensor(
                        out=SCT[:].rearrange("p (a b) -> p a b", b=DS),
                        in0=XN.rearrange("p (a b) -> p a b", b=DS),
                        in1=OHj.unsqueeze(2).broadcast_to([128, 64, DS]),
                        op=ALU.mult))
                    vrun(lambda e, j=j: e.tensor_reduce(
                        out=XS[:, j * DS:(j + 1) * DS],
                        in_=SCT[:].rearrange("p (t k) -> p k t", k=DS),
                        axis=AX.X, op=ALU.add))
                ewait("tensor", "v", v_oh)
                for j in (jp, jp + 1):
                    OHj = OHALL[:, j * 64:(j + 1) * 64]
                    if jp or j != jp:
                        ewait("tensor", "v", cnt["v"])
                    run("tensor", lambda e, OHj=OHj: e.transpose(
                        PSD[0:64, 128:256], OHj, IDNF), inc="t")
                    ewait("vector", "t", cnt["t"])
                    vrun(lambda e, j=j: e.tensor_copy(
                        out=OHF[0:64, j * 128:(j + 1) * 128],
                        in_=PSD[0:64, 128:256]))
            vrun(lambda e: e.tensor_reduce(
                out=TKN[:, 0:8], in_=XS[:].rearrange("p (j k) -> p j k", k=DS),
                axis=AX.X, op=ALU.min))
            vrun(lambda e: e.tensor_reduce(
                out=TKN[:, 8:16], in_=XS[:].rearrange("p (j k) -> p j k", k=DS),
                axis=AX.X, op=ALU.max))
            MMB = SCT[:, 0:512]
            vrun(lambda e: e.tensor_scalar(out=TSC[:, 32:40], in0=TKN[:, 0:8],
                                           scalar1=-1024.0, scalar2=None,
                                           op0=ALU.add))
            vrun(lambda e: e.tensor_scalar(out=TSC[:, 40:48], in0=TKN[:, 8:16],
                                           scalar1=1024.0, scalar2=None,
                                           op0=ALU.add))
            vrun(lambda e, MMB=MMB: e.tensor_tensor(
                out=MMB.rearrange("p (j t) -> p j t", t=64),
                in0=OHALL[:].rearrange("p (j t) -> p j t", t=64),
                in1=TSC[:, 32:40].unsqueeze(2).broadcast_to([128, NJ, 64]),
                op=ALU.mult))
            vrun(lambda e, MMB=MMB: e.tensor_scalar(
                out=MMB, in0=MMB, scalar1=1024.0, scalar2=None, op0=ALU.add))
            vrun(lambda e, MMB=MMB: e.tensor_reduce(
                out=MMIN[:], in_=MMB.rearrange("p (j t) -> p t j", t=64),
                axis=AX.X, op=ALU.min))
            vrun(lambda e, MMB=MMB: e.tensor_tensor(
                out=MMB.rearrange("p (j t) -> p j t", t=64),
                in0=OHALL[:].rearrange("p (j t) -> p j t", t=64),
                in1=TSC[:, 40:48].unsqueeze(2).broadcast_to([128, NJ, 64]),
                op=ALU.mult))
            vrun(lambda e, MMB=MMB: e.tensor_scalar(
                out=MMB, in0=MMB, scalar1=-1024.0, scalar2=None, op0=ALU.add))
            vrun(lambda e, MMB=MMB: e.tensor_reduce(
                out=MMAX[:], in_=MMB.rearrange("p (j t) -> p t j", t=64),
                axis=AX.X, op=ALU.max))
            v_loopA = cnt["v"]


            ewait("tensor", "v", v_loopA)
            def _mmT(e):
                e.transpose(PSD[0:64, 128:256], MMIN[:], IDNF)
                return e.transpose(PSD[0:64, 256:384], MMAX[:], IDNF)
            run("tensor", _mmT, inc="t")
            ewait("vector", "t", cnt["t"])
            vrun(lambda e: e.tensor_copy(out=T64A[0:64, :], in_=PSD[0:64, 128:256]))
            vrun(lambda e: e.tensor_reduce(out=ARB[0:64, 0:1], in_=T64A[0:64, :],
                                           axis=AX.X, op=ALU.min))
            vrun(lambda e: e.tensor_copy(out=T64A[0:64, :], in_=PSD[0:64, 256:384]))
            vrun(lambda e: e.tensor_reduce(out=ARB[0:64, 1:2], in_=T64A[0:64, :],
                                           axis=AX.X, op=ALU.max))
            vrun(lambda e: e.tensor_scalar(out=ARB[0:64, 0:1], in0=ARB[0:64, 0:1],
                                           scalar1=-1.0, scalar2=None,
                                           op0=ALU.mult))
            ewait("gpsimd", "v", cnt["v"])
            edma("gpsimd", "dk", d_ar_in[ar_idx].ap(), ARB[0:64, 0:2])
            ewait("gpsimd", "dk")
            cnt["cc"] += 1
            on("gpsimd", lambda e, ar_idx=ar_idx: e.collective_compute(
                "AllReduce", ALU.max, replica_groups=[list(range(8))],
                ins=[d_ar_in[ar_idx].ap().opt()],
                outs=[d_ar_out[ar_idx].ap().opt()]).then_inc(sems["cc"], 1))
            ewait("gpsimd", "cc")
            edma("gpsimd", "dk", ARB[0:64, 0:2], d_ar_out[ar_idx].ap())
            ewait("vector", "dk", cnt["dk"])
            vrun(lambda e: e.tensor_scalar(out=MISC[0:64, 0:1], in0=ARB[0:64, 0:1],
                                           scalar1=-1.0, scalar2=None,
                                           op0=ALU.mult))
            vrun(lambda e: e.tensor_copy(out=MISC[0:64, 1:2], in_=ARB[0:64, 1:2]))
            vrun(lambda e: e.tensor_copy(out=MISC[0:64, 2:3], in_=OSC[0:64, :]))
            v_misc = cnt["v"]

            ewait("tensor", "v", max(v_misc, v_tbp))
            v_tabev = 0
            for j in range(NJ):
                OHFj = OHF[0:NT, j * 128:(j + 1) * 128]
                if v_tabev:
                    ewait("tensor", "v", v_tabev)
                run("tensor", lambda e, OHFj=OHFj: e.matmul(
                    PSA[:, 0:512], OHFj, TBP[0:NT, :], start=True, stop=True))
                run("tensor", lambda e, OHFj=OHFj: e.matmul(
                    PSB[:, 0:512], OHFj, SL64[0:NT, :], start=True, stop=True))
                run("tensor", lambda e, OHFj=OHFj: e.matmul(
                    PSD[:, 0:4], OHFj, MISC[0:NT, :], start=True, stop=True),
                    inc="t")
                ewait("vector", "t", cnt["t"])
                def _tabev(e, j=j):
                    e.tensor_copy(out=TBALL[:, j * 512:(j + 1) * 512],
                                  in_=PSA[:, 0:512])
                    e.tensor_copy(out=TSALL[:, j * 512:(j + 1) * 512],
                                  in_=PSB[:, 0:512])
                    return e.tensor_copy(out=MNXS[:, j * 4:(j + 1) * 4],
                                         in_=PSD[:, 0:4])
                run("vector", _tabev, inc="v")
                v_tabev = cnt["v"]

            MNv = MNXS[:].rearrange("p (j c) -> p j c", c=4)[:, :, 0:1]
            MXv = MNXS[:].rearrange("p (j c) -> p j c", c=4)[:, :, 1:2]
            SCv = MNXS[:].rearrange("p (j c) -> p j c", c=4)[:, :, 2:3]
            XSv = XS[:].rearrange("p (j k) -> p j k", k=DS)
            R8 = TSC[:, 8:16]; INV8 = TSC[:, 16:24]; SCH8 = TSC[:, 24:32]
            vsync()
            vrun(lambda e: e.tensor_tensor(
                out=R8.unsqueeze(2), in0=MXv, in1=MNv, op=ALU.subtract))
            vrun(lambda e: e.tensor_scalar(out=R8, in0=R8, scalar1=1e-8,
                                           scalar2=None, op0=ALU.add))
            vrun(lambda e: e.reciprocal(out=INV8, in_=R8))
            vrun(lambda e: e.tensor_scalar(out=SCH8.unsqueeze(2), in0=SCv,
                                           scalar1=0.5, scalar2=None,
                                           op0=ALU.mult))
            SP1v = SCT[:, 0:256]
            SP2v = SCT[:, 256:512]
            BACv = SCT[:, 512:768]
            SACv = SCT[:, 768:1024]
            MGv = EQM[:, 0:64]
            vrun(lambda e: e.tensor_tensor(
                out=SP1v.rearrange("p (j k) -> p j k", k=DS), in0=XSv,
                in1=MNv.broadcast_to([128, NJ, DS]), op=ALU.subtract))
            vrun(lambda e: e.tensor_tensor(
                out=SP1v.rearrange("p (j k) -> p j k", k=DS),
                in0=SP1v.rearrange("p (j k) -> p j k", k=DS),
                in1=INV8.unsqueeze(2).broadcast_to([128, NJ, DS]), op=ALU.mult))
            vrun(lambda e: e.tensor_scalar(out=SP1v, in0=SP1v,
                                           scalar1=1.0 - 1e-6, scalar2=0.0,
                                           op0=ALU.min, op1=ALU.max))
            vrun(lambda e: e.tensor_scalar(out=SP1v, in0=SP1v,
                                           scalar1=float(GRID), scalar2=None,
                                           op0=ALU.mult))
            TBv = TBALL[:].rearrange("p (j k g) -> p j k g", k=DS, g=GRID)
            TSv = TSALL[:].rearrange("p (j k g) -> p j k g", k=DS, g=GRID)
            vrun(lambda e: e.tensor_copy(
                out=BACv.rearrange("p (j k) -> p j k", k=DS), in_=TBv[:, :, :, 0]))
            vrun(lambda e: e.tensor_copy(
                out=SACv.rearrange("p (j k) -> p j k", k=DS), in_=TSv[:, :, :, 0]))
            for g in range(1, GRID):
                vrun(lambda e, g=g: e.tensor_scalar(
                    out=MG256[:], in0=SP1v, scalar1=float(g), scalar2=None,
                    op0=ALU.is_ge))
                vrun(lambda e, g=g: e.select(
                    out=BACv.rearrange("p (j k) -> p j k", k=DS),
                    mask=MG256[:].rearrange("p (j k) -> p j k", k=DS),
                    on_true=TBv[:, :, :, g],
                    on_false=BACv.rearrange("p (j k) -> p j k", k=DS),
                    add_drain=True))
                vrun(lambda e, g=g: e.select(
                    out=SACv.rearrange("p (j k) -> p j k", k=DS),
                    mask=MG256[:].rearrange("p (j k) -> p j k", k=DS),
                    on_true=TSv[:, :, :, g],
                    on_false=SACv.rearrange("p (j k) -> p j k", k=DS),
                    add_drain=True))
            vrun(lambda e: e.tensor_tensor(out=SP2v, in0=SACv, in1=SP1v,
                                           op=ALU.mult))
            vrun(lambda e: e.tensor_tensor(out=SP2v, in0=SP2v, in1=BACv,
                                           op=ALU.add))
            vrun(lambda e: e.tensor_tensor(
                out=SP2v.rearrange("p (j k) -> p j k", k=DS),
                in0=SP2v.rearrange("p (j k) -> p j k", k=DS),
                in1=SCH8.unsqueeze(2).broadcast_to([128, NJ, DS]), op=ALU.mult))
            g_scat = 0
            for j in range(NJ):
                Xj = X[:, j * D:(j + 1) * D]
                OHj = OHALL[:, j * 64:(j + 1) * 64]
                if g_scat:
                    ewait("vector", "g", g_scat)
                vrun(lambda e, Xj=Xj, j=j: e.tensor_scalar(
                    out=Xj, in0=Xj, scalar1=AY[:, j:j + 1],
                    scalar2=CY[:, j:j + 1], op0=ALU.mult, op1=ALU.add))
                vrun(lambda e, OHj=OHj, j=j: e.tensor_tensor(
                    out=SCTB[:].rearrange("p (a b) -> p a b", b=DS),
                    in0=OHj.unsqueeze(2).broadcast_to([128, 64, DS]),
                    in1=SP2v.rearrange("p (j2 k) -> p j2 k", k=DS)[:, j, :]
                        .unsqueeze(1).broadcast_to([128, 64, DS]),
                    op=ALU.mult))
                ewait("gpsimd", "v", cnt["v"])
                run("gpsimd", lambda e, Xj=Xj: e.tensor_tensor(
                    out=Xj, in0=Xj, in1=SCTB[:], op=ALU.add), inc="g")
                g_scat = cnt["g"]
            ewait("vector", "g", cnt["g"])
            ewait("scalar", "g", cnt["g"])
            ewait("tensor", "g", cnt["g"])

        def attention():
            _vred = [0] * NJ
            for j in range(NJ):
                Xj = X[:, j * D:(j + 1) * D]
                if j >= 2:
                    ewait("scalar", "v", _vred[j - 2])
                run("scalar", lambda e, Xj=Xj, j=j: e.activation(
                    out=SGN[j % 2], in_=Xj, func=AF.Square), inc="s")
                s_sq = cnt["s"]
                run("vector", lambda e, Xj=Xj, j=j: e.reduce_sum(
                    out=STAT[:, j:j + 1], in_=Xj, axis=AX.X))
                ewait("vector", "s", s_sq)
                run("vector", lambda e, j=j: e.reduce_sum(
                    out=SQS[:, j:j + 1], in_=SGN[j % 2], axis=AX.X), inc="v")
                _vred[j] = cnt["v"]
            A3 = AFC[:, 0:8]; C3 = AFC[:, 8:16]
            ewait("vector", "s", cnt["s"])
            vsync()
            vrun(lambda e: e.tensor_scalar(out=A3, in0=SQS[:], scalar1=1.0 / D,
                                           scalar2=None, op0=ALU.mult))
            vrun(lambda e: e.tensor_scalar(out=C3, in0=STAT[:, 0:8],
                                           scalar1=1.0 / D, scalar2=None,
                                           op0=ALU.mult))
            vrun(lambda e: e.tensor_tensor(out=AFC[:, 16:24], in0=C3, in1=C3,
                                           op=ALU.mult))
            vrun(lambda e: e.tensor_tensor(out=A3, in0=A3, in1=AFC[:, 16:24],
                                           op=ALU.subtract))
            vrun(lambda e: e.tensor_scalar(out=A3, in0=A3, scalar1=LNEPS,
                                           scalar2=None, op0=ALU.add))
            vrun(lambda e: e.reciprocal(out=A3, in_=A3))
            ewait("scalar", "v", cnt["v"])
            run("scalar", lambda e: e.activation(out=A3, in_=A3, func=AF.Sqrt),
                inc="s")
            ewait("vector", "s", cnt["s"])
            vrun(lambda e: e.tensor_tensor(out=C3, in0=C3, in1=A3, op=ALU.mult))
            vrun(lambda e: e.tensor_scalar(out=C3, in0=C3, scalar1=-1.0,
                                           scalar2=None, op0=ALU.mult))
            v_l3 = cnt["v"]

            ewait("gpsimd", "v", v_l3)
            ewait("gpsimd", "s", cnt["s"])
            dw_cc = {}
            def wqkv_load(hf, cc):
                s_ = "dwa" if cc % 2 == 0 else "dwb"
                edma("gpsimd", s_, WBUF[cc % 2][:].rearrange(
                    "p (dc c) -> p dc c", c=128),
                    d_wqkv.rearrange("(dc p) c -> p dc c", p=128)[:, :, cc * 128:(cc + 1) * 128])
                dw_cc[(hf, cc)] = (s_, cnt[s_])
            wqkv_load(0, 0)
            wqkv_load(0, 1)
            s_x3f = 0
            s_evac_prev = [0, 0]
            s_qkv_ev = [0, 0]
            for hf in range(2):
                v_xn3 = 0
                for j4 in range(4):
                    j = hf * 4 + j4
                    Xj = X[:, j * D:(j + 1) * D]
                    if hf == 1 and j4 == 0:
                        ewait("vector", "t", cnt["t"])
                    vrun(lambda e, Xj=Xj, j=j, j4=j4: e.tensor_scalar(
                        out=XN3B[0][:, j4 * 2048:(j4 + 1) * 2048], in0=Xj,
                        scalar1=A3[:, j:j + 1], scalar2=C3[:, j:j + 1],
                        op0=ALU.mult, op1=ALU.add))
                v_xn3 = cnt["v"]
                X3F = XN3B[1]
                ewait("tensor", "v", v_xn3)
                for dc in range(16):
                    if s_evac_prev[0]:
                        ewait("tensor", "s", s_evac_prev[0])
                    for j4 in range(4):
                        run("tensor", lambda e, dc=dc, j4=j4: e.transpose(
                            PSBK[:, j4 * 128:(j4 + 1) * 128],
                            XN3B[0][:, j4 * 2048 + dc * 128: j4 * 2048 + (dc + 1) * 128],
                            IDNB[:]), inc="t" if j4 == 3 else None)
                    t_tp = cnt["t"]
                    ewait("scalar", "t", t_tp)
                    run("scalar", lambda e, dc=dc, X3F=X3F: e.copy(
                        out=X3F[:, dc * 512:(dc + 1) * 512],
                        in_=PSBK[:, 0:512]), inc="s")
                    s_evac_prev[0] = cnt["s"]
                s_x3f = cnt["s"]
                for cc in range(CCQ):
                    ewait("tensor", dw_cc[(hf, cc)][0], dw_cc[(hf, cc)][1])
                    ewait("tensor", "s", s_x3f)
                    half = cc % 2
                    if s_qkv_ev[half]:
                        ewait("tensor", "s", s_qkv_ev[half])
                    for dc in range(16):
                        run("tensor", lambda e, cc=cc, dc=dc, half=half, X3F=X3F:
                            e.matmul(
                                PSA[:, half * 512:(half + 1) * 512],
                                WBUF[cc % 2][:, dc * 128:(dc + 1) * 128],
                                X3F[:, dc * 512:(dc + 1) * 512],
                                start=(dc == 0), stop=(dc == 15)),
                            inc="t" if dc == 15 else None)
                    t_qkv = cnt["t"]
                    ewait("scalar", "t", t_qkv)
                    run("scalar", lambda e, cc=cc, hf=hf, half=half: e.activation(
                        out=QKVB[:, cc * 1024 + hf * 512: cc * 1024 + hf * 512 + 512],
                        in_=PSA[:, half * 512:(half + 1) * 512], func=AF.Identity,
                        bias=BQ[:, cc:cc + 1], scale=1.0), inc="s")
                    s_qkv_ev[half] = cnt["s"]
                    nhf, ncc = hf, cc + 2
                    if ncc >= CCQ:
                        nhf, ncc = hf + 1, ncc - CCQ
                    if nhf < 2 and (nhf, ncc) not in dw_cc:
                        ewait("gpsimd", "t", t_qkv)
                        wqkv_load(nhf, ncc)
            s_qkv_done = cnt["s"]

            ewait("tensor", "s", s_qkv_done)
            v_vto = 0
            for j in range(NJ):
                if v_vto:
                    ewait("tensor", "v", v_vto)
                run("tensor", lambda e, j=j: e.transpose(
                    PSBK[:, 0:128],
                    QKVB[:, 17 * 1024 + j * 128: 17 * 1024 + (j + 1) * 128],
                    IDNB[:]), inc="t")
                ewait("vector", "t", cnt["t"])
                run("vector", lambda e, j=j: e.tensor_copy(
                    out=VTO[:, j * 128:(j + 1) * 128],
                    in_=PSBK[:, 0:128]), inc="v")
                v_vto = cnt["v"]
            ewait("gpsimd", "v", v_vto)
            ewait("gpsimd", "s", s_qkv_done)
            edma("gpsimd", "dk", d_kv_in.ap()[0:128, :], QKVB[:, 16 * 1024:17 * 1024])
            edma("gpsimd", "dk", d_kv_in.ap()[128:256, :], VTO[:])
            ewait("gpsimd", "dk")
            cnt["cc"] += 1
            on("gpsimd", lambda e: e.collective_compute(
                "AllGather", ALU.bypass,
                replica_groups=[[0, 1, 2, 3], [4, 5, 6, 7]],
                ins=[d_kv_in.ap().opt()],
                outs=[d_kv_out.ap().opt()]).then_inc(sems["cc"], 1))
            ewait("gpsimd", "cc")
            kvo = d_kv_out.ap().rearrange("(r b p) t -> r b p t", b=2, p=128)
            edma("gpsimd", "dk",
                 KT[:].rearrange("p (s r q) -> p s r q", r=4, q=128),
                 kvo[:, 0, :, :].rearrange("r p (s q) -> p s r q", q=128))
            edma("gpsimd", "dk",
                 VT[:].rearrange("p (s r q) -> p s r q", r=4, q=128),
                 kvo[:, 1, :, :].rearrange("r p (s q) -> p s r q", q=128))
            dk_kv = cnt["dk"]

            v_qsq_prev = 0
            for dc in range(16):
                ewait("vector", "t", cnt["t"])
                vrun(lambda e, dc=dc: e.tensor_tensor(
                    out=QSQ[:], in0=QKVB[:, dc * 1024:(dc + 1) * 1024],
                    in1=QKVB[:, dc * 1024:(dc + 1) * 1024], op=ALU.mult))
                ewait("tensor", "v", cnt["v"])
                run("tensor", lambda e, dc=dc: e.matmul(
                    PSD[0:2, 0:512], ONEB2[:], QSQ[:, 0:512],
                    start=(dc == 0), stop=(dc == 15)))
                run("tensor", lambda e, dc=dc: e.matmul(
                    PSBKF[0:2, 0:512], ONEB2[:], QSQ[:, 512:1024],
                    start=(dc == 0), stop=(dc == 15)), inc="t")
            t_qn = cnt["t"]
            ewait("vector", "t", t_qn)
            def _qmax(e):
                e.tensor_reduce(out=KQM[0:1, 0:1], in_=PSD[0:1, 0:512], axis=AX.X,
                                op=ALU.max)
                return e.tensor_reduce(out=KQM[0:1, 1:2], in_=PSBKF[0:1, 0:512],
                                       axis=AX.X, op=ALU.max)
            run("vector", _qmax, inc="v")
            vrun(lambda e: e.tensor_tensor(
                out=KQM[0:1, 0:1], in0=KQM[0:1, 0:1], in1=KQM[0:1, 1:2],
                op=ALU.max))
            ewait("vector", "dk", dk_kv)
            for kc in range(4):
                vrun(lambda e, kc=kc: e.tensor_tensor(
                    out=QSQ[:], in0=KT[:, kc * 1024:(kc + 1) * 1024],
                    in1=KT[:, kc * 1024:(kc + 1) * 1024], op=ALU.mult))
                ewait("tensor", "v", cnt["v"])
                if kc > 0:
                    ewait("tensor", "v", v_kn)
                run("tensor", lambda e: e.matmul(
                    PSD[0:2, 0:512], ONEB2[:], QSQ[:, 0:512], start=True,
                    stop=True))
                run("tensor", lambda e: e.matmul(
                    PSBKF[0:2, 0:512], ONEB2[:], QSQ[:, 512:1024], start=True,
                    stop=True), inc="t")
                ewait("vector", "t", cnt["t"])
                def _kmaxA(e):
                    e.tensor_reduce(out=KQM[0:1, 2:3], in_=PSD[0:1, 0:512],
                                    axis=AX.X, op=ALU.max)
                    return e.tensor_reduce(out=KQM[0:1, 3:4], in_=PSBKF[0:1, 0:512],
                                           axis=AX.X, op=ALU.max)
                run("vector", _kmaxA, inc="v")
                vsync()
                vrun(lambda e: e.tensor_tensor(out=KQM[0:1, 2:3],
                                               in0=KQM[0:1, 2:3],
                                               in1=KQM[0:1, 3:4], op=ALU.max))
                if kc == 0:
                    vrun(lambda e: e.tensor_copy(out=KQM[0:1, 4:5],
                                                 in_=KQM[0:1, 2:3]))
                else:
                    vrun(lambda e: e.tensor_tensor(out=KQM[0:1, 4:5],
                                                   in0=KQM[0:1, 4:5],
                                                   in1=KQM[0:1, 2:3],
                                                   op=ALU.max))
                v_kn = cnt["v"]
            run("vector", lambda e: e.tensor_tensor(
                out=KQM[0:1, 5:6], in0=KQM[0:1, 0:1], in1=KQM[0:1, 4:5],
                op=ALU.mult), inc="v")
            ewait("scalar", "v", cnt["v"])
            run("scalar", lambda e: e.activation(out=KQM[0:1, 5:6],
                                                 in_=KQM[0:1, 5:6], func=AF.Sqrt),
                inc="s")
            ewait("vector", "s", cnt["s"])
            vrun(lambda e: e.tensor_scalar(
                out=KQM[0:1, 6:7], in0=KQM[0:1, 5:6], scalar1=-SCALE,
                scalar2=EXPSHIFT, op0=ALU.mult, op1=ALU.add))
            ewait("tensor", "v", cnt["v"])
            run("tensor", lambda e: e.matmul(
                PSD[:, 384:385], ONE1[0:1, :], KQM[0:1, 6:7], start=True,
                stop=True), inc="t")
            ewait("vector", "t", cnt["t"])
            run("vector", lambda e: e.tensor_copy(out=BB[:], in_=PSD[:, 384:385]),
                inc="v")
            v_bb = cnt["v"]

            def emit_pv(kt, EJ):
                buf = kt % 2
                kind, val = cons_done[buf]
                ewait("tensor", kind, val)
                if kt == 0 and v_aob:
                    ewait("tensor", "v", v_aob)
                for h2 in range(8):
                    run("tensor", lambda e, h2=h2, kt=kt, buf=buf, EJ=EJ:
                        e.matmul(
                            PSB[:, h2 * 128:(h2 + 1) * 128],
                            VT[:, kt * 128:(kt + 1) * 128],
                            PT[buf][:, h2 * 128:(h2 + 1) * 128],
                            start=(kt == 0 and h2 % 4 == 0),
                            stop=(kt == EJ - 1 and h2 % 4 == 3)))
                run("tensor", lambda e, kt=kt, buf=buf, EJ=EJ: e.matmul(
                    PSD[0:2, 0:512], ONEB2[:], PT[buf][:, 0:512],
                    start=(kt == 0), stop=(kt == EJ - 1)))
                run("tensor", lambda e, kt=kt, buf=buf, EJ=EJ: e.matmul(
                    PSBKF[0:2, 0:512], ONEB2[:], PT[buf][:, 512:1024],
                    start=(kt == 0), stop=(kt == EJ - 1)), inc="t")
                pv_done[kt % 2] = cnt["t"]

            ewait("tensor", "dk", dk_kv)
            ewait("tensor", "v", v_bb)
            ewait("scalar", "v", v_bb)
            SSTRIP = [PSA, PSC]
            exp_done = {}
            cons_done = {}
            pv_done = {}
            v_aob = 0
            v_rb = 0
            for j in range(NJ):
                EJ = 4 * (j + 1)
                for hf in range(2):
                    pending = None
                    for kt in range(EJ):
                        buf = kt % 2
                        strip = SSTRIP[buf]
                        prev = exp_done.get(buf)
                        if prev:
                            ewait("tensor", "s", prev)
                        if v_rb:
                            ewait("tensor", "v", v_rb)
                            v_rb = 0
                        for h2 in range(8):
                            h = hf * 8 + h2
                            run("tensor", lambda e, h=h, h2=h2, j=j, kt=kt,
                                strip=strip: e.matmul(
                                    strip[:, h2 * 128:(h2 + 1) * 128],
                                    KT[:, kt * 128:(kt + 1) * 128],
                                    QKVB[:, h * 1024 + j * 128: h * 1024 + (j + 1) * 128],
                                    start=True, stop=True),
                                inc="t" if h2 == 7 else None)
                        t_sc = cnt["t"]
                        ewait("scalar", "t", t_sc)
                        prevpv = pv_done.get(buf)
                        if prevpv:
                            ewait("scalar", "t", prevpv)
                        run("scalar", lambda e, buf=buf, strip=strip: e.activation(
                            out=PT[buf][:], in_=strip[:], func=AF.Exp,
                            bias=BB[:], scale=SCALE), inc="s")
                        exp_done[buf] = cnt["s"]
                        if kt >= 4 * j:
                            ewait("vector", "s", exp_done[buf])
                            run("vector", lambda e, buf=buf, kt=kt: e.tensor_tensor(
                                out=PT[buf][:].rearrange("p (h q) -> p h q", q=128),
                                in0=PT[buf][:].rearrange("p (h q) -> p h q", q=128),
                                in1=MSK[:, kt * 128:(kt + 1) * 128].unsqueeze(1)
                                    .broadcast_to([128, 8, 128]),
                                op=ALU.mult))
                            cons_done[buf] = ("v", cnt["v"])
                        else:
                            cons_done[buf] = ("s", exp_done[buf])
                        if pending is not None:
                            emit_pv(pending, EJ)
                        pending = kt
                    emit_pv(pending, EJ)
                    pending = None
                    t_last = cnt["t"]
                    ewait("vector", "t", t_last)
                    vrun(lambda e: e.tensor_copy(out=SREC[0], in_=PSD[0:1, 0:512]))
                    vrun(lambda e: e.tensor_copy(out=SREC[1],
                                                 in_=PSBKF[0:1, 0:512]))
                    vrun(lambda e: e.reciprocal(out=SREC[0], in_=SREC[0]))
                    vrun(lambda e: e.reciprocal(out=SREC[1], in_=SREC[1]))
                    v_recd = cnt["v"]
                    ewait("tensor", "v", v_recd)
                    for q in range(2):
                        qp = 0 if q == 0 else 32
                        run("tensor", lambda e, q=q, qp=qp: e.matmul(
                            PSA[:, q * 512:(q + 1) * 512], ONE1[qp:qp + 1, :],
                            SREC[q], start=True, stop=True),
                            inc="t" if q == 1 else None)
                    t_rb = cnt["t"]
                    ewait("vector", "t", t_rb)
                    run("vector", lambda e: e.tensor_copy(out=RB[:],
                                                          in_=PSA[:, 0:1024]))
                    v_rb = cnt["v"]
                    run("vector", lambda e, j=j, hf=hf: e.tensor_tensor(
                        out=AOB[:, j * 2048 + hf * 1024: j * 2048 + (hf + 1) * 1024],
                        in0=PSB[:], in1=RB[:], op=ALU.mult))
                    v_aob = cnt["v"]
                    exp_done = {}
                    pv_done = {}
                    cons_done = {}

            dwp = {}
            def wp_load(wq):
                s_ = "dwa" if wq % 2 == 0 else "dwb"
                edma("gpsimd", s_, WPB[wq % 2][:].rearrange(
                    "p (dc c) -> p dc c", c=512),
                    d_wproj.rearrange("(dc p) c -> p dc c", p=128)[:, :, wq * 512:(wq + 1) * 512])
                dwp[wq] = (s_, cnt[s_])
            ewait("gpsimd", "v", v_aob)
            wp_load(0)
            wp_load(1)
            v_pe = [0, 0]
            t_wq_last = 0
            for wq in range(4):
                if wq >= 1 and wq + 1 < 4 and (wq + 1) not in dwp:
                    pass
                ewait("tensor", dwp[wq][0], dwp[wq][1])
                ewait("tensor", "v", v_aob)
                for j in range(NJ):
                    half = j % 2
                    if v_pe[half]:
                        ewait("tensor", "v", v_pe[half])
                    for dc in range(16):
                        run("tensor", lambda e, wq=wq, j=j, dc=dc, half=half:
                            e.matmul(
                                PSA[:, half * 512:(half + 1) * 512],
                                AOB[:, j * 2048 + dc * 128: j * 2048 + (dc + 1) * 128],
                                WPB[wq % 2][:, dc * 512:(dc + 1) * 512],
                                start=(dc == 0), stop=False))
                    run("tensor", lambda e, wq=wq, half=half: e.matmul(
                        PSA[:, half * 512:(half + 1) * 512], ONEBR[:],
                        BPR[0:1, wq * 512:(wq + 1) * 512],
                        start=False, stop=True), inc="t")
                    t_pp = cnt["t"]
                    ewait("vector", "t", t_pp)
                    run("vector", lambda e, wq=wq, j=j, half=half: e.tensor_tensor(
                        out=X[:, j * D + wq * 512: j * D + (wq + 1) * 512],
                        in0=X[:, j * D + wq * 512: j * D + (wq + 1) * 512],
                        in1=PSA[:, half * 512:(half + 1) * 512], op=ALU.add),
                        inc="v")
                    v_pe[half] = cnt["v"]
                t_wq_last = cnt["t"]
                if wq + 2 < 4:
                    ewait("gpsimd", "t", t_wq_last)
                    wp_load(wq + 2)
            ewait("vector", "v", cnt["v"])
            ewait("scalar", "v", cnt["v"])
            ewait("tensor", "v", cnt["v"])
            ewait("gpsimd", "v", cnt["v"])

        ewait("vector", "v", max(v_gb, v_ss))
        ffn(0)
        if stop >= 2:
            attention()
        if stop >= 3:
            ffn(1)

        ewait("sync", "v", cnt["v"])
        ewait("sync", "g", cnt["g"])
        ewait("sync", "s", cnt["s"])
        ewait("sync", "t", cnt["t"])
        edma("sync", "so", d_y.rearrange("(j p) d -> p j d", p=128),
             X[:].rearrange("p (j d) -> p j d", d=D))
        if _dbg():
            edma("sync", "so", dbg["dSTAT"], STAT[:])
            edma("sync", "so", dbg["dAFC"], AFC[:])
            edma("sync", "so", dbg["dOHALL"], OHALL[:])
            edma("sync", "so", dbg["dXS"], XS[:])
            edma("sync", "so", dbg["dMNXS"], MNXS[:])
            edma("sync", "so", dbg["dARB"], ARB[:, 0:2])
            edma("sync", "so", dbg["dTKN"], TKN[:])
            edma("gpsimd", "so", dbg["dSS"], SS128[:])
            edma("sync", "so", dbg["dGAM"], GAM[:])
            edma("sync", "so", dbg["dBET"], BET[:])
            edma("sync", "so", dbg["dDBG1"], DBG1[:])
            edma("sync", "so", dbg["dSQS"], SQS[:])
        ewait("sync", "so")

        block = est.enter_context(nc.Block())

        @block.sync
        def _(e):
            for f in ops["sync"]:
                f(e)

        @block.vector
        def _(e):
            for f in ops["vector"]:
                f(e)

        @block.scalar
        def _(e):
            for f in ops["scalar"]:
                f(e)

        @block.tensor
        def _(e):
            for f in ops["tensor"]:
                f(e)

        @block.gpsimd
        def _(e):
            for f in ops["gpsimd"]:
                f(e)

    return nc



def _host_constf():
    cf = np.zeros((128, 256), np.float32)
    cf[:, 0:64] = np.arange(64, dtype=np.float32)[None, :]
    cf[:, 64:128] = (np.arange(64, dtype=np.float32) // 8)[None, :]
    cf[:, 128:256] = np.eye(128, dtype=np.float32)
    return cf


def _host_masks(cid):
    m = np.zeros((NKT, QT, QT), np.float32)
    for kt in range(NKT):
        g = 4 * (kt // 4) + cid
        kpos = kt * QT + np.arange(QT)[:, None]
        qpos = g * QT + np.arange(QT)[None, :]
        m[kt] = (kpos <= qpos).astype(np.float32)
    return m.reshape(NKT * QT, QT).astype(ml_dtypes.bfloat16)


def kernel(L0_attn_x, L0_attn_W_qkv, L0_attn_b_qkv, L0_attn_W_proj,
           L0_attn_b_proj, L0_ffn_ln_gamma, L0_ffn_ln_beta,
           L0_ffn_spline_bases, L0_ffn_spline_slopes, L0_ffn_output_scale):
    x = np.ascontiguousarray(np.asarray(L0_attn_x, dtype=np.float32))
    wqkv = np.ascontiguousarray(np.asarray(L0_attn_W_qkv, dtype=np.float32))
    bqkv = np.ascontiguousarray(np.asarray(L0_attn_b_qkv, dtype=np.float32))
    wproj = np.ascontiguousarray(np.asarray(L0_attn_W_proj, dtype=np.float32))
    bproj = np.ascontiguousarray(np.asarray(L0_attn_b_proj, dtype=np.float32)).reshape(1, D)
    gamma = np.ascontiguousarray(np.asarray(L0_ffn_ln_gamma, dtype=np.float32)).reshape(1, D)
    beta = np.ascontiguousarray(np.asarray(L0_ffn_ln_beta, dtype=np.float32)).reshape(1, D)
    bases = np.ascontiguousarray(np.asarray(L0_ffn_spline_bases, dtype=np.float32)).reshape(NT, DS * GRID)
    slopes = np.ascontiguousarray(np.asarray(L0_ffn_spline_slopes, dtype=np.float32)).reshape(NT, DS * GRID)
    oscale = np.ascontiguousarray(np.asarray(L0_ffn_output_scale, dtype=np.float32)).reshape(NT, 1)

    cf = _host_constf()
    idn = np.eye(128, dtype=ml_dtypes.bfloat16)

    nc = build_nc(stop=_BUILD_STOP)
    in_maps = []
    for core in range(8):
        batch, cid = core // 4, core % 4
        xs = np.concatenate(
            [x[batch, (4 * j + cid) * QT:(4 * j + cid + 1) * QT, :]
             for j in range(NJ)], axis=0)
        in_maps.append({
            "x": np.ascontiguousarray(xs),
            "wqkv": wqkv, "bqkv": bqkv, "wproj": wproj, "bproj": bproj,
            "gamma": gamma, "beta": beta, "bases": bases, "slopes": slopes,
            "oscale": oscale, "msk": _host_masks(cid), "constf": cf,
            "idnb": idn,
        })
    res = run_bass_kernel_spmd(nc, in_maps, core_ids=list(range(8)))
    out = np.empty((B, S, D), np.float32)
    for core in range(8):
        batch, cid = core // 4, core % 4
        y = res.results[core]["y"]
        for j in range(NJ):
            g = 4 * j + cid
            out[batch, g * QT:(g + 1) * QT, :] = y[j * QT:(j + 1) * QT, :]
    return out


_BUILD_STOP = 3


# version 05
# speedup vs baseline: 1.2896x; 1.0127x over previous
import numpy as np
import ml_dtypes
from contextlib import ExitStack

import concourse.bass as bass
import concourse.mybir as mybir
from concourse.bass_utils import run_bass_kernel_spmd

_DEBUG_DUMP = False

F32 = mybir.dt.float32
BF16 = mybir.dt.bfloat16
U8 = mybir.dt.uint8
AF = mybir.ActivationFunctionType
ALU = mybir.AluOpType
AX = mybir.AxisListType

B, S, D = 2, 4096, 2048
NJ, QT = 8, 128
T = NJ * QT
NH, HD = 16, 128
SCALE = float(HD) ** -0.5
NT, DS, GRID = 64, 32, 16
LNEPS = 1e-5
BIG = 1.0e9
FBIG = 1.0e30
EXPSHIFT = 20.0
NKT = 32
CCQ = 18


def _dbg():
    return _DEBUG_DUMP


def build_nc(stop=3):
    nc = bass.Bass("TRN2", target_bir_lowering=False, debug=False,
                   detect_race_conditions=True)

    d_x = nc.dram_tensor("x", [T, D], F32, kind="ExternalInput").ap()
    d_wqkv = nc.dram_tensor("wqkv", [D, 2304], F32, kind="ExternalInput").ap()
    d_bqkv = nc.dram_tensor("bqkv", [2304], F32, kind="ExternalInput").ap()
    d_wproj = nc.dram_tensor("wproj", [D, D], F32, kind="ExternalInput").ap()
    d_bproj = nc.dram_tensor("bproj", [1, D], F32, kind="ExternalInput").ap()
    d_gamma = nc.dram_tensor("gamma", [1, D], F32, kind="ExternalInput").ap()
    d_beta = nc.dram_tensor("beta", [1, D], F32, kind="ExternalInput").ap()
    d_bases = nc.dram_tensor("bases", [NT, DS * GRID], F32, kind="ExternalInput").ap()
    d_slopes = nc.dram_tensor("slopes", [NT, DS * GRID], F32, kind="ExternalInput").ap()
    d_oscale = nc.dram_tensor("oscale", [NT, 1], F32, kind="ExternalInput").ap()
    d_msk = nc.dram_tensor("msk", [NKT * QT, QT], BF16, kind="ExternalInput").ap()
    d_cf = nc.dram_tensor("constf", [128, 256], F32, kind="ExternalInput").ap()
    d_idnb = nc.dram_tensor("idnb", [128, 128], BF16, kind="ExternalInput").ap()
    d_y = nc.dram_tensor("y", [T, D], F32, kind="ExternalOutput").ap()
    dbg = {}
    if _dbg():
        for nm, shp in [("dSTAT", [128, 16]), ("dAFC", [128, 48]),
                        ("dOHALL", [128, 512]), ("dXS", [128, 256]),
                        ("dMNXS", [128, 32]), ("dARB", [128, 2]),
                        ("dTKN", [128, 16]), ("dSS", [128, 2048]),
                        ("dGAM", [128, 2048]), ("dBET", [128, 2048]),
                        ("dDBG1", [128, 16]), ("dDBG2", [128, 1024]), ("dSQS", [128, 8]),
                        ("dDBG3", [128, 1024])]:
            dbg[nm] = nc.dram_tensor(nm, shp, F32, kind="ExternalOutput").ap()

    d_ssrow = nc.dram_tensor("ssrow", [NT, DS], F32)
    d_ar_in = [nc.dram_tensor(f"ar_in{i}", [NT, 2], F32) for i in range(2)]
    d_ar_out = [nc.dram_tensor(f"ar_out{i}", [NT, 2], F32) for i in range(2)]
    d_kv_in = nc.dram_tensor("kv_in", [2 * 128, T], BF16)
    d_kv_out = nc.dram_tensor("kv_out", [4 * 2 * 128, T], BF16)

    est = ExitStack()
    with est:
        def sbt(name, shape, dt):
            return est.enter_context(nc.sbuf_tensor(name, shape, dt))

        X = sbt("X", [128, NJ * D], F32)
        MSK = sbt("MSK", [128, NKT * QT], BF16)
        SS128 = sbt("SS128", [128, D], BF16)
        GAM = sbt("GAM", [128, D], F32)
        BET = sbt("BET", [128, D], F32)
        CF = sbt("CF", [128, 256], F32)
        IDNB = sbt("IDNB", [128, 128], BF16)
        SL64 = sbt("SL64", [128, DS * GRID], F32)
        TBP = sbt("TBP", [128, DS * GRID], F32)
        MISC = sbt("MISC", [128, 4], F32)
        BQ = sbt("BQ", [128, CCQ], F32)
        BPR = sbt("BPR", [1, D], BF16)
        ONE1 = sbt("ONE1", [33, 128], F32)
        ONEB2 = sbt("ONEB2", [128, 2], BF16)
        ONEBR = sbt("ONEBR", [1, 128], BF16)
        BIGT = sbt("BIGT", [128, 64], F32)
        NBIGT = sbt("NBIGT", [128, 64], F32)
        R1 = sbt("R1", [128, 16384], BF16)
        R2 = sbt("R2", [128, 18432], BF16)
        KT = sbt("KT", [128, NKT * QT], BF16)
        VT = sbt("VT", [128, NKT * QT], BF16)
        PT = [sbt(f"PT{i}", [128, 1024], BF16) for i in range(2)]
        QSQ = sbt("QSQ", [128, 1024], BF16)
        VTRB = sbt("VTRB", [128, 1024], F32)
        TB2 = sbt("TB2", [128, DS * GRID], F32)
        TS2 = sbt("TS2", [128, DS * GRID], F32)
        OHALL = sbt("OHALL", [128, NJ * NT], F32)
        XS = sbt("XS", [128, NJ * DS], F32)
        MNXS = sbt("MNXS", [128, NJ * 4], F32)
        STAT = sbt("STAT", [128, 16], F32)
        SQS = sbt("SQS", [128, 8], F32)
        AFC = sbt("AFC", [128, 48], F32)
        TKN = sbt("TKN", [128, 16], F32)
        TSC = sbt("TSC", [128, 64], F32)
        TSCALL = sbt("TSCALL", [128, 512], F32)
        MMB2 = None
        MM1 = sbt("MM1", [128, 64], F32)
        MMIN = sbt("MMIN", [128, 64], F32)
        MMAX = sbt("MMAX", [128, 64], F32)
        EQM = sbt("EQM", [128, 512], U8)

        SC1 = sbt("SC1", [128, 1], F32)
        SC2 = sbt("SC2", [128, 1], F32)
        ARB = sbt("ARB", [128, 2], F32)
        OSC = sbt("OSC", [128, 1], F32)
        BB = sbt("BB", [128, 1], F32)
        KQM = sbt("KQM", [1, 8], F32)

        WBUF = [KT[:, 0:2048], KT[:, 2048:4096]]
        XNv = [R1[:, 0:4096].bitcast(F32), R1[:, 4096:8192].bitcast(F32)]
        TBALL = R1[:, 8192:16384].bitcast(F32)
        SCTB = R2[:, 8192:12288].bitcast(F32)
        TSALL = R2[:, 4096:8192]
        SCT = R2[:, 0:4096].bitcast(F32)
        SGN = [R2[:, 8192 + i * 4096: 8192 + i * 4096 + 2048] for i in range(2)]
        QQB = [R2[:, 10240 + i * 4096: 10240 + i * 4096 + 2048] for i in range(2)]
        GROW = R2[:, 0:4096].bitcast(F32)[0:1, :]
        BROW = R2[:, 4096:8192].bitcast(F32)[0:1, :]
        XN3B = [R1[:, 0:8192], R1[:, 8192:16384]]
        X3Fh = R1[:, 8192:16384]
        AOB = R1
        QKVB = R2
        WPB = [R2[:, 0:8192], R2[:, 8192:16384]]
        OHF = VTRB
        RB = VTRB
        VTO = VTRB[:, 0:512].bitcast(BF16)
        QSQF = QSQ.bitcast(F32)
        G64 = QSQF
        SREC = [QSQF[0:1, 0:512], QSQF[32:33, 0:512]]
        MMB2 = SCT[:, 512:640]
        MG256 = EQM[:, 0:256]

        T64A = SCT[:, 1024:1152]
        DBG1 = SCT[:, 1152:1168]
        PSBKF = None

        PSA = est.enter_context(nc.psum_tensor("PSA", [128, 1024], F32))
        PSB = est.enter_context(nc.psum_tensor("PSB", [128, 1024], F32))
        PSC = est.enter_context(nc.psum_tensor("PSC", [128, 1024], F32))
        PSD = est.enter_context(nc.psum_tensor("PSD", [128, 512], F32))
        PSBK = est.enter_context(nc.psum_tensor("PSBK", [128, 1024], BF16))
        PSBKF = PSBK.ap().bitcast(F32)

        sems = {}
        for s_ in ("di", "dw", "dwa", "dwb", "dk", "cc", "t", "v", "s", "g",
                   "so"):
            sems[s_] = est.enter_context(nc.semaphore(s_))
        cnt = {k: 0 for k in sems}
        ops = {e: [] for e in ("sync", "vector", "scalar", "tensor", "gpsimd")}

        def on(eng, fn):
            ops[eng].append(fn)

        def ewait(eng, s_, val=None):
            v = cnt[s_] if val is None else val
            if v <= 0:
                return
            on(eng, lambda e, s_=s_, v=v: e.wait_ge(sems[s_], v))

        def edma(eng, s_, out, in_, slow=False):
            cnt[s_] += 16
            if slow:
                on(eng, lambda e, out=out, in_=in_, s_=s_:
                   e.dma_start(out=out, in_=in_, allow_slow_non_contiguous=True)
                   .then_inc(sems[s_], 16))
            else:
                on(eng, lambda e, out=out, in_=in_, s_=s_:
                   e.dma_start(out=out, in_=in_).then_inc(sems[s_], 16))

        def vsync():
            ewait("vector", "v", cnt["v"])

        def vrun(fn):
            run("vector", fn, inc="v")

        def run(eng, fn, inc=None):
            if eng == "vector":
                ewait("vector", "v", cnt["v"])
                if inc is None:
                    inc = "v"
            elif eng == "scalar":
                ewait("scalar", "s", cnt["s"])
                if inc is None:
                    inc = "s"
            if inc is not None:
                cnt[inc] += 1
            def _f(e, fn=fn, inc=inc):
                r = fn(e)
                if inc is not None:
                    r.then_inc(sems[inc], 1)
            on(eng, _f)

        IOTA64 = CF[:, 0:64]
        IOTAD8 = CF[:, 64:128]
        IDNF = CF[:, 128:256]

        edma("sync", "di", X[:].rearrange("p (j d) -> p j d", d=D),
             d_x.rearrange("(j p) d -> p j d", p=128))
        edma("sync", "di", MSK[:].rearrange("p (n q) -> p n q", q=QT),
             d_msk.rearrange("(n p) q -> p n q", p=128))
        edma("sync", "di", CF[:], d_cf)
        edma("sync", "di", IDNB[:], d_idnb)
        edma("sync", "di", SL64[0:NT, :], d_slopes)
        edma("sync", "di", TBP[0:NT, :], d_bases)
        edma("sync", "di", OSC[0:NT, :], d_oscale)
        edma("sync", "di", BQ[:], d_bqkv.rearrange("(cc p) -> p cc", p=128),
             slow=True)
        ld_all = cnt["di"]
        edma("gpsimd", "dw", GROW, d_gamma)
        edma("gpsimd", "dw", BROW, d_beta)
        edma("gpsimd", "dw", BPR[:], d_bproj)
        g_rows = cnt["dw"]

        def _init(e):
            e.memset(ONE1[:], 1.0)
            e.memset(ONEB2[:], 1.0)
            e.memset(ONEBR[:], 1.0)
            e.memset(BIGT[:], BIG)
            e.memset(NBIGT[:], -BIG)
            return e.memset(MISC[:], 0.0)
        run("gpsimd", _init, inc="g")
        g_init = cnt["g"]

        ewait("tensor", "dw", g_rows)
        ewait("tensor", "g", g_init)
        for part in range(2):
            row = GROW if part == 0 else BROW
            dst = GAM if part == 0 else BET
            for h in range(2):
                ewait("tensor", "v", cnt["v"])
                for q in range(2):
                    run("tensor", lambda e, row=row, h=h, q=q: e.matmul(
                        PSA[:, q * 512:(q + 1) * 512], ONE1[0:1, :],
                        row[0:1, h * 1024 + q * 512: h * 1024 + (q + 1) * 512],
                        start=True, stop=True), inc="t" if q == 1 else None)
                ewait("vector", "t", cnt["t"])
                run("vector", lambda e, dst=dst, h=h: e.tensor_copy(
                    out=dst[:, h * 1024:(h + 1) * 1024], in_=PSA[:, 0:1024]),
                    inc="v")
        v_gb = cnt["v"]

        ewait("vector", "di", ld_all)
        if _dbg():
            run("vector", lambda e: e.tensor_copy(
                out=VTRB[:], in_=X[:, 7 * 2048: 7 * 2048 + 1024]), inc="v")
            ewait("sync", "v", cnt["v"])
            edma("sync", "so", dbg["dDBG2"], VTRB[:])
        run("vector", lambda e: e.tensor_reduce(
            out=TSC[0:NT, 0:DS],
            in_=SL64[0:NT, :].rearrange("p (j g) -> p j g", g=GRID),
            axis=AX.X, op=ALU.add), inc="v")
        ewait("scalar", "v", cnt["v"])
        run("scalar", lambda e: e.activation(
            out=TSC[0:NT, 0:DS], in_=TSC[0:NT, 0:DS], func=AF.Sign), inc="s")
        s_ss = cnt["s"]
        ewait("gpsimd", "g", g_init)
        run("gpsimd", lambda e: e.iota(
            G64[0:NT, :], [[0, DS], [1, GRID]], channel_multiplier=0,
            allow_small_or_imprecise_dtypes=True), inc="g")
        g_g64 = cnt["g"]
        ewait("vector", "g", g_g64)
        vrun(lambda e: e.tensor_tensor(
            out=G64[0:NT, :], in0=SL64[0:NT, :], in1=G64[0:NT, :], op=ALU.mult))
        vrun(lambda e: e.tensor_tensor(
            out=TBP[0:NT, :], in0=TBP[0:NT, :], in1=G64[0:NT, :],
            op=ALU.subtract))
        v_tbp = cnt["v"]
        ewait("gpsimd", "s", s_ss)
        edma("gpsimd", "dk", d_ssrow.ap(), TSC[0:NT, 0:DS])
        ewait("gpsimd", "dk")
        edma("gpsimd", "dk", VTRB[0:1, 0:1024],
             d_ssrow.ap().rearrange("a b -> (a b)")[None, 0:1024])
        edma("gpsimd", "dk", VTRB[32:33, 0:1024],
             d_ssrow.ap().rearrange("a b -> (a b)")[None, 1024:2048])
        g_ssrow = cnt["dk"]
        ewait("tensor", "dk", g_ssrow)
        for h in range(2):
            hp = 0 if h == 0 else 32
            ewait("tensor", "v", cnt["v"])
            for q in range(2):
                run("tensor", lambda e, hp=hp, q=q: e.matmul(
                    PSB[:, q * 512:(q + 1) * 512], ONE1[hp:hp + 1, :],
                    VTRB[hp:hp + 1, q * 512:(q + 1) * 512],
                    start=True, stop=True), inc="t" if q == 1 else None)
            ewait("vector", "t", cnt["t"])
            run("vector", lambda e, h=h: e.tensor_copy(
                out=SS128[:, h * 1024:(h + 1) * 1024], in_=PSB[:, 0:1024]),
                inc="v")
        v_ss = cnt["v"]

        def ffn(ar_idx):
            ewait("vector", "t", cnt["t"])
            ewait("scalar", "t", cnt["t"])
            ewait("vector", "v", cnt["v"])
            ewait("scalar", "v", cnt["v"])
            ewait("scalar", "g", cnt["g"])
            ewait("vector", "g", cnt["g"])
            for j in range(NJ):
                Xj = X[:, j * D:(j + 1) * D]
                run("scalar", lambda e, Xj=Xj, j=j: e.activation(
                    out=SGN[j % 2], in_=Xj, func=AF.Square,
                    accum_out=SQS[:, j:j + 1]), inc="s")
                vrun(lambda e, Xj=Xj, j=j: e.reduce_sum(
                    out=STAT[:, j:j + 1], in_=Xj, axis=AX.X))
            MU = AFC[:, 0:8]; VAR = AFC[:, 8:16]; RS1 = AFC[:, 16:24]
            RS2 = AFC[:, 24:32]; AFj = AFC[:, 32:40]; CFj = AFC[:, 40:48]
            AY = STAT[:, 0:8]; CY = STAT[:, 8:16]
            ewait("vector", "s", cnt["s"])
            vrun(lambda e: e.tensor_scalar(out=MU, in0=STAT[:, 0:8],
                                           scalar1=1.0 / D, scalar2=None,
                                           op0=ALU.mult))
            vrun(lambda e: e.tensor_scalar(out=VAR, in0=SQS[:], scalar1=1.0 / D,
                                           scalar2=None, op0=ALU.mult))
            vrun(lambda e: e.tensor_tensor(out=TSC[:, 0:8], in0=MU, in1=MU,
                                           op=ALU.mult))
            vrun(lambda e: e.tensor_tensor(out=VAR, in0=VAR, in1=TSC[:, 0:8],
                                           op=ALU.subtract))
            vrun(lambda e: e.tensor_scalar(out=RS1, in0=VAR, scalar1=LNEPS,
                                           scalar2=None, op0=ALU.add))
            vrun(lambda e: e.reciprocal(out=RS1, in_=RS1))
            ewait("scalar", "v", cnt["v"])
            run("scalar", lambda e: e.activation(out=RS1, in_=RS1, func=AF.Sqrt),
                inc="s")
            ewait("vector", "s", cnt["s"])
            vrun(lambda e: e.tensor_tensor(out=RS2, in0=RS1, in1=RS1, op=ALU.mult))
            vrun(lambda e: e.tensor_tensor(out=RS2, in0=RS2, in1=VAR, op=ALU.mult))
            vrun(lambda e: e.tensor_scalar(out=RS2, in0=RS2, scalar1=0.25,
                                           scalar2=LNEPS, op0=ALU.mult,
                                           op1=ALU.add))
            vrun(lambda e: e.reciprocal(out=RS2, in_=RS2))
            ewait("scalar", "v", cnt["v"])
            run("scalar", lambda e: e.activation(out=RS2, in_=RS2, func=AF.Sqrt),
                inc="s")
            ewait("vector", "s", cnt["s"])
            vrun(lambda e: e.tensor_tensor(out=AFj, in0=RS1, in1=RS2, op=ALU.mult))
            vrun(lambda e: e.tensor_scalar(out=AFj, in0=AFj, scalar1=0.5,
                                           scalar2=None, op0=ALU.mult))
            vrun(lambda e: e.tensor_tensor(out=CFj, in0=MU, in1=AFj, op=ALU.mult))
            vrun(lambda e: e.tensor_scalar(out=CFj, in0=CFj, scalar1=-1.0,
                                           scalar2=None, op0=ALU.mult))
            vrun(lambda e: e.tensor_tensor(out=CY, in0=MU, in1=RS1, op=ALU.mult))
            vrun(lambda e: e.tensor_scalar(out=CY, in0=CY, scalar1=-0.5,
                                           scalar2=None, op0=ALU.mult))
            vrun(lambda e: e.tensor_scalar(out=AY, in0=RS1, scalar1=0.5,
                                           scalar2=1.0, op0=ALU.mult, op1=ALU.add))

            s_sgn = [0] * NJ
            g_gam = [0] * NJ
            for jp in range(0, NJ, 2):
                for j in (jp, jp + 1):
                    Xj = X[:, j * D:(j + 1) * D]
                    XN = XNv[j % 2]
                    if j >= 2:
                        ewait("vector", "s", s_sgn[j - 2])
                    vrun(lambda e, Xj=Xj, XN=XN, j=j: e.tensor_scalar(
                        out=XN, in0=Xj, scalar1=AFj[:, j:j + 1],
                        scalar2=CFj[:, j:j + 1], op0=ALU.mult, op1=ALU.add))
                    vrun(lambda e, XN=XN: e.tensor_tensor(
                        out=XN, in0=XN, in1=GAM[:], op=ALU.mult))
                    vrun(lambda e, XN=XN: e.tensor_tensor(
                        out=XN, in0=XN, in1=BET[:], op=ALU.add))
                    v_xn = cnt["v"]
                    ewait("scalar", "v", v_xn)
                    run("scalar", lambda e, XN=XN, j=j: e.activation(
                        out=SGN[j % 2], in_=XN, func=AF.Sign), inc="s")
                    s_sgn[j] = cnt["s"]
                    ewait("vector", "s", s_sgn[j])
                    vrun(lambda e, j=j: e.tensor_tensor(
                        out=QQB[j % 2], in0=SGN[j % 2], in1=SS128[:],
                        op=ALU.mult))
                    vrun(lambda e, j=j: e.tensor_reduce(
                        out=TSCALL[:, j * 64:(j + 1) * 64],
                        in_=QQB[j % 2].rearrange("p (t k) -> p t k", k=DS),
                        axis=AX.X, op=ALU.add))
                TS2v = TSCALL[:, jp * 64:(jp + 2) * 64]
                CS64 = MMIN[:, 0:16]
                CM8 = MM1[:, 0:2]; BC8 = MM1[:, 2:4]
                TM8 = MM1[:, 4:6]; TI8 = MM1[:, 6:8]
                vrun(lambda e, TS2v=TS2v, CS64=CS64: e.tensor_reduce(
                    out=CS64, in_=TS2v.rearrange("p (a c t) -> p (a c) t", t=8,
                                                 c=8), axis=AX.X, op=ALU.add))
                vrun(lambda e, CS64=CS64, CM8=CM8: e.tensor_reduce(
                    out=CM8, in_=CS64.rearrange("p (a c) -> p a c", c=8),
                    axis=AX.X, op=ALU.max))
                vrun(lambda e, CS64=CS64, CM8=CM8: e.tensor_tensor(
                    out=EQM[:, 0:16].rearrange("p (a c) -> p a c", c=8),
                    in0=CS64.rearrange("p (a c) -> p a c", c=8),
                    in1=CM8.unsqueeze(2).broadcast_to([128, 2, 8]),
                    op=ALU.is_equal))
                for a in range(2):
                    vrun(lambda e, a=a: e.select(
                        out=MMAX[:, a * 8:(a + 1) * 8],
                        mask=EQM[:, a * 8:(a + 1) * 8],
                        on_true=IOTA64[:, 0:8], on_false=BIGT[:, 0:8],
                        add_drain=True))
                vrun(lambda e, BC8=BC8: e.tensor_reduce(
                    out=BC8, in_=MMAX[:, 0:16].rearrange("p (a c) -> p a c", c=8),
                    axis=AX.X, op=ALU.min))
                vrun(lambda e, BC8=BC8: e.tensor_tensor(
                    out=MMB2.rearrange("p (a t) -> p a t", t=64),
                    in0=IOTAD8[:].unsqueeze(1).broadcast_to([128, 2, 64]),
                    in1=BC8.unsqueeze(2).broadcast_to([128, 2, 64]),
                    op=ALU.is_equal))
                vrun(lambda e, TS2v=TS2v: e.tensor_scalar(
                    out=MMB2, in0=MMB2, scalar1=65536.0, scalar2=None,
                    op0=ALU.mult))
                vrun(lambda e, TS2v=TS2v: e.tensor_tensor(
                    out=MMB2, in0=MMB2, in1=TS2v, op=ALU.add))
                vrun(lambda e, TM8=TM8: e.tensor_reduce(
                    out=TM8, in_=MMB2.rearrange("p (a t) -> p a t", t=64),
                    axis=AX.X, op=ALU.max))
                vrun(lambda e, TM8=TM8: e.tensor_tensor(
                    out=EQM[:, 0:128].rearrange("p (a t) -> p a t", t=64),
                    in0=MMB2.rearrange("p (a t) -> p a t", t=64),
                    in1=TM8.unsqueeze(2).broadcast_to([128, 2, 64]),
                    op=ALU.is_equal))
                for a in range(2):
                    vrun(lambda e, a=a: e.select(
                        out=MMB2[:, a * 64:(a + 1) * 64],
                        mask=EQM[:, a * 64:(a + 1) * 64],
                        on_true=IOTA64[:], on_false=BIGT[:], add_drain=True))
                vrun(lambda e, TI8=TI8: e.tensor_reduce(
                    out=TI8, in_=MMB2.rearrange("p (a t) -> p a t", t=64),
                    axis=AX.X, op=ALU.min))
                vrun(lambda e, TI8=TI8, jp=jp: e.tensor_tensor(
                    out=OHALL[:, jp * 64:(jp + 2) * 64].rearrange(
                        "p (a t) -> p a t", t=64),
                    in0=IOTA64[:].unsqueeze(1).broadcast_to([128, 2, 64]),
                    in1=TI8.unsqueeze(2).broadcast_to([128, 2, 64]),
                    op=ALU.is_equal))
                v_oh = cnt["v"]
                for j in (jp, jp + 1):
                    XN = XNv[j % 2]
                    OHj = OHALL[:, j * 64:(j + 1) * 64]
                    ewait("vector", "g", cnt["g"])
                    vrun(lambda e, XN=XN, OHj=OHj: e.tensor_tensor(
                        out=SCT[:].rearrange("p (a b) -> p a b", b=DS),
                        in0=XN.rearrange("p (a b) -> p a b", b=DS),
                        in1=OHj.unsqueeze(2).broadcast_to([128, 64, DS]),
                        op=ALU.mult))
                    vrun(lambda e, j=j: e.tensor_reduce(
                        out=XS[:, j * DS:(j + 1) * DS],
                        in_=SCT[:].rearrange("p (t k) -> p k t", k=DS),
                        axis=AX.X, op=ALU.add))
                ewait("tensor", "v", v_oh)
                for j in (jp, jp + 1):
                    OHj = OHALL[:, j * 64:(j + 1) * 64]
                    if jp or j != jp:
                        ewait("tensor", "v", cnt["v"])
                    run("tensor", lambda e, OHj=OHj: e.transpose(
                        PSD[0:64, 128:256], OHj, IDNF), inc="t")
                    ewait("vector", "t", cnt["t"])
                    vrun(lambda e, j=j: e.tensor_copy(
                        out=OHF[0:64, j * 128:(j + 1) * 128],
                        in_=PSD[0:64, 128:256]))
            vrun(lambda e: e.tensor_reduce(
                out=TKN[:, 0:8], in_=XS[:].rearrange("p (j k) -> p j k", k=DS),
                axis=AX.X, op=ALU.min))
            vrun(lambda e: e.tensor_reduce(
                out=TKN[:, 8:16], in_=XS[:].rearrange("p (j k) -> p j k", k=DS),
                axis=AX.X, op=ALU.max))
            MMB = SCT[:, 0:512]
            vrun(lambda e: e.tensor_scalar(out=TSC[:, 32:40], in0=TKN[:, 0:8],
                                           scalar1=-1024.0, scalar2=None,
                                           op0=ALU.add))
            vrun(lambda e: e.tensor_scalar(out=TSC[:, 40:48], in0=TKN[:, 8:16],
                                           scalar1=1024.0, scalar2=None,
                                           op0=ALU.add))
            vrun(lambda e, MMB=MMB: e.tensor_tensor(
                out=MMB.rearrange("p (j t) -> p j t", t=64),
                in0=OHALL[:].rearrange("p (j t) -> p j t", t=64),
                in1=TSC[:, 32:40].unsqueeze(2).broadcast_to([128, NJ, 64]),
                op=ALU.mult))
            vrun(lambda e, MMB=MMB: e.tensor_scalar(
                out=MMB, in0=MMB, scalar1=1024.0, scalar2=None, op0=ALU.add))
            vrun(lambda e, MMB=MMB: e.tensor_reduce(
                out=MMIN[:], in_=MMB.rearrange("p (j t) -> p t j", t=64),
                axis=AX.X, op=ALU.min))
            vrun(lambda e, MMB=MMB: e.tensor_tensor(
                out=MMB.rearrange("p (j t) -> p j t", t=64),
                in0=OHALL[:].rearrange("p (j t) -> p j t", t=64),
                in1=TSC[:, 40:48].unsqueeze(2).broadcast_to([128, NJ, 64]),
                op=ALU.mult))
            vrun(lambda e, MMB=MMB: e.tensor_scalar(
                out=MMB, in0=MMB, scalar1=-1024.0, scalar2=None, op0=ALU.add))
            vrun(lambda e, MMB=MMB: e.tensor_reduce(
                out=MMAX[:], in_=MMB.rearrange("p (j t) -> p t j", t=64),
                axis=AX.X, op=ALU.max))
            v_loopA = cnt["v"]


            ewait("tensor", "v", v_loopA)
            def _mmT(e):
                e.transpose(PSD[0:64, 128:256], MMIN[:], IDNF)
                return e.transpose(PSD[0:64, 256:384], MMAX[:], IDNF)
            run("tensor", _mmT, inc="t")
            ewait("vector", "t", cnt["t"])
            vrun(lambda e: e.tensor_copy(out=T64A[0:64, :], in_=PSD[0:64, 128:256]))
            vrun(lambda e: e.tensor_reduce(out=ARB[0:64, 0:1], in_=T64A[0:64, :],
                                           axis=AX.X, op=ALU.min))
            vrun(lambda e: e.tensor_copy(out=T64A[0:64, :], in_=PSD[0:64, 256:384]))
            vrun(lambda e: e.tensor_reduce(out=ARB[0:64, 1:2], in_=T64A[0:64, :],
                                           axis=AX.X, op=ALU.max))
            vrun(lambda e: e.tensor_scalar(out=ARB[0:64, 0:1], in0=ARB[0:64, 0:1],
                                           scalar1=-1.0, scalar2=None,
                                           op0=ALU.mult))
            ewait("gpsimd", "v", cnt["v"])
            edma("gpsimd", "dk", d_ar_in[ar_idx].ap(), ARB[0:64, 0:2])
            ewait("gpsimd", "dk")
            cnt["cc"] += 1
            on("gpsimd", lambda e, ar_idx=ar_idx: e.collective_compute(
                "AllReduce", ALU.max, replica_groups=[list(range(8))],
                ins=[d_ar_in[ar_idx].ap().opt()],
                outs=[d_ar_out[ar_idx].ap().opt()]).then_inc(sems["cc"], 1))
            ewait("gpsimd", "cc")
            edma("gpsimd", "dk", ARB[0:64, 0:2], d_ar_out[ar_idx].ap())
            ewait("vector", "dk", cnt["dk"])
            vrun(lambda e: e.tensor_scalar(out=MISC[0:64, 0:1], in0=ARB[0:64, 0:1],
                                           scalar1=-1.0, scalar2=None,
                                           op0=ALU.mult))
            vrun(lambda e: e.tensor_copy(out=MISC[0:64, 1:2], in_=ARB[0:64, 1:2]))
            vrun(lambda e: e.tensor_copy(out=MISC[0:64, 2:3], in_=OSC[0:64, :]))
            v_misc = cnt["v"]

            ewait("tensor", "v", max(v_misc, v_tbp))
            v_tabev = 0
            for j in range(NJ):
                OHFj = OHF[0:NT, j * 128:(j + 1) * 128]
                if v_tabev:
                    ewait("tensor", "v", v_tabev)
                run("tensor", lambda e, OHFj=OHFj: e.matmul(
                    PSA[:, 0:512], OHFj, TBP[0:NT, :], start=True, stop=True))
                run("tensor", lambda e, OHFj=OHFj: e.matmul(
                    PSB[:, 0:512], OHFj, SL64[0:NT, :], start=True, stop=True))
                run("tensor", lambda e, OHFj=OHFj: e.matmul(
                    PSD[:, 0:4], OHFj, MISC[0:NT, :], start=True, stop=True),
                    inc="t")
                ewait("vector", "t", cnt["t"])
                def _tabev(e, j=j):
                    e.tensor_copy(out=TBALL[:, j * 512:(j + 1) * 512],
                                  in_=PSA[:, 0:512])
                    e.tensor_copy(out=TSALL[:, j * 512:(j + 1) * 512],
                                  in_=PSB[:, 0:512])
                    return e.tensor_copy(out=MNXS[:, j * 4:(j + 1) * 4],
                                         in_=PSD[:, 0:4])
                run("vector", _tabev, inc="v")
                v_tabev = cnt["v"]

            MNv = MNXS[:].rearrange("p (j c) -> p j c", c=4)[:, :, 0:1]
            MXv = MNXS[:].rearrange("p (j c) -> p j c", c=4)[:, :, 1:2]
            SCv = MNXS[:].rearrange("p (j c) -> p j c", c=4)[:, :, 2:3]
            XSv = XS[:].rearrange("p (j k) -> p j k", k=DS)
            R8 = TSC[:, 8:16]; INV8 = TSC[:, 16:24]; SCH8 = TSC[:, 24:32]
            vsync()
            vrun(lambda e: e.tensor_tensor(
                out=R8.unsqueeze(2), in0=MXv, in1=MNv, op=ALU.subtract))
            vrun(lambda e: e.tensor_scalar(out=R8, in0=R8, scalar1=1e-8,
                                           scalar2=None, op0=ALU.add))
            vrun(lambda e: e.reciprocal(out=INV8, in_=R8))
            vrun(lambda e: e.tensor_scalar(out=SCH8.unsqueeze(2), in0=SCv,
                                           scalar1=0.5, scalar2=None,
                                           op0=ALU.mult))
            SP1v = SCT[:, 0:256]
            SP2v = SCT[:, 256:512]
            BACv = SCT[:, 512:768]
            SACv = SCT[:, 768:1024]
            MGv = EQM[:, 0:64]
            vrun(lambda e: e.tensor_tensor(
                out=SP1v.rearrange("p (j k) -> p j k", k=DS), in0=XSv,
                in1=MNv.broadcast_to([128, NJ, DS]), op=ALU.subtract))
            vrun(lambda e: e.tensor_tensor(
                out=SP1v.rearrange("p (j k) -> p j k", k=DS),
                in0=SP1v.rearrange("p (j k) -> p j k", k=DS),
                in1=INV8.unsqueeze(2).broadcast_to([128, NJ, DS]), op=ALU.mult))
            vrun(lambda e: e.tensor_scalar(out=SP1v, in0=SP1v,
                                           scalar1=1.0 - 1e-6, scalar2=0.0,
                                           op0=ALU.min, op1=ALU.max))
            vrun(lambda e: e.tensor_scalar(out=SP1v, in0=SP1v,
                                           scalar1=float(GRID), scalar2=None,
                                           op0=ALU.mult))
            TBv = TBALL[:].rearrange("p (j k g) -> p j k g", k=DS, g=GRID)
            TSv = TSALL[:].rearrange("p (j k g) -> p j k g", k=DS, g=GRID)
            vrun(lambda e: e.tensor_copy(
                out=BACv.rearrange("p (j k) -> p j k", k=DS), in_=TBv[:, :, :, 0]))
            vrun(lambda e: e.tensor_copy(
                out=SACv.rearrange("p (j k) -> p j k", k=DS), in_=TSv[:, :, :, 0]))
            for g in range(1, GRID):
                vrun(lambda e, g=g: e.tensor_scalar(
                    out=MG256[:], in0=SP1v, scalar1=float(g), scalar2=None,
                    op0=ALU.is_ge))
                vrun(lambda e, g=g: e.select(
                    out=BACv.rearrange("p (j k) -> p j k", k=DS),
                    mask=MG256[:].rearrange("p (j k) -> p j k", k=DS),
                    on_true=TBv[:, :, :, g],
                    on_false=BACv.rearrange("p (j k) -> p j k", k=DS),
                    add_drain=True))
                vrun(lambda e, g=g: e.select(
                    out=SACv.rearrange("p (j k) -> p j k", k=DS),
                    mask=MG256[:].rearrange("p (j k) -> p j k", k=DS),
                    on_true=TSv[:, :, :, g],
                    on_false=SACv.rearrange("p (j k) -> p j k", k=DS),
                    add_drain=True))
            vrun(lambda e: e.tensor_tensor(out=SP2v, in0=SACv, in1=SP1v,
                                           op=ALU.mult))
            vrun(lambda e: e.tensor_tensor(out=SP2v, in0=SP2v, in1=BACv,
                                           op=ALU.add))
            vrun(lambda e: e.tensor_tensor(
                out=SP2v.rearrange("p (j k) -> p j k", k=DS),
                in0=SP2v.rearrange("p (j k) -> p j k", k=DS),
                in1=SCH8.unsqueeze(2).broadcast_to([128, NJ, DS]), op=ALU.mult))
            g_scat = 0
            for j in range(NJ):
                Xj = X[:, j * D:(j + 1) * D]
                OHj = OHALL[:, j * 64:(j + 1) * 64]
                if g_scat:
                    ewait("vector", "g", g_scat)
                vrun(lambda e, Xj=Xj, j=j: e.tensor_scalar(
                    out=Xj, in0=Xj, scalar1=AY[:, j:j + 1],
                    scalar2=CY[:, j:j + 1], op0=ALU.mult, op1=ALU.add))
                vrun(lambda e, OHj=OHj, j=j: e.tensor_tensor(
                    out=SCTB[:].rearrange("p (a b) -> p a b", b=DS),
                    in0=OHj.unsqueeze(2).broadcast_to([128, 64, DS]),
                    in1=SP2v.rearrange("p (j2 k) -> p j2 k", k=DS)[:, j, :]
                        .unsqueeze(1).broadcast_to([128, 64, DS]),
                    op=ALU.mult))
                ewait("gpsimd", "v", cnt["v"])
                run("gpsimd", lambda e, Xj=Xj: e.tensor_tensor(
                    out=Xj, in0=Xj, in1=SCTB[:], op=ALU.add), inc="g")
                g_scat = cnt["g"]
            ewait("vector", "g", cnt["g"])
            ewait("scalar", "g", cnt["g"])
            ewait("tensor", "g", cnt["g"])

        def attention():
            _vred = [0] * NJ
            for j in range(NJ):
                Xj = X[:, j * D:(j + 1) * D]
                if j >= 2:
                    ewait("scalar", "v", _vred[j - 2])
                run("scalar", lambda e, Xj=Xj, j=j: e.activation(
                    out=SGN[j % 2], in_=Xj, func=AF.Square), inc="s")
                s_sq = cnt["s"]
                run("vector", lambda e, Xj=Xj, j=j: e.reduce_sum(
                    out=STAT[:, j:j + 1], in_=Xj, axis=AX.X))
                ewait("vector", "s", s_sq)
                run("vector", lambda e, j=j: e.reduce_sum(
                    out=SQS[:, j:j + 1], in_=SGN[j % 2], axis=AX.X), inc="v")
                _vred[j] = cnt["v"]
            A3 = AFC[:, 0:8]; C3 = AFC[:, 8:16]
            ewait("vector", "s", cnt["s"])
            vsync()
            vrun(lambda e: e.tensor_scalar(out=A3, in0=SQS[:], scalar1=1.0 / D,
                                           scalar2=None, op0=ALU.mult))
            vrun(lambda e: e.tensor_scalar(out=C3, in0=STAT[:, 0:8],
                                           scalar1=1.0 / D, scalar2=None,
                                           op0=ALU.mult))
            vrun(lambda e: e.tensor_tensor(out=AFC[:, 16:24], in0=C3, in1=C3,
                                           op=ALU.mult))
            vrun(lambda e: e.tensor_tensor(out=A3, in0=A3, in1=AFC[:, 16:24],
                                           op=ALU.subtract))
            vrun(lambda e: e.tensor_scalar(out=A3, in0=A3, scalar1=LNEPS,
                                           scalar2=None, op0=ALU.add))
            vrun(lambda e: e.reciprocal(out=A3, in_=A3))
            ewait("scalar", "v", cnt["v"])
            run("scalar", lambda e: e.activation(out=A3, in_=A3, func=AF.Sqrt),
                inc="s")
            ewait("vector", "s", cnt["s"])
            vrun(lambda e: e.tensor_tensor(out=C3, in0=C3, in1=A3, op=ALU.mult))
            vrun(lambda e: e.tensor_scalar(out=C3, in0=C3, scalar1=-1.0,
                                           scalar2=None, op0=ALU.mult))
            v_l3 = cnt["v"]

            ewait("gpsimd", "v", v_l3)
            ewait("gpsimd", "s", cnt["s"])
            dw_cc = {}
            def wqkv_load(hf, cc):
                s_ = "dwa" if cc % 2 == 0 else "dwb"
                edma("gpsimd", s_, WBUF[cc % 2][:].rearrange(
                    "p (dc c) -> p dc c", c=128),
                    d_wqkv.rearrange("(dc p) c -> p dc c", p=128)[:, :, cc * 128:(cc + 1) * 128])
                dw_cc[(hf, cc)] = (s_, cnt[s_])
            wqkv_load(0, 0)
            wqkv_load(0, 1)
            s_x3f = 0
            s_evac_prev = [0, 0]
            s_qkv_ev = [0, 0]
            for hf in range(2):
                v_xn3 = 0
                for j4 in range(4):
                    j = hf * 4 + j4
                    Xj = X[:, j * D:(j + 1) * D]
                    if hf == 1 and j4 == 0:
                        ewait("vector", "t", cnt["t"])
                    vrun(lambda e, Xj=Xj, j=j, j4=j4: e.tensor_scalar(
                        out=XN3B[0][:, j4 * 2048:(j4 + 1) * 2048], in0=Xj,
                        scalar1=A3[:, j:j + 1], scalar2=C3[:, j:j + 1],
                        op0=ALU.mult, op1=ALU.add))
                v_xn3 = cnt["v"]
                X3F = XN3B[1]
                ewait("tensor", "v", v_xn3)
                for dc in range(16):
                    if s_evac_prev[0]:
                        ewait("tensor", "s", s_evac_prev[0])
                    for j4 in range(4):
                        run("tensor", lambda e, dc=dc, j4=j4: e.transpose(
                            PSBK[:, j4 * 128:(j4 + 1) * 128],
                            XN3B[0][:, j4 * 2048 + dc * 128: j4 * 2048 + (dc + 1) * 128],
                            IDNB[:]), inc="t" if j4 == 3 else None)
                    t_tp = cnt["t"]
                    ewait("scalar", "t", t_tp)
                    run("scalar", lambda e, dc=dc, X3F=X3F: e.copy(
                        out=X3F[:, dc * 512:(dc + 1) * 512],
                        in_=PSBK[:, 0:512]), inc="s")
                    s_evac_prev[0] = cnt["s"]
                s_x3f = cnt["s"]
                for cc in range(CCQ):
                    ewait("tensor", dw_cc[(hf, cc)][0], dw_cc[(hf, cc)][1])
                    ewait("tensor", "s", s_x3f)
                    half = cc % 2
                    if s_qkv_ev[half]:
                        ewait("tensor", "s", s_qkv_ev[half])
                    for dc in range(16):
                        run("tensor", lambda e, cc=cc, dc=dc, half=half, X3F=X3F:
                            e.matmul(
                                PSA[:, half * 512:(half + 1) * 512],
                                WBUF[cc % 2][:, dc * 128:(dc + 1) * 128],
                                X3F[:, dc * 512:(dc + 1) * 512],
                                start=(dc == 0), stop=(dc == 15)),
                            inc="t" if dc == 15 else None)
                    t_qkv = cnt["t"]
                    ewait("scalar", "t", t_qkv)
                    run("scalar", lambda e, cc=cc, hf=hf, half=half: e.activation(
                        out=QKVB[:, cc * 1024 + hf * 512: cc * 1024 + hf * 512 + 512],
                        in_=PSA[:, half * 512:(half + 1) * 512], func=AF.Identity,
                        bias=BQ[:, cc:cc + 1], scale=1.0), inc="s")
                    s_qkv_ev[half] = cnt["s"]
                    nhf, ncc = hf, cc + 2
                    if ncc >= CCQ:
                        nhf, ncc = hf + 1, ncc - CCQ
                    if nhf < 2 and (nhf, ncc) not in dw_cc:
                        ewait("gpsimd", "t", t_qkv)
                        wqkv_load(nhf, ncc)
            s_qkv_done = cnt["s"]

            ewait("tensor", "s", s_qkv_done)
            v_vto = 0
            for j in range(NJ):
                if v_vto:
                    ewait("tensor", "v", v_vto)
                run("tensor", lambda e, j=j: e.transpose(
                    PSBK[:, 0:128],
                    QKVB[:, 17 * 1024 + j * 128: 17 * 1024 + (j + 1) * 128],
                    IDNB[:]), inc="t")
                ewait("vector", "t", cnt["t"])
                run("vector", lambda e, j=j: e.tensor_copy(
                    out=VTO[:, j * 128:(j + 1) * 128],
                    in_=PSBK[:, 0:128]), inc="v")
                v_vto = cnt["v"]
            ewait("gpsimd", "v", v_vto)
            ewait("gpsimd", "s", s_qkv_done)
            edma("gpsimd", "dk", d_kv_in.ap()[0:128, :], QKVB[:, 16 * 1024:17 * 1024])
            edma("gpsimd", "dk", d_kv_in.ap()[128:256, :], VTO[:])
            ewait("gpsimd", "dk")
            cnt["cc"] += 1
            on("gpsimd", lambda e: e.collective_compute(
                "AllGather", ALU.bypass,
                replica_groups=[[0, 1, 2, 3], [4, 5, 6, 7]],
                ins=[d_kv_in.ap().opt()],
                outs=[d_kv_out.ap().opt()]).then_inc(sems["cc"], 1))
            ewait("gpsimd", "cc")
            kvo = d_kv_out.ap().rearrange("(r b p) t -> r b p t", b=2, p=128)
            edma("gpsimd", "dk",
                 KT[:].rearrange("p (s r q) -> p s r q", r=4, q=128),
                 kvo[:, 0, :, :].rearrange("r p (s q) -> p s r q", q=128))
            edma("gpsimd", "dk",
                 VT[:].rearrange("p (s r q) -> p s r q", r=4, q=128),
                 kvo[:, 1, :, :].rearrange("r p (s q) -> p s r q", q=128))
            dk_kv = cnt["dk"]

            v_qsq_prev = 0
            for dc in range(16):
                ewait("vector", "t", cnt["t"])
                vrun(lambda e, dc=dc: e.tensor_tensor(
                    out=QSQ[:], in0=QKVB[:, dc * 1024:(dc + 1) * 1024],
                    in1=QKVB[:, dc * 1024:(dc + 1) * 1024], op=ALU.mult))
                ewait("tensor", "v", cnt["v"])
                run("tensor", lambda e, dc=dc: e.matmul(
                    PSD[0:2, 0:512], ONEB2[:], QSQ[:, 0:512],
                    start=(dc == 0), stop=(dc == 15)))
                run("tensor", lambda e, dc=dc: e.matmul(
                    PSBKF[0:2, 0:512], ONEB2[:], QSQ[:, 512:1024],
                    start=(dc == 0), stop=(dc == 15)), inc="t")
            t_qn = cnt["t"]
            ewait("vector", "t", t_qn)
            def _qmax(e):
                e.tensor_reduce(out=KQM[0:1, 0:1], in_=PSD[0:1, 0:512], axis=AX.X,
                                op=ALU.max)
                return e.tensor_reduce(out=KQM[0:1, 1:2], in_=PSBKF[0:1, 0:512],
                                       axis=AX.X, op=ALU.max)
            run("vector", _qmax, inc="v")
            vrun(lambda e: e.tensor_tensor(
                out=KQM[0:1, 0:1], in0=KQM[0:1, 0:1], in1=KQM[0:1, 1:2],
                op=ALU.max))
            ewait("vector", "dk", dk_kv)
            for kc in range(4):
                vrun(lambda e, kc=kc: e.tensor_tensor(
                    out=QSQ[:], in0=KT[:, kc * 1024:(kc + 1) * 1024],
                    in1=KT[:, kc * 1024:(kc + 1) * 1024], op=ALU.mult))
                ewait("tensor", "v", cnt["v"])
                if kc > 0:
                    ewait("tensor", "v", v_kn)
                run("tensor", lambda e: e.matmul(
                    PSD[0:2, 0:512], ONEB2[:], QSQ[:, 0:512], start=True,
                    stop=True))
                run("tensor", lambda e: e.matmul(
                    PSBKF[0:2, 0:512], ONEB2[:], QSQ[:, 512:1024], start=True,
                    stop=True), inc="t")
                ewait("vector", "t", cnt["t"])
                def _kmaxA(e):
                    e.tensor_reduce(out=KQM[0:1, 2:3], in_=PSD[0:1, 0:512],
                                    axis=AX.X, op=ALU.max)
                    return e.tensor_reduce(out=KQM[0:1, 3:4], in_=PSBKF[0:1, 0:512],
                                           axis=AX.X, op=ALU.max)
                run("vector", _kmaxA, inc="v")
                vsync()
                vrun(lambda e: e.tensor_tensor(out=KQM[0:1, 2:3],
                                               in0=KQM[0:1, 2:3],
                                               in1=KQM[0:1, 3:4], op=ALU.max))
                if kc == 0:
                    vrun(lambda e: e.tensor_copy(out=KQM[0:1, 4:5],
                                                 in_=KQM[0:1, 2:3]))
                else:
                    vrun(lambda e: e.tensor_tensor(out=KQM[0:1, 4:5],
                                                   in0=KQM[0:1, 4:5],
                                                   in1=KQM[0:1, 2:3],
                                                   op=ALU.max))
                v_kn = cnt["v"]
            run("vector", lambda e: e.tensor_tensor(
                out=KQM[0:1, 5:6], in0=KQM[0:1, 0:1], in1=KQM[0:1, 4:5],
                op=ALU.mult), inc="v")
            ewait("scalar", "v", cnt["v"])
            run("scalar", lambda e: e.activation(out=KQM[0:1, 5:6],
                                                 in_=KQM[0:1, 5:6], func=AF.Sqrt),
                inc="s")
            ewait("vector", "s", cnt["s"])
            vrun(lambda e: e.tensor_scalar(
                out=KQM[0:1, 6:7], in0=KQM[0:1, 5:6], scalar1=-SCALE,
                scalar2=EXPSHIFT, op0=ALU.mult, op1=ALU.add))
            ewait("tensor", "v", cnt["v"])
            run("tensor", lambda e: e.matmul(
                PSD[:, 384:385], ONE1[0:1, :], KQM[0:1, 6:7], start=True,
                stop=True), inc="t")
            ewait("vector", "t", cnt["t"])
            run("vector", lambda e: e.tensor_copy(out=BB[:], in_=PSD[:, 384:385]),
                inc="v")
            v_bb = cnt["v"]

            def emit_pv(kt, EJ):
                buf = kt % 2
                kind, val = cons_done[buf]
                ewait("tensor", kind, val)
                if kt == 0 and v_aob:
                    ewait("tensor", "v", v_aob)
                for q4 in range(2):
                    run("tensor", lambda e, q4=q4, kt=kt, buf=buf, EJ=EJ:
                        e.matmul(
                            PSB[:, q4 * 512:(q4 + 1) * 512],
                            VT[:, kt * 128:(kt + 1) * 128],
                            PT[buf][:, q4 * 512:(q4 + 1) * 512],
                            start=(kt == 0), stop=(kt == EJ - 1)))
                run("tensor", lambda e, kt=kt, buf=buf, EJ=EJ: e.matmul(
                    PSD[0:2, 0:512], ONEB2[:], PT[buf][:, 0:512],
                    start=(kt == 0), stop=(kt == EJ - 1)))
                run("tensor", lambda e, kt=kt, buf=buf, EJ=EJ: e.matmul(
                    PSBKF[0:2, 0:512], ONEB2[:], PT[buf][:, 512:1024],
                    start=(kt == 0), stop=(kt == EJ - 1)), inc="t")
                pv_done[kt % 2] = cnt["t"]

            ewait("tensor", "dk", dk_kv)
            ewait("tensor", "v", v_bb)
            ewait("scalar", "v", v_bb)
            SSTRIP = [PSA, PSC]
            exp_done = {}
            cons_done = {}
            pv_done = {}
            v_aob = 0
            v_rb = 0
            for j in range(NJ):
                EJ = 4 * (j + 1)
                for hf in range(2):
                    pending = None
                    for kt in range(EJ):
                        buf = kt % 2
                        strip = SSTRIP[buf]
                        prev = exp_done.get(buf)
                        if prev:
                            ewait("tensor", "s", prev)
                        if v_rb:
                            ewait("tensor", "v", v_rb)
                            v_rb = 0
                        for q4 in range(2):
                            h0 = hf * 8 + q4 * 4
                            rhs = QKVB[:].rearrange(
                                "p (c t) -> p c t", t=1024)[
                                :, h0:h0 + 4, j * 128:(j + 1) * 128]
                            run("tensor", lambda e, q4=q4, kt=kt, strip=strip,
                                rhs=rhs: e.matmul(
                                    strip[:, q4 * 512:(q4 + 1) * 512],
                                    KT[:, kt * 128:(kt + 1) * 128], rhs,
                                    start=True, stop=True),
                                inc="t" if q4 == 1 else None)
                        t_sc = cnt["t"]
                        ewait("scalar", "t", t_sc)
                        prevpv = pv_done.get(buf)
                        if prevpv:
                            ewait("scalar", "t", prevpv)
                        run("scalar", lambda e, buf=buf, strip=strip: e.activation(
                            out=PT[buf][:], in_=strip[:], func=AF.Exp,
                            bias=BB[:], scale=SCALE), inc="s")
                        exp_done[buf] = cnt["s"]
                        if kt >= 4 * j:
                            ewait("vector", "s", exp_done[buf])
                            run("vector", lambda e, buf=buf, kt=kt: e.tensor_tensor(
                                out=PT[buf][:].rearrange("p (h q) -> p h q", q=128),
                                in0=PT[buf][:].rearrange("p (h q) -> p h q", q=128),
                                in1=MSK[:, kt * 128:(kt + 1) * 128].unsqueeze(1)
                                    .broadcast_to([128, 8, 128]),
                                op=ALU.mult))
                            cons_done[buf] = ("v", cnt["v"])
                        else:
                            cons_done[buf] = ("s", exp_done[buf])
                        if pending is not None:
                            emit_pv(pending, EJ)
                        pending = kt
                    emit_pv(pending, EJ)
                    pending = None
                    t_last = cnt["t"]
                    ewait("vector", "t", t_last)
                    vrun(lambda e: e.tensor_copy(out=SREC[0], in_=PSD[0:1, 0:512]))
                    vrun(lambda e: e.tensor_copy(out=SREC[1],
                                                 in_=PSBKF[0:1, 0:512]))
                    vrun(lambda e: e.reciprocal(out=SREC[0], in_=SREC[0]))
                    vrun(lambda e: e.reciprocal(out=SREC[1], in_=SREC[1]))
                    v_recd = cnt["v"]
                    ewait("tensor", "v", v_recd)
                    for q in range(2):
                        qp = 0 if q == 0 else 32
                        run("tensor", lambda e, q=q, qp=qp: e.matmul(
                            PSA[:, q * 512:(q + 1) * 512], ONE1[qp:qp + 1, :],
                            SREC[q], start=True, stop=True),
                            inc="t" if q == 1 else None)
                    t_rb = cnt["t"]
                    ewait("vector", "t", t_rb)
                    run("vector", lambda e: e.tensor_copy(out=RB[:],
                                                          in_=PSA[:, 0:1024]))
                    v_rb = cnt["v"]
                    run("vector", lambda e, j=j, hf=hf: e.tensor_tensor(
                        out=AOB[:, j * 2048 + hf * 1024: j * 2048 + (hf + 1) * 1024],
                        in0=PSB[:], in1=RB[:], op=ALU.mult))
                    v_aob = cnt["v"]
                    exp_done = {}
                    pv_done = {}
                    cons_done = {}

            dwp = {}
            def wp_load(wq):
                s_ = "dwa" if wq % 2 == 0 else "dwb"
                edma("gpsimd", s_, WPB[wq % 2][:].rearrange(
                    "p (dc c) -> p dc c", c=512),
                    d_wproj.rearrange("(dc p) c -> p dc c", p=128)[:, :, wq * 512:(wq + 1) * 512])
                dwp[wq] = (s_, cnt[s_])
            ewait("gpsimd", "v", v_aob)
            wp_load(0)
            wp_load(1)
            v_pe = [0, 0]
            t_wq_last = 0
            for wq in range(4):
                if wq >= 1 and wq + 1 < 4 and (wq + 1) not in dwp:
                    pass
                ewait("tensor", dwp[wq][0], dwp[wq][1])
                ewait("tensor", "v", v_aob)
                for j in range(NJ):
                    half = j % 2
                    if v_pe[half]:
                        ewait("tensor", "v", v_pe[half])
                    for dc in range(16):
                        run("tensor", lambda e, wq=wq, j=j, dc=dc, half=half:
                            e.matmul(
                                PSA[:, half * 512:(half + 1) * 512],
                                AOB[:, j * 2048 + dc * 128: j * 2048 + (dc + 1) * 128],
                                WPB[wq % 2][:, dc * 512:(dc + 1) * 512],
                                start=(dc == 0), stop=False))
                    run("tensor", lambda e, wq=wq, half=half: e.matmul(
                        PSA[:, half * 512:(half + 1) * 512], ONEBR[:],
                        BPR[0:1, wq * 512:(wq + 1) * 512],
                        start=False, stop=True), inc="t")
                    t_pp = cnt["t"]
                    ewait("vector", "t", t_pp)
                    run("vector", lambda e, wq=wq, j=j, half=half: e.tensor_tensor(
                        out=X[:, j * D + wq * 512: j * D + (wq + 1) * 512],
                        in0=X[:, j * D + wq * 512: j * D + (wq + 1) * 512],
                        in1=PSA[:, half * 512:(half + 1) * 512], op=ALU.add),
                        inc="v")
                    v_pe[half] = cnt["v"]
                t_wq_last = cnt["t"]
                if wq + 2 < 4:
                    ewait("gpsimd", "t", t_wq_last)
                    wp_load(wq + 2)
            ewait("vector", "v", cnt["v"])
            ewait("scalar", "v", cnt["v"])
            ewait("tensor", "v", cnt["v"])
            ewait("gpsimd", "v", cnt["v"])

        ewait("vector", "v", max(v_gb, v_ss))
        ffn(0)
        if stop >= 2:
            attention()
        if stop >= 3:
            ffn(1)

        ewait("sync", "v", cnt["v"])
        ewait("sync", "g", cnt["g"])
        ewait("sync", "s", cnt["s"])
        ewait("sync", "t", cnt["t"])
        edma("sync", "so", d_y.rearrange("(j p) d -> p j d", p=128),
             X[:].rearrange("p (j d) -> p j d", d=D))
        if _dbg():
            edma("sync", "so", dbg["dSTAT"], STAT[:])
            edma("sync", "so", dbg["dAFC"], AFC[:])
            edma("sync", "so", dbg["dOHALL"], OHALL[:])
            edma("sync", "so", dbg["dXS"], XS[:])
            edma("sync", "so", dbg["dMNXS"], MNXS[:])
            edma("sync", "so", dbg["dARB"], ARB[:, 0:2])
            edma("sync", "so", dbg["dTKN"], TKN[:])
            edma("gpsimd", "so", dbg["dSS"], SS128[:])
            edma("sync", "so", dbg["dGAM"], GAM[:])
            edma("sync", "so", dbg["dBET"], BET[:])
            edma("sync", "so", dbg["dDBG1"], DBG1[:])
            edma("sync", "so", dbg["dSQS"], SQS[:])
        ewait("sync", "so")

        block = est.enter_context(nc.Block())

        @block.sync
        def _(e):
            for f in ops["sync"]:
                f(e)

        @block.vector
        def _(e):
            for f in ops["vector"]:
                f(e)

        @block.scalar
        def _(e):
            for f in ops["scalar"]:
                f(e)

        @block.tensor
        def _(e):
            for f in ops["tensor"]:
                f(e)

        @block.gpsimd
        def _(e):
            for f in ops["gpsimd"]:
                f(e)

    return nc



def _host_constf():
    cf = np.zeros((128, 256), np.float32)
    cf[:, 0:64] = np.arange(64, dtype=np.float32)[None, :]
    cf[:, 64:128] = (np.arange(64, dtype=np.float32) // 8)[None, :]
    cf[:, 128:256] = np.eye(128, dtype=np.float32)
    return cf


def _host_masks(cid):
    m = np.zeros((NKT, QT, QT), np.float32)
    for kt in range(NKT):
        g = 4 * (kt // 4) + cid
        kpos = kt * QT + np.arange(QT)[:, None]
        qpos = g * QT + np.arange(QT)[None, :]
        m[kt] = (kpos <= qpos).astype(np.float32)
    return m.reshape(NKT * QT, QT).astype(ml_dtypes.bfloat16)


def kernel(L0_attn_x, L0_attn_W_qkv, L0_attn_b_qkv, L0_attn_W_proj,
           L0_attn_b_proj, L0_ffn_ln_gamma, L0_ffn_ln_beta,
           L0_ffn_spline_bases, L0_ffn_spline_slopes, L0_ffn_output_scale):
    x = np.ascontiguousarray(np.asarray(L0_attn_x, dtype=np.float32))
    wqkv = np.ascontiguousarray(np.asarray(L0_attn_W_qkv, dtype=np.float32))
    bqkv = np.ascontiguousarray(np.asarray(L0_attn_b_qkv, dtype=np.float32))
    wproj = np.ascontiguousarray(np.asarray(L0_attn_W_proj, dtype=np.float32))
    bproj = np.ascontiguousarray(np.asarray(L0_attn_b_proj, dtype=np.float32)).reshape(1, D)
    gamma = np.ascontiguousarray(np.asarray(L0_ffn_ln_gamma, dtype=np.float32)).reshape(1, D)
    beta = np.ascontiguousarray(np.asarray(L0_ffn_ln_beta, dtype=np.float32)).reshape(1, D)
    bases = np.ascontiguousarray(np.asarray(L0_ffn_spline_bases, dtype=np.float32)).reshape(NT, DS * GRID)
    slopes = np.ascontiguousarray(np.asarray(L0_ffn_spline_slopes, dtype=np.float32)).reshape(NT, DS * GRID)
    oscale = np.ascontiguousarray(np.asarray(L0_ffn_output_scale, dtype=np.float32)).reshape(NT, 1)

    cf = _host_constf()
    idn = np.eye(128, dtype=ml_dtypes.bfloat16)

    nc = build_nc(stop=_BUILD_STOP)
    in_maps = []
    for core in range(8):
        batch, cid = core // 4, core % 4
        xs = np.concatenate(
            [x[batch, (4 * j + cid) * QT:(4 * j + cid + 1) * QT, :]
             for j in range(NJ)], axis=0)
        in_maps.append({
            "x": np.ascontiguousarray(xs),
            "wqkv": wqkv, "bqkv": bqkv, "wproj": wproj, "bproj": bproj,
            "gamma": gamma, "beta": beta, "bases": bases, "slopes": slopes,
            "oscale": oscale, "msk": _host_masks(cid), "constf": cf,
            "idnb": idn,
        })
    res = run_bass_kernel_spmd(nc, in_maps, core_ids=list(range(8)))
    out = np.empty((B, S, D), np.float32)
    for core in range(8):
        batch, cid = core // 4, core % 4
        y = res.results[core]["y"]
        for j in range(NJ):
            g = 4 * j + cid
            out[batch, g * QT:(g + 1) * QT, :] = y[j * QT:(j + 1) * QT, :]
    return out


_BUILD_STOP = 3


# version 07
# speedup vs baseline: 1.3658x; 1.0591x over previous
import numpy as np
import ml_dtypes
from contextlib import ExitStack

import concourse.bass as bass
import concourse.mybir as mybir
from concourse.bass_utils import run_bass_kernel_spmd

_DEBUG_DUMP = False

F32 = mybir.dt.float32
BF16 = mybir.dt.bfloat16
U8 = mybir.dt.uint8
AF = mybir.ActivationFunctionType
ALU = mybir.AluOpType
AX = mybir.AxisListType

B, S, D = 2, 4096, 2048
NJ, QT = 8, 128
T = NJ * QT
NH, HD = 16, 128
SCALE = float(HD) ** -0.5
NT, DS, GRID = 64, 32, 16
LNEPS = 1e-5
BIG = 1.0e9
FBIG = 1.0e30
EXPSHIFT = 20.0
NKT = 32
CCQ = 18


def _dbg():
    return _DEBUG_DUMP


def build_nc(stop=3):
    nc = bass.Bass("TRN2", target_bir_lowering=False, debug=False,
                   detect_race_conditions=True)

    d_x = nc.dram_tensor("x", [T, D], F32, kind="ExternalInput").ap()
    d_wqkv = nc.dram_tensor("wqkv", [D, 2304], F32, kind="ExternalInput").ap()
    d_bqkv = nc.dram_tensor("bqkv", [2304], F32, kind="ExternalInput").ap()
    d_wproj = nc.dram_tensor("wproj", [D, D], F32, kind="ExternalInput").ap()
    d_bproj = nc.dram_tensor("bproj", [1, D], F32, kind="ExternalInput").ap()
    d_gamma = nc.dram_tensor("gamma", [1, D], F32, kind="ExternalInput").ap()
    d_beta = nc.dram_tensor("beta", [1, D], F32, kind="ExternalInput").ap()
    d_bases = nc.dram_tensor("bases", [NT, DS * GRID], F32, kind="ExternalInput").ap()
    d_slopes = nc.dram_tensor("slopes", [NT, DS * GRID], F32, kind="ExternalInput").ap()
    d_oscale = nc.dram_tensor("oscale", [NT, 1], F32, kind="ExternalInput").ap()
    d_msk = nc.dram_tensor("msk", [NKT * QT, QT], BF16, kind="ExternalInput").ap()
    d_cf = nc.dram_tensor("constf", [128, 256], F32, kind="ExternalInput").ap()
    d_idnb = nc.dram_tensor("idnb", [128, 128], BF16, kind="ExternalInput").ap()
    d_y = nc.dram_tensor("y", [T, D], F32, kind="ExternalOutput").ap()
    dbg = {}
    if _dbg():
        for nm, shp in [("dSTAT", [128, 16]), ("dAFC", [128, 48]),
                        ("dOHALL", [128, 512]), ("dXS", [128, 256]),
                        ("dMNXS", [128, 32]), ("dARB", [128, 2]),
                        ("dTKN", [128, 16]), ("dSS", [128, 2048]),
                        ("dGAM", [128, 2048]), ("dBET", [128, 2048]),
                        ("dDBG1", [128, 16]), ("dDBG2", [128, 1024]), ("dSQS", [128, 8]),
                        ("dDBG3", [128, 1024])]:
            dbg[nm] = nc.dram_tensor(nm, shp, F32, kind="ExternalOutput").ap()

    d_ssrow = nc.dram_tensor("ssrow", [NT, DS], F32)
    d_ar_in = [nc.dram_tensor(f"ar_in{i}", [NT, 2], F32) for i in range(2)]
    d_ar_out = [nc.dram_tensor(f"ar_out{i}", [NT, 2], F32) for i in range(2)]
    d_kv_in = nc.dram_tensor("kv_in", [2 * 128, T], BF16)
    d_kv_out = nc.dram_tensor("kv_out", [4 * 2 * 128, T], BF16)

    est = ExitStack()
    with est:
        def sbt(name, shape, dt):
            return est.enter_context(nc.sbuf_tensor(name, shape, dt))

        X = sbt("X", [128, NJ * D], F32)
        MSK = sbt("MSK", [128, NKT * QT], BF16)
        SS128 = sbt("SS128", [128, D], BF16)
        GAM = sbt("GAM", [128, D], F32)
        BET = sbt("BET", [128, D], F32)
        CF = sbt("CF", [128, 256], F32)
        IDNB = sbt("IDNB", [128, 128], BF16)
        SL64 = sbt("SL64", [128, DS * GRID], F32)
        TBP = sbt("TBP", [128, DS * GRID], F32)
        MISC = sbt("MISC", [128, 4], F32)
        BQ = sbt("BQ", [128, CCQ], F32)
        BPR = sbt("BPR", [1, D], BF16)
        ONE1 = sbt("ONE1", [33, 128], F32)
        ONEB2 = sbt("ONEB2", [128, 2], BF16)
        ONEBR = sbt("ONEBR", [1, 128], BF16)
        BIGT = sbt("BIGT", [128, 64], F32)
        NBIGT = sbt("NBIGT", [128, 64], F32)
        R1 = sbt("R1", [128, 16384], BF16)
        R2 = sbt("R2", [128, 18432], BF16)
        KT = sbt("KT", [128, NKT * QT], BF16)
        VT = sbt("VT", [128, NKT * QT], BF16)
        PT = [sbt(f"PT{i}", [128, 1024], BF16) for i in range(2)]
        QSQ = sbt("QSQ", [128, 1024], BF16)
        VTRB = sbt("VTRB", [128, 1024], F32)
        TB2 = sbt("TB2", [128, DS * GRID], F32)
        TS2 = sbt("TS2", [128, DS * GRID], F32)
        OHALL = sbt("OHALL", [128, NJ * NT], F32)
        XS = sbt("XS", [128, NJ * DS], F32)
        MNXS = sbt("MNXS", [128, NJ * 4], F32)
        STAT = sbt("STAT", [128, 16], F32)
        SQS = sbt("SQS", [128, 8], F32)
        AFC = sbt("AFC", [128, 48], F32)
        TKN = sbt("TKN", [128, 16], F32)
        TSC = sbt("TSC", [128, 64], F32)
        TSCALL = sbt("TSCALL", [128, 512], F32)
        MMB2 = None
        MM1 = sbt("MM1", [128, 64], F32)
        MMIN = sbt("MMIN", [128, 64], F32)
        MMAX = sbt("MMAX", [128, 64], F32)
        EQM = sbt("EQM", [128, 512], U8)

        SC1 = sbt("SC1", [128, 1], F32)
        SC2 = sbt("SC2", [128, 1], F32)
        ARB = sbt("ARB", [128, 2], F32)
        OSC = sbt("OSC", [128, 1], F32)
        BB = sbt("BB", [128, 1], F32)
        KQM = sbt("KQM", [1, 8], F32)

        WBUF = [KT[:, 0:2048], KT[:, 2048:4096]]
        XNv = [R1[:, 0:4096].bitcast(F32), R1[:, 4096:8192].bitcast(F32)]
        TBALL = R1[:, 8192:16384].bitcast(F32)
        SCTB = R2[:, 8192:12288].bitcast(F32)
        TSALL = R2[:, 4096:8192]
        SCT = R2[:, 0:4096].bitcast(F32)
        SGN = [R2[:, 8192 + i * 4096: 8192 + i * 4096 + 2048] for i in range(2)]
        QQB = [R2[:, 10240 + i * 4096: 10240 + i * 4096 + 2048] for i in range(2)]
        GROW = R2[:, 0:4096].bitcast(F32)[0:1, :]
        BROW = R2[:, 4096:8192].bitcast(F32)[0:1, :]
        XN3B = [R1[:, 0:8192], R1[:, 8192:16384]]
        X3Fh = R1[:, 8192:16384]
        AOB = R1
        QKVB = R2
        WPB = [R2[:, 0:8192], R2[:, 8192:16384]]
        OHF = VTRB
        RB = VTRB
        VTO = VTRB[:, 0:512].bitcast(BF16)
        QSQF = QSQ.bitcast(F32)
        G64 = QSQF
        SREC = [QSQF[0:1, 0:512], QSQF[32:33, 0:512]]
        MMB2 = SCT[:, 512:640]
        MG256 = EQM[:, 0:256]

        T64A = SCT[:, 1024:1152]
        DBG1 = SCT[:, 1152:1168]
        PSBKF = None

        PSA = est.enter_context(nc.psum_tensor("PSA", [128, 1024], F32))
        PSB = est.enter_context(nc.psum_tensor("PSB", [128, 1024], F32))
        PSC = est.enter_context(nc.psum_tensor("PSC", [128, 1024], F32))
        PSD = est.enter_context(nc.psum_tensor("PSD", [128, 512], F32))
        PSBK = est.enter_context(nc.psum_tensor("PSBK", [128, 1024], BF16))
        PSBKF = PSBK.ap().bitcast(F32)

        sems = {}
        for s_ in ("di", "dw", "dwa", "dwb", "dk", "cc", "t", "v", "s", "g",
                   "so"):
            sems[s_] = est.enter_context(nc.semaphore(s_))
        cnt = {k: 0 for k in sems}
        ops = {e: [] for e in ("sync", "vector", "scalar", "tensor", "gpsimd")}

        def on(eng, fn):
            ops[eng].append(fn)

        def ewait(eng, s_, val=None):
            v = cnt[s_] if val is None else val
            if v <= 0:
                return
            on(eng, lambda e, s_=s_, v=v: e.wait_ge(sems[s_], v))

        def edma(eng, s_, out, in_, slow=False):
            cnt[s_] += 16
            if slow:
                on(eng, lambda e, out=out, in_=in_, s_=s_:
                   e.dma_start(out=out, in_=in_, allow_slow_non_contiguous=True)
                   .then_inc(sems[s_], 16))
            else:
                on(eng, lambda e, out=out, in_=in_, s_=s_:
                   e.dma_start(out=out, in_=in_).then_inc(sems[s_], 16))

        def vsync():
            ewait("vector", "v", cnt["v"])

        def vrun(fn):
            run("vector", fn, inc="v")

        def run(eng, fn, inc=None):
            if eng == "vector":
                ewait("vector", "v", cnt["v"])
                if inc is None:
                    inc = "v"
            elif eng == "scalar":
                ewait("scalar", "s", cnt["s"])
                if inc is None:
                    inc = "s"
            if inc is not None:
                cnt[inc] += 1
            def _f(e, fn=fn, inc=inc):
                r = fn(e)
                if inc is not None:
                    r.then_inc(sems[inc], 1)
            on(eng, _f)

        IOTA64 = CF[:, 0:64]
        IOTAD8 = CF[:, 64:128]
        IDNF = CF[:, 128:256]

        edma("sync", "di", X[:].rearrange("p (j d) -> p j d", d=D),
             d_x.rearrange("(j p) d -> p j d", p=128))
        edma("sync", "di", MSK[:].rearrange("p (n q) -> p n q", q=QT),
             d_msk.rearrange("(n p) q -> p n q", p=128))
        edma("sync", "di", CF[:], d_cf)
        edma("sync", "di", IDNB[:], d_idnb)
        edma("sync", "di", SL64[0:NT, :], d_slopes)
        edma("sync", "di", TBP[0:NT, :], d_bases)
        edma("sync", "di", OSC[0:NT, :], d_oscale)
        edma("sync", "di", BQ[:], d_bqkv.rearrange("(cc p) -> p cc", p=128),
             slow=True)
        ld_all = cnt["di"]
        edma("gpsimd", "dw", GROW, d_gamma)
        edma("gpsimd", "dw", BROW, d_beta)
        edma("gpsimd", "dw", BPR[:], d_bproj)
        g_rows = cnt["dw"]

        def _init(e):
            e.memset(ONE1[:], 1.0)
            e.memset(ONEB2[:], 1.0)
            e.memset(ONEBR[:], 1.0)
            e.memset(BIGT[:], BIG)
            e.memset(NBIGT[:], -BIG)
            return e.memset(MISC[:], 0.0)
        run("gpsimd", _init, inc="g")
        g_init = cnt["g"]

        ewait("tensor", "dw", g_rows)
        ewait("tensor", "g", g_init)
        for part in range(2):
            row = GROW if part == 0 else BROW
            dst = GAM if part == 0 else BET
            for h in range(2):
                ewait("tensor", "v", cnt["v"])
                for q in range(2):
                    run("tensor", lambda e, row=row, h=h, q=q: e.matmul(
                        PSA[:, q * 512:(q + 1) * 512], ONE1[0:1, :],
                        row[0:1, h * 1024 + q * 512: h * 1024 + (q + 1) * 512],
                        start=True, stop=True), inc="t" if q == 1 else None)
                ewait("vector", "t", cnt["t"])
                run("vector", lambda e, dst=dst, h=h: e.tensor_copy(
                    out=dst[:, h * 1024:(h + 1) * 1024], in_=PSA[:, 0:1024]),
                    inc="v")
        v_gb = cnt["v"]

        ewait("vector", "di", ld_all)
        if _dbg():
            run("vector", lambda e: e.tensor_copy(
                out=VTRB[:], in_=X[:, 7 * 2048: 7 * 2048 + 1024]), inc="v")
            ewait("sync", "v", cnt["v"])
            edma("sync", "so", dbg["dDBG2"], VTRB[:])
        run("vector", lambda e: e.tensor_reduce(
            out=TSC[0:NT, 0:DS],
            in_=SL64[0:NT, :].rearrange("p (j g) -> p j g", g=GRID),
            axis=AX.X, op=ALU.add), inc="v")
        ewait("scalar", "v", cnt["v"])
        run("scalar", lambda e: e.activation(
            out=TSC[0:NT, 0:DS], in_=TSC[0:NT, 0:DS], func=AF.Sign), inc="s")
        s_ss = cnt["s"]
        ewait("gpsimd", "g", g_init)
        run("gpsimd", lambda e: e.iota(
            G64[0:NT, :], [[0, DS], [1, GRID]], channel_multiplier=0,
            allow_small_or_imprecise_dtypes=True), inc="g")
        g_g64 = cnt["g"]
        ewait("vector", "g", g_g64)
        vrun(lambda e: e.tensor_tensor(
            out=G64[0:NT, :], in0=SL64[0:NT, :], in1=G64[0:NT, :], op=ALU.mult))
        vrun(lambda e: e.tensor_tensor(
            out=TBP[0:NT, :], in0=TBP[0:NT, :], in1=G64[0:NT, :],
            op=ALU.subtract))
        v_tbp = cnt["v"]
        ewait("gpsimd", "s", s_ss)
        edma("gpsimd", "dk", d_ssrow.ap(), TSC[0:NT, 0:DS])
        ewait("gpsimd", "dk")
        edma("gpsimd", "dk", VTRB[0:1, 0:1024],
             d_ssrow.ap().rearrange("a b -> (a b)")[None, 0:1024])
        edma("gpsimd", "dk", VTRB[32:33, 0:1024],
             d_ssrow.ap().rearrange("a b -> (a b)")[None, 1024:2048])
        g_ssrow = cnt["dk"]
        ewait("tensor", "dk", g_ssrow)
        for h in range(2):
            hp = 0 if h == 0 else 32
            ewait("tensor", "v", cnt["v"])
            for q in range(2):
                run("tensor", lambda e, hp=hp, q=q: e.matmul(
                    PSB[:, q * 512:(q + 1) * 512], ONE1[hp:hp + 1, :],
                    VTRB[hp:hp + 1, q * 512:(q + 1) * 512],
                    start=True, stop=True), inc="t" if q == 1 else None)
            ewait("vector", "t", cnt["t"])
            run("vector", lambda e, h=h: e.tensor_copy(
                out=SS128[:, h * 1024:(h + 1) * 1024], in_=PSB[:, 0:1024]),
                inc="v")
        v_ss = cnt["v"]

        def ffn(ar_idx):
            ewait("vector", "t", cnt["t"])
            ewait("scalar", "t", cnt["t"])
            ewait("vector", "v", cnt["v"])
            ewait("scalar", "v", cnt["v"])
            ewait("scalar", "g", cnt["g"])
            ewait("vector", "g", cnt["g"])
            for j in range(NJ):
                Xj = X[:, j * D:(j + 1) * D]
                run("scalar", lambda e, Xj=Xj, j=j: e.activation(
                    out=SGN[j % 2], in_=Xj, func=AF.Square,
                    accum_out=SQS[:, j:j + 1]), inc="s")
                vrun(lambda e, Xj=Xj, j=j: e.reduce_sum(
                    out=STAT[:, j:j + 1], in_=Xj, axis=AX.X))
            MU = AFC[:, 0:8]; VAR = AFC[:, 8:16]; RS1 = AFC[:, 16:24]
            RS2 = AFC[:, 24:32]; AFj = AFC[:, 32:40]; CFj = AFC[:, 40:48]
            AY = STAT[:, 0:8]; CY = STAT[:, 8:16]
            ewait("vector", "s", cnt["s"])
            vrun(lambda e: e.tensor_scalar(out=MU, in0=STAT[:, 0:8],
                                           scalar1=1.0 / D, scalar2=None,
                                           op0=ALU.mult))
            vrun(lambda e: e.tensor_scalar(out=VAR, in0=SQS[:], scalar1=1.0 / D,
                                           scalar2=None, op0=ALU.mult))
            vrun(lambda e: e.tensor_tensor(out=TSC[:, 0:8], in0=MU, in1=MU,
                                           op=ALU.mult))
            vrun(lambda e: e.tensor_tensor(out=VAR, in0=VAR, in1=TSC[:, 0:8],
                                           op=ALU.subtract))
            vrun(lambda e: e.tensor_scalar(out=RS1, in0=VAR, scalar1=LNEPS,
                                           scalar2=None, op0=ALU.add))
            vrun(lambda e: e.reciprocal(out=RS1, in_=RS1))
            ewait("scalar", "v", cnt["v"])
            run("scalar", lambda e: e.activation(out=RS1, in_=RS1, func=AF.Sqrt),
                inc="s")
            ewait("vector", "s", cnt["s"])
            vrun(lambda e: e.tensor_tensor(out=RS2, in0=RS1, in1=RS1, op=ALU.mult))
            vrun(lambda e: e.tensor_tensor(out=RS2, in0=RS2, in1=VAR, op=ALU.mult))
            vrun(lambda e: e.tensor_scalar(out=RS2, in0=RS2, scalar1=0.25,
                                           scalar2=LNEPS, op0=ALU.mult,
                                           op1=ALU.add))
            vrun(lambda e: e.reciprocal(out=RS2, in_=RS2))
            ewait("scalar", "v", cnt["v"])
            run("scalar", lambda e: e.activation(out=RS2, in_=RS2, func=AF.Sqrt),
                inc="s")
            ewait("vector", "s", cnt["s"])
            vrun(lambda e: e.tensor_tensor(out=AFj, in0=RS1, in1=RS2, op=ALU.mult))
            vrun(lambda e: e.tensor_scalar(out=AFj, in0=AFj, scalar1=0.5,
                                           scalar2=None, op0=ALU.mult))
            vrun(lambda e: e.tensor_tensor(out=CFj, in0=MU, in1=AFj, op=ALU.mult))
            vrun(lambda e: e.tensor_scalar(out=CFj, in0=CFj, scalar1=-1.0,
                                           scalar2=None, op0=ALU.mult))
            vrun(lambda e: e.tensor_tensor(out=CY, in0=MU, in1=RS1, op=ALU.mult))
            vrun(lambda e: e.tensor_scalar(out=CY, in0=CY, scalar1=-0.5,
                                           scalar2=None, op0=ALU.mult))
            vrun(lambda e: e.tensor_scalar(out=AY, in0=RS1, scalar1=0.5,
                                           scalar2=1.0, op0=ALU.mult, op1=ALU.add))

            s_sgn = [0] * NJ
            g_gam = [0] * NJ
            for jp in range(0, NJ, 2):
                for j in (jp, jp + 1):
                    Xj = X[:, j * D:(j + 1) * D]
                    XN = XNv[j % 2]
                    if j >= 2:
                        ewait("vector", "s", s_sgn[j - 2])
                    vrun(lambda e, Xj=Xj, XN=XN, j=j: e.tensor_scalar(
                        out=XN, in0=Xj, scalar1=AFj[:, j:j + 1],
                        scalar2=CFj[:, j:j + 1], op0=ALU.mult, op1=ALU.add))
                    vrun(lambda e, XN=XN: e.tensor_tensor(
                        out=XN, in0=XN, in1=GAM[:], op=ALU.mult))
                    vrun(lambda e, XN=XN: e.tensor_tensor(
                        out=XN, in0=XN, in1=BET[:], op=ALU.add))
                    v_xn = cnt["v"]
                    ewait("scalar", "v", v_xn)
                    run("scalar", lambda e, XN=XN, j=j: e.activation(
                        out=SGN[j % 2], in_=XN, func=AF.Sign), inc="s")
                    s_sgn[j] = cnt["s"]
                    ewait("vector", "s", s_sgn[j])
                    vrun(lambda e, j=j: e.tensor_tensor(
                        out=QQB[j % 2], in0=SGN[j % 2], in1=SS128[:],
                        op=ALU.mult))
                    vrun(lambda e, j=j: e.tensor_reduce(
                        out=TSCALL[:, j * 64:(j + 1) * 64],
                        in_=QQB[j % 2].rearrange("p (t k) -> p t k", k=DS),
                        axis=AX.X, op=ALU.add))
                TS2v = TSCALL[:, jp * 64:(jp + 2) * 64]
                CS64 = MMIN[:, 0:16]
                CM8 = MM1[:, 0:2]; BC8 = MM1[:, 2:4]
                TM8 = MM1[:, 4:6]; TI8 = MM1[:, 6:8]
                vrun(lambda e, TS2v=TS2v, CS64=CS64: e.tensor_reduce(
                    out=CS64, in_=TS2v.rearrange("p (a c t) -> p (a c) t", t=8,
                                                 c=8), axis=AX.X, op=ALU.add))
                vrun(lambda e, CS64=CS64, CM8=CM8: e.tensor_reduce(
                    out=CM8, in_=CS64.rearrange("p (a c) -> p a c", c=8),
                    axis=AX.X, op=ALU.max))
                vrun(lambda e, CS64=CS64, CM8=CM8: e.tensor_tensor(
                    out=EQM[:, 0:16].rearrange("p (a c) -> p a c", c=8),
                    in0=CS64.rearrange("p (a c) -> p a c", c=8),
                    in1=CM8.unsqueeze(2).broadcast_to([128, 2, 8]),
                    op=ALU.is_equal))
                for a in range(2):
                    vrun(lambda e, a=a: e.select(
                        out=MMAX[:, a * 8:(a + 1) * 8],
                        mask=EQM[:, a * 8:(a + 1) * 8],
                        on_true=IOTA64[:, 0:8], on_false=BIGT[:, 0:8],
                        add_drain=True))
                vrun(lambda e, BC8=BC8: e.tensor_reduce(
                    out=BC8, in_=MMAX[:, 0:16].rearrange("p (a c) -> p a c", c=8),
                    axis=AX.X, op=ALU.min))
                vrun(lambda e, BC8=BC8: e.tensor_tensor(
                    out=MMB2.rearrange("p (a t) -> p a t", t=64),
                    in0=IOTAD8[:].unsqueeze(1).broadcast_to([128, 2, 64]),
                    in1=BC8.unsqueeze(2).broadcast_to([128, 2, 64]),
                    op=ALU.is_equal))
                vrun(lambda e, TS2v=TS2v: e.tensor_scalar(
                    out=MMB2, in0=MMB2, scalar1=65536.0, scalar2=None,
                    op0=ALU.mult))
                vrun(lambda e, TS2v=TS2v: e.tensor_tensor(
                    out=MMB2, in0=MMB2, in1=TS2v, op=ALU.add))
                vrun(lambda e, TM8=TM8: e.tensor_reduce(
                    out=TM8, in_=MMB2.rearrange("p (a t) -> p a t", t=64),
                    axis=AX.X, op=ALU.max))
                vrun(lambda e, TM8=TM8: e.tensor_tensor(
                    out=EQM[:, 0:128].rearrange("p (a t) -> p a t", t=64),
                    in0=MMB2.rearrange("p (a t) -> p a t", t=64),
                    in1=TM8.unsqueeze(2).broadcast_to([128, 2, 64]),
                    op=ALU.is_equal))
                for a in range(2):
                    vrun(lambda e, a=a: e.select(
                        out=MMB2[:, a * 64:(a + 1) * 64],
                        mask=EQM[:, a * 64:(a + 1) * 64],
                        on_true=IOTA64[:], on_false=BIGT[:], add_drain=True))
                vrun(lambda e, TI8=TI8: e.tensor_reduce(
                    out=TI8, in_=MMB2.rearrange("p (a t) -> p a t", t=64),
                    axis=AX.X, op=ALU.min))
                vrun(lambda e, TI8=TI8, jp=jp: e.tensor_tensor(
                    out=OHALL[:, jp * 64:(jp + 2) * 64].rearrange(
                        "p (a t) -> p a t", t=64),
                    in0=IOTA64[:].unsqueeze(1).broadcast_to([128, 2, 64]),
                    in1=TI8.unsqueeze(2).broadcast_to([128, 2, 64]),
                    op=ALU.is_equal))
                v_oh = cnt["v"]
                for j in (jp, jp + 1):
                    XN = XNv[j % 2]
                    OHj = OHALL[:, j * 64:(j + 1) * 64]
                    ewait("vector", "g", cnt["g"])
                    vrun(lambda e, XN=XN, OHj=OHj: e.tensor_tensor(
                        out=SCT[:].rearrange("p (a b) -> p a b", b=DS),
                        in0=XN.rearrange("p (a b) -> p a b", b=DS),
                        in1=OHj.unsqueeze(2).broadcast_to([128, 64, DS]),
                        op=ALU.mult))
                    vrun(lambda e, j=j: e.tensor_reduce(
                        out=XS[:, j * DS:(j + 1) * DS],
                        in_=SCT[:].rearrange("p (t k) -> p k t", k=DS),
                        axis=AX.X, op=ALU.add))
                ewait("tensor", "v", v_oh)
                for j in (jp, jp + 1):
                    OHj = OHALL[:, j * 64:(j + 1) * 64]
                    if jp or j != jp:
                        ewait("tensor", "v", cnt["v"])
                    run("tensor", lambda e, OHj=OHj: e.transpose(
                        PSD[0:64, 128:256], OHj, IDNF), inc="t")
                    ewait("vector", "t", cnt["t"])
                    vrun(lambda e, j=j: e.tensor_copy(
                        out=OHF[0:64, j * 128:(j + 1) * 128],
                        in_=PSD[0:64, 128:256]))
            vrun(lambda e: e.tensor_reduce(
                out=TKN[:, 0:8], in_=XS[:].rearrange("p (j k) -> p j k", k=DS),
                axis=AX.X, op=ALU.min))
            vrun(lambda e: e.tensor_reduce(
                out=TKN[:, 8:16], in_=XS[:].rearrange("p (j k) -> p j k", k=DS),
                axis=AX.X, op=ALU.max))
            MMB = SCT[:, 0:512]
            vrun(lambda e: e.tensor_scalar(out=TSC[:, 32:40], in0=TKN[:, 0:8],
                                           scalar1=-1024.0, scalar2=None,
                                           op0=ALU.add))
            vrun(lambda e: e.tensor_scalar(out=TSC[:, 40:48], in0=TKN[:, 8:16],
                                           scalar1=1024.0, scalar2=None,
                                           op0=ALU.add))
            vrun(lambda e, MMB=MMB: e.tensor_tensor(
                out=MMB.rearrange("p (j t) -> p j t", t=64),
                in0=OHALL[:].rearrange("p (j t) -> p j t", t=64),
                in1=TSC[:, 32:40].unsqueeze(2).broadcast_to([128, NJ, 64]),
                op=ALU.mult))
            vrun(lambda e, MMB=MMB: e.tensor_scalar(
                out=MMB, in0=MMB, scalar1=1024.0, scalar2=None, op0=ALU.add))
            vrun(lambda e, MMB=MMB: e.tensor_reduce(
                out=MMIN[:], in_=MMB.rearrange("p (j t) -> p t j", t=64),
                axis=AX.X, op=ALU.min))
            vrun(lambda e, MMB=MMB: e.tensor_tensor(
                out=MMB.rearrange("p (j t) -> p j t", t=64),
                in0=OHALL[:].rearrange("p (j t) -> p j t", t=64),
                in1=TSC[:, 40:48].unsqueeze(2).broadcast_to([128, NJ, 64]),
                op=ALU.mult))
            vrun(lambda e, MMB=MMB: e.tensor_scalar(
                out=MMB, in0=MMB, scalar1=-1024.0, scalar2=None, op0=ALU.add))
            vrun(lambda e, MMB=MMB: e.tensor_reduce(
                out=MMAX[:], in_=MMB.rearrange("p (j t) -> p t j", t=64),
                axis=AX.X, op=ALU.max))
            v_loopA = cnt["v"]


            ewait("tensor", "v", v_loopA)
            def _mmT(e):
                e.transpose(PSD[0:64, 128:256], MMIN[:], IDNF)
                return e.transpose(PSD[0:64, 256:384], MMAX[:], IDNF)
            run("tensor", _mmT, inc="t")
            ewait("vector", "t", cnt["t"])
            vrun(lambda e: e.tensor_copy(out=T64A[0:64, :], in_=PSD[0:64, 128:256]))
            vrun(lambda e: e.tensor_reduce(out=ARB[0:64, 0:1], in_=T64A[0:64, :],
                                           axis=AX.X, op=ALU.min))
            vrun(lambda e: e.tensor_copy(out=T64A[0:64, :], in_=PSD[0:64, 256:384]))
            vrun(lambda e: e.tensor_reduce(out=ARB[0:64, 1:2], in_=T64A[0:64, :],
                                           axis=AX.X, op=ALU.max))
            vrun(lambda e: e.tensor_scalar(out=ARB[0:64, 0:1], in0=ARB[0:64, 0:1],
                                           scalar1=-1.0, scalar2=None,
                                           op0=ALU.mult))
            ewait("gpsimd", "v", cnt["v"])
            edma("gpsimd", "dk", d_ar_in[ar_idx].ap(), ARB[0:64, 0:2])
            ewait("gpsimd", "dk")
            cnt["cc"] += 1
            on("gpsimd", lambda e, ar_idx=ar_idx: e.collective_compute(
                "AllReduce", ALU.max, replica_groups=[list(range(8))],
                ins=[d_ar_in[ar_idx].ap().opt()],
                outs=[d_ar_out[ar_idx].ap().opt()]).then_inc(sems["cc"], 1))
            ewait("tensor", "v", max(v_loopA, v_tbp))
            ev_j = {}
            for j in range(NJ):
                OHFj = OHF[0:NT, j * 128:(j + 1) * 128]
                h = (j % 2) * 512
                if j >= 2:
                    ewait("tensor", "v", ev_j[j - 2])
                run("tensor", lambda e, OHFj=OHFj, h=h: e.matmul(
                    PSA[:, h:h + 512], OHFj, TBP[0:NT, :], start=True,
                    stop=True))
                run("tensor", lambda e, OHFj=OHFj, h=h: e.matmul(
                    PSB[:, h:h + 512], OHFj, SL64[0:NT, :], start=True,
                    stop=True), inc="t")
                ewait("vector", "t", cnt["t"])
                def _tev(e, j=j, h=h):
                    e.tensor_copy(out=TBALL[:, j * 512:(j + 1) * 512],
                                  in_=PSA[:, h:h + 512])
                    return e.tensor_copy(out=TSALL[:, j * 512:(j + 1) * 512],
                                         in_=PSB[:, h:h + 512])
                run("vector", _tev, inc="v")
                ev_j[j] = cnt["v"]

            ewait("gpsimd", "cc")
            edma("gpsimd", "dk", ARB[0:64, 0:2], d_ar_out[ar_idx].ap())
            ewait("vector", "dk", cnt["dk"])
            vrun(lambda e: e.tensor_scalar(out=MISC[0:64, 0:1], in0=ARB[0:64, 0:1],
                                           scalar1=-1.0, scalar2=None,
                                           op0=ALU.mult))
            vrun(lambda e: e.tensor_copy(out=MISC[0:64, 1:2], in_=ARB[0:64, 1:2]))
            vrun(lambda e: e.tensor_copy(out=MISC[0:64, 2:3], in_=OSC[0:64, :]))
            v_misc = cnt["v"]
            ewait("tensor", "v", v_misc)
            for j in range(NJ):
                OHFj = OHF[0:NT, j * 128:(j + 1) * 128]
                run("tensor", lambda e, OHFj=OHFj, j=j: e.matmul(
                    PSD[:, j * 4:(j + 1) * 4], OHFj, MISC[0:NT, :],
                    start=True, stop=True), inc="t" if j == NJ - 1 else None)
            ewait("vector", "t", cnt["t"])
            vrun(lambda e: e.tensor_copy(out=MNXS[:, 0:32], in_=PSD[:, 0:32]))

            MNv = MNXS[:].rearrange("p (j c) -> p j c", c=4)[:, :, 0:1]
            MXv = MNXS[:].rearrange("p (j c) -> p j c", c=4)[:, :, 1:2]
            SCv = MNXS[:].rearrange("p (j c) -> p j c", c=4)[:, :, 2:3]
            XSv = XS[:].rearrange("p (j k) -> p j k", k=DS)
            R8 = TSC[:, 8:16]; INV8 = TSC[:, 16:24]; SCH8 = TSC[:, 24:32]
            vsync()
            vrun(lambda e: e.tensor_tensor(
                out=R8.unsqueeze(2), in0=MXv, in1=MNv, op=ALU.subtract))
            vrun(lambda e: e.tensor_scalar(out=R8, in0=R8, scalar1=1e-8,
                                           scalar2=None, op0=ALU.add))
            vrun(lambda e: e.reciprocal(out=INV8, in_=R8))
            vrun(lambda e: e.tensor_scalar(out=SCH8.unsqueeze(2), in0=SCv,
                                           scalar1=0.5, scalar2=None,
                                           op0=ALU.mult))
            SP1v = SCT[:, 0:256]
            SP2v = SCT[:, 256:512]
            BACv = SCT[:, 512:768]
            SACv = SCT[:, 768:1024]
            MGv = EQM[:, 0:64]
            vrun(lambda e: e.tensor_tensor(
                out=SP1v.rearrange("p (j k) -> p j k", k=DS), in0=XSv,
                in1=MNv.broadcast_to([128, NJ, DS]), op=ALU.subtract))
            vrun(lambda e: e.tensor_tensor(
                out=SP1v.rearrange("p (j k) -> p j k", k=DS),
                in0=SP1v.rearrange("p (j k) -> p j k", k=DS),
                in1=INV8.unsqueeze(2).broadcast_to([128, NJ, DS]), op=ALU.mult))
            vrun(lambda e: e.tensor_scalar(out=SP1v, in0=SP1v,
                                           scalar1=1.0 - 1e-6, scalar2=0.0,
                                           op0=ALU.min, op1=ALU.max))
            vrun(lambda e: e.tensor_scalar(out=SP1v, in0=SP1v,
                                           scalar1=float(GRID), scalar2=None,
                                           op0=ALU.mult))
            TBv = TBALL[:].rearrange("p (j k g) -> p j k g", k=DS, g=GRID)
            TSv = TSALL[:].rearrange("p (j k g) -> p j k g", k=DS, g=GRID)
            vrun(lambda e: e.tensor_copy(
                out=BACv.rearrange("p (j k) -> p j k", k=DS), in_=TBv[:, :, :, 0]))
            vrun(lambda e: e.tensor_copy(
                out=SACv.rearrange("p (j k) -> p j k", k=DS), in_=TSv[:, :, :, 0]))
            for g in range(1, GRID):
                vrun(lambda e, g=g: e.tensor_scalar(
                    out=MG256[:], in0=SP1v, scalar1=float(g), scalar2=None,
                    op0=ALU.is_ge))
                vrun(lambda e, g=g: e.select(
                    out=BACv.rearrange("p (j k) -> p j k", k=DS),
                    mask=MG256[:].rearrange("p (j k) -> p j k", k=DS),
                    on_true=TBv[:, :, :, g],
                    on_false=BACv.rearrange("p (j k) -> p j k", k=DS),
                    add_drain=True))
                vrun(lambda e, g=g: e.select(
                    out=SACv.rearrange("p (j k) -> p j k", k=DS),
                    mask=MG256[:].rearrange("p (j k) -> p j k", k=DS),
                    on_true=TSv[:, :, :, g],
                    on_false=SACv.rearrange("p (j k) -> p j k", k=DS),
                    add_drain=True))
            vrun(lambda e: e.tensor_tensor(out=SP2v, in0=SACv, in1=SP1v,
                                           op=ALU.mult))
            vrun(lambda e: e.tensor_tensor(out=SP2v, in0=SP2v, in1=BACv,
                                           op=ALU.add))
            vrun(lambda e: e.tensor_tensor(
                out=SP2v.rearrange("p (j k) -> p j k", k=DS),
                in0=SP2v.rearrange("p (j k) -> p j k", k=DS),
                in1=SCH8.unsqueeze(2).broadcast_to([128, NJ, DS]), op=ALU.mult))
            g_scat = 0
            for j in range(NJ):
                Xj = X[:, j * D:(j + 1) * D]
                OHj = OHALL[:, j * 64:(j + 1) * 64]
                if g_scat:
                    ewait("vector", "g", g_scat)
                vrun(lambda e, Xj=Xj, j=j: e.tensor_scalar(
                    out=Xj, in0=Xj, scalar1=AY[:, j:j + 1],
                    scalar2=CY[:, j:j + 1], op0=ALU.mult, op1=ALU.add))
                vrun(lambda e, OHj=OHj, j=j: e.tensor_tensor(
                    out=SCTB[:].rearrange("p (a b) -> p a b", b=DS),
                    in0=OHj.unsqueeze(2).broadcast_to([128, 64, DS]),
                    in1=SP2v.rearrange("p (j2 k) -> p j2 k", k=DS)[:, j, :]
                        .unsqueeze(1).broadcast_to([128, 64, DS]),
                    op=ALU.mult))
                ewait("gpsimd", "v", cnt["v"])
                run("gpsimd", lambda e, Xj=Xj: e.tensor_tensor(
                    out=Xj, in0=Xj, in1=SCTB[:], op=ALU.add), inc="g")
                g_scat = cnt["g"]
            ewait("vector", "g", cnt["g"])
            ewait("scalar", "g", cnt["g"])
            ewait("tensor", "g", cnt["g"])

        _pre_dw_map = {}

        def attention():
            _vred = [0] * NJ
            for j in range(NJ):
                Xj = X[:, j * D:(j + 1) * D]
                if j >= 2:
                    ewait("scalar", "v", _vred[j - 2])
                run("scalar", lambda e, Xj=Xj, j=j: e.activation(
                    out=SGN[j % 2], in_=Xj, func=AF.Square), inc="s")
                s_sq = cnt["s"]
                run("vector", lambda e, Xj=Xj, j=j: e.reduce_sum(
                    out=STAT[:, j:j + 1], in_=Xj, axis=AX.X))
                ewait("vector", "s", s_sq)
                run("vector", lambda e, j=j: e.reduce_sum(
                    out=SQS[:, j:j + 1], in_=SGN[j % 2], axis=AX.X), inc="v")
                _vred[j] = cnt["v"]
            A3 = AFC[:, 0:8]; C3 = AFC[:, 8:16]
            ewait("vector", "s", cnt["s"])
            vsync()
            vrun(lambda e: e.tensor_scalar(out=A3, in0=SQS[:], scalar1=1.0 / D,
                                           scalar2=None, op0=ALU.mult))
            vrun(lambda e: e.tensor_scalar(out=C3, in0=STAT[:, 0:8],
                                           scalar1=1.0 / D, scalar2=None,
                                           op0=ALU.mult))
            vrun(lambda e: e.tensor_tensor(out=AFC[:, 16:24], in0=C3, in1=C3,
                                           op=ALU.mult))
            vrun(lambda e: e.tensor_tensor(out=A3, in0=A3, in1=AFC[:, 16:24],
                                           op=ALU.subtract))
            vrun(lambda e: e.tensor_scalar(out=A3, in0=A3, scalar1=LNEPS,
                                           scalar2=None, op0=ALU.add))
            vrun(lambda e: e.reciprocal(out=A3, in_=A3))
            ewait("scalar", "v", cnt["v"])
            run("scalar", lambda e: e.activation(out=A3, in_=A3, func=AF.Sqrt),
                inc="s")
            ewait("vector", "s", cnt["s"])
            vrun(lambda e: e.tensor_tensor(out=C3, in0=C3, in1=A3, op=ALU.mult))
            vrun(lambda e: e.tensor_scalar(out=C3, in0=C3, scalar1=-1.0,
                                           scalar2=None, op0=ALU.mult))
            v_l3 = cnt["v"]

            ewait("gpsimd", "v", v_l3)
            ewait("gpsimd", "s", cnt["s"])
            dw_cc = dict(_pre_dw_map)
            def wqkv_load(hf, cc):
                s_ = "dwa" if cc % 2 == 0 else "dwb"
                edma("gpsimd", s_, WBUF[cc % 2][:].rearrange(
                    "p (dc c) -> p dc c", c=128),
                    d_wqkv.rearrange("(dc p) c -> p dc c", p=128)[:, :, cc * 128:(cc + 1) * 128])
                dw_cc[(hf, cc)] = (s_, cnt[s_])
            s_x3f = 0
            s_evac_prev = [0, 0]
            s_qkv_ev = [0, 0]
            for hf in range(2):
                v_xn3 = 0
                for j4 in range(4):
                    j = hf * 4 + j4
                    Xj = X[:, j * D:(j + 1) * D]
                    if hf == 1 and j4 == 0:
                        ewait("vector", "t", cnt["t"])
                    vrun(lambda e, Xj=Xj, j=j, j4=j4: e.tensor_scalar(
                        out=XN3B[0][:, j4 * 2048:(j4 + 1) * 2048], in0=Xj,
                        scalar1=A3[:, j:j + 1], scalar2=C3[:, j:j + 1],
                        op0=ALU.mult, op1=ALU.add))
                v_xn3 = cnt["v"]
                X3F = XN3B[1]
                ewait("tensor", "v", v_xn3)
                for dc in range(16):
                    if s_evac_prev[0]:
                        ewait("tensor", "s", s_evac_prev[0])
                    for j4 in range(4):
                        run("tensor", lambda e, dc=dc, j4=j4: e.transpose(
                            PSBK[:, j4 * 128:(j4 + 1) * 128],
                            XN3B[0][:, j4 * 2048 + dc * 128: j4 * 2048 + (dc + 1) * 128],
                            IDNB[:]), inc="t" if j4 == 3 else None)
                    t_tp = cnt["t"]
                    ewait("scalar", "t", t_tp)
                    run("scalar", lambda e, dc=dc, X3F=X3F: e.copy(
                        out=X3F[:, dc * 512:(dc + 1) * 512],
                        in_=PSBK[:, 0:512]), inc="s")
                    s_evac_prev[0] = cnt["s"]
                s_x3f = cnt["s"]
                for cc in range(CCQ):
                    ewait("tensor", dw_cc[(hf, cc)][0], dw_cc[(hf, cc)][1])
                    ewait("tensor", "s", s_x3f)
                    half = cc % 2
                    if s_qkv_ev[half]:
                        ewait("tensor", "s", s_qkv_ev[half])
                    for dc in range(16):
                        run("tensor", lambda e, cc=cc, dc=dc, half=half, X3F=X3F:
                            e.matmul(
                                PSA[:, half * 512:(half + 1) * 512],
                                WBUF[cc % 2][:, dc * 128:(dc + 1) * 128],
                                X3F[:, dc * 512:(dc + 1) * 512],
                                start=(dc == 0), stop=(dc == 15)),
                            inc="t" if dc == 15 else None)
                    t_qkv = cnt["t"]
                    ewait("scalar", "t", t_qkv)
                    run("scalar", lambda e, cc=cc, hf=hf, half=half: e.activation(
                        out=QKVB[:, cc * 1024 + hf * 512: cc * 1024 + hf * 512 + 512],
                        in_=PSA[:, half * 512:(half + 1) * 512], func=AF.Identity,
                        bias=BQ[:, cc:cc + 1], scale=1.0), inc="s")
                    s_qkv_ev[half] = cnt["s"]
                    nhf, ncc = hf, cc + 2
                    if ncc >= CCQ:
                        nhf, ncc = hf + 1, ncc - CCQ
                    if nhf < 2 and (nhf, ncc) not in dw_cc:
                        ewait("gpsimd", "t", t_qkv)
                        wqkv_load(nhf, ncc)
            s_qkv_done = cnt["s"]

            ewait("tensor", "s", s_qkv_done)
            v_vto = 0
            for j in range(NJ):
                if v_vto:
                    ewait("tensor", "v", v_vto)
                run("tensor", lambda e, j=j: e.transpose(
                    PSBK[:, 0:128],
                    QKVB[:, 17 * 1024 + j * 128: 17 * 1024 + (j + 1) * 128],
                    IDNB[:]), inc="t")
                ewait("vector", "t", cnt["t"])
                run("vector", lambda e, j=j: e.tensor_copy(
                    out=VTO[:, j * 128:(j + 1) * 128],
                    in_=PSBK[:, 0:128]), inc="v")
                v_vto = cnt["v"]
            ewait("gpsimd", "v", v_vto)
            ewait("gpsimd", "s", s_qkv_done)
            edma("gpsimd", "dk", d_kv_in.ap()[0:128, :], QKVB[:, 16 * 1024:17 * 1024])
            edma("gpsimd", "dk", d_kv_in.ap()[128:256, :], VTO[:])
            ewait("gpsimd", "dk")
            cnt["cc"] += 1
            on("gpsimd", lambda e: e.collective_compute(
                "AllGather", ALU.bypass,
                replica_groups=[[0, 1, 2, 3], [4, 5, 6, 7]],
                ins=[d_kv_in.ap().opt()],
                outs=[d_kv_out.ap().opt()]).then_inc(sems["cc"], 1))
            ewait("gpsimd", "cc")
            kvo = d_kv_out.ap().rearrange("(r b p) t -> r b p t", b=2, p=128)
            edma("gpsimd", "dk",
                 KT[:].rearrange("p (s r q) -> p s r q", r=4, q=128),
                 kvo[:, 0, :, :].rearrange("r p (s q) -> p s r q", q=128))
            edma("gpsimd", "dk",
                 VT[:].rearrange("p (s r q) -> p s r q", r=4, q=128),
                 kvo[:, 1, :, :].rearrange("r p (s q) -> p s r q", q=128))
            dk_kv = cnt["dk"]

            v_qsq_prev = 0
            for dc in range(16):
                ewait("vector", "t", cnt["t"])
                vrun(lambda e, dc=dc: e.tensor_tensor(
                    out=QSQ[:], in0=QKVB[:, dc * 1024:(dc + 1) * 1024],
                    in1=QKVB[:, dc * 1024:(dc + 1) * 1024], op=ALU.mult))
                ewait("tensor", "v", cnt["v"])
                run("tensor", lambda e, dc=dc: e.matmul(
                    PSD[0:2, 0:512], ONEB2[:], QSQ[:, 0:512],
                    start=(dc == 0), stop=(dc == 15)))
                run("tensor", lambda e, dc=dc: e.matmul(
                    PSD[32:34, 0:512], ONEB2[:], QSQ[:, 512:1024],
                    start=(dc == 0), stop=(dc == 15)), inc="t")
            t_qn = cnt["t"]
            ewait("vector", "t", t_qn)
            def _qmax(e):
                e.tensor_reduce(out=KQM[0:1, 0:1], in_=PSD[0:1, 0:512], axis=AX.X,
                                op=ALU.max)
                return e.tensor_reduce(out=KQM[0:1, 1:2], in_=PSD[32:33, 0:512],
                                       axis=AX.X, op=ALU.max)
            run("vector", _qmax, inc="v")
            vrun(lambda e: e.tensor_tensor(
                out=KQM[0:1, 0:1], in0=KQM[0:1, 0:1], in1=KQM[0:1, 1:2],
                op=ALU.max))
            ewait("vector", "dk", dk_kv)
            for kc in range(4):
                vrun(lambda e, kc=kc: e.tensor_tensor(
                    out=QSQ[:], in0=KT[:, kc * 1024:(kc + 1) * 1024],
                    in1=KT[:, kc * 1024:(kc + 1) * 1024], op=ALU.mult))
                ewait("tensor", "v", cnt["v"])
                if kc > 0:
                    ewait("tensor", "v", v_kn)
                run("tensor", lambda e: e.matmul(
                    PSD[0:2, 0:512], ONEB2[:], QSQ[:, 0:512], start=True,
                    stop=True))
                run("tensor", lambda e: e.matmul(
                    PSD[32:34, 0:512], ONEB2[:], QSQ[:, 512:1024], start=True,
                    stop=True), inc="t")
                ewait("vector", "t", cnt["t"])
                def _kmaxA(e):
                    e.tensor_reduce(out=KQM[0:1, 2:3], in_=PSD[0:1, 0:512],
                                    axis=AX.X, op=ALU.max)
                    return e.tensor_reduce(out=KQM[0:1, 3:4], in_=PSD[32:33, 0:512],
                                           axis=AX.X, op=ALU.max)
                run("vector", _kmaxA, inc="v")
                vsync()
                vrun(lambda e: e.tensor_tensor(out=KQM[0:1, 2:3],
                                               in0=KQM[0:1, 2:3],
                                               in1=KQM[0:1, 3:4], op=ALU.max))
                if kc == 0:
                    vrun(lambda e: e.tensor_copy(out=KQM[0:1, 4:5],
                                                 in_=KQM[0:1, 2:3]))
                else:
                    vrun(lambda e: e.tensor_tensor(out=KQM[0:1, 4:5],
                                                   in0=KQM[0:1, 4:5],
                                                   in1=KQM[0:1, 2:3],
                                                   op=ALU.max))
                v_kn = cnt["v"]
            run("vector", lambda e: e.tensor_tensor(
                out=KQM[0:1, 5:6], in0=KQM[0:1, 0:1], in1=KQM[0:1, 4:5],
                op=ALU.mult), inc="v")
            ewait("scalar", "v", cnt["v"])
            run("scalar", lambda e: e.activation(out=KQM[0:1, 5:6],
                                                 in_=KQM[0:1, 5:6], func=AF.Sqrt),
                inc="s")
            ewait("vector", "s", cnt["s"])
            vrun(lambda e: e.tensor_scalar(
                out=KQM[0:1, 6:7], in0=KQM[0:1, 5:6], scalar1=-SCALE,
                scalar2=EXPSHIFT, op0=ALU.mult, op1=ALU.add))
            ewait("tensor", "v", cnt["v"])
            run("tensor", lambda e: e.matmul(
                PSD[:, 384:385], ONE1[0:1, :], KQM[0:1, 6:7], start=True,
                stop=True), inc="t")
            ewait("vector", "t", cnt["t"])
            run("vector", lambda e: e.tensor_copy(out=BB[:], in_=PSD[:, 384:385]),
                inc="v")
            v_bb = cnt["v"]

            def emit_pv(kt, EJ):
                buf = kt % 2
                kind, val = cons_done[buf]
                ewait("tensor", kind, val)
                if kt == 0 and v_aob:
                    ewait("tensor", "v", v_aob)
                for q4 in range(2):
                    run("tensor", lambda e, q4=q4, kt=kt, buf=buf, EJ=EJ:
                        e.matmul(
                            PSB[:, q4 * 512:(q4 + 1) * 512],
                            VT[:, kt * 128:(kt + 1) * 128],
                            PT[buf][:, q4 * 512:(q4 + 1) * 512],
                            start=(kt == 0), stop=(kt == EJ - 1)))
                run("tensor", lambda e, kt=kt, buf=buf, EJ=EJ: e.matmul(
                    PSD[0:2, 0:512], ONEB2[:], PT[buf][:, 0:512],
                    start=(kt == 0), stop=(kt == EJ - 1)))
                run("tensor", lambda e, kt=kt, buf=buf, EJ=EJ: e.matmul(
                    PSD[32:34, 0:512], ONEB2[:], PT[buf][:, 512:1024],
                    start=(kt == 0), stop=(kt == EJ - 1)), inc="t")
                pv_done[kt % 2] = cnt["t"]

            ewait("tensor", "dk", dk_kv)
            ewait("tensor", "v", v_bb)
            ewait("scalar", "v", v_bb)
            SSTRIP = [PSA, PSC]
            exp_done = {}
            cons_done = {}
            pv_done = {}
            v_aob = 0
            v_rb = 0
            for j in range(NJ):
                EJ = 4 * (j + 1)
                for hf in range(2):
                    pending = None
                    for kt in range(EJ):
                        buf = kt % 2
                        strip = SSTRIP[buf]
                        prev = exp_done.get(buf)
                        if prev:
                            ewait("tensor", "s", prev)
                        if v_rb:
                            ewait("tensor", "v", v_rb)
                            v_rb = 0
                        for q4 in range(2):
                            h0 = hf * 8 + q4 * 4
                            rhs = QKVB[:].rearrange(
                                "p (c t) -> p c t", t=1024)[
                                :, h0:h0 + 4, j * 128:(j + 1) * 128]
                            run("tensor", lambda e, q4=q4, kt=kt, strip=strip,
                                rhs=rhs: e.matmul(
                                    strip[:, q4 * 512:(q4 + 1) * 512],
                                    KT[:, kt * 128:(kt + 1) * 128], rhs,
                                    start=True, stop=True),
                                inc="t" if q4 == 1 else None)
                        t_sc = cnt["t"]
                        ewait("scalar", "t", t_sc)
                        prevpv = pv_done.get(buf)
                        if prevpv:
                            ewait("scalar", "t", prevpv)
                        run("scalar", lambda e, buf=buf, strip=strip: e.activation(
                            out=PT[buf][:], in_=strip[:], func=AF.Exp,
                            bias=BB[:], scale=SCALE), inc="s")
                        exp_done[buf] = cnt["s"]
                        if kt >= 4 * j:
                            ewait("vector", "s", exp_done[buf])
                            run("vector", lambda e, buf=buf, kt=kt: e.tensor_tensor(
                                out=PT[buf][:].rearrange("p (h q) -> p h q", q=128),
                                in0=PT[buf][:].rearrange("p (h q) -> p h q", q=128),
                                in1=MSK[:, kt * 128:(kt + 1) * 128].unsqueeze(1)
                                    .broadcast_to([128, 8, 128]),
                                op=ALU.mult))
                            cons_done[buf] = ("v", cnt["v"])
                        else:
                            cons_done[buf] = ("s", exp_done[buf])
                        if pending is not None:
                            emit_pv(pending, EJ)
                        pending = kt
                    emit_pv(pending, EJ)
                    pending = None
                    t_last = cnt["t"]
                    ewait("vector", "t", t_last)
                    vrun(lambda e: e.tensor_copy(out=SREC[0], in_=PSD[0:1, 0:512]))
                    vrun(lambda e: e.tensor_copy(out=SREC[1],
                                                 in_=PSD[32:33, 0:512]))
                    vrun(lambda e: e.reciprocal(out=SREC[0], in_=SREC[0]))
                    vrun(lambda e: e.reciprocal(out=SREC[1], in_=SREC[1]))
                    v_recd = cnt["v"]
                    ewait("tensor", "v", v_recd)
                    for q in range(2):
                        qp = 0 if q == 0 else 32
                        run("tensor", lambda e, q=q, qp=qp: e.matmul(
                            PSA[:, q * 512:(q + 1) * 512], ONE1[qp:qp + 1, :],
                            SREC[q], start=True, stop=True),
                            inc="t" if q == 1 else None)
                    t_rb = cnt["t"]
                    ewait("vector", "t", t_rb)
                    run("vector", lambda e: e.tensor_copy(out=RB[:],
                                                          in_=PSA[:, 0:1024]))
                    v_rb = cnt["v"]
                    run("vector", lambda e, j=j, hf=hf: e.tensor_tensor(
                        out=AOB[:, j * 2048 + hf * 1024: j * 2048 + (hf + 1) * 1024],
                        in0=PSB[:], in1=RB[:], op=ALU.mult))
                    v_aob = cnt["v"]
                    exp_done = {}
                    pv_done = {}
                    cons_done = {}

            dwp = {}
            def wp_load(wq):
                s_ = "dwa" if wq % 2 == 0 else "dwb"
                edma("gpsimd", s_, WPB[wq % 2][:].rearrange(
                    "p (dc c) -> p dc c", c=512),
                    d_wproj.rearrange("(dc p) c -> p dc c", p=128)[:, :, wq * 512:(wq + 1) * 512])
                dwp[wq] = (s_, cnt[s_])
            ewait("gpsimd", "v", v_aob)
            wp_load(0)
            wp_load(1)
            v_pe = [0, 0]
            t_wq_last = 0
            for wq in range(4):
                if wq >= 1 and wq + 1 < 4 and (wq + 1) not in dwp:
                    pass
                ewait("tensor", dwp[wq][0], dwp[wq][1])
                ewait("tensor", "v", v_aob)
                for j in range(NJ):
                    half = j % 2
                    if v_pe[half]:
                        ewait("tensor", "v", v_pe[half])
                    for dc in range(16):
                        run("tensor", lambda e, wq=wq, j=j, dc=dc, half=half:
                            e.matmul(
                                PSA[:, half * 512:(half + 1) * 512],
                                AOB[:, j * 2048 + dc * 128: j * 2048 + (dc + 1) * 128],
                                WPB[wq % 2][:, dc * 512:(dc + 1) * 512],
                                start=(dc == 0), stop=False))
                    run("tensor", lambda e, wq=wq, half=half: e.matmul(
                        PSA[:, half * 512:(half + 1) * 512], ONEBR[:],
                        BPR[0:1, wq * 512:(wq + 1) * 512],
                        start=False, stop=True), inc="t")
                    t_pp = cnt["t"]
                    ewait("vector", "t", t_pp)
                    run("vector", lambda e, wq=wq, j=j, half=half: e.tensor_tensor(
                        out=X[:, j * D + wq * 512: j * D + (wq + 1) * 512],
                        in0=X[:, j * D + wq * 512: j * D + (wq + 1) * 512],
                        in1=PSA[:, half * 512:(half + 1) * 512], op=ALU.add),
                        inc="v")
                    v_pe[half] = cnt["v"]
                t_wq_last = cnt["t"]
                if wq + 2 < 4:
                    ewait("gpsimd", "t", t_wq_last)
                    wp_load(wq + 2)
            ewait("vector", "v", cnt["v"])
            ewait("scalar", "v", cnt["v"])
            ewait("tensor", "v", cnt["v"])
            ewait("gpsimd", "v", cnt["v"])

        ewait("vector", "v", max(v_gb, v_ss))
        _pre_dw = {}
        for _cc in (0, 1):
            _s = "dwa" if _cc % 2 == 0 else "dwb"
            edma("gpsimd", _s, WBUF[_cc % 2][:].rearrange(
                "p (dc c) -> p dc c", c=128),
                d_wqkv.rearrange("(dc p) c -> p dc c", p=128)[
                    :, :, _cc * 128:(_cc + 1) * 128])
            _pre_dw[_cc] = (_s, cnt[_s])
        for _cc in (0, 1):
            _pre_dw_map[(0, _cc)] = _pre_dw[_cc]
        ffn(0)
        if stop >= 2:
            attention()
        if stop >= 3:
            ffn(1)

        ewait("sync", "v", cnt["v"])
        ewait("sync", "g", cnt["g"])
        ewait("sync", "s", cnt["s"])
        ewait("sync", "t", cnt["t"])
        edma("sync", "so", d_y.rearrange("(j p) d -> p j d", p=128),
             X[:].rearrange("p (j d) -> p j d", d=D))
        if _dbg():
            edma("sync", "so", dbg["dSTAT"], STAT[:])
            edma("sync", "so", dbg["dAFC"], AFC[:])
            edma("sync", "so", dbg["dOHALL"], OHALL[:])
            edma("sync", "so", dbg["dXS"], XS[:])
            edma("sync", "so", dbg["dMNXS"], MNXS[:])
            edma("sync", "so", dbg["dARB"], ARB[:, 0:2])
            edma("sync", "so", dbg["dTKN"], TKN[:])
            edma("gpsimd", "so", dbg["dSS"], SS128[:])
            edma("sync", "so", dbg["dGAM"], GAM[:])
            edma("sync", "so", dbg["dBET"], BET[:])
            edma("sync", "so", dbg["dDBG1"], DBG1[:])
            edma("sync", "so", dbg["dSQS"], SQS[:])
        ewait("sync", "so")

        block = est.enter_context(nc.Block())

        @block.sync
        def _(e):
            for f in ops["sync"]:
                f(e)

        @block.vector
        def _(e):
            for f in ops["vector"]:
                f(e)

        @block.scalar
        def _(e):
            for f in ops["scalar"]:
                f(e)

        @block.tensor
        def _(e):
            for f in ops["tensor"]:
                f(e)

        @block.gpsimd
        def _(e):
            for f in ops["gpsimd"]:
                f(e)

    return nc



def _host_constf():
    cf = np.zeros((128, 256), np.float32)
    cf[:, 0:64] = np.arange(64, dtype=np.float32)[None, :]
    cf[:, 64:128] = (np.arange(64, dtype=np.float32) // 8)[None, :]
    cf[:, 128:256] = np.eye(128, dtype=np.float32)
    return cf


def _host_masks(cid):
    m = np.zeros((NKT, QT, QT), np.float32)
    for kt in range(NKT):
        g = 4 * (kt // 4) + cid
        kpos = kt * QT + np.arange(QT)[:, None]
        qpos = g * QT + np.arange(QT)[None, :]
        m[kt] = (kpos <= qpos).astype(np.float32)
    return m.reshape(NKT * QT, QT).astype(ml_dtypes.bfloat16)


def kernel(L0_attn_x, L0_attn_W_qkv, L0_attn_b_qkv, L0_attn_W_proj,
           L0_attn_b_proj, L0_ffn_ln_gamma, L0_ffn_ln_beta,
           L0_ffn_spline_bases, L0_ffn_spline_slopes, L0_ffn_output_scale):
    x = np.ascontiguousarray(np.asarray(L0_attn_x, dtype=np.float32))
    wqkv = np.ascontiguousarray(np.asarray(L0_attn_W_qkv, dtype=np.float32))
    bqkv = np.ascontiguousarray(np.asarray(L0_attn_b_qkv, dtype=np.float32))
    wproj = np.ascontiguousarray(np.asarray(L0_attn_W_proj, dtype=np.float32))
    bproj = np.ascontiguousarray(np.asarray(L0_attn_b_proj, dtype=np.float32)).reshape(1, D)
    gamma = np.ascontiguousarray(np.asarray(L0_ffn_ln_gamma, dtype=np.float32)).reshape(1, D)
    beta = np.ascontiguousarray(np.asarray(L0_ffn_ln_beta, dtype=np.float32)).reshape(1, D)
    bases = np.ascontiguousarray(np.asarray(L0_ffn_spline_bases, dtype=np.float32)).reshape(NT, DS * GRID)
    slopes = np.ascontiguousarray(np.asarray(L0_ffn_spline_slopes, dtype=np.float32)).reshape(NT, DS * GRID)
    oscale = np.ascontiguousarray(np.asarray(L0_ffn_output_scale, dtype=np.float32)).reshape(NT, 1)

    cf = _host_constf()
    idn = np.eye(128, dtype=ml_dtypes.bfloat16)

    nc = build_nc(stop=_BUILD_STOP)
    in_maps = []
    for core in range(8):
        batch, cid = core // 4, core % 4
        xs = np.concatenate(
            [x[batch, (4 * j + cid) * QT:(4 * j + cid + 1) * QT, :]
             for j in range(NJ)], axis=0)
        in_maps.append({
            "x": np.ascontiguousarray(xs),
            "wqkv": wqkv, "bqkv": bqkv, "wproj": wproj, "bproj": bproj,
            "gamma": gamma, "beta": beta, "bases": bases, "slopes": slopes,
            "oscale": oscale, "msk": _host_masks(cid), "constf": cf,
            "idnb": idn,
        })
    res = run_bass_kernel_spmd(nc, in_maps, core_ids=list(range(8)))
    out = np.empty((B, S, D), np.float32)
    for core in range(8):
        batch, cid = core // 4, core % 4
        y = res.results[core]["y"]
        for j in range(NJ):
            g = 4 * j + cid
            out[batch, g * QT:(g + 1) * QT, :] = y[j * QT:(j + 1) * QT, :]
    return out


_BUILD_STOP = 3


# version 09
# speedup vs baseline: 1.3809x; 1.0111x over previous
import numpy as np
import ml_dtypes
from contextlib import ExitStack

import concourse.bass as bass
import concourse.mybir as mybir
from concourse.bass_utils import run_bass_kernel_spmd

_DEBUG_DUMP = False

F32 = mybir.dt.float32
BF16 = mybir.dt.bfloat16
U8 = mybir.dt.uint8
AF = mybir.ActivationFunctionType
ALU = mybir.AluOpType
AX = mybir.AxisListType

B, S, D = 2, 4096, 2048
NJ, QT = 8, 128
T = NJ * QT
NH, HD = 16, 128
SCALE = float(HD) ** -0.5
NT, DS, GRID = 64, 32, 16
LNEPS = 1e-5
BIG = 1.0e9
FBIG = 1.0e30
EXPSHIFT = 20.0
NKT = 32
CCQ = 18


def _dbg():
    return _DEBUG_DUMP


def build_nc(stop=3):
    nc = bass.Bass("TRN2", target_bir_lowering=False, debug=False,
                   detect_race_conditions=True)

    d_x = nc.dram_tensor("x", [T, D], F32, kind="ExternalInput").ap()
    d_wqkv = nc.dram_tensor("wqkv", [D, 2304], F32, kind="ExternalInput").ap()
    d_bqkv = nc.dram_tensor("bqkv", [2304], F32, kind="ExternalInput").ap()
    d_wproj = nc.dram_tensor("wproj", [D, D], F32, kind="ExternalInput").ap()
    d_bproj = nc.dram_tensor("bproj", [1, D], F32, kind="ExternalInput").ap()
    d_gamma = nc.dram_tensor("gamma", [1, D], F32, kind="ExternalInput").ap()
    d_beta = nc.dram_tensor("beta", [1, D], F32, kind="ExternalInput").ap()
    d_bases = nc.dram_tensor("bases", [NT, DS * GRID], F32, kind="ExternalInput").ap()
    d_slopes = nc.dram_tensor("slopes", [NT, DS * GRID], F32, kind="ExternalInput").ap()
    d_oscale = nc.dram_tensor("oscale", [NT, 1], F32, kind="ExternalInput").ap()
    d_msk = nc.dram_tensor("msk", [NKT * QT, QT], BF16, kind="ExternalInput").ap()
    d_cf = nc.dram_tensor("constf", [128, 256], F32, kind="ExternalInput").ap()
    d_idnb = nc.dram_tensor("idnb", [128, 128], BF16, kind="ExternalInput").ap()
    d_y = nc.dram_tensor("y", [T, D], F32, kind="ExternalOutput").ap()
    dbg = {}
    if _dbg():
        for nm, shp in [("dSTAT", [128, 16]), ("dAFC", [128, 48]),
                        ("dOHALL", [128, 512]), ("dXS", [128, 256]),
                        ("dMNXS", [128, 32]), ("dARB", [128, 2]),
                        ("dTKN", [128, 16]), ("dSS", [128, 2048]),
                        ("dGAM", [128, 2048]), ("dBET", [128, 2048]),
                        ("dDBG1", [128, 16]), ("dDBG2", [128, 1024]), ("dSQS", [128, 8]),
                        ("dDBG3", [128, 1024])]:
            dbg[nm] = nc.dram_tensor(nm, shp, F32, kind="ExternalOutput").ap()

    d_ssrow = nc.dram_tensor("ssrow", [NT, DS], F32)
    d_ar_in = [nc.dram_tensor(f"ar_in{i}", [NT, 2], F32) for i in range(2)]
    d_ar_out = [nc.dram_tensor(f"ar_out{i}", [NT, 2], F32) for i in range(2)]
    d_kv_in = nc.dram_tensor("kv_in", [2 * 128, T], BF16)
    d_kv_out = nc.dram_tensor("kv_out", [4 * 2 * 128, T], BF16)

    est = ExitStack()
    with est:
        def sbt(name, shape, dt):
            return est.enter_context(nc.sbuf_tensor(name, shape, dt))

        X = sbt("X", [128, NJ * D], F32)
        MSK = sbt("MSK", [128, NKT * QT], BF16)
        SS128 = sbt("SS128", [128, D], BF16)
        GAM = sbt("GAM", [128, D], F32)
        BET = sbt("BET", [128, D], F32)
        CF = sbt("CF", [128, 256], F32)
        IDNB = sbt("IDNB", [128, 128], BF16)
        SL64 = sbt("SL64", [128, DS * GRID], F32)
        TBP = sbt("TBP", [128, DS * GRID], F32)
        MISC = sbt("MISC", [128, 4], F32)
        BQ = sbt("BQ", [128, CCQ], F32)
        BPR = sbt("BPR", [1, D], BF16)
        ONE1 = sbt("ONE1", [33, 128], F32)
        ONEB2 = sbt("ONEB2", [128, 2], BF16)
        ONEBR = sbt("ONEBR", [1, 128], BF16)
        BIGT = sbt("BIGT", [128, 64], F32)
        NBIGT = sbt("NBIGT", [128, 64], F32)
        R1 = sbt("R1", [128, 16384], BF16)
        R2 = sbt("R2", [128, 18432], BF16)
        KT = sbt("KT", [128, NKT * QT], BF16)
        VT = sbt("VT", [128, NKT * QT], BF16)
        PT = [sbt(f"PT{i}", [128, 1024], BF16) for i in range(2)]
        QSQ = sbt("QSQ", [128, 1024], BF16)
        VTRB = sbt("VTRB", [128, 1024], F32)
        TB2 = sbt("TB2", [128, DS * GRID], F32)
        TS2 = sbt("TS2", [128, DS * GRID], F32)
        OHALL = sbt("OHALL", [128, NJ * NT], F32)
        XS = sbt("XS", [128, NJ * DS], F32)
        MNXS = sbt("MNXS", [128, NJ * 4], F32)
        STAT = sbt("STAT", [128, 16], F32)
        SQS = sbt("SQS", [128, 8], F32)
        AFC = sbt("AFC", [128, 48], F32)
        TKN = sbt("TKN", [128, 16], F32)
        TSC = sbt("TSC", [128, 64], F32)
        TSCALL = sbt("TSCALL", [128, 512], F32)
        MMB2 = None
        MM1 = sbt("MM1", [128, 64], F32)
        MMIN = sbt("MMIN", [128, 64], F32)
        MMAX = sbt("MMAX", [128, 64], F32)
        EQM = sbt("EQM", [128, 512], U8)

        SC1 = sbt("SC1", [128, 1], F32)
        SC2 = sbt("SC2", [128, 1], F32)
        ARB = sbt("ARB", [128, 2], F32)
        OSC = sbt("OSC", [128, 1], F32)
        BB = sbt("BB", [128, 1], F32)
        KQM = sbt("KQM", [1, 8], F32)

        WBUF = [KT[:, 0:2048], KT[:, 2048:4096]]
        XNv = [R1[:, 0:4096].bitcast(F32), R1[:, 4096:8192].bitcast(F32)]
        TBALL = R1[:, 8192:16384].bitcast(F32)
        SCTB = R2[:, 8192:12288].bitcast(F32)
        TSALL = R2[:, 4096:8192]
        SCT = R2[:, 0:4096].bitcast(F32)
        SGN = [R2[:, 8192 + i * 4096: 8192 + i * 4096 + 2048] for i in range(2)]
        QQB = [R2[:, 10240 + i * 4096: 10240 + i * 4096 + 2048] for i in range(2)]
        GROW = R2[:, 0:4096].bitcast(F32)[0:1, :]
        BROW = R2[:, 4096:8192].bitcast(F32)[0:1, :]
        XN3B = [R1[:, 0:8192], R1[:, 8192:16384]]
        X3Fh = R1[:, 8192:16384]
        AOB = R1
        QKVB = R2
        WPB = [R2[:, 0:8192], R2[:, 8192:16384]]
        OHF = VTRB
        RB = VTRB
        VTO = VTRB[:, 0:512].bitcast(BF16)
        QSQF = QSQ.bitcast(F32)
        G64 = QSQF
        SREC = [QSQF[0:1, 0:512], QSQF[32:33, 0:512]]
        MMB2 = SCT[:, 512:640]
        MG256 = EQM[:, 0:256]

        T64A = SCT[:, 1024:1152]
        DBG1 = SCT[:, 1152:1168]
        PSBKF = None

        PSA = est.enter_context(nc.psum_tensor("PSA", [128, 1024], F32))
        PSB = est.enter_context(nc.psum_tensor("PSB", [128, 1024], F32))
        PSC = est.enter_context(nc.psum_tensor("PSC", [128, 1024], F32))
        PSD = est.enter_context(nc.psum_tensor("PSD", [128, 512], F32))
        PSBK = est.enter_context(nc.psum_tensor("PSBK", [128, 1024], BF16))
        PSBKF = PSBK.ap().bitcast(F32)

        sems = {}
        for s_ in ("di", "dw", "dwa", "dwb", "dk", "cc", "t", "v", "s", "g",
                   "so"):
            sems[s_] = est.enter_context(nc.semaphore(s_))
        cnt = {k: 0 for k in sems}
        ops = {e: [] for e in ("sync", "vector", "scalar", "tensor", "gpsimd")}

        def on(eng, fn):
            ops[eng].append(fn)

        def ewait(eng, s_, val=None):
            v = cnt[s_] if val is None else val
            if v <= 0:
                return
            on(eng, lambda e, s_=s_, v=v: e.wait_ge(sems[s_], v))

        def edma(eng, s_, out, in_, slow=False):
            cnt[s_] += 16
            if slow:
                on(eng, lambda e, out=out, in_=in_, s_=s_:
                   e.dma_start(out=out, in_=in_, allow_slow_non_contiguous=True)
                   .then_inc(sems[s_], 16))
            else:
                on(eng, lambda e, out=out, in_=in_, s_=s_:
                   e.dma_start(out=out, in_=in_).then_inc(sems[s_], 16))

        def vsync():
            ewait("vector", "v", cnt["v"])

        def vrun(fn):
            run("vector", fn, inc="v")

        def run(eng, fn, inc=None):
            if eng == "vector":
                ewait("vector", "v", cnt["v"])
                if inc is None:
                    inc = "v"
            elif eng == "scalar":
                ewait("scalar", "s", cnt["s"])
                if inc is None:
                    inc = "s"
            if inc is not None:
                cnt[inc] += 1
            def _f(e, fn=fn, inc=inc):
                r = fn(e)
                if inc is not None:
                    r.then_inc(sems[inc], 1)
            on(eng, _f)

        IOTA64 = CF[:, 0:64]
        IOTAD8 = CF[:, 64:128]
        IDNF = CF[:, 128:256]

        edma("sync", "di", X[:].rearrange("p (j d) -> p j d", d=D),
             d_x.rearrange("(j p) d -> p j d", p=128))
        edma("sync", "di", MSK[:].rearrange("p (n q) -> p n q", q=QT),
             d_msk.rearrange("(n p) q -> p n q", p=128))
        edma("sync", "di", CF[:], d_cf)
        edma("sync", "di", IDNB[:], d_idnb)
        edma("sync", "di", SL64[0:NT, :], d_slopes)
        edma("sync", "di", TBP[0:NT, :], d_bases)
        edma("sync", "di", OSC[0:NT, :], d_oscale)
        edma("sync", "di", BQ[:], d_bqkv.rearrange("(cc p) -> p cc", p=128),
             slow=True)
        ld_all = cnt["di"]
        edma("gpsimd", "dw", GROW, d_gamma)
        edma("gpsimd", "dw", BROW, d_beta)
        edma("gpsimd", "dw", BPR[:], d_bproj)
        g_rows = cnt["dw"]

        def _init(e):
            e.memset(ONE1[:], 1.0)
            e.memset(ONEB2[:], 1.0)
            e.memset(ONEBR[:], 1.0)
            e.memset(BIGT[:], BIG)
            e.memset(NBIGT[:], -BIG)
            return e.memset(MISC[:], 0.0)
        run("gpsimd", _init, inc="g")
        g_init = cnt["g"]

        ewait("tensor", "dw", g_rows)
        ewait("tensor", "g", g_init)
        for part in range(2):
            row = GROW if part == 0 else BROW
            dst = GAM if part == 0 else BET
            for h in range(2):
                ewait("tensor", "v", cnt["v"])
                for q in range(2):
                    run("tensor", lambda e, row=row, h=h, q=q: e.matmul(
                        PSA[:, q * 512:(q + 1) * 512], ONE1[0:1, :],
                        row[0:1, h * 1024 + q * 512: h * 1024 + (q + 1) * 512],
                        start=True, stop=True), inc="t" if q == 1 else None)
                ewait("vector", "t", cnt["t"])
                run("vector", lambda e, dst=dst, h=h: e.tensor_copy(
                    out=dst[:, h * 1024:(h + 1) * 1024], in_=PSA[:, 0:1024]),
                    inc="v")
        v_gb = cnt["v"]

        ewait("vector", "di", ld_all)
        if _dbg():
            run("vector", lambda e: e.tensor_copy(
                out=VTRB[:], in_=X[:, 7 * 2048: 7 * 2048 + 1024]), inc="v")
            ewait("sync", "v", cnt["v"])
            edma("sync", "so", dbg["dDBG2"], VTRB[:])
        run("vector", lambda e: e.tensor_reduce(
            out=TSC[0:NT, 0:DS],
            in_=SL64[0:NT, :].rearrange("p (j g) -> p j g", g=GRID),
            axis=AX.X, op=ALU.add), inc="v")
        ewait("scalar", "v", cnt["v"])
        run("scalar", lambda e: e.activation(
            out=TSC[0:NT, 0:DS], in_=TSC[0:NT, 0:DS], func=AF.Sign), inc="s")
        s_ss = cnt["s"]
        ewait("gpsimd", "g", g_init)
        run("gpsimd", lambda e: e.iota(
            G64[0:NT, :], [[0, DS], [1, GRID]], channel_multiplier=0,
            allow_small_or_imprecise_dtypes=True), inc="g")
        g_g64 = cnt["g"]
        ewait("vector", "g", g_g64)
        vrun(lambda e: e.tensor_tensor(
            out=G64[0:NT, :], in0=SL64[0:NT, :], in1=G64[0:NT, :], op=ALU.mult))
        vrun(lambda e: e.tensor_tensor(
            out=TBP[0:NT, :], in0=TBP[0:NT, :], in1=G64[0:NT, :],
            op=ALU.subtract))
        v_tbp = cnt["v"]
        ewait("gpsimd", "s", s_ss)
        edma("gpsimd", "dk", d_ssrow.ap(), TSC[0:NT, 0:DS])
        ewait("gpsimd", "dk")
        edma("gpsimd", "dk", VTRB[0:1, 0:1024],
             d_ssrow.ap().rearrange("a b -> (a b)")[None, 0:1024])
        edma("gpsimd", "dk", VTRB[32:33, 0:1024],
             d_ssrow.ap().rearrange("a b -> (a b)")[None, 1024:2048])
        g_ssrow = cnt["dk"]
        ewait("tensor", "dk", g_ssrow)
        for h in range(2):
            hp = 0 if h == 0 else 32
            ewait("tensor", "v", cnt["v"])
            for q in range(2):
                run("tensor", lambda e, hp=hp, q=q: e.matmul(
                    PSB[:, q * 512:(q + 1) * 512], ONE1[hp:hp + 1, :],
                    VTRB[hp:hp + 1, q * 512:(q + 1) * 512],
                    start=True, stop=True), inc="t" if q == 1 else None)
            ewait("vector", "t", cnt["t"])
            run("vector", lambda e, h=h: e.tensor_copy(
                out=SS128[:, h * 1024:(h + 1) * 1024], in_=PSB[:, 0:1024]),
                inc="v")
        v_ss = cnt["v"]

        def ffn(ar_idx, scatter_hook=None):
            ewait("vector", "t", cnt["t"])
            ewait("scalar", "t", cnt["t"])
            ewait("vector", "v", cnt["v"])
            ewait("scalar", "v", cnt["v"])
            ewait("scalar", "g", cnt["g"])
            ewait("vector", "g", cnt["g"])
            for j in range(NJ):
                Xj = X[:, j * D:(j + 1) * D]
                run("scalar", lambda e, Xj=Xj, j=j: e.activation(
                    out=SGN[j % 2], in_=Xj, func=AF.Square,
                    accum_out=SQS[:, j:j + 1]), inc="s")
                vrun(lambda e, Xj=Xj, j=j: e.reduce_sum(
                    out=STAT[:, j:j + 1], in_=Xj, axis=AX.X))
            MU = AFC[:, 0:8]; VAR = AFC[:, 8:16]; RS1 = AFC[:, 16:24]
            RS2 = AFC[:, 24:32]; AFj = AFC[:, 32:40]; CFj = AFC[:, 40:48]
            AY = STAT[:, 0:8]; CY = STAT[:, 8:16]
            ewait("vector", "s", cnt["s"])
            vrun(lambda e: e.tensor_scalar(out=MU, in0=STAT[:, 0:8],
                                           scalar1=1.0 / D, scalar2=None,
                                           op0=ALU.mult))
            vrun(lambda e: e.tensor_scalar(out=VAR, in0=SQS[:], scalar1=1.0 / D,
                                           scalar2=None, op0=ALU.mult))
            vrun(lambda e: e.tensor_tensor(out=TSC[:, 0:8], in0=MU, in1=MU,
                                           op=ALU.mult))
            vrun(lambda e: e.tensor_tensor(out=VAR, in0=VAR, in1=TSC[:, 0:8],
                                           op=ALU.subtract))
            vrun(lambda e: e.tensor_scalar(out=RS1, in0=VAR, scalar1=LNEPS,
                                           scalar2=None, op0=ALU.add))
            vrun(lambda e: e.reciprocal(out=RS1, in_=RS1))
            ewait("scalar", "v", cnt["v"])
            run("scalar", lambda e: e.activation(out=RS1, in_=RS1, func=AF.Sqrt),
                inc="s")
            ewait("vector", "s", cnt["s"])
            vrun(lambda e: e.tensor_tensor(out=RS2, in0=RS1, in1=RS1, op=ALU.mult))
            vrun(lambda e: e.tensor_tensor(out=RS2, in0=RS2, in1=VAR, op=ALU.mult))
            vrun(lambda e: e.tensor_scalar(out=RS2, in0=RS2, scalar1=0.25,
                                           scalar2=LNEPS, op0=ALU.mult,
                                           op1=ALU.add))
            vrun(lambda e: e.reciprocal(out=RS2, in_=RS2))
            ewait("scalar", "v", cnt["v"])
            run("scalar", lambda e: e.activation(out=RS2, in_=RS2, func=AF.Sqrt),
                inc="s")
            ewait("vector", "s", cnt["s"])
            vrun(lambda e: e.tensor_tensor(out=AFj, in0=RS1, in1=RS2, op=ALU.mult))
            vrun(lambda e: e.tensor_scalar(out=AFj, in0=AFj, scalar1=0.5,
                                           scalar2=None, op0=ALU.mult))
            vrun(lambda e: e.tensor_tensor(out=CFj, in0=MU, in1=AFj, op=ALU.mult))
            vrun(lambda e: e.tensor_scalar(out=CFj, in0=CFj, scalar1=-1.0,
                                           scalar2=None, op0=ALU.mult))
            vrun(lambda e: e.tensor_tensor(out=CY, in0=MU, in1=RS1, op=ALU.mult))
            vrun(lambda e: e.tensor_scalar(out=CY, in0=CY, scalar1=-0.5,
                                           scalar2=None, op0=ALU.mult))
            vrun(lambda e: e.tensor_scalar(out=AY, in0=RS1, scalar1=0.5,
                                           scalar2=1.0, op0=ALU.mult, op1=ALU.add))

            s_sgn = [0] * NJ
            v_gath = [0] * NJ
            ewait("scalar", "v", cnt["v"])
            g_gam = [0] * NJ
            for jp in range(0, NJ, 2):
                for j in (jp, jp + 1):
                    Xj = X[:, j * D:(j + 1) * D]
                    XN = XNv[j % 2]
                    if j >= 2:
                        ewait("scalar", "v", v_gath[j - 2])
                    run("scalar", lambda e, Xj=Xj, XN=XN, j=j: e.activation(
                        out=XN, in_=Xj, func=AF.Identity,
                        bias=CFj[:, j:j + 1], scale=AFj[:, j:j + 1]), inc="s")
                    ewait("vector", "s", cnt["s"])
                    vrun(lambda e, XN=XN: e.tensor_tensor(
                        out=XN, in0=XN, in1=GAM[:], op=ALU.mult))
                    vrun(lambda e, XN=XN: e.tensor_tensor(
                        out=XN, in0=XN, in1=BET[:], op=ALU.add))
                    v_xn = cnt["v"]
                    ewait("scalar", "v", v_xn)
                    run("scalar", lambda e, XN=XN, j=j: e.activation(
                        out=SGN[j % 2], in_=XN, func=AF.Sign), inc="s")
                    s_sgn[j] = cnt["s"]
                    ewait("vector", "s", s_sgn[j])
                    vrun(lambda e, j=j: e.tensor_tensor(
                        out=QQB[j % 2], in0=SGN[j % 2], in1=SS128[:],
                        op=ALU.mult))
                    vrun(lambda e, j=j: e.tensor_reduce(
                        out=TSCALL[:, j * 64:(j + 1) * 64],
                        in_=QQB[j % 2].rearrange("p (t k) -> p t k", k=DS),
                        axis=AX.X, op=ALU.add))
                TS2v = TSCALL[:, jp * 64:(jp + 2) * 64]
                CS64 = MMIN[:, 0:16]
                CM8 = MM1[:, 0:2]; BC8 = MM1[:, 2:4]
                TM8 = MM1[:, 4:6]; TI8 = MM1[:, 6:8]
                vrun(lambda e, TS2v=TS2v, CS64=CS64: e.tensor_reduce(
                    out=CS64, in_=TS2v.rearrange("p (a c t) -> p (a c) t", t=8,
                                                 c=8), axis=AX.X, op=ALU.add))
                vrun(lambda e, CS64=CS64, CM8=CM8: e.tensor_reduce(
                    out=CM8, in_=CS64.rearrange("p (a c) -> p a c", c=8),
                    axis=AX.X, op=ALU.max))
                vrun(lambda e, CS64=CS64, CM8=CM8: e.tensor_tensor(
                    out=EQM[:, 0:16].rearrange("p (a c) -> p a c", c=8),
                    in0=CS64.rearrange("p (a c) -> p a c", c=8),
                    in1=CM8.unsqueeze(2).broadcast_to([128, 2, 8]),
                    op=ALU.is_equal))
                for a in range(2):
                    vrun(lambda e, a=a: e.select(
                        out=MMAX[:, a * 8:(a + 1) * 8],
                        mask=EQM[:, a * 8:(a + 1) * 8],
                        on_true=IOTA64[:, 0:8], on_false=BIGT[:, 0:8],
                        add_drain=True))
                vrun(lambda e, BC8=BC8: e.tensor_reduce(
                    out=BC8, in_=MMAX[:, 0:16].rearrange("p (a c) -> p a c", c=8),
                    axis=AX.X, op=ALU.min))
                vrun(lambda e, BC8=BC8: e.tensor_tensor(
                    out=MMB2.rearrange("p (a t) -> p a t", t=64),
                    in0=IOTAD8[:].unsqueeze(1).broadcast_to([128, 2, 64]),
                    in1=BC8.unsqueeze(2).broadcast_to([128, 2, 64]),
                    op=ALU.is_equal))
                vrun(lambda e, TS2v=TS2v: e.tensor_scalar(
                    out=MMB2, in0=MMB2, scalar1=65536.0, scalar2=None,
                    op0=ALU.mult))
                vrun(lambda e, TS2v=TS2v: e.tensor_tensor(
                    out=MMB2, in0=MMB2, in1=TS2v, op=ALU.add))
                vrun(lambda e, TM8=TM8: e.tensor_reduce(
                    out=TM8, in_=MMB2.rearrange("p (a t) -> p a t", t=64),
                    axis=AX.X, op=ALU.max))
                vrun(lambda e, TM8=TM8: e.tensor_tensor(
                    out=EQM[:, 0:128].rearrange("p (a t) -> p a t", t=64),
                    in0=MMB2.rearrange("p (a t) -> p a t", t=64),
                    in1=TM8.unsqueeze(2).broadcast_to([128, 2, 64]),
                    op=ALU.is_equal))
                for a in range(2):
                    vrun(lambda e, a=a: e.select(
                        out=MMB2[:, a * 64:(a + 1) * 64],
                        mask=EQM[:, a * 64:(a + 1) * 64],
                        on_true=IOTA64[:], on_false=BIGT[:], add_drain=True))
                vrun(lambda e, TI8=TI8: e.tensor_reduce(
                    out=TI8, in_=MMB2.rearrange("p (a t) -> p a t", t=64),
                    axis=AX.X, op=ALU.min))
                vrun(lambda e, TI8=TI8, jp=jp: e.tensor_tensor(
                    out=OHALL[:, jp * 64:(jp + 2) * 64].rearrange(
                        "p (a t) -> p a t", t=64),
                    in0=IOTA64[:].unsqueeze(1).broadcast_to([128, 2, 64]),
                    in1=TI8.unsqueeze(2).broadcast_to([128, 2, 64]),
                    op=ALU.is_equal))
                v_oh = cnt["v"]
                for j in (jp, jp + 1):
                    XN = XNv[j % 2]
                    OHj = OHALL[:, j * 64:(j + 1) * 64]
                    ewait("vector", "g", cnt["g"])
                    vrun(lambda e, XN=XN, OHj=OHj: e.tensor_tensor(
                        out=SCT[:].rearrange("p (a b) -> p a b", b=DS),
                        in0=XN.rearrange("p (a b) -> p a b", b=DS),
                        in1=OHj.unsqueeze(2).broadcast_to([128, 64, DS]),
                        op=ALU.mult))
                    v_gath[j] = cnt["v"]
                    vrun(lambda e, j=j: e.tensor_reduce(
                        out=XS[:, j * DS:(j + 1) * DS],
                        in_=SCT[:].rearrange("p (t k) -> p k t", k=DS),
                        axis=AX.X, op=ALU.add))
                ewait("tensor", "v", v_oh)
                for j in (jp, jp + 1):
                    OHj = OHALL[:, j * 64:(j + 1) * 64]
                    if jp or j != jp:
                        ewait("tensor", "v", cnt["v"])
                    run("tensor", lambda e, OHj=OHj: e.transpose(
                        PSD[0:64, 128:256], OHj, IDNF), inc="t")
                    ewait("vector", "t", cnt["t"])
                    vrun(lambda e, j=j: e.tensor_copy(
                        out=OHF[0:64, j * 128:(j + 1) * 128],
                        in_=PSD[0:64, 128:256]))
            vrun(lambda e: e.tensor_reduce(
                out=TKN[:, 0:8], in_=XS[:].rearrange("p (j k) -> p j k", k=DS),
                axis=AX.X, op=ALU.min))
            vrun(lambda e: e.tensor_reduce(
                out=TKN[:, 8:16], in_=XS[:].rearrange("p (j k) -> p j k", k=DS),
                axis=AX.X, op=ALU.max))
            MMB = SCT[:, 0:512]
            vrun(lambda e: e.tensor_scalar(out=TSC[:, 32:40], in0=TKN[:, 0:8],
                                           scalar1=-1024.0, scalar2=None,
                                           op0=ALU.add))
            vrun(lambda e: e.tensor_scalar(out=TSC[:, 40:48], in0=TKN[:, 8:16],
                                           scalar1=1024.0, scalar2=None,
                                           op0=ALU.add))
            vrun(lambda e, MMB=MMB: e.tensor_tensor(
                out=MMB.rearrange("p (j t) -> p j t", t=64),
                in0=OHALL[:].rearrange("p (j t) -> p j t", t=64),
                in1=TSC[:, 32:40].unsqueeze(2).broadcast_to([128, NJ, 64]),
                op=ALU.mult))
            vrun(lambda e, MMB=MMB: e.tensor_scalar(
                out=MMB, in0=MMB, scalar1=1024.0, scalar2=None, op0=ALU.add))
            vrun(lambda e, MMB=MMB: e.tensor_reduce(
                out=MMIN[:], in_=MMB.rearrange("p (j t) -> p t j", t=64),
                axis=AX.X, op=ALU.min))
            vrun(lambda e, MMB=MMB: e.tensor_tensor(
                out=MMB.rearrange("p (j t) -> p j t", t=64),
                in0=OHALL[:].rearrange("p (j t) -> p j t", t=64),
                in1=TSC[:, 40:48].unsqueeze(2).broadcast_to([128, NJ, 64]),
                op=ALU.mult))
            vrun(lambda e, MMB=MMB: e.tensor_scalar(
                out=MMB, in0=MMB, scalar1=-1024.0, scalar2=None, op0=ALU.add))
            vrun(lambda e, MMB=MMB: e.tensor_reduce(
                out=MMAX[:], in_=MMB.rearrange("p (j t) -> p t j", t=64),
                axis=AX.X, op=ALU.max))
            v_loopA = cnt["v"]


            ewait("tensor", "v", v_loopA)
            def _mmT(e):
                e.transpose(PSD[0:64, 128:256], MMIN[:], IDNF)
                return e.transpose(PSD[0:64, 256:384], MMAX[:], IDNF)
            run("tensor", _mmT, inc="t")
            ewait("vector", "t", cnt["t"])
            vrun(lambda e: e.tensor_copy(out=T64A[0:64, :], in_=PSD[0:64, 128:256]))
            vrun(lambda e: e.tensor_reduce(out=ARB[0:64, 0:1], in_=T64A[0:64, :],
                                           axis=AX.X, op=ALU.min))
            vrun(lambda e: e.tensor_copy(out=T64A[0:64, :], in_=PSD[0:64, 256:384]))
            vrun(lambda e: e.tensor_reduce(out=ARB[0:64, 1:2], in_=T64A[0:64, :],
                                           axis=AX.X, op=ALU.max))
            vrun(lambda e: e.tensor_scalar(out=ARB[0:64, 0:1], in0=ARB[0:64, 0:1],
                                           scalar1=-1.0, scalar2=None,
                                           op0=ALU.mult))
            ewait("gpsimd", "v", cnt["v"])
            edma("gpsimd", "dk", d_ar_in[ar_idx].ap(), ARB[0:64, 0:2])
            ewait("gpsimd", "dk")
            cnt["cc"] += 1
            on("gpsimd", lambda e, ar_idx=ar_idx: e.collective_compute(
                "AllReduce", ALU.max, replica_groups=[list(range(8))],
                ins=[d_ar_in[ar_idx].ap().opt()],
                outs=[d_ar_out[ar_idx].ap().opt()]).then_inc(sems["cc"], 1))
            ewait("tensor", "v", max(v_loopA, v_tbp))
            ev_j = {}
            for j in range(NJ):
                OHFj = OHF[0:NT, j * 128:(j + 1) * 128]
                h = (j % 2) * 512
                if j >= 2:
                    ewait("tensor", "v", ev_j[j - 2])
                run("tensor", lambda e, OHFj=OHFj, h=h: e.matmul(
                    PSA[:, h:h + 512], OHFj, TBP[0:NT, :], start=True,
                    stop=True))
                run("tensor", lambda e, OHFj=OHFj, h=h: e.matmul(
                    PSB[:, h:h + 512], OHFj, SL64[0:NT, :], start=True,
                    stop=True), inc="t")
                ewait("vector", "t", cnt["t"])
                def _tev(e, j=j, h=h):
                    e.tensor_copy(out=TBALL[:, j * 512:(j + 1) * 512],
                                  in_=PSA[:, h:h + 512])
                    return e.tensor_copy(out=TSALL[:, j * 512:(j + 1) * 512],
                                         in_=PSB[:, h:h + 512])
                run("vector", _tev, inc="v")
                ev_j[j] = cnt["v"]

            ewait("gpsimd", "cc")
            edma("gpsimd", "dk", ARB[0:64, 0:2], d_ar_out[ar_idx].ap())
            ewait("vector", "dk", cnt["dk"])
            vrun(lambda e: e.tensor_scalar(out=MISC[0:64, 0:1], in0=ARB[0:64, 0:1],
                                           scalar1=-1.0, scalar2=None,
                                           op0=ALU.mult))
            vrun(lambda e: e.tensor_copy(out=MISC[0:64, 1:2], in_=ARB[0:64, 1:2]))
            vrun(lambda e: e.tensor_copy(out=MISC[0:64, 2:3], in_=OSC[0:64, :]))
            v_misc = cnt["v"]
            ewait("tensor", "v", v_misc)
            for j in range(NJ):
                OHFj = OHF[0:NT, j * 128:(j + 1) * 128]
                run("tensor", lambda e, OHFj=OHFj, j=j: e.matmul(
                    PSD[:, j * 4:(j + 1) * 4], OHFj, MISC[0:NT, :],
                    start=True, stop=True), inc="t" if j == NJ - 1 else None)
            ewait("vector", "t", cnt["t"])
            vrun(lambda e: e.tensor_copy(out=MNXS[:, 0:32], in_=PSD[:, 0:32]))

            MNv = MNXS[:].rearrange("p (j c) -> p j c", c=4)[:, :, 0:1]
            MXv = MNXS[:].rearrange("p (j c) -> p j c", c=4)[:, :, 1:2]
            SCv = MNXS[:].rearrange("p (j c) -> p j c", c=4)[:, :, 2:3]
            XSv = XS[:].rearrange("p (j k) -> p j k", k=DS)
            R8 = TSC[:, 8:16]; INV8 = TSC[:, 16:24]; SCH8 = TSC[:, 24:32]
            vsync()
            vrun(lambda e: e.tensor_tensor(
                out=R8.unsqueeze(2), in0=MXv, in1=MNv, op=ALU.subtract))
            vrun(lambda e: e.tensor_scalar(out=R8, in0=R8, scalar1=1e-8,
                                           scalar2=None, op0=ALU.add))
            vrun(lambda e: e.reciprocal(out=INV8, in_=R8))
            vrun(lambda e: e.tensor_scalar(out=SCH8.unsqueeze(2), in0=SCv,
                                           scalar1=0.5, scalar2=None,
                                           op0=ALU.mult))
            SP1v = SCT[:, 0:256]
            SP2v = SCT[:, 256:512]
            BACv = SCT[:, 512:768]
            SACv = SCT[:, 768:1024]
            MGv = EQM[:, 0:64]
            vrun(lambda e: e.tensor_tensor(
                out=SP1v.rearrange("p (j k) -> p j k", k=DS), in0=XSv,
                in1=MNv.broadcast_to([128, NJ, DS]), op=ALU.subtract))
            vrun(lambda e: e.tensor_tensor(
                out=SP1v.rearrange("p (j k) -> p j k", k=DS),
                in0=SP1v.rearrange("p (j k) -> p j k", k=DS),
                in1=INV8.unsqueeze(2).broadcast_to([128, NJ, DS]), op=ALU.mult))
            vrun(lambda e: e.tensor_scalar(out=SP1v, in0=SP1v,
                                           scalar1=1.0 - 1e-6, scalar2=0.0,
                                           op0=ALU.min, op1=ALU.max))
            vrun(lambda e: e.tensor_scalar(out=SP1v, in0=SP1v,
                                           scalar1=float(GRID), scalar2=None,
                                           op0=ALU.mult))
            TBv = TBALL[:].rearrange("p (j k g) -> p j k g", k=DS, g=GRID)
            TSv = TSALL[:].rearrange("p (j k g) -> p j k g", k=DS, g=GRID)
            vrun(lambda e: e.tensor_copy(
                out=BACv.rearrange("p (j k) -> p j k", k=DS), in_=TBv[:, :, :, 0]))
            vrun(lambda e: e.tensor_copy(
                out=SACv.rearrange("p (j k) -> p j k", k=DS), in_=TSv[:, :, :, 0]))
            for g in range(1, GRID):
                vrun(lambda e, g=g: e.tensor_scalar(
                    out=MG256[:], in0=SP1v, scalar1=float(g), scalar2=None,
                    op0=ALU.is_ge))
                vrun(lambda e, g=g: e.select(
                    out=BACv.rearrange("p (j k) -> p j k", k=DS),
                    mask=MG256[:].rearrange("p (j k) -> p j k", k=DS),
                    on_true=TBv[:, :, :, g],
                    on_false=BACv.rearrange("p (j k) -> p j k", k=DS),
                    add_drain=True))
                vrun(lambda e, g=g: e.select(
                    out=SACv.rearrange("p (j k) -> p j k", k=DS),
                    mask=MG256[:].rearrange("p (j k) -> p j k", k=DS),
                    on_true=TSv[:, :, :, g],
                    on_false=SACv.rearrange("p (j k) -> p j k", k=DS),
                    add_drain=True))
            vrun(lambda e: e.tensor_tensor(out=SP2v, in0=SACv, in1=SP1v,
                                           op=ALU.mult))
            vrun(lambda e: e.tensor_tensor(out=SP2v, in0=SP2v, in1=BACv,
                                           op=ALU.add))
            vrun(lambda e: e.tensor_tensor(
                out=SP2v.rearrange("p (j k) -> p j k", k=DS),
                in0=SP2v.rearrange("p (j k) -> p j k", k=DS),
                in1=SCH8.unsqueeze(2).broadcast_to([128, NJ, DS]), op=ALU.mult))
            ewait("scalar", "v", cnt["v"])
            g_scat = 0
            for j in range(NJ):
                Xj = X[:, j * D:(j + 1) * D]
                OHj = OHALL[:, j * 64:(j + 1) * 64]
                if g_scat:
                    ewait("vector", "g", g_scat)
                run("scalar", lambda e, Xj=Xj, j=j: e.activation(
                    out=Xj, in_=Xj, func=AF.Identity,
                    bias=CY[:, j:j + 1], scale=AY[:, j:j + 1]), inc="s")
                vrun(lambda e, OHj=OHj, j=j: e.tensor_tensor(
                    out=SCTB[:].rearrange("p (a b) -> p a b", b=DS),
                    in0=OHj.unsqueeze(2).broadcast_to([128, 64, DS]),
                    in1=SP2v.rearrange("p (j2 k) -> p j2 k", k=DS)[:, j, :]
                        .unsqueeze(1).broadcast_to([128, 64, DS]),
                    op=ALU.mult))
                ewait("gpsimd", "v", cnt["v"])
                ewait("gpsimd", "s", cnt["s"])
                run("gpsimd", lambda e, Xj=Xj: e.tensor_tensor(
                    out=Xj, in0=Xj, in1=SCTB[:], op=ALU.add), inc="g")
                g_scat = cnt["g"]
                if scatter_hook is not None:
                    scatter_hook(j, g_scat)
            ewait("vector", "g", cnt["g"])
            ewait("scalar", "g", cnt["g"])
            ewait("tensor", "g", cnt["g"])

        _pre_dw_map = {}

        def attention():
            dw_cc = dict(_pre_dw_map)
            def wqkv_load(hf, cc):
                s_ = "dwa" if cc % 2 == 0 else "dwb"
                edma("gpsimd", s_, WBUF[cc % 2][:].rearrange(
                    "p (dc c) -> p dc c", c=128),
                    d_wqkv.rearrange("(dc p) c -> p dc c", p=128)[:, :, cc * 128:(cc + 1) * 128])
                dw_cc[(hf, cc)] = (s_, cnt[s_])
            CC0 = [12, 13, 14, 15, 16, 17, 4, 5, 6, 7, 8, 9, 10, 11, 0, 1, 2, 3]
            seq = [(0, c) for c in CC0] + [(1, c) for c in range(CCQ)]
            s_x3f = 0
            s_evac_prev = [0, 0]
            s_qkv_ev = [0, 0]
            for hf in range(2):
                if hf == 0:
                    v_xn3 = front["v_xn3"]
                else:
                    ewait("scalar", "g", front["g_last"])
                    ewait("vector", "g", front["g_last"])
                    for j in range(4, NJ):
                        run("scalar", lambda e, j=j: e.activation(
                            out=SGNT, in_=X[:, j * D:(j + 1) * D],
                            func=AF.Square, accum_out=SQS[:, j:j + 1]),
                            inc="s")
                        vrun(lambda e, j=j: e.reduce_sum(
                            out=TKN[:, j:j + 1], in_=X[:, j * D:(j + 1) * D],
                            axis=AX.X))
                    ewait("vector", "s", cnt["s"])
                    _ln3_fold(4)
                    ewait("vector", "t", cnt["t"])
                    for j4 in range(4):
                        j = 4 + j4
                        vrun(lambda e, j=j, j4=j4: e.tensor_scalar(
                            out=XN3B[0][:, j4 * 2048:(j4 + 1) * 2048],
                            in0=X[:, j * D:(j + 1) * D],
                            scalar1=A3[:, j:j + 1], scalar2=C3[:, j:j + 1],
                            op0=ALU.mult, op1=ALU.add))
                    v_xn3 = cnt["v"]
                X3F = XN3B[1]
                ewait("tensor", "v", v_xn3)
                for dc in range(16):
                    if s_evac_prev[0]:
                        ewait("tensor", "s", s_evac_prev[0])
                    for j4 in range(4):
                        run("tensor", lambda e, dc=dc, j4=j4: e.transpose(
                            PSBK[:, j4 * 128:(j4 + 1) * 128],
                            XN3B[0][:, j4 * 2048 + dc * 128: j4 * 2048 + (dc + 1) * 128],
                            IDNB[:]), inc="t" if j4 == 3 else None)
                    t_tp = cnt["t"]
                    ewait("scalar", "t", t_tp)
                    run("scalar", lambda e, dc=dc, X3F=X3F: e.copy(
                        out=X3F[:, dc * 512:(dc + 1) * 512],
                        in_=PSBK[:, 0:512]), inc="s")
                    s_evac_prev[0] = cnt["s"]
                s_x3f = cnt["s"]
                order = CC0 if hf == 0 else list(range(CCQ))
                gate_v = False
                gate_g = False
                for cc in order:
                    ewait("tensor", dw_cc[(hf, cc)][0], dw_cc[(hf, cc)][1])
                    ewait("tensor", "s", s_x3f)
                    half = cc % 2
                    if s_qkv_ev[half]:
                        ewait("tensor", "s", s_qkv_ev[half])
                    for dc in range(16):
                        run("tensor", lambda e, cc=cc, dc=dc, half=half, X3F=X3F:
                            e.matmul(
                                PSA[:, half * 512:(half + 1) * 512],
                                WBUF[cc % 2][:, dc * 128:(dc + 1) * 128],
                                X3F[:, dc * 512:(dc + 1) * 512],
                                start=(dc == 0), stop=(dc == 15)),
                            inc="t" if dc == 15 else None)
                    t_qkv = cnt["t"]
                    ewait("scalar", "t", t_qkv)
                    if hf == 0 and cc in (8, 9, 10, 11) and not gate_g:
                        ewait("scalar", "g", front["g_last"])
                        gate_g = True
                    if hf == 0 and cc in (0, 1, 2, 3) and not gate_v:
                        ewait("scalar", "v", front["v_scat"])
                        gate_v = True
                    run("scalar", lambda e, cc=cc, hf=hf, half=half: e.activation(
                        out=QKVB[:, cc * 1024 + hf * 512: cc * 1024 + hf * 512 + 512],
                        in_=PSA[:, half * 512:(half + 1) * 512], func=AF.Identity,
                        bias=BQ[:, cc:cc + 1], scale=1.0), inc="s")
                    s_qkv_ev[half] = cnt["s"]
                    gi = seq.index((hf, cc))
                    if gi + 2 < len(seq) and seq[gi + 2] not in dw_cc:
                        ewait("gpsimd", "t", t_qkv)
                        wqkv_load(*seq[gi + 2])
            s_qkv_done = cnt["s"]

            ewait("tensor", "s", s_qkv_done)
            v_vto = 0
            for j in range(NJ):
                if v_vto:
                    ewait("tensor", "v", v_vto)
                run("tensor", lambda e, j=j: e.transpose(
                    PSBK[:, 0:128],
                    QKVB[:, 17 * 1024 + j * 128: 17 * 1024 + (j + 1) * 128],
                    IDNB[:]), inc="t")
                ewait("vector", "t", cnt["t"])
                run("vector", lambda e, j=j: e.tensor_copy(
                    out=VTO[:, j * 128:(j + 1) * 128],
                    in_=PSBK[:, 0:128]), inc="v")
                v_vto = cnt["v"]
            ewait("gpsimd", "v", v_vto)
            ewait("gpsimd", "s", s_qkv_done)
            edma("gpsimd", "dk", d_kv_in.ap()[0:128, :], QKVB[:, 16 * 1024:17 * 1024])
            edma("gpsimd", "dk", d_kv_in.ap()[128:256, :], VTO[:])
            ewait("gpsimd", "dk")
            cnt["cc"] += 1
            on("gpsimd", lambda e: e.collective_compute(
                "AllGather", ALU.bypass,
                replica_groups=[[0, 1, 2, 3], [4, 5, 6, 7]],
                ins=[d_kv_in.ap().opt()],
                outs=[d_kv_out.ap().opt()]).then_inc(sems["cc"], 1))
            ewait("gpsimd", "cc")
            kvo = d_kv_out.ap().rearrange("(r b p) t -> r b p t", b=2, p=128)
            edma("gpsimd", "dk",
                 KT[:].rearrange("p (s r q) -> p s r q", r=4, q=128),
                 kvo[:, 0, :, :].rearrange("r p (s q) -> p s r q", q=128))
            edma("gpsimd", "dk",
                 VT[:].rearrange("p (s r q) -> p s r q", r=4, q=128),
                 kvo[:, 1, :, :].rearrange("r p (s q) -> p s r q", q=128))
            dk_kv = cnt["dk"]

            v_qsq_prev = 0
            for dc in range(16):
                ewait("vector", "t", cnt["t"])
                vrun(lambda e, dc=dc: e.tensor_tensor(
                    out=QSQ[:], in0=QKVB[:, dc * 1024:(dc + 1) * 1024],
                    in1=QKVB[:, dc * 1024:(dc + 1) * 1024], op=ALU.mult))
                ewait("tensor", "v", cnt["v"])
                run("tensor", lambda e, dc=dc: e.matmul(
                    PSD[0:2, 0:512], ONEB2[:], QSQ[:, 0:512],
                    start=(dc == 0), stop=(dc == 15)))
                run("tensor", lambda e, dc=dc: e.matmul(
                    PSD[32:34, 0:512], ONEB2[:], QSQ[:, 512:1024],
                    start=(dc == 0), stop=(dc == 15)), inc="t")
            t_qn = cnt["t"]
            ewait("vector", "t", t_qn)
            def _qmax(e):
                e.tensor_reduce(out=KQM[0:1, 0:1], in_=PSD[0:1, 0:512], axis=AX.X,
                                op=ALU.max)
                return e.tensor_reduce(out=KQM[0:1, 1:2], in_=PSD[32:33, 0:512],
                                       axis=AX.X, op=ALU.max)
            run("vector", _qmax, inc="v")
            vrun(lambda e: e.tensor_tensor(
                out=KQM[0:1, 0:1], in0=KQM[0:1, 0:1], in1=KQM[0:1, 1:2],
                op=ALU.max))
            ewait("vector", "dk", dk_kv)
            for kc in range(4):
                vrun(lambda e, kc=kc: e.tensor_tensor(
                    out=QSQ[:], in0=KT[:, kc * 1024:(kc + 1) * 1024],
                    in1=KT[:, kc * 1024:(kc + 1) * 1024], op=ALU.mult))
                ewait("tensor", "v", cnt["v"])
                if kc > 0:
                    ewait("tensor", "v", v_kn)
                run("tensor", lambda e: e.matmul(
                    PSD[0:2, 0:512], ONEB2[:], QSQ[:, 0:512], start=True,
                    stop=True))
                run("tensor", lambda e: e.matmul(
                    PSD[32:34, 0:512], ONEB2[:], QSQ[:, 512:1024], start=True,
                    stop=True), inc="t")
                ewait("vector", "t", cnt["t"])
                def _kmaxA(e):
                    e.tensor_reduce(out=KQM[0:1, 2:3], in_=PSD[0:1, 0:512],
                                    axis=AX.X, op=ALU.max)
                    return e.tensor_reduce(out=KQM[0:1, 3:4], in_=PSD[32:33, 0:512],
                                           axis=AX.X, op=ALU.max)
                run("vector", _kmaxA, inc="v")
                vsync()
                vrun(lambda e: e.tensor_tensor(out=KQM[0:1, 2:3],
                                               in0=KQM[0:1, 2:3],
                                               in1=KQM[0:1, 3:4], op=ALU.max))
                if kc == 0:
                    vrun(lambda e: e.tensor_copy(out=KQM[0:1, 4:5],
                                                 in_=KQM[0:1, 2:3]))
                else:
                    vrun(lambda e: e.tensor_tensor(out=KQM[0:1, 4:5],
                                                   in0=KQM[0:1, 4:5],
                                                   in1=KQM[0:1, 2:3],
                                                   op=ALU.max))
                v_kn = cnt["v"]
            run("vector", lambda e: e.tensor_tensor(
                out=KQM[0:1, 5:6], in0=KQM[0:1, 0:1], in1=KQM[0:1, 4:5],
                op=ALU.mult), inc="v")
            ewait("scalar", "v", cnt["v"])
            run("scalar", lambda e: e.activation(out=KQM[0:1, 5:6],
                                                 in_=KQM[0:1, 5:6], func=AF.Sqrt),
                inc="s")
            ewait("vector", "s", cnt["s"])
            vrun(lambda e: e.tensor_scalar(
                out=KQM[0:1, 6:7], in0=KQM[0:1, 5:6], scalar1=-SCALE,
                scalar2=EXPSHIFT, op0=ALU.mult, op1=ALU.add))
            ewait("tensor", "v", cnt["v"])
            run("tensor", lambda e: e.matmul(
                PSD[:, 384:385], ONE1[0:1, :], KQM[0:1, 6:7], start=True,
                stop=True), inc="t")
            ewait("vector", "t", cnt["t"])
            run("vector", lambda e: e.tensor_copy(out=BB[:], in_=PSD[:, 384:385]),
                inc="v")
            v_bb = cnt["v"]

            def emit_pv(kt, EJ):
                buf = kt % 2
                kind, val = cons_done[buf]
                ewait("tensor", kind, val)
                if kt == 0 and v_aob:
                    ewait("tensor", "v", v_aob)
                for q4 in range(2):
                    run("tensor", lambda e, q4=q4, kt=kt, buf=buf, EJ=EJ:
                        e.matmul(
                            PSB[:, q4 * 512:(q4 + 1) * 512],
                            VT[:, kt * 128:(kt + 1) * 128],
                            PT[buf][:, q4 * 512:(q4 + 1) * 512],
                            start=(kt == 0), stop=(kt == EJ - 1)))
                run("tensor", lambda e, kt=kt, buf=buf, EJ=EJ: e.matmul(
                    PSD[0:2, 0:512], ONEB2[:], PT[buf][:, 0:512],
                    start=(kt == 0), stop=(kt == EJ - 1)))
                run("tensor", lambda e, kt=kt, buf=buf, EJ=EJ: e.matmul(
                    PSD[32:34, 0:512], ONEB2[:], PT[buf][:, 512:1024],
                    start=(kt == 0), stop=(kt == EJ - 1)), inc="t")
                pv_done[kt % 2] = cnt["t"]

            ewait("tensor", "dk", dk_kv)
            ewait("tensor", "v", v_bb)
            ewait("scalar", "v", v_bb)
            SSTRIP = [PSA, PSC]
            exp_done = {}
            cons_done = {}
            pv_done = {}
            v_aob = 0
            v_rb = 0
            s_rb = 0
            for j in range(NJ):
                EJ = 4 * (j + 1)
                for hf in range(2):
                    pending = None
                    for kt in range(EJ):
                        buf = kt % 2
                        strip = SSTRIP[buf]
                        prev = exp_done.get(buf)
                        if prev:
                            ewait("tensor", "s", prev)
                        if v_rb:
                            ewait("tensor", "s", s_rb)
                            ewait("tensor", "v", v_aob)
                            v_rb = 0
                        for q4 in range(2):
                            h0 = hf * 8 + q4 * 4
                            rhs = QKVB[:].rearrange(
                                "p (c t) -> p c t", t=1024)[
                                :, h0:h0 + 4, j * 128:(j + 1) * 128]
                            run("tensor", lambda e, q4=q4, kt=kt, strip=strip,
                                rhs=rhs: e.matmul(
                                    strip[:, q4 * 512:(q4 + 1) * 512],
                                    KT[:, kt * 128:(kt + 1) * 128], rhs,
                                    start=True, stop=True),
                                inc="t" if q4 == 1 else None)
                        t_sc = cnt["t"]
                        ewait("scalar", "t", t_sc)
                        prevpv = pv_done.get(buf)
                        if prevpv:
                            ewait("scalar", "t", prevpv)
                        run("scalar", lambda e, buf=buf, strip=strip: e.activation(
                            out=PT[buf][:], in_=strip[:], func=AF.Exp,
                            bias=BB[:], scale=SCALE), inc="s")
                        exp_done[buf] = cnt["s"]
                        if kt >= 4 * j:
                            ewait("gpsimd", "s", exp_done[buf])
                            run("gpsimd", lambda e, buf=buf, kt=kt: e.tensor_tensor(
                                out=PT[buf][:].rearrange("p (h q) -> p h q", q=128),
                                in0=PT[buf][:].rearrange("p (h q) -> p h q", q=128),
                                in1=MSK[:, kt * 128:(kt + 1) * 128].unsqueeze(1)
                                    .broadcast_to([128, 8, 128]),
                                op=ALU.mult), inc="g")
                            cons_done[buf] = ("g", cnt["g"])
                        else:
                            cons_done[buf] = ("s", exp_done[buf])
                        if pending is not None:
                            emit_pv(pending, EJ)
                        pending = kt
                    emit_pv(pending, EJ)
                    pending = None
                    t_last = cnt["t"]
                    ewait("scalar", "t", t_last)
                    run("scalar", lambda e: e.copy(out=SREC[0],
                                                   in_=PSD[0:1, 0:512]))
                    run("scalar", lambda e: e.copy(out=SREC[1],
                                                   in_=PSD[32:33, 0:512]),
                        inc="s")
                    ewait("vector", "s", cnt["s"])
                    vrun(lambda e: e.reciprocal(out=QSQF[0:33, 0:512],
                                                in_=QSQF[0:33, 0:512]))
                    s_recd = cnt["v"]
                    ewait("tensor", "v", s_recd)
                    for q in range(2):
                        qp = 0 if q == 0 else 32
                        run("tensor", lambda e, q=q, qp=qp: e.matmul(
                            PSA[:, q * 512:(q + 1) * 512], ONE1[qp:qp + 1, :],
                            SREC[q], start=True, stop=True),
                            inc="t" if q == 1 else None)
                    t_rb = cnt["t"]
                    ewait("scalar", "t", t_rb)
                    run("scalar", lambda e: e.copy(out=RB[:],
                                                   in_=PSA[:, 0:1024]), inc="s")
                    s_rb = cnt["s"]
                    ewait("vector", "s", s_rb)
                    v_rb = cnt["v"]
                    run("vector", lambda e, j=j, hf=hf: e.tensor_tensor(
                        out=AOB[:, j * 2048 + hf * 1024: j * 2048 + (hf + 1) * 1024],
                        in0=PSB[:], in1=RB[:], op=ALU.mult))
                    v_aob = cnt["v"]
                    exp_done = {}
                    pv_done = {}
                    cons_done = {}

            dwp = {}
            def wp_load(wq):
                s_ = "dwa" if wq % 2 == 0 else "dwb"
                edma("gpsimd", s_, WPB[wq % 2][:].rearrange(
                    "p (dc c) -> p dc c", c=512),
                    d_wproj.rearrange("(dc p) c -> p dc c", p=128)[:, :, wq * 512:(wq + 1) * 512])
                dwp[wq] = (s_, cnt[s_])
            ewait("gpsimd", "v", v_aob)
            wp_load(0)
            wp_load(1)
            v_pe = [0, 0]
            t_wq_last = 0
            for wq in range(4):
                if wq >= 1 and wq + 1 < 4 and (wq + 1) not in dwp:
                    pass
                ewait("tensor", dwp[wq][0], dwp[wq][1])
                ewait("tensor", "v", v_aob)
                for j in range(NJ):
                    half = j % 2
                    if v_pe[half]:
                        ewait("tensor", "v", v_pe[half])
                    for dc in range(16):
                        run("tensor", lambda e, wq=wq, j=j, dc=dc, half=half:
                            e.matmul(
                                PSA[:, half * 512:(half + 1) * 512],
                                AOB[:, j * 2048 + dc * 128: j * 2048 + (dc + 1) * 128],
                                WPB[wq % 2][:, dc * 512:(dc + 1) * 512],
                                start=(dc == 0), stop=False))
                    run("tensor", lambda e, wq=wq, half=half: e.matmul(
                        PSA[:, half * 512:(half + 1) * 512], ONEBR[:],
                        BPR[0:1, wq * 512:(wq + 1) * 512],
                        start=False, stop=True), inc="t")
                    t_pp = cnt["t"]
                    ewait("vector", "t", t_pp)
                    run("vector", lambda e, wq=wq, j=j, half=half: e.tensor_tensor(
                        out=X[:, j * D + wq * 512: j * D + (wq + 1) * 512],
                        in0=X[:, j * D + wq * 512: j * D + (wq + 1) * 512],
                        in1=PSA[:, half * 512:(half + 1) * 512], op=ALU.add),
                        inc="v")
                    v_pe[half] = cnt["v"]
                t_wq_last = cnt["t"]
                if wq + 2 < 4:
                    ewait("gpsimd", "t", t_wq_last)
                    wp_load(wq + 2)
            ewait("vector", "v", cnt["v"])
            ewait("scalar", "v", cnt["v"])
            ewait("tensor", "v", cnt["v"])
            ewait("gpsimd", "v", cnt["v"])

        ewait("vector", "v", max(v_gb, v_ss))
        _pre_dw = {}
        for _cc in (12, 13):
            _s = "dwa" if _cc % 2 == 0 else "dwb"
            edma("gpsimd", _s, WBUF[_cc % 2][:].rearrange(
                "p (dc c) -> p dc c", c=128),
                d_wqkv.rearrange("(dc p) c -> p dc c", p=128)[
                    :, :, _cc * 128:(_cc + 1) * 128])
            _pre_dw[_cc] = (_s, cnt[_s])
        for _cc in (12, 13):
            _pre_dw_map[(0, _cc)] = _pre_dw[_cc]

        A3 = AFC[:, 0:8]
        C3 = AFC[:, 8:16]
        SGNT = VTRB[:].bitcast(BF16)

        def _ln3_fold(b):
            vrun(lambda e: e.tensor_scalar(
                out=A3[:, b:b + 4], in0=SQS[:, b:b + 4], scalar1=1.0 / D,
                scalar2=None, op0=ALU.mult))
            vrun(lambda e: e.tensor_scalar(
                out=C3[:, b:b + 4], in0=TKN[:, b:b + 4], scalar1=1.0 / D,
                scalar2=None, op0=ALU.mult))
            vrun(lambda e: e.tensor_tensor(
                out=AFC[:, 16 + b:20 + b], in0=C3[:, b:b + 4],
                in1=C3[:, b:b + 4], op=ALU.mult))
            vrun(lambda e: e.tensor_tensor(
                out=A3[:, b:b + 4], in0=A3[:, b:b + 4],
                in1=AFC[:, 16 + b:20 + b], op=ALU.subtract))
            vrun(lambda e: e.tensor_scalar(
                out=A3[:, b:b + 4], in0=A3[:, b:b + 4], scalar1=LNEPS,
                scalar2=None, op0=ALU.add))
            vrun(lambda e: e.reciprocal(out=A3[:, b:b + 4], in_=A3[:, b:b + 4]))
            ewait("scalar", "v", cnt["v"])
            run("scalar", lambda e: e.activation(
                out=A3[:, b:b + 4], in_=A3[:, b:b + 4], func=AF.Sqrt), inc="s")
            ewait("vector", "s", cnt["s"])
            vrun(lambda e: e.tensor_tensor(
                out=C3[:, b:b + 4], in0=C3[:, b:b + 4], in1=A3[:, b:b + 4],
                op=ALU.mult))
            vrun(lambda e: e.tensor_scalar(
                out=C3[:, b:b + 4], in0=C3[:, b:b + 4], scalar1=-1.0,
                scalar2=None, op0=ALU.mult))

        front = {}

        def _front_hook(j, gv):
            if j <= 3:
                ewait("scalar", "g", gv)
                run("scalar", lambda e, j=j: e.activation(
                    out=SGNT, in_=X[:, j * D:(j + 1) * D], func=AF.Square,
                    accum_out=SQS[:, j:j + 1]), inc="s")
                ewait("vector", "g", gv)
                vrun(lambda e, j=j: e.reduce_sum(
                    out=TKN[:, j:j + 1], in_=X[:, j * D:(j + 1) * D],
                    axis=AX.X))
            if j == 3:
                ewait("vector", "s", cnt["s"])
                _ln3_fold(0)
            if j >= 4:
                j4 = j - 4
                vrun(lambda e, j4=j4: e.tensor_scalar(
                    out=XN3B[0][:, j4 * 2048:(j4 + 1) * 2048],
                    in0=X[:, j4 * D:(j4 + 1) * D],
                    scalar1=A3[:, j4:j4 + 1], scalar2=C3[:, j4:j4 + 1],
                    op0=ALU.mult, op1=ALU.add))
            if j == 7:
                front["v_xn3"] = cnt["v"]
                front["v_scat"] = cnt["v"]

        ffn(0, scatter_hook=_front_hook)
        front["g_last"] = cnt["g"]
        if stop >= 2:
            attention()
        def _store_hook(j, gv):
            ewait("sync", "g", gv)
            edma("sync", "so",
                 d_y.rearrange("(j2 p) d -> p j2 d", p=128)[:, j, :],
                 X[:, j * D:(j + 1) * D])

        if stop >= 3:
            ffn(1, scatter_hook=_store_hook)

        ewait("sync", "v", cnt["v"])
        ewait("sync", "g", cnt["g"])
        ewait("sync", "s", cnt["s"])
        ewait("sync", "t", cnt["t"])
        if stop < 3:
            edma("sync", "so", d_y.rearrange("(j p) d -> p j d", p=128),
                 X[:].rearrange("p (j d) -> p j d", d=D))
        if _dbg():
            edma("sync", "so", dbg["dSTAT"], STAT[:])
            edma("sync", "so", dbg["dAFC"], AFC[:])
            edma("sync", "so", dbg["dOHALL"], OHALL[:])
            edma("sync", "so", dbg["dXS"], XS[:])
            edma("sync", "so", dbg["dMNXS"], MNXS[:])
            edma("sync", "so", dbg["dARB"], ARB[:, 0:2])
            edma("sync", "so", dbg["dTKN"], TKN[:])
            edma("gpsimd", "so", dbg["dSS"], SS128[:])
            edma("sync", "so", dbg["dGAM"], GAM[:])
            edma("sync", "so", dbg["dBET"], BET[:])
            edma("sync", "so", dbg["dDBG1"], DBG1[:])
            edma("sync", "so", dbg["dSQS"], SQS[:])
        ewait("sync", "so")

        block = est.enter_context(nc.Block())

        @block.sync
        def _(e):
            for f in ops["sync"]:
                f(e)

        @block.vector
        def _(e):
            for f in ops["vector"]:
                f(e)

        @block.scalar
        def _(e):
            for f in ops["scalar"]:
                f(e)

        @block.tensor
        def _(e):
            for f in ops["tensor"]:
                f(e)

        @block.gpsimd
        def _(e):
            for f in ops["gpsimd"]:
                f(e)

    return nc



def _host_constf():
    cf = np.zeros((128, 256), np.float32)
    cf[:, 0:64] = np.arange(64, dtype=np.float32)[None, :]
    cf[:, 64:128] = (np.arange(64, dtype=np.float32) // 8)[None, :]
    cf[:, 128:256] = np.eye(128, dtype=np.float32)
    return cf


def _host_masks(cid):
    m = np.zeros((NKT, QT, QT), np.float32)
    for kt in range(NKT):
        g = 4 * (kt // 4) + cid
        kpos = kt * QT + np.arange(QT)[:, None]
        qpos = g * QT + np.arange(QT)[None, :]
        m[kt] = (kpos <= qpos).astype(np.float32)
    return m.reshape(NKT * QT, QT).astype(ml_dtypes.bfloat16)


def kernel(L0_attn_x, L0_attn_W_qkv, L0_attn_b_qkv, L0_attn_W_proj,
           L0_attn_b_proj, L0_ffn_ln_gamma, L0_ffn_ln_beta,
           L0_ffn_spline_bases, L0_ffn_spline_slopes, L0_ffn_output_scale):
    x = np.ascontiguousarray(np.asarray(L0_attn_x, dtype=np.float32))
    wqkv = np.ascontiguousarray(np.asarray(L0_attn_W_qkv, dtype=np.float32))
    bqkv = np.ascontiguousarray(np.asarray(L0_attn_b_qkv, dtype=np.float32))
    wproj = np.ascontiguousarray(np.asarray(L0_attn_W_proj, dtype=np.float32))
    bproj = np.ascontiguousarray(np.asarray(L0_attn_b_proj, dtype=np.float32)).reshape(1, D)
    gamma = np.ascontiguousarray(np.asarray(L0_ffn_ln_gamma, dtype=np.float32)).reshape(1, D)
    beta = np.ascontiguousarray(np.asarray(L0_ffn_ln_beta, dtype=np.float32)).reshape(1, D)
    bases = np.ascontiguousarray(np.asarray(L0_ffn_spline_bases, dtype=np.float32)).reshape(NT, DS * GRID)
    slopes = np.ascontiguousarray(np.asarray(L0_ffn_spline_slopes, dtype=np.float32)).reshape(NT, DS * GRID)
    oscale = np.ascontiguousarray(np.asarray(L0_ffn_output_scale, dtype=np.float32)).reshape(NT, 1)

    cf = _host_constf()
    idn = np.eye(128, dtype=ml_dtypes.bfloat16)

    nc = build_nc(stop=_BUILD_STOP)
    in_maps = []
    for core in range(8):
        batch, cid = core // 4, core % 4
        xs = np.concatenate(
            [x[batch, (4 * j + cid) * QT:(4 * j + cid + 1) * QT, :]
             for j in range(NJ)], axis=0)
        in_maps.append({
            "x": np.ascontiguousarray(xs),
            "wqkv": wqkv, "bqkv": bqkv, "wproj": wproj, "bproj": bproj,
            "gamma": gamma, "beta": beta, "bases": bases, "slopes": slopes,
            "oscale": oscale, "msk": _host_masks(cid), "constf": cf,
            "idnb": idn,
        })
    res = run_bass_kernel_spmd(nc, in_maps, core_ids=list(range(8)))
    out = np.empty((B, S, D), np.float32)
    for core in range(8):
        batch, cid = core // 4, core % 4
        y = res.results[core]["y"]
        for j in range(NJ):
            g = 4 * j + cid
            out[batch, g * QT:(g + 1) * QT, :] = y[j * QT:(j + 1) * QT, :]
    return out


_BUILD_STOP = 3


# version 10
# speedup vs baseline: 1.4495x; 1.0497x over previous
import numpy as np
import ml_dtypes
from contextlib import ExitStack

import concourse.bass as bass
import concourse.mybir as mybir
from concourse.bass_utils import run_bass_kernel_spmd

_DEBUG_DUMP = False

F32 = mybir.dt.float32
BF16 = mybir.dt.bfloat16
U8 = mybir.dt.uint8
AF = mybir.ActivationFunctionType
ALU = mybir.AluOpType
AX = mybir.AxisListType

B, S, D = 2, 4096, 2048
NJ, QT = 8, 128
T = NJ * QT
NH, HD = 16, 128
SCALE = float(HD) ** -0.5
NT, DS, GRID = 64, 32, 16
LNEPS = 1e-5
BIG = 1.0e9
FBIG = 1.0e30
EXPSHIFT = 20.0
NKT = 32
CCQ = 18


def _dbg():
    return _DEBUG_DUMP


def build_nc(stop=3):
    nc = bass.Bass("TRN2", target_bir_lowering=False, debug=False,
                   detect_race_conditions=True)

    d_x = nc.dram_tensor("x", [T, D], F32, kind="ExternalInput").ap()
    d_wqkv = nc.dram_tensor("wqkv", [D, 2304], F32, kind="ExternalInput").ap()
    d_bqkv = nc.dram_tensor("bqkv", [2304], F32, kind="ExternalInput").ap()
    d_wproj = nc.dram_tensor("wproj", [D, D], F32, kind="ExternalInput").ap()
    d_bproj = nc.dram_tensor("bproj", [1, D], F32, kind="ExternalInput").ap()
    d_gamma = nc.dram_tensor("gamma", [1, D], F32, kind="ExternalInput").ap()
    d_beta = nc.dram_tensor("beta", [1, D], F32, kind="ExternalInput").ap()
    d_bases = nc.dram_tensor("bases", [NT, DS * GRID], F32, kind="ExternalInput").ap()
    d_slopes = nc.dram_tensor("slopes", [NT, DS * GRID], F32, kind="ExternalInput").ap()
    d_oscale = nc.dram_tensor("oscale", [NT, 1], F32, kind="ExternalInput").ap()
    d_msk = nc.dram_tensor("msk", [NKT * QT, QT], BF16, kind="ExternalInput").ap()
    d_cf = nc.dram_tensor("constf", [128, 256], F32, kind="ExternalInput").ap()
    d_idnb = nc.dram_tensor("idnb", [128, 128], BF16, kind="ExternalInput").ap()
    d_y = nc.dram_tensor("y", [T, D], F32, kind="ExternalOutput").ap()
    dbg = {}
    if _dbg():
        for nm, shp in [("dSTAT", [128, 16]), ("dAFC", [128, 48]),
                        ("dOHALL", [128, 512]), ("dXS", [128, 256]),
                        ("dMNXS", [128, 32]), ("dARB", [128, 2]),
                        ("dTKN", [128, 16]), ("dSS", [128, 2048]),
                        ("dGAM", [128, 2048]), ("dBET", [128, 2048]),
                        ("dDBG1", [128, 16]), ("dDBG2", [128, 1024]), ("dSQS", [128, 8]),
                        ("dDBG3", [128, 1024])]:
            dbg[nm] = nc.dram_tensor(nm, shp, F32, kind="ExternalOutput").ap()

    d_ssrow = nc.dram_tensor("ssrow", [NT, DS], F32)
    d_ar_in = [nc.dram_tensor(f"ar_in{i}", [NT, 2], F32) for i in range(2)]
    d_ar_out = [nc.dram_tensor(f"ar_out{i}", [NT, 2], F32) for i in range(2)]
    d_kv_in = nc.dram_tensor("kv_in", [2 * 128, T], BF16)
    d_kv_out = nc.dram_tensor("kv_out", [4 * 2 * 128, T], BF16)

    est = ExitStack()
    with est:
        def sbt(name, shape, dt):
            return est.enter_context(nc.sbuf_tensor(name, shape, dt))

        X = sbt("X", [128, NJ * D], F32)
        MSK = sbt("MSK", [128, NKT * QT], BF16)
        SS128 = sbt("SS128", [128, D], BF16)
        GAM = sbt("GAM", [128, D], F32)
        BET = sbt("BET", [128, D], F32)
        CF = sbt("CF", [128, 256], F32)
        IDNB = sbt("IDNB", [128, 128], BF16)
        SL64 = sbt("SL64", [128, DS * GRID], F32)
        TBP = sbt("TBP", [128, DS * GRID], F32)
        MISC = sbt("MISC", [128, 4], F32)
        BQ = sbt("BQ", [128, CCQ], F32)
        BPR = sbt("BPR", [1, D], BF16)
        ONE1 = sbt("ONE1", [33, 128], F32)
        ONEB2 = sbt("ONEB2", [128, 2], BF16)
        ONEBR = sbt("ONEBR", [1, 128], BF16)
        BIGT = sbt("BIGT", [128, 64], F32)
        NBIGT = sbt("NBIGT", [128, 64], F32)
        R1 = sbt("R1", [128, 16384], BF16)
        R2 = sbt("R2", [128, 18432], BF16)
        KT = sbt("KT", [128, NKT * QT], BF16)
        VT = sbt("VT", [128, NKT * QT], BF16)
        PT = [sbt(f"PT{i}", [128, 1024], BF16) for i in range(2)]
        QSQ = sbt("QSQ", [128, 1024], BF16)
        VTRB = sbt("VTRB", [128, 1024], F32)
        TB2 = sbt("TB2", [128, DS * GRID], F32)
        TS2 = sbt("TS2", [128, DS * GRID], F32)
        OHALL = sbt("OHALL", [128, NJ * NT], F32)
        XS = sbt("XS", [128, NJ * DS], F32)
        MNXS = sbt("MNXS", [128, NJ * 4], F32)
        STAT = sbt("STAT", [128, 16], F32)
        SQS = sbt("SQS", [128, 8], F32)
        AFC = sbt("AFC", [128, 48], F32)
        TKN = sbt("TKN", [128, 16], F32)
        TSC = sbt("TSC", [128, 64], F32)
        TSCALL = sbt("TSCALL", [128, 512], F32)
        MMB2 = None
        MM1 = sbt("MM1", [128, 64], F32)
        MMIN = sbt("MMIN", [128, 64], F32)
        MMAX = sbt("MMAX", [128, 64], F32)
        EQM = sbt("EQM", [128, 512], U8)

        SC1 = sbt("SC1", [128, 1], F32)
        SC2 = sbt("SC2", [128, 1], F32)
        ARB = sbt("ARB", [128, 2], F32)
        OSC = sbt("OSC", [128, 1], F32)
        BB = sbt("BB", [128, 1], F32)
        KQM = sbt("KQM", [1, 8], F32)

        WBUF = [KT[:, 0:2048], KT[:, 2048:4096]]
        XNv = [R1[:, 0:4096].bitcast(F32), R1[:, 4096:8192].bitcast(F32)]
        TBALL = R1[:, 8192:16384].bitcast(F32)
        SCTB = R2[:, 8192:12288].bitcast(F32)
        TSALL = R2[:, 4096:8192]
        SCT = R2[:, 0:4096].bitcast(F32)
        SGN = [R2[:, 8192 + i * 4096: 8192 + i * 4096 + 2048] for i in range(2)]
        QQB = [R2[:, 10240 + i * 4096: 10240 + i * 4096 + 2048] for i in range(2)]
        GROW = R2[:, 0:4096].bitcast(F32)[0:1, :]
        BROW = R2[:, 4096:8192].bitcast(F32)[0:1, :]
        XN3B = [R1[:, 0:8192], R1[:, 8192:16384]]
        X3Fh = R1[:, 8192:16384]
        AOB = R1
        QKVB = R2
        WPB = [R2[:, 0:8192], R2[:, 8192:16384]]
        OHF = VTRB
        RB = VTRB
        VTO = VTRB[:, 0:512].bitcast(BF16)
        QSQF = QSQ.bitcast(F32)
        G64 = QSQF
        SREC = [QSQF[0:1, 0:512], QSQF[32:33, 0:512]]
        MMB2 = SCT[:, 512:640]
        MG256 = EQM[:, 0:256]

        T64A = SCT[:, 1024:1152]
        DBG1 = SCT[:, 1152:1168]
        PSBKF = None

        PSA = est.enter_context(nc.psum_tensor("PSA", [128, 1024], F32))
        PSB = est.enter_context(nc.psum_tensor("PSB", [128, 1024], F32))
        PSC = est.enter_context(nc.psum_tensor("PSC", [128, 1024], F32))
        PSD = est.enter_context(nc.psum_tensor("PSD", [128, 512], F32))
        PSBK = est.enter_context(nc.psum_tensor("PSBK", [128, 1024], BF16))
        PSBKF = PSBK.ap().bitcast(F32)

        sems = {}
        for s_ in ("di", "dw", "dwa", "dwb", "dk", "cc", "t", "v", "s", "g",
                   "so"):
            sems[s_] = est.enter_context(nc.semaphore(s_))
        cnt = {k: 0 for k in sems}
        ops = {e: [] for e in ("sync", "vector", "scalar", "tensor", "gpsimd")}

        def on(eng, fn):
            ops[eng].append(fn)

        def ewait(eng, s_, val=None):
            v = cnt[s_] if val is None else val
            if v <= 0:
                return
            on(eng, lambda e, s_=s_, v=v: e.wait_ge(sems[s_], v))

        def edma(eng, s_, out, in_, slow=False):
            cnt[s_] += 16
            if slow:
                on(eng, lambda e, out=out, in_=in_, s_=s_:
                   e.dma_start(out=out, in_=in_, allow_slow_non_contiguous=True)
                   .then_inc(sems[s_], 16))
            else:
                on(eng, lambda e, out=out, in_=in_, s_=s_:
                   e.dma_start(out=out, in_=in_).then_inc(sems[s_], 16))

        def vsync():
            ewait("vector", "v", cnt["v"])

        def vrun(fn):
            run("vector", fn, inc="v")

        def run(eng, fn, inc=None):
            if eng == "vector":
                ewait("vector", "v", cnt["v"])
                if inc is None:
                    inc = "v"
            elif eng == "scalar":
                ewait("scalar", "s", cnt["s"])
                if inc is None:
                    inc = "s"
            if inc is not None:
                cnt[inc] += 1
            def _f(e, fn=fn, inc=inc):
                r = fn(e)
                if inc is not None:
                    r.then_inc(sems[inc], 1)
            on(eng, _f)

        IOTA64 = CF[:, 0:64]
        IOTAD8 = CF[:, 64:128]
        IDNF = CF[:, 128:256]

        edma("sync", "di", X[:].rearrange("p (j d) -> p j d", d=D),
             d_x.rearrange("(j p) d -> p j d", p=128))
        edma("sync", "di", MSK[:].rearrange("p (n q) -> p n q", q=QT),
             d_msk.rearrange("(n p) q -> p n q", p=128))
        edma("sync", "di", CF[:], d_cf)
        edma("sync", "di", IDNB[:], d_idnb)
        edma("sync", "di", SL64[0:NT, :], d_slopes)
        edma("sync", "di", TBP[0:NT, :], d_bases)
        edma("sync", "di", OSC[0:NT, :], d_oscale)
        edma("sync", "di", BQ[:], d_bqkv.rearrange("(cc p) -> p cc", p=128),
             slow=True)
        ld_all = cnt["di"]
        edma("gpsimd", "dw", GROW, d_gamma)
        edma("gpsimd", "dw", BROW, d_beta)
        edma("gpsimd", "dw", BPR[:], d_bproj)
        g_rows = cnt["dw"]

        def _init(e):
            e.memset(ONE1[:], 1.0)
            e.memset(ONEB2[:], 1.0)
            e.memset(ONEBR[:], 1.0)
            e.memset(BIGT[:], BIG)
            e.memset(NBIGT[:], -BIG)
            return e.memset(MISC[:], 0.0)
        run("gpsimd", _init, inc="g")
        g_init = cnt["g"]

        ewait("tensor", "dw", g_rows)
        ewait("tensor", "g", g_init)
        for part in range(2):
            row = GROW if part == 0 else BROW
            dst = GAM if part == 0 else BET
            for h in range(2):
                ewait("tensor", "v", cnt["v"])
                for q in range(2):
                    run("tensor", lambda e, row=row, h=h, q=q: e.matmul(
                        PSA[:, q * 512:(q + 1) * 512], ONE1[0:1, :],
                        row[0:1, h * 1024 + q * 512: h * 1024 + (q + 1) * 512],
                        start=True, stop=True), inc="t" if q == 1 else None)
                ewait("vector", "t", cnt["t"])
                run("vector", lambda e, dst=dst, h=h: e.tensor_copy(
                    out=dst[:, h * 1024:(h + 1) * 1024], in_=PSA[:, 0:1024]),
                    inc="v")
        v_gb = cnt["v"]

        ewait("vector", "di", ld_all)
        if _dbg():
            run("vector", lambda e: e.tensor_copy(
                out=VTRB[:], in_=X[:, 7 * 2048: 7 * 2048 + 1024]), inc="v")
            ewait("sync", "v", cnt["v"])
            edma("sync", "so", dbg["dDBG2"], VTRB[:])
        run("vector", lambda e: e.tensor_reduce(
            out=TSC[0:NT, 0:DS],
            in_=SL64[0:NT, :].rearrange("p (j g) -> p j g", g=GRID),
            axis=AX.X, op=ALU.add), inc="v")
        ewait("scalar", "v", cnt["v"])
        run("scalar", lambda e: e.activation(
            out=TSC[0:NT, 0:DS], in_=TSC[0:NT, 0:DS], func=AF.Sign), inc="s")
        s_ss = cnt["s"]
        ewait("gpsimd", "g", g_init)
        run("gpsimd", lambda e: e.iota(
            G64[0:NT, :], [[0, DS], [1, GRID]], channel_multiplier=0,
            allow_small_or_imprecise_dtypes=True), inc="g")
        g_g64 = cnt["g"]
        ewait("vector", "g", g_g64)
        vrun(lambda e: e.tensor_tensor(
            out=G64[0:NT, :], in0=SL64[0:NT, :], in1=G64[0:NT, :], op=ALU.mult))
        vrun(lambda e: e.tensor_tensor(
            out=TBP[0:NT, :], in0=TBP[0:NT, :], in1=G64[0:NT, :],
            op=ALU.subtract))
        v_tbp = cnt["v"]
        ewait("gpsimd", "s", s_ss)
        edma("gpsimd", "dk", d_ssrow.ap(), TSC[0:NT, 0:DS])
        ewait("gpsimd", "dk")
        edma("gpsimd", "dk", VTRB[0:1, 0:1024],
             d_ssrow.ap().rearrange("a b -> (a b)")[None, 0:1024])
        edma("gpsimd", "dk", VTRB[32:33, 0:1024],
             d_ssrow.ap().rearrange("a b -> (a b)")[None, 1024:2048])
        g_ssrow = cnt["dk"]
        ewait("tensor", "dk", g_ssrow)
        for h in range(2):
            hp = 0 if h == 0 else 32
            ewait("tensor", "v", cnt["v"])
            for q in range(2):
                run("tensor", lambda e, hp=hp, q=q: e.matmul(
                    PSB[:, q * 512:(q + 1) * 512], ONE1[hp:hp + 1, :],
                    VTRB[hp:hp + 1, q * 512:(q + 1) * 512],
                    start=True, stop=True), inc="t" if q == 1 else None)
            ewait("vector", "t", cnt["t"])
            run("vector", lambda e, h=h: e.tensor_copy(
                out=SS128[:, h * 1024:(h + 1) * 1024], in_=PSB[:, 0:1024]),
                inc="v")
        v_ss = cnt["v"]

        def ffn(ar_idx, scatter_hook=None, stats_done=0):
            ewait("vector", "t", cnt["t"])
            ewait("scalar", "t", cnt["t"])
            ewait("vector", "v", cnt["v"])
            ewait("scalar", "v", cnt["v"])
            ewait("scalar", "g", cnt["g"])
            ewait("vector", "g", cnt["g"])
            for j in range(NJ if not stats_done else 0):
                Xj = X[:, j * D:(j + 1) * D]
                run("scalar", lambda e, Xj=Xj, j=j: e.activation(
                    out=SGN[j % 2], in_=Xj, func=AF.Square,
                    accum_out=SQS[:, j:j + 1]), inc="s")
                vrun(lambda e, Xj=Xj, j=j: e.reduce_sum(
                    out=STAT[:, j:j + 1], in_=Xj, axis=AX.X))
            MU = AFC[:, 0:8]; VAR = AFC[:, 8:16]; RS1 = AFC[:, 16:24]
            RS2 = AFC[:, 24:32]; AFj = AFC[:, 32:40]; CFj = AFC[:, 40:48]
            AY = STAT[:, 0:8]; CY = STAT[:, 8:16]
            ewait("vector", "s", cnt["s"])
            vrun(lambda e: e.tensor_scalar(out=MU, in0=STAT[:, 0:8],
                                           scalar1=1.0 / D, scalar2=None,
                                           op0=ALU.mult))
            vrun(lambda e: e.tensor_scalar(out=VAR, in0=SQS[:], scalar1=1.0 / D,
                                           scalar2=None, op0=ALU.mult))
            vrun(lambda e: e.tensor_tensor(out=TSC[:, 0:8], in0=MU, in1=MU,
                                           op=ALU.mult))
            vrun(lambda e: e.tensor_tensor(out=VAR, in0=VAR, in1=TSC[:, 0:8],
                                           op=ALU.subtract))
            vrun(lambda e: e.tensor_scalar(out=RS1, in0=VAR, scalar1=LNEPS,
                                           scalar2=None, op0=ALU.add))
            vrun(lambda e: e.reciprocal(out=RS1, in_=RS1))
            ewait("scalar", "v", cnt["v"])
            run("scalar", lambda e: e.activation(out=RS1, in_=RS1, func=AF.Sqrt),
                inc="s")
            ewait("vector", "s", cnt["s"])
            vrun(lambda e: e.tensor_tensor(out=RS2, in0=RS1, in1=RS1, op=ALU.mult))
            vrun(lambda e: e.tensor_tensor(out=RS2, in0=RS2, in1=VAR, op=ALU.mult))
            vrun(lambda e: e.tensor_scalar(out=RS2, in0=RS2, scalar1=0.25,
                                           scalar2=LNEPS, op0=ALU.mult,
                                           op1=ALU.add))
            vrun(lambda e: e.reciprocal(out=RS2, in_=RS2))
            ewait("scalar", "v", cnt["v"])
            run("scalar", lambda e: e.activation(out=RS2, in_=RS2, func=AF.Sqrt),
                inc="s")
            ewait("vector", "s", cnt["s"])
            vrun(lambda e: e.tensor_tensor(out=AFj, in0=RS1, in1=RS2, op=ALU.mult))
            vrun(lambda e: e.tensor_scalar(out=AFj, in0=AFj, scalar1=0.5,
                                           scalar2=None, op0=ALU.mult))
            vrun(lambda e: e.tensor_tensor(out=CFj, in0=MU, in1=AFj, op=ALU.mult))
            vrun(lambda e: e.tensor_scalar(out=CFj, in0=CFj, scalar1=-1.0,
                                           scalar2=None, op0=ALU.mult))
            vrun(lambda e: e.tensor_tensor(out=CY, in0=MU, in1=RS1, op=ALU.mult))
            vrun(lambda e: e.tensor_scalar(out=CY, in0=CY, scalar1=-0.5,
                                           scalar2=None, op0=ALU.mult))
            vrun(lambda e: e.tensor_scalar(out=AY, in0=RS1, scalar1=0.5,
                                           scalar2=1.0, op0=ALU.mult, op1=ALU.add))

            s_sgn = [0] * NJ
            g_gam = [0] * NJ
            for jp in range(0, NJ, 2):
                for j in (jp, jp + 1):
                    Xj = X[:, j * D:(j + 1) * D]
                    XN = XNv[j % 2]
                    if j >= 2:
                        ewait("vector", "s", s_sgn[j - 2])
                    vrun(lambda e, Xj=Xj, XN=XN, j=j: e.tensor_scalar(
                        out=XN, in0=Xj, scalar1=AFj[:, j:j + 1],
                        scalar2=CFj[:, j:j + 1], op0=ALU.mult, op1=ALU.add))
                    vrun(lambda e, XN=XN: e.tensor_tensor(
                        out=XN, in0=XN, in1=GAM[:], op=ALU.mult))
                    vrun(lambda e, XN=XN: e.tensor_tensor(
                        out=XN, in0=XN, in1=BET[:], op=ALU.add))
                    v_xn = cnt["v"]
                    ewait("scalar", "v", v_xn)
                    run("scalar", lambda e, XN=XN, j=j: e.activation(
                        out=SGN[j % 2], in_=XN, func=AF.Sign), inc="s")
                    s_sgn[j] = cnt["s"]
                    ewait("vector", "s", s_sgn[j])
                    vrun(lambda e, j=j: e.tensor_tensor(
                        out=QQB[j % 2], in0=SGN[j % 2], in1=SS128[:],
                        op=ALU.mult))
                    vrun(lambda e, j=j: e.tensor_reduce(
                        out=TSCALL[:, j * 64:(j + 1) * 64],
                        in_=QQB[j % 2].rearrange("p (t k) -> p t k", k=DS),
                        axis=AX.X, op=ALU.add))
                TS2v = TSCALL[:, jp * 64:(jp + 2) * 64]
                CS64 = MMIN[:, 0:16]
                CM8 = MM1[:, 0:2]; BC8 = MM1[:, 2:4]
                TM8 = MM1[:, 4:6]; TI8 = MM1[:, 6:8]
                vrun(lambda e, TS2v=TS2v, CS64=CS64: e.tensor_reduce(
                    out=CS64, in_=TS2v.rearrange("p (a c t) -> p (a c) t", t=8,
                                                 c=8), axis=AX.X, op=ALU.add))
                vrun(lambda e, CS64=CS64, CM8=CM8: e.tensor_reduce(
                    out=CM8, in_=CS64.rearrange("p (a c) -> p a c", c=8),
                    axis=AX.X, op=ALU.max))
                vrun(lambda e, CS64=CS64, CM8=CM8: e.tensor_tensor(
                    out=EQM[:, 0:16].rearrange("p (a c) -> p a c", c=8),
                    in0=CS64.rearrange("p (a c) -> p a c", c=8),
                    in1=CM8.unsqueeze(2).broadcast_to([128, 2, 8]),
                    op=ALU.is_equal))
                for a in range(2):
                    vrun(lambda e, a=a: e.select(
                        out=MMAX[:, a * 8:(a + 1) * 8],
                        mask=EQM[:, a * 8:(a + 1) * 8],
                        on_true=IOTA64[:, 0:8], on_false=BIGT[:, 0:8],
                        add_drain=True))
                vrun(lambda e, BC8=BC8: e.tensor_reduce(
                    out=BC8, in_=MMAX[:, 0:16].rearrange("p (a c) -> p a c", c=8),
                    axis=AX.X, op=ALU.min))
                vrun(lambda e, BC8=BC8: e.tensor_tensor(
                    out=MMB2.rearrange("p (a t) -> p a t", t=64),
                    in0=IOTAD8[:].unsqueeze(1).broadcast_to([128, 2, 64]),
                    in1=BC8.unsqueeze(2).broadcast_to([128, 2, 64]),
                    op=ALU.is_equal))
                vrun(lambda e, TS2v=TS2v: e.tensor_scalar(
                    out=MMB2, in0=MMB2, scalar1=65536.0, scalar2=None,
                    op0=ALU.mult))
                vrun(lambda e, TS2v=TS2v: e.tensor_tensor(
                    out=MMB2, in0=MMB2, in1=TS2v, op=ALU.add))
                vrun(lambda e, TM8=TM8: e.tensor_reduce(
                    out=TM8, in_=MMB2.rearrange("p (a t) -> p a t", t=64),
                    axis=AX.X, op=ALU.max))
                vrun(lambda e, TM8=TM8: e.tensor_tensor(
                    out=EQM[:, 0:128].rearrange("p (a t) -> p a t", t=64),
                    in0=MMB2.rearrange("p (a t) -> p a t", t=64),
                    in1=TM8.unsqueeze(2).broadcast_to([128, 2, 64]),
                    op=ALU.is_equal))
                for a in range(2):
                    vrun(lambda e, a=a: e.select(
                        out=MMB2[:, a * 64:(a + 1) * 64],
                        mask=EQM[:, a * 64:(a + 1) * 64],
                        on_true=IOTA64[:], on_false=BIGT[:], add_drain=True))
                vrun(lambda e, TI8=TI8: e.tensor_reduce(
                    out=TI8, in_=MMB2.rearrange("p (a t) -> p a t", t=64),
                    axis=AX.X, op=ALU.min))
                vrun(lambda e, TI8=TI8, jp=jp: e.tensor_tensor(
                    out=OHALL[:, jp * 64:(jp + 2) * 64].rearrange(
                        "p (a t) -> p a t", t=64),
                    in0=IOTA64[:].unsqueeze(1).broadcast_to([128, 2, 64]),
                    in1=TI8.unsqueeze(2).broadcast_to([128, 2, 64]),
                    op=ALU.is_equal))
                v_oh = cnt["v"]
                for j in (jp, jp + 1):
                    XN = XNv[j % 2]
                    OHj = OHALL[:, j * 64:(j + 1) * 64]
                    ewait("vector", "g", cnt["g"])
                    vrun(lambda e, XN=XN, OHj=OHj: e.tensor_tensor(
                        out=SCT[:].rearrange("p (a b) -> p a b", b=DS),
                        in0=XN.rearrange("p (a b) -> p a b", b=DS),
                        in1=OHj.unsqueeze(2).broadcast_to([128, 64, DS]),
                        op=ALU.mult))
                    vrun(lambda e, j=j: e.tensor_reduce(
                        out=XS[:, j * DS:(j + 1) * DS],
                        in_=SCT[:].rearrange("p (t k) -> p k t", k=DS),
                        axis=AX.X, op=ALU.add))
                ewait("tensor", "v", v_oh)
                for j in (jp, jp + 1):
                    OHj = OHALL[:, j * 64:(j + 1) * 64]
                    if jp or j != jp:
                        ewait("tensor", "v", cnt["v"])
                    run("tensor", lambda e, OHj=OHj: e.transpose(
                        PSD[0:64, 128:256], OHj, IDNF), inc="t")
                    ewait("vector", "t", cnt["t"])
                    vrun(lambda e, j=j: e.tensor_copy(
                        out=OHF[0:64, j * 128:(j + 1) * 128],
                        in_=PSD[0:64, 128:256]))
            vrun(lambda e: e.tensor_reduce(
                out=TKN[:, 0:8], in_=XS[:].rearrange("p (j k) -> p j k", k=DS),
                axis=AX.X, op=ALU.min))
            vrun(lambda e: e.tensor_reduce(
                out=TKN[:, 8:16], in_=XS[:].rearrange("p (j k) -> p j k", k=DS),
                axis=AX.X, op=ALU.max))
            MMB = SCT[:, 0:512]
            vrun(lambda e: e.tensor_scalar(out=TSC[:, 32:40], in0=TKN[:, 0:8],
                                           scalar1=-1024.0, scalar2=None,
                                           op0=ALU.add))
            vrun(lambda e: e.tensor_scalar(out=TSC[:, 40:48], in0=TKN[:, 8:16],
                                           scalar1=1024.0, scalar2=None,
                                           op0=ALU.add))
            vrun(lambda e, MMB=MMB: e.tensor_tensor(
                out=MMB.rearrange("p (j t) -> p j t", t=64),
                in0=OHALL[:].rearrange("p (j t) -> p j t", t=64),
                in1=TSC[:, 32:40].unsqueeze(2).broadcast_to([128, NJ, 64]),
                op=ALU.mult))
            vrun(lambda e, MMB=MMB: e.tensor_scalar(
                out=MMB, in0=MMB, scalar1=1024.0, scalar2=None, op0=ALU.add))
            vrun(lambda e, MMB=MMB: e.tensor_reduce(
                out=MMIN[:], in_=MMB.rearrange("p (j t) -> p t j", t=64),
                axis=AX.X, op=ALU.min))
            vrun(lambda e, MMB=MMB: e.tensor_tensor(
                out=MMB.rearrange("p (j t) -> p j t", t=64),
                in0=OHALL[:].rearrange("p (j t) -> p j t", t=64),
                in1=TSC[:, 40:48].unsqueeze(2).broadcast_to([128, NJ, 64]),
                op=ALU.mult))
            vrun(lambda e, MMB=MMB: e.tensor_scalar(
                out=MMB, in0=MMB, scalar1=-1024.0, scalar2=None, op0=ALU.add))
            vrun(lambda e, MMB=MMB: e.tensor_reduce(
                out=MMAX[:], in_=MMB.rearrange("p (j t) -> p t j", t=64),
                axis=AX.X, op=ALU.max))
            v_loopA = cnt["v"]


            ewait("tensor", "v", v_loopA)
            def _mmT(e):
                e.transpose(PSD[0:64, 128:256], MMIN[:], IDNF)
                return e.transpose(PSD[0:64, 256:384], MMAX[:], IDNF)
            run("tensor", _mmT, inc="t")
            ewait("vector", "t", cnt["t"])
            vrun(lambda e: e.tensor_copy(out=T64A[0:64, :], in_=PSD[0:64, 128:256]))
            vrun(lambda e: e.tensor_reduce(out=ARB[0:64, 0:1], in_=T64A[0:64, :],
                                           axis=AX.X, op=ALU.min))
            vrun(lambda e: e.tensor_copy(out=T64A[0:64, :], in_=PSD[0:64, 256:384]))
            vrun(lambda e: e.tensor_reduce(out=ARB[0:64, 1:2], in_=T64A[0:64, :],
                                           axis=AX.X, op=ALU.max))
            vrun(lambda e: e.tensor_scalar(out=ARB[0:64, 0:1], in0=ARB[0:64, 0:1],
                                           scalar1=-1.0, scalar2=None,
                                           op0=ALU.mult))
            ewait("gpsimd", "v", cnt["v"])
            edma("gpsimd", "dk", d_ar_in[ar_idx].ap(), ARB[0:64, 0:2])
            ewait("gpsimd", "dk")
            cnt["cc"] += 1
            on("gpsimd", lambda e, ar_idx=ar_idx: e.collective_compute(
                "AllReduce", ALU.max, replica_groups=[list(range(8))],
                ins=[d_ar_in[ar_idx].ap().opt()],
                outs=[d_ar_out[ar_idx].ap().opt()]).then_inc(sems["cc"], 1))
            ewait("tensor", "v", max(v_loopA, v_tbp))
            ev_j = {}
            for j in range(NJ):
                OHFj = OHF[0:NT, j * 128:(j + 1) * 128]
                h = (j % 2) * 512
                if j >= 2:
                    ewait("tensor", "v", ev_j[j - 2])
                run("tensor", lambda e, OHFj=OHFj, h=h: e.matmul(
                    PSA[:, h:h + 512], OHFj, TBP[0:NT, :], start=True,
                    stop=True))
                run("tensor", lambda e, OHFj=OHFj, h=h: e.matmul(
                    PSB[:, h:h + 512], OHFj, SL64[0:NT, :], start=True,
                    stop=True), inc="t")
                ewait("vector", "t", cnt["t"])
                def _tev(e, j=j, h=h):
                    e.tensor_copy(out=TBALL[:, j * 512:(j + 1) * 512],
                                  in_=PSA[:, h:h + 512])
                    return e.tensor_copy(out=TSALL[:, j * 512:(j + 1) * 512],
                                         in_=PSB[:, h:h + 512])
                run("vector", _tev, inc="v")
                ev_j[j] = cnt["v"]

            ewait("gpsimd", "cc")
            edma("gpsimd", "dk", ARB[0:64, 0:2], d_ar_out[ar_idx].ap())
            ewait("vector", "dk", cnt["dk"])
            vrun(lambda e: e.tensor_scalar(out=MISC[0:64, 0:1], in0=ARB[0:64, 0:1],
                                           scalar1=-1.0, scalar2=None,
                                           op0=ALU.mult))
            vrun(lambda e: e.tensor_copy(out=MISC[0:64, 1:2], in_=ARB[0:64, 1:2]))
            vrun(lambda e: e.tensor_copy(out=MISC[0:64, 2:3], in_=OSC[0:64, :]))
            v_misc = cnt["v"]
            ewait("tensor", "v", v_misc)
            for j in range(NJ):
                OHFj = OHF[0:NT, j * 128:(j + 1) * 128]
                run("tensor", lambda e, OHFj=OHFj, j=j: e.matmul(
                    PSD[:, j * 4:(j + 1) * 4], OHFj, MISC[0:NT, :],
                    start=True, stop=True), inc="t" if j == NJ - 1 else None)
            ewait("vector", "t", cnt["t"])
            vrun(lambda e: e.tensor_copy(out=MNXS[:, 0:32], in_=PSD[:, 0:32]))

            MNv = MNXS[:].rearrange("p (j c) -> p j c", c=4)[:, :, 0:1]
            MXv = MNXS[:].rearrange("p (j c) -> p j c", c=4)[:, :, 1:2]
            SCv = MNXS[:].rearrange("p (j c) -> p j c", c=4)[:, :, 2:3]
            XSv = XS[:].rearrange("p (j k) -> p j k", k=DS)
            R8 = TSC[:, 8:16]; INV8 = TSC[:, 16:24]; SCH8 = TSC[:, 24:32]
            vsync()
            vrun(lambda e: e.tensor_tensor(
                out=R8.unsqueeze(2), in0=MXv, in1=MNv, op=ALU.subtract))
            vrun(lambda e: e.tensor_scalar(out=R8, in0=R8, scalar1=1e-8,
                                           scalar2=None, op0=ALU.add))
            vrun(lambda e: e.reciprocal(out=INV8, in_=R8))
            vrun(lambda e: e.tensor_scalar(out=SCH8.unsqueeze(2), in0=SCv,
                                           scalar1=0.5, scalar2=None,
                                           op0=ALU.mult))
            SP1v = SCT[:, 0:256]
            SP2v = SCT[:, 256:512]
            BACv = SCT[:, 512:768]
            SACv = SCT[:, 768:1024]
            MGv = EQM[:, 0:64]
            vrun(lambda e: e.tensor_tensor(
                out=SP1v.rearrange("p (j k) -> p j k", k=DS), in0=XSv,
                in1=MNv.broadcast_to([128, NJ, DS]), op=ALU.subtract))
            vrun(lambda e: e.tensor_tensor(
                out=SP1v.rearrange("p (j k) -> p j k", k=DS),
                in0=SP1v.rearrange("p (j k) -> p j k", k=DS),
                in1=INV8.unsqueeze(2).broadcast_to([128, NJ, DS]), op=ALU.mult))
            vrun(lambda e: e.tensor_scalar(out=SP1v, in0=SP1v,
                                           scalar1=1.0 - 1e-6, scalar2=0.0,
                                           op0=ALU.min, op1=ALU.max))
            vrun(lambda e: e.tensor_scalar(out=SP1v, in0=SP1v,
                                           scalar1=float(GRID), scalar2=None,
                                           op0=ALU.mult))
            TBv = TBALL[:].rearrange("p (j k g) -> p j k g", k=DS, g=GRID)
            TSv = TSALL[:].rearrange("p (j k g) -> p j k g", k=DS, g=GRID)
            vrun(lambda e: e.tensor_copy(
                out=BACv.rearrange("p (j k) -> p j k", k=DS), in_=TBv[:, :, :, 0]))
            vrun(lambda e: e.tensor_copy(
                out=SACv.rearrange("p (j k) -> p j k", k=DS), in_=TSv[:, :, :, 0]))
            for g in range(1, GRID):
                vrun(lambda e, g=g: e.tensor_scalar(
                    out=MG256[:], in0=SP1v, scalar1=float(g), scalar2=None,
                    op0=ALU.is_ge))
                vrun(lambda e, g=g: e.select(
                    out=BACv.rearrange("p (j k) -> p j k", k=DS),
                    mask=MG256[:].rearrange("p (j k) -> p j k", k=DS),
                    on_true=TBv[:, :, :, g],
                    on_false=BACv.rearrange("p (j k) -> p j k", k=DS),
                    add_drain=True))
                vrun(lambda e, g=g: e.select(
                    out=SACv.rearrange("p (j k) -> p j k", k=DS),
                    mask=MG256[:].rearrange("p (j k) -> p j k", k=DS),
                    on_true=TSv[:, :, :, g],
                    on_false=SACv.rearrange("p (j k) -> p j k", k=DS),
                    add_drain=True))
            vrun(lambda e: e.tensor_tensor(out=SP2v, in0=SACv, in1=SP1v,
                                           op=ALU.mult))
            vrun(lambda e: e.tensor_tensor(out=SP2v, in0=SP2v, in1=BACv,
                                           op=ALU.add))
            vrun(lambda e: e.tensor_tensor(
                out=SP2v.rearrange("p (j k) -> p j k", k=DS),
                in0=SP2v.rearrange("p (j k) -> p j k", k=DS),
                in1=SCH8.unsqueeze(2).broadcast_to([128, NJ, DS]), op=ALU.mult))
            g_scat = 0
            for j in range(NJ):
                Xj = X[:, j * D:(j + 1) * D]
                OHj = OHALL[:, j * 64:(j + 1) * 64]
                if g_scat:
                    ewait("vector", "g", g_scat)
                vrun(lambda e, Xj=Xj, j=j: e.tensor_scalar(
                    out=Xj, in0=Xj, scalar1=AY[:, j:j + 1],
                    scalar2=CY[:, j:j + 1], op0=ALU.mult, op1=ALU.add))
                vrun(lambda e, OHj=OHj, j=j: e.tensor_tensor(
                    out=SCTB[:].rearrange("p (a b) -> p a b", b=DS),
                    in0=OHj.unsqueeze(2).broadcast_to([128, 64, DS]),
                    in1=SP2v.rearrange("p (j2 k) -> p j2 k", k=DS)[:, j, :]
                        .unsqueeze(1).broadcast_to([128, 64, DS]),
                    op=ALU.mult))
                ewait("gpsimd", "v", cnt["v"])
                run("gpsimd", lambda e, Xj=Xj: e.tensor_tensor(
                    out=Xj, in0=Xj, in1=SCTB[:], op=ALU.add), inc="g")
                g_scat = cnt["g"]
                if scatter_hook is not None:
                    scatter_hook(j, g_scat)
            ewait("vector", "g", cnt["g"])
            ewait("scalar", "g", cnt["g"])
            ewait("tensor", "g", cnt["g"])

        _pre_dw_map = {}

        def attention():
            dw_cc = dict(_pre_dw_map)
            def wqkv_load(hf, cc):
                s_ = "dwa" if cc % 2 == 0 else "dwb"
                edma("gpsimd", s_, WBUF[cc % 2][:].rearrange(
                    "p (dc c) -> p dc c", c=128),
                    d_wqkv.rearrange("(dc p) c -> p dc c", p=128)[:, :, cc * 128:(cc + 1) * 128])
                dw_cc[(hf, cc)] = (s_, cnt[s_])
            CC0 = [12, 13, 14, 15, 16, 17, 4, 5, 6, 7, 8, 9, 10, 11, 0, 1, 2, 3]
            seq = [(0, c) for c in CC0] + [(1, c) for c in range(CCQ)]
            s_x3f = 0
            s_evac_prev = [0, 0]
            s_qkv_ev4 = [0, 0, 0, 0]
            for hf in range(2):
                if hf == 0:
                    v_xn3 = front["v_xn3"]
                else:
                    ewait("scalar", "g", front["g_last"])
                    ewait("vector", "g", front["g_last"])
                    for j in range(4, NJ):
                        run("scalar", lambda e, j=j: e.activation(
                            out=SGNT, in_=X[:, j * D:(j + 1) * D],
                            func=AF.Square, accum_out=SQS[:, j:j + 1]),
                            inc="s")
                        vrun(lambda e, j=j: e.reduce_sum(
                            out=TKN[:, j:j + 1], in_=X[:, j * D:(j + 1) * D],
                            axis=AX.X))
                    ewait("vector", "s", cnt["s"])
                    _ln3_fold(4)
                    ewait("vector", "t", cnt["t"])
                    for j4 in range(4):
                        j = 4 + j4
                        vrun(lambda e, j=j, j4=j4: e.tensor_scalar(
                            out=XN3B[0][:, j4 * 2048:(j4 + 1) * 2048],
                            in0=X[:, j * D:(j + 1) * D],
                            scalar1=A3[:, j:j + 1], scalar2=C3[:, j:j + 1],
                            op0=ALU.mult, op1=ALU.add))
                    v_xn3 = cnt["v"]
                X3F = XN3B[1]
                ewait("tensor", "v", v_xn3)
                for dc in range(16):
                    if s_evac_prev[0]:
                        ewait("tensor", "s", s_evac_prev[0])
                    for j4 in range(4):
                        run("tensor", lambda e, dc=dc, j4=j4: e.transpose(
                            PSBK[:, j4 * 128:(j4 + 1) * 128],
                            XN3B[0][:, j4 * 2048 + dc * 128: j4 * 2048 + (dc + 1) * 128],
                            IDNB[:]), inc="t" if j4 == 3 else None)
                    t_tp = cnt["t"]
                    ewait("scalar", "t", t_tp)
                    run("scalar", lambda e, dc=dc, X3F=X3F: e.copy(
                        out=X3F[:, dc * 512:(dc + 1) * 512],
                        in_=PSBK[:, 0:512]), inc="s")
                    s_evac_prev[0] = cnt["s"]
                s_x3f = cnt["s"]
                order = CC0 if hf == 0 else list(range(CCQ))
                gate_v = False
                gate_g = False
                for ci, cc in enumerate(order):
                    ewait("tensor", dw_cc[(hf, cc)][0], dw_cc[(hf, cc)][1])
                    ewait("tensor", "s", s_x3f)
                    qs = ci % 4
                    STR4 = (PSA, PSA, PSC, PSC)[qs]
                    qh = (qs % 2) * 512
                    if s_qkv_ev4[qs]:
                        ewait("tensor", "s", s_qkv_ev4[qs])
                    for dc in range(16):
                        run("tensor", lambda e, cc=cc, dc=dc, STR4=STR4, qh=qh,
                            X3F=X3F: e.matmul(
                                STR4[:, qh:qh + 512],
                                WBUF[cc % 2][:, dc * 128:(dc + 1) * 128],
                                X3F[:, dc * 512:(dc + 1) * 512],
                                start=(dc == 0), stop=(dc == 15)),
                            inc="t" if dc == 15 else None)
                    t_qkv = cnt["t"]
                    ewait("scalar", "t", t_qkv)
                    if hf == 0 and cc in (8, 9, 10, 11) and not gate_g:
                        ewait("scalar", "g", front["g_last"])
                        gate_g = True
                    if hf == 0 and cc in (0, 1, 2, 3) and not gate_v:
                        ewait("scalar", "v", front["v_scat"])
                        gate_v = True
                    run("scalar", lambda e, cc=cc, hf=hf, STR4=STR4, qh=qh:
                        e.activation(
                            out=QKVB[:, cc * 1024 + hf * 512: cc * 1024 + hf * 512 + 512],
                            in_=STR4[:, qh:qh + 512], func=AF.Identity,
                            bias=BQ[:, cc:cc + 1], scale=1.0), inc="s")
                    s_qkv_ev4[qs] = cnt["s"]
                    gi = seq.index((hf, cc))
                    if gi + 2 < len(seq) and seq[gi + 2] not in dw_cc:
                        ewait("gpsimd", "t", t_qkv)
                        wqkv_load(*seq[gi + 2])
            s_qkv_done = cnt["s"]

            ewait("tensor", "s", s_qkv_done)
            v_vto = 0
            for j in range(NJ):
                if v_vto:
                    ewait("tensor", "v", v_vto)
                run("tensor", lambda e, j=j: e.transpose(
                    PSBK[:, 0:128],
                    QKVB[:, 17 * 1024 + j * 128: 17 * 1024 + (j + 1) * 128],
                    IDNB[:]), inc="t")
                ewait("vector", "t", cnt["t"])
                run("vector", lambda e, j=j: e.tensor_copy(
                    out=VTO[:, j * 128:(j + 1) * 128],
                    in_=PSBK[:, 0:128]), inc="v")
                v_vto = cnt["v"]
            ewait("gpsimd", "v", v_vto)
            ewait("gpsimd", "s", s_qkv_done)
            edma("gpsimd", "dk", d_kv_in.ap()[0:128, :], QKVB[:, 16 * 1024:17 * 1024])
            edma("gpsimd", "dk", d_kv_in.ap()[128:256, :], VTO[:])
            ewait("gpsimd", "dk")
            cnt["cc"] += 1
            on("gpsimd", lambda e: e.collective_compute(
                "AllGather", ALU.bypass,
                replica_groups=[[0, 1, 2, 3], [4, 5, 6, 7]],
                ins=[d_kv_in.ap().opt()],
                outs=[d_kv_out.ap().opt()]).then_inc(sems["cc"], 1))
            ewait("gpsimd", "cc")
            kvo = d_kv_out.ap().rearrange("(r b p) t -> r b p t", b=2, p=128)
            edma("gpsimd", "dk",
                 KT[:].rearrange("p (s r q) -> p s r q", r=4, q=128),
                 kvo[:, 0, :, :].rearrange("r p (s q) -> p s r q", q=128))
            edma("gpsimd", "dk",
                 VT[:].rearrange("p (s r q) -> p s r q", r=4, q=128),
                 kvo[:, 1, :, :].rearrange("r p (s q) -> p s r q", q=128))
            dk_kv = cnt["dk"]

            v_qsq_prev = 0
            for dc in range(16):
                ewait("vector", "t", cnt["t"])
                vrun(lambda e, dc=dc: e.tensor_tensor(
                    out=QSQ[:], in0=QKVB[:, dc * 1024:(dc + 1) * 1024],
                    in1=QKVB[:, dc * 1024:(dc + 1) * 1024], op=ALU.mult))
                ewait("tensor", "v", cnt["v"])
                run("tensor", lambda e, dc=dc: e.matmul(
                    PSD[0:2, 0:512], ONEB2[:], QSQ[:, 0:512],
                    start=(dc == 0), stop=(dc == 15)))
                run("tensor", lambda e, dc=dc: e.matmul(
                    PSD[32:34, 0:512], ONEB2[:], QSQ[:, 512:1024],
                    start=(dc == 0), stop=(dc == 15)), inc="t")
            t_qn = cnt["t"]
            ewait("vector", "t", t_qn)
            def _qmax(e):
                e.tensor_reduce(out=KQM[0:1, 0:1], in_=PSD[0:1, 0:512], axis=AX.X,
                                op=ALU.max)
                return e.tensor_reduce(out=KQM[0:1, 1:2], in_=PSD[32:33, 0:512],
                                       axis=AX.X, op=ALU.max)
            run("vector", _qmax, inc="v")
            vrun(lambda e: e.tensor_tensor(
                out=KQM[0:1, 0:1], in0=KQM[0:1, 0:1], in1=KQM[0:1, 1:2],
                op=ALU.max))
            ewait("vector", "dk", dk_kv)
            for kc in range(4):
                vrun(lambda e, kc=kc: e.tensor_tensor(
                    out=QSQ[:], in0=KT[:, kc * 1024:(kc + 1) * 1024],
                    in1=KT[:, kc * 1024:(kc + 1) * 1024], op=ALU.mult))
                ewait("tensor", "v", cnt["v"])
                if kc > 0:
                    ewait("tensor", "v", v_kn)
                run("tensor", lambda e: e.matmul(
                    PSD[0:2, 0:512], ONEB2[:], QSQ[:, 0:512], start=True,
                    stop=True))
                run("tensor", lambda e: e.matmul(
                    PSD[32:34, 0:512], ONEB2[:], QSQ[:, 512:1024], start=True,
                    stop=True), inc="t")
                ewait("vector", "t", cnt["t"])
                def _kmaxA(e):
                    e.tensor_reduce(out=KQM[0:1, 2:3], in_=PSD[0:1, 0:512],
                                    axis=AX.X, op=ALU.max)
                    return e.tensor_reduce(out=KQM[0:1, 3:4], in_=PSD[32:33, 0:512],
                                           axis=AX.X, op=ALU.max)
                run("vector", _kmaxA, inc="v")
                vsync()
                vrun(lambda e: e.tensor_tensor(out=KQM[0:1, 2:3],
                                               in0=KQM[0:1, 2:3],
                                               in1=KQM[0:1, 3:4], op=ALU.max))
                if kc == 0:
                    vrun(lambda e: e.tensor_copy(out=KQM[0:1, 4:5],
                                                 in_=KQM[0:1, 2:3]))
                else:
                    vrun(lambda e: e.tensor_tensor(out=KQM[0:1, 4:5],
                                                   in0=KQM[0:1, 4:5],
                                                   in1=KQM[0:1, 2:3],
                                                   op=ALU.max))
                v_kn = cnt["v"]
            run("vector", lambda e: e.tensor_tensor(
                out=KQM[0:1, 5:6], in0=KQM[0:1, 0:1], in1=KQM[0:1, 4:5],
                op=ALU.mult), inc="v")
            ewait("scalar", "v", cnt["v"])
            run("scalar", lambda e: e.activation(out=KQM[0:1, 5:6],
                                                 in_=KQM[0:1, 5:6], func=AF.Sqrt),
                inc="s")
            ewait("vector", "s", cnt["s"])
            vrun(lambda e: e.tensor_scalar(
                out=KQM[0:1, 6:7], in0=KQM[0:1, 5:6], scalar1=-SCALE,
                scalar2=EXPSHIFT, op0=ALU.mult, op1=ALU.add))
            ewait("tensor", "v", cnt["v"])
            run("tensor", lambda e: e.matmul(
                PSD[:, 384:385], ONE1[0:1, :], KQM[0:1, 6:7], start=True,
                stop=True), inc="t")
            ewait("vector", "t", cnt["t"])
            run("vector", lambda e: e.tensor_copy(out=BB[:], in_=PSD[:, 384:385]),
                inc="v")
            v_bb = cnt["v"]

            def emit_pv(kt, EJ):
                buf = kt % 2
                kind, val = cons_done[buf]
                ewait("tensor", kind, val)
                if kt == 0 and v_aob:
                    ewait("tensor", "v", v_aob)
                for q4 in range(2):
                    run("tensor", lambda e, q4=q4, kt=kt, buf=buf, EJ=EJ:
                        e.matmul(
                            PSB[:, q4 * 512:(q4 + 1) * 512],
                            VT[:, kt * 128:(kt + 1) * 128],
                            PT[buf][:, q4 * 512:(q4 + 1) * 512],
                            start=(kt == 0), stop=(kt == EJ - 1)))
                run("tensor", lambda e, kt=kt, buf=buf, EJ=EJ: e.matmul(
                    PSD[0:2, 0:512], ONEB2[:], PT[buf][:, 0:512],
                    start=(kt == 0), stop=(kt == EJ - 1)))
                run("tensor", lambda e, kt=kt, buf=buf, EJ=EJ: e.matmul(
                    PSD[32:34, 0:512], ONEB2[:], PT[buf][:, 512:1024],
                    start=(kt == 0), stop=(kt == EJ - 1)), inc="t")
                pv_done[kt % 2] = cnt["t"]

            ewait("tensor", "dk", dk_kv)
            ewait("tensor", "v", v_bb)
            ewait("scalar", "v", v_bb)
            SSTRIP = [PSA, PSC]
            exp_done = {}
            cons_done = {}
            pv_done = {}
            v_aob = 0
            v_rb = 0
            for j in range(NJ):
                EJ = 4 * (j + 1)
                for hf in range(2):
                    pending = None
                    for kt in range(EJ):
                        buf = kt % 2
                        strip = SSTRIP[buf]
                        prev = exp_done.get(buf)
                        if prev:
                            ewait("tensor", "s", prev)
                        if v_rb:
                            ewait("tensor", "v", v_rb)
                            v_rb = 0
                        for q4 in range(2):
                            h0 = hf * 8 + q4 * 4
                            rhs = QKVB[:].rearrange(
                                "p (c t) -> p c t", t=1024)[
                                :, h0:h0 + 4, j * 128:(j + 1) * 128]
                            run("tensor", lambda e, q4=q4, kt=kt, strip=strip,
                                rhs=rhs: e.matmul(
                                    strip[:, q4 * 512:(q4 + 1) * 512],
                                    KT[:, kt * 128:(kt + 1) * 128], rhs,
                                    start=True, stop=True),
                                inc="t" if q4 == 1 else None)
                        t_sc = cnt["t"]
                        ewait("scalar", "t", t_sc)
                        prevpv = pv_done.get(buf)
                        if prevpv:
                            ewait("scalar", "t", prevpv)
                        run("scalar", lambda e, buf=buf, strip=strip: e.activation(
                            out=PT[buf][:], in_=strip[:], func=AF.Exp,
                            bias=BB[:], scale=SCALE), inc="s")
                        exp_done[buf] = cnt["s"]
                        if kt >= 4 * j:
                            ewait("vector", "s", exp_done[buf])
                            run("vector", lambda e, buf=buf, kt=kt: e.tensor_tensor(
                                out=PT[buf][:].rearrange("p (h q) -> p h q", q=128),
                                in0=PT[buf][:].rearrange("p (h q) -> p h q", q=128),
                                in1=MSK[:, kt * 128:(kt + 1) * 128].unsqueeze(1)
                                    .broadcast_to([128, 8, 128]),
                                op=ALU.mult))
                            cons_done[buf] = ("v", cnt["v"])
                        else:
                            cons_done[buf] = ("s", exp_done[buf])
                        if pending is not None:
                            emit_pv(pending, EJ)
                        pending = kt
                    emit_pv(pending, EJ)
                    pending = None
                    t_last = cnt["t"]
                    ewait("scalar", "t", t_last)
                    run("scalar", lambda e: e.copy(out=SREC[0],
                                                   in_=PSD[0:1, 0:512]))
                    run("scalar", lambda e: e.copy(out=SREC[1],
                                                   in_=PSD[32:33, 0:512]),
                        inc="s")
                    ewait("vector", "s", cnt["s"])
                    vrun(lambda e: e.reciprocal(out=QSQF[0:33, 0:512],
                                                in_=QSQF[0:33, 0:512]))
                    v_recd = cnt["v"]
                    ewait("tensor", "v", v_recd)
                    for q in range(2):
                        qp = 0 if q == 0 else 32
                        run("tensor", lambda e, q=q, qp=qp: e.matmul(
                            PSA[:, q * 512:(q + 1) * 512], ONE1[qp:qp + 1, :],
                            SREC[q], start=True, stop=True),
                            inc="t" if q == 1 else None)
                    t_rb = cnt["t"]
                    ewait("vector", "t", t_rb)
                    run("vector", lambda e: e.tensor_copy(out=RB[:],
                                                          in_=PSA[:, 0:1024]))
                    v_rb = cnt["v"]
                    run("vector", lambda e, j=j, hf=hf: e.tensor_tensor(
                        out=AOB[:, j * 2048 + hf * 1024: j * 2048 + (hf + 1) * 1024],
                        in0=PSB[:], in1=RB[:], op=ALU.mult))
                    v_aob = cnt["v"]
                    exp_done = {}
                    pv_done = {}
                    cons_done = {}

            dwp = {}
            def wp_load(wq):
                s_ = "dwa" if wq % 2 == 0 else "dwb"
                edma("gpsimd", s_, WPB[wq % 2][:].rearrange(
                    "p (dc c) -> p dc c", c=512),
                    d_wproj.rearrange("(dc p) c -> p dc c", p=128)[:, :, wq * 512:(wq + 1) * 512])
                dwp[wq] = (s_, cnt[s_])
            ewait("gpsimd", "v", v_aob)
            wp_load(0)
            wp_load(1)
            v_pe = [0, 0]
            t_wq_last = 0
            s_f2stats = 0
            for wq in range(4):
                if wq >= 1 and wq + 1 < 4 and (wq + 1) not in dwp:
                    pass
                ewait("tensor", dwp[wq][0], dwp[wq][1])
                ewait("tensor", "v", v_aob)
                for j in range(NJ):
                    half = j % 2
                    if v_pe[half]:
                        ewait("tensor", "v", v_pe[half])
                    for dc in range(16):
                        run("tensor", lambda e, wq=wq, j=j, dc=dc, half=half:
                            e.matmul(
                                PSA[:, half * 512:(half + 1) * 512],
                                AOB[:, j * 2048 + dc * 128: j * 2048 + (dc + 1) * 128],
                                WPB[wq % 2][:, dc * 512:(dc + 1) * 512],
                                start=(dc == 0), stop=False))
                    run("tensor", lambda e, wq=wq, half=half: e.matmul(
                        PSA[:, half * 512:(half + 1) * 512], ONEBR[:],
                        BPR[0:1, wq * 512:(wq + 1) * 512],
                        start=False, stop=True), inc="t")
                    t_pp = cnt["t"]
                    ewait("vector", "t", t_pp)
                    run("vector", lambda e, wq=wq, j=j, half=half: e.tensor_tensor(
                        out=X[:, j * D + wq * 512: j * D + (wq + 1) * 512],
                        in0=X[:, j * D + wq * 512: j * D + (wq + 1) * 512],
                        in1=PSA[:, half * 512:(half + 1) * 512], op=ALU.add),
                        inc="v")
                    v_pe[half] = cnt["v"]
                    if wq == 3:
                        ewait("scalar", "v", cnt["v"])
                        run("scalar", lambda e, j=j: e.activation(
                            out=SGNT, in_=X[:, j * D:(j + 1) * D],
                            func=AF.Square, accum_out=SQS[:, j:j + 1]),
                            inc="s")
                        vrun(lambda e, j=j: e.reduce_sum(
                            out=STAT[:, j:j + 1], in_=X[:, j * D:(j + 1) * D],
                            axis=AX.X))
                        s_f2stats = cnt["s"]
                t_wq_last = cnt["t"]
                if wq + 2 < 4:
                    ewait("gpsimd", "t", t_wq_last)
                    wp_load(wq + 2)
            ewait("vector", "v", cnt["v"])
            ewait("scalar", "v", cnt["v"])
            ewait("tensor", "v", cnt["v"])
            ewait("gpsimd", "v", cnt["v"])

        ewait("vector", "v", max(v_gb, v_ss))
        _pre_dw = {}
        for _cc in (12, 13):
            _s = "dwa" if _cc % 2 == 0 else "dwb"
            edma("gpsimd", _s, WBUF[_cc % 2][:].rearrange(
                "p (dc c) -> p dc c", c=128),
                d_wqkv.rearrange("(dc p) c -> p dc c", p=128)[
                    :, :, _cc * 128:(_cc + 1) * 128])
            _pre_dw[_cc] = (_s, cnt[_s])
        for _cc in (12, 13):
            _pre_dw_map[(0, _cc)] = _pre_dw[_cc]

        A3 = AFC[:, 0:8]
        C3 = AFC[:, 8:16]
        SGNT = VTRB[:].bitcast(BF16)

        def _ln3_fold(b):
            vrun(lambda e: e.tensor_scalar(
                out=A3[:, b:b + 4], in0=SQS[:, b:b + 4], scalar1=1.0 / D,
                scalar2=None, op0=ALU.mult))
            vrun(lambda e: e.tensor_scalar(
                out=C3[:, b:b + 4], in0=TKN[:, b:b + 4], scalar1=1.0 / D,
                scalar2=None, op0=ALU.mult))
            vrun(lambda e: e.tensor_tensor(
                out=AFC[:, 16 + b:20 + b], in0=C3[:, b:b + 4],
                in1=C3[:, b:b + 4], op=ALU.mult))
            vrun(lambda e: e.tensor_tensor(
                out=A3[:, b:b + 4], in0=A3[:, b:b + 4],
                in1=AFC[:, 16 + b:20 + b], op=ALU.subtract))
            vrun(lambda e: e.tensor_scalar(
                out=A3[:, b:b + 4], in0=A3[:, b:b + 4], scalar1=LNEPS,
                scalar2=None, op0=ALU.add))
            vrun(lambda e: e.reciprocal(out=A3[:, b:b + 4], in_=A3[:, b:b + 4]))
            ewait("scalar", "v", cnt["v"])
            run("scalar", lambda e: e.activation(
                out=A3[:, b:b + 4], in_=A3[:, b:b + 4], func=AF.Sqrt), inc="s")
            ewait("vector", "s", cnt["s"])
            vrun(lambda e: e.tensor_tensor(
                out=C3[:, b:b + 4], in0=C3[:, b:b + 4], in1=A3[:, b:b + 4],
                op=ALU.mult))
            vrun(lambda e: e.tensor_scalar(
                out=C3[:, b:b + 4], in0=C3[:, b:b + 4], scalar1=-1.0,
                scalar2=None, op0=ALU.mult))

        front = {}

        def _front_hook(j, gv):
            if j <= 3:
                ewait("scalar", "g", gv)
                run("scalar", lambda e, j=j: e.activation(
                    out=SGNT, in_=X[:, j * D:(j + 1) * D], func=AF.Square,
                    accum_out=SQS[:, j:j + 1]), inc="s")
                ewait("vector", "g", gv)
                vrun(lambda e, j=j: e.reduce_sum(
                    out=TKN[:, j:j + 1], in_=X[:, j * D:(j + 1) * D],
                    axis=AX.X))
            if j == 3:
                ewait("vector", "s", cnt["s"])
                _ln3_fold(0)
            if j >= 4:
                j4 = j - 4
                vrun(lambda e, j4=j4: e.tensor_scalar(
                    out=XN3B[0][:, j4 * 2048:(j4 + 1) * 2048],
                    in0=X[:, j4 * D:(j4 + 1) * D],
                    scalar1=A3[:, j4:j4 + 1], scalar2=C3[:, j4:j4 + 1],
                    op0=ALU.mult, op1=ALU.add))
            if j == 7:
                front["v_xn3"] = cnt["v"]
                front["v_scat"] = cnt["v"]

        ffn(0, scatter_hook=_front_hook)
        front["g_last"] = cnt["g"]
        if stop >= 2:
            attention()
        def _store_hook(j, gv):
            ewait("sync", "g", gv)
            edma("sync", "so",
                 d_y.rearrange("(j2 p) d -> p j2 d", p=128)[:, j, :],
                 X[:, j * D:(j + 1) * D])

        if stop >= 3:
            ffn(1, scatter_hook=_store_hook, stats_done=1)

        ewait("sync", "v", cnt["v"])
        ewait("sync", "g", cnt["g"])
        ewait("sync", "s", cnt["s"])
        ewait("sync", "t", cnt["t"])
        if stop < 3:
            edma("sync", "so", d_y.rearrange("(j p) d -> p j d", p=128),
                 X[:].rearrange("p (j d) -> p j d", d=D))
        if _dbg():
            edma("sync", "so", dbg["dSTAT"], STAT[:])
            edma("sync", "so", dbg["dAFC"], AFC[:])
            edma("sync", "so", dbg["dOHALL"], OHALL[:])
            edma("sync", "so", dbg["dXS"], XS[:])
            edma("sync", "so", dbg["dMNXS"], MNXS[:])
            edma("sync", "so", dbg["dARB"], ARB[:, 0:2])
            edma("sync", "so", dbg["dTKN"], TKN[:])
            edma("gpsimd", "so", dbg["dSS"], SS128[:])
            edma("sync", "so", dbg["dGAM"], GAM[:])
            edma("sync", "so", dbg["dBET"], BET[:])
            edma("sync", "so", dbg["dDBG1"], DBG1[:])
            edma("sync", "so", dbg["dSQS"], SQS[:])
        ewait("sync", "so")

        block = est.enter_context(nc.Block())

        @block.sync
        def _(e):
            for f in ops["sync"]:
                f(e)

        @block.vector
        def _(e):
            for f in ops["vector"]:
                f(e)

        @block.scalar
        def _(e):
            for f in ops["scalar"]:
                f(e)

        @block.tensor
        def _(e):
            for f in ops["tensor"]:
                f(e)

        @block.gpsimd
        def _(e):
            for f in ops["gpsimd"]:
                f(e)

    return nc



def _host_constf():
    cf = np.zeros((128, 256), np.float32)
    cf[:, 0:64] = np.arange(64, dtype=np.float32)[None, :]
    cf[:, 64:128] = (np.arange(64, dtype=np.float32) // 8)[None, :]
    cf[:, 128:256] = np.eye(128, dtype=np.float32)
    return cf


def _host_masks(cid):
    m = np.zeros((NKT, QT, QT), np.float32)
    for kt in range(NKT):
        g = 4 * (kt // 4) + cid
        kpos = kt * QT + np.arange(QT)[:, None]
        qpos = g * QT + np.arange(QT)[None, :]
        m[kt] = (kpos <= qpos).astype(np.float32)
    return m.reshape(NKT * QT, QT).astype(ml_dtypes.bfloat16)


def kernel(L0_attn_x, L0_attn_W_qkv, L0_attn_b_qkv, L0_attn_W_proj,
           L0_attn_b_proj, L0_ffn_ln_gamma, L0_ffn_ln_beta,
           L0_ffn_spline_bases, L0_ffn_spline_slopes, L0_ffn_output_scale):
    x = np.ascontiguousarray(np.asarray(L0_attn_x, dtype=np.float32))
    wqkv = np.ascontiguousarray(np.asarray(L0_attn_W_qkv, dtype=np.float32))
    bqkv = np.ascontiguousarray(np.asarray(L0_attn_b_qkv, dtype=np.float32))
    wproj = np.ascontiguousarray(np.asarray(L0_attn_W_proj, dtype=np.float32))
    bproj = np.ascontiguousarray(np.asarray(L0_attn_b_proj, dtype=np.float32)).reshape(1, D)
    gamma = np.ascontiguousarray(np.asarray(L0_ffn_ln_gamma, dtype=np.float32)).reshape(1, D)
    beta = np.ascontiguousarray(np.asarray(L0_ffn_ln_beta, dtype=np.float32)).reshape(1, D)
    bases = np.ascontiguousarray(np.asarray(L0_ffn_spline_bases, dtype=np.float32)).reshape(NT, DS * GRID)
    slopes = np.ascontiguousarray(np.asarray(L0_ffn_spline_slopes, dtype=np.float32)).reshape(NT, DS * GRID)
    oscale = np.ascontiguousarray(np.asarray(L0_ffn_output_scale, dtype=np.float32)).reshape(NT, 1)

    cf = _host_constf()
    idn = np.eye(128, dtype=ml_dtypes.bfloat16)

    nc = build_nc(stop=_BUILD_STOP)
    in_maps = []
    for core in range(8):
        batch, cid = core // 4, core % 4
        xs = np.concatenate(
            [x[batch, (4 * j + cid) * QT:(4 * j + cid + 1) * QT, :]
             for j in range(NJ)], axis=0)
        in_maps.append({
            "x": np.ascontiguousarray(xs),
            "wqkv": wqkv, "bqkv": bqkv, "wproj": wproj, "bproj": bproj,
            "gamma": gamma, "beta": beta, "bases": bases, "slopes": slopes,
            "oscale": oscale, "msk": _host_masks(cid), "constf": cf,
            "idnb": idn,
        })
    res = run_bass_kernel_spmd(nc, in_maps, core_ids=list(range(8)))
    out = np.empty((B, S, D), np.float32)
    for core in range(8):
        batch, cid = core // 4, core % 4
        y = res.results[core]["y"]
        for j in range(NJ):
            g = 4 * j + cid
            out[batch, g * QT:(g + 1) * QT, :] = y[j * QT:(j + 1) * QT, :]
    return out


_BUILD_STOP = 3
